# version 12
# baseline (speedup 1.0000x reference)
import numpy as np
from contextlib import ExitStack

import concourse.bass as bass
import concourse.mybir as mybir

F32 = mybir.dt.float32
BF16 = mybir.dt.bfloat16
I32 = mybir.dt.int32
AF = mybir.ActivationFunctionType
ALU = mybir.AluOpType
AX = mybir.AxisListType

N = 4096
D = 1024
KC = 8
NB = 8
NT = 32
H, HD = 16, 64
G, HG = 4, 4
HID, HCN = 3072, 24
EPS = 1e-5
VW = HG * (HD + 1)
SUBPH = 99
SELF_FENCE = True
GLIM = 99

ENGINES = ("sync", "tensor", "vector", "scalar", "gpsimd")


class Prog:
    def __init__(self, nc, stack):
        self.nc = nc
        self.stack = stack
        self.ops = {e: [] for e in ENGINES}
        self.sems = {}
        self.waited = {e: {} for e in ENGINES}
        self.bank = {}
        self.slot = {}
        self.semgate = {}

    def sem(self, name):
        if name not in self.sems:
            h = self.stack.enter_context(self.nc.semaphore(name))
            self.sems[name] = [h, 0]
        return name

    def cnt(self, name):
        return self.sems[name][1]

    def _wait(self, eng, s, v):
        if v <= 0:
            return
        w = self.waited[eng]
        if w.get(s, 0) >= v:
            return
        w[s] = v
        h = self.sems[s][0]
        self.ops[eng].append(lambda e, h=h, v=v: e.wait_ge(h, v))

    def do(self, eng, fn, waits=(), inc=None, n=1):
        if SELF_FENCE:
            if eng == "vector":
                self._wait(eng, "V", self.cnt("V"))
            elif eng == "scalar":
                self._wait(eng, "S", self.cnt("S"))
        for (s, v) in waits:
            self._wait(eng, s, v)
        if inc is not None:
            self.sem(inc)
            self.sems[inc][1] += n
            h = self.sems[inc][0]
            self.ops[eng].append(lambda e, fn=fn, h=h, n=n: fn(e).then_inc(h, n))
            if inc in ("T", "V", "S"):
                for (s, v) in waits:
                    if s.startswith("l"):
                        self.semgate[s] = (inc, self.sems[inc][1])
            return self.sems[inc][1]
        self.ops[eng].append(lambda e, fn=fn: fn(e))
        return None

    def load(self, dst, src, sem, waits=(), eng="sync", **kw):
        g = self.semgate.get(sem)
        allw = list(waits) + ([g] if g else [])
        return self.do(eng, lambda e, dst=dst, src=src, kw=kw: e.dma_start(out=dst, in_=src, **kw),
                       waits=allw, inc=sem, n=16)

    def store(self, dst, src, sem, waits=(), eng="scalar", **kw):
        return self.do(eng, lambda e, dst=dst, src=src, kw=kw: e.dma_start(out=dst, in_=src, **kw),
                       waits=waits, inc=sem, n=16)

    def bank_waits(self, name):
        return list(self.bank.get(name, ()))

    def bank_consumed(self, name, sem, val):
        self.bank[name] = [(sem, val)]

    def bank_consumed_add(self, name, sem, val):
        self.bank.setdefault(name, []).append((sem, val))

    def slot_waits(self, key):
        return list(self.slot.get(key, ()))

    def slot_mark(self, key, sem, val):
        self.slot[key] = [(sem, val)]

    def slot_mark_add(self, key, sem, val):
        self.slot.setdefault(key, []).append((sem, val))

    def mm_group(self, out_ap, pairs, waits=(), bankname=None, first_start=True,
                 last_stop=True, inc="T"):
        n = len(pairs)
        allw = list(waits)
        if bankname is not None and first_start:
            allw += self.bank_waits(bankname)
        r = None
        for i, (l, rr) in enumerate(pairs):
            st = first_start and (i == 0)
            sp = last_stop and (i == n - 1)
            r = self.do("tensor",
                        lambda e, l=l, rr=rr, st=st, sp=sp, o=out_ap: e.matmul(o, l, rr, start=st, stop=sp),
                        waits=allw if i == 0 else (),
                        inc=inc if (i == n - 1 and inc) else None)
        if r is not None:
            for (s, v) in allw:
                if s.startswith("l"):
                    self.semgate[s] = ("T", r)
        return r

    def check(self):
        for e in ENGINES:
            for s, v in self.waited[e].items():
                assert v <= self.sems[s][1], f"wait {s}>={v} never reached (max {self.sems[s][1]})"


def _recip_bf16(e, out, in_):
    with e.bass.allow_low_precision(reason="softmax 1/Z feeds a bf16 matmul operand"):
        return e.reciprocal(out, in_)


def build(which="full", dbg=(), upto=99):
    nc = bass.Bass("TRN2", target_bir_lowering=False)
    stack = ExitStack()
    T = {}

    def dram(name, shape, dt, kind):
        T[name] = nc.dram_tensor(name, list(shape), dt, kind=kind).ap()
        return T[name]

    emit_l0 = which in ("l0", "full")
    emit_l1 = which in ("l1", "full")

    if emit_l0:
        dram("x", (D, N), F32, "ExternalInput")
        for nm in ("wq", "wk", "wo"):
            dram(nm, (D, D), F32, "ExternalInput")
        dram("wv", (D, H * (HD + 1)), F32, "ExternalInput")
        for nm in ("bq", "bk", "bo"):
            dram(nm, (128, KC), F32, "ExternalInput")
        dram("bv", (1, H * (HD + 1)), F32, "ExternalInput")
        dram("wg", (HID, D), I32, "ExternalInput")
        dram("wu", (HID, D), I32, "ExternalInput")
        dram("wd", (D, HID), I32, "ExternalInput")
        dram("masks", (4, 128, 512), F32, "ExternalInput")
        dram("wgT", (HCN, D, 128), BF16, "Internal")
        dram("wuT", (HCN, D, 128), BF16, "Internal")
        dram("wdT", (KC, HID, 128), BF16, "Internal")
        dram("s_s", (D, N), BF16, "Internal")
    if emit_l1:
        dram("wqkv", (3 * D, D), F32, "ExternalInput")
        dram("aR", (4, 512, 512), F32, "ExternalInput")
        dram("bR", (4, 512, 512), F32, "ExternalInput")
        dram("aI", (4, 512, 512), F32, "ExternalInput")
        dram("bI", (4, 512, 512), F32, "ExternalInput")
        dram("br", (128, 4), F32, "ExternalInput")
        dram("bi", (128, 4), F32, "ExternalInput")
        dram("wqkvT", (3 * D // 128, D, 128), BF16, "Internal")
        dram("g_s", (D, N), BF16, "Internal")
    dram("ident", (128, 128), F32, "ExternalInput")
    if which == "full":
        dram("x1", (D, N), F32, "Internal")
        dram("out", (D, N), F32, "ExternalOutput")
    elif which == "l0":
        dram("x1", (D, N), F32, "ExternalOutput")
    else:
        dram("x1", (D, N), F32, "ExternalInput")
        dram("out", (D, N), F32, "ExternalOutput")
    for d in dbg:
        if d in ("xn0", "xn1", "kvq0", "att", "kq1", "v1", "gg"):
            names = {"kq1": ("dbg_k1", "dbg_q1")}.get(d, ("dbg_" + d,))
            for nm in names:
                dram(nm, (128, 32768), BF16, "ExternalOutput")
        elif d in ("s0", "g1"):
            dram("dbg_" + d, (D, N), BF16, "ExternalOutput")
        elif d == "WT":
            dram("dbg_WT", (128, 6144), BF16, "ExternalOutput")
        elif d == "x1d":
            dram("dbg_x1d", (D, N), F32, "ExternalOutput")

    big0 = stack.enter_context(nc.sbuf_tensor("big0", [128, 32768], BF16))
    big1 = stack.enter_context(nc.sbuf_tensor("big1", [128, 32768], BF16))
    big2 = stack.enter_context(nc.sbuf_tensor("big2", [128, 32768], BF16))
    stA = stack.enter_context(nc.sbuf_tensor("stA", [128, 7168], BF16))
    ident_f = stack.enter_context(nc.sbuf_tensor("ident_f", [128, 128], F32))
    ident_b = stack.enter_context(nc.sbuf_tensor("ident_b", [128, 128], BF16))
    ones_c = stack.enter_context(nc.sbuf_tensor("ones_c", [128, 1], BF16))
    ones_r = stack.enter_context(nc.sbuf_tensor("ones_r", [1, 128], BF16))
    ones_rf = stack.enter_context(nc.sbuf_tensor("ones_rf", [1, 128], F32))
    bias_q = stack.enter_context(nc.sbuf_tensor("bias_q", [128, KC], F32))
    bias_k = stack.enter_context(nc.sbuf_tensor("bias_k", [128, KC], F32))
    bias_o = stack.enter_context(nc.sbuf_tensor("bias_o", [128, KC], F32))
    bias_r = stack.enter_context(nc.sbuf_tensor("bias_r", [128, 4], F32))
    bias_i = stack.enter_context(nc.sbuf_tensor("bias_i", [128, 4], F32))
    negc = stack.enter_context(nc.sbuf_tensor("negc", [128, 2], F32))
    sc1 = stack.enter_context(nc.sbuf_tensor("sc1", [1, 16], F32))
    qmr = stack.enter_context(nc.sbuf_tensor("qmr", [1, 16], F32))

    pS0 = stack.enter_context(nc.psum_tensor("pS0", [128, 1024], F32))
    pS1 = stack.enter_context(nc.psum_tensor("pS1", [128, 1024], F32))
    pS2 = stack.enter_context(nc.psum_tensor("pS2", [128, 1024], F32))
    pCh = stack.enter_context(nc.psum_tensor("pC", [128, 512], F32))
    pDh = stack.enter_context(nc.psum_tensor("pD", [128, 512], F32))
    pS0a, pS1a, pS2a = pS0[:], pS1[:], pS2[:]
    pA = pS0a[:, 0:512]
    pB = pS0a[:, 512:1024]
    pR = pS1a[:, 0:512]
    pT = pS1a[:, 512:1024]
    pE = pS2a[:, 0:512]
    pF = pS2a[:, 512:1024]
    pC, pD = pCh[:], pDh[:]
    PSUM = {"pA": pA, "pB": pB, "pC": pC, "pD": pD, "pE": pE, "pF": pF, "pR": pR, "pT": pT}

    P = Prog(nc, stack)
    for s in ("T", "V", "S", "la0", "la1", "lb0", "lb1", "lb2", "lb3",
              "lc0", "lc1", "ld0", "ld1", "oa0", "oa1", "ob0", "ob1",
              "og", "ou", "od", "oq", "og1", "ou1", "od1", "oq1", "gd"):
        P.sem(s)

    stAf = stA[:].bitcast(F32)
    big1f = big1[:].bitcast(F32)
    big1i = big1[:].bitcast(I32)

    if emit_l0:
        P.load(bias_q[:], T["bq"], "la0")
        P.load(bias_k[:], T["bk"], "la0")
        P.load(bias_o[:], T["bo"], "la0")
    if emit_l1:
        P.load(bias_r[:], T["br"], "la0")
        P.load(bias_i[:], T["bi"], "la0")
    P.load(ident_f[:], T["ident"], "la0")
    vld = P.cnt("la0")
    P.do("vector", lambda e: e.tensor_copy(ident_b[:], ident_f[:]), waits=[("la0", vld)], inc="V")
    P.do("vector", lambda e: e.memset(ones_c[:], 1.0), inc="V")
    P.do("vector", lambda e: e.memset(ones_r[:], 1.0), inc="V")
    P.do("vector", lambda e: e.memset(ones_rf[:], 1.0), inc="V")
    P.do("vector", lambda e: e.memset(sc1[0:1, 15:16], EPS), inc="V")
    if emit_l0:
        for i in range(4):
            fv = stAf[:, 512 * (i % 2):512 * (i % 2) + 512]
            P.load(fv, T["masks"][i], "la1", waits=[("V", P.cnt("V"))])
            v = P.cnt("la1")
            P.do("vector", lambda e, i=i, fv=fv: e.tensor_copy(
                big2[:, 24704 + 512 * i:24704 + 512 * i + 512], fv),
                waits=[("la1", v)], inc="V")
    vconsts = P.cnt("la0")
    vconstsV = P.cnt("V")

    phase = 0

    def emit_wbuild(src_name, dst_name, R, C, src_dt, osem, stg=None, stoff=0):
        stg = big1 if stg is None else stg
        stg_i = stg[:].bitcast(I32)
        stg_f = stg[:].bitcast(F32)
        src, dst = T[src_name], T[dst_name]
        tw = ("T", P.cnt("T"))
        for k in (("braw", 0), ("braw", 1), ("bconv", 0), ("bconv", 1),
                  ("bev", 0), ("bev", 1)):
            P.slot_mark_add(k, *tw)
        it = 0
        for r in range(R // 128):
            for cc in range(C // 1024):
                slot = it % 2
                it += 1
                raw = (stg_i if src_dt == I32 else stg_f)[
                    :, stoff // 2 + 1024 * slot:stoff // 2 + 1024 * slot + 1024]
                lsm = "lb%d" % slot
                co = stoff + 4096 + 1024 * slot
                eo = stoff + 6144 + 1024 * slot
                lv = P.load(raw, src[r * 128:(r + 1) * 128, cc * 1024:(cc + 1) * 1024],
                            lsm, waits=P.slot_waits(("braw", slot)))
                cv = P.do("vector", lambda e, raw=raw, co=co, stg=stg: e.tensor_copy(
                    stg[:, co:co + 1024], raw),
                    waits=[(lsm, lv)] + P.slot_waits(("bconv", slot)), inc="V")
                P.slot_mark(("braw", slot), "V", cv)
                vts = []
                for kc in range(8):
                    wT = P.bank_waits("pT") + [("V", cv)]
                    vt = P.do("tensor", lambda e, kc=kc, co=co, stg=stg: e.transpose(
                        pT.bitcast(BF16)[:, 128 * kc:128 * kc + 128],
                        stg[:, co + 128 * kc:co + 128 * kc + 128], ident_b[:]),
                        waits=wT, inc="T")
                    vts.append(vt)
                P.slot_mark(("bconv", slot), "T", vts[-1])
                ve = P.do("scalar", lambda e, eo=eo, stg=stg: e.copy(
                    stg[:, eo:eo + 1024], pT.bitcast(BF16)[:, 0:1024]),
                    waits=[("T", vts[-1])] + P.slot_waits(("bev", slot)), inc="S")
                P.bank_consumed("pT", "S", ve)
                dstap = dst[r][cc * 1024:(cc + 1) * 1024, :].rearrange(
                    "(c p) q -> p c q", p=128)
                ssem = osem if slot == 0 else osem + "1"
                sv = P.store(dstap, stg[:, eo:eo + 1024].rearrange("p (c q) -> p c q", q=128),
                             ssem, waits=[("S", ve)])
                P.slot_mark(("bev", slot), ssem, sv)
        return [(osem, P.cnt(osem)), (osem + "1", P.cnt(osem + "1"))]

    def emit_ln(src_name, wait_src=()):
        mu_row = big1[0:1, 30720:31232]
        rstd_row = big1[0:1, 31232:31744]
        var_row = big1f[0:1, 15872:16384]
        war0 = [(s, P.cnt(s)) for s in ("og", "ou", "od", "oq", "og1", "ou1",
                                        "od1", "oq1", "oa0", "oa1", "ob0", "ob1")
                if P.cnt(s)]
        war0 += [("T", P.cnt("T")), ("S", P.cnt("S")), ("V", P.cnt("V"))]
        for j in range(NB):
            for c in range(KC):
                slot = c % 2
                fv = stAf[:, 2048 + 512 * slot:2560 + 512 * slot]
                lv = P.load(fv, T[src_name][c * 128:(c + 1) * 128, j * 512:(j + 1) * 512],
                            "la%d" % slot,
                            waits=P.slot_waits(("lnf", slot)) + list(wait_src) + war0)
                cv = P.do("vector", lambda e, c=c, fv=fv: e.tensor_copy(
                    stA[:, 512 * c:512 * c + 512], fv),
                    waits=[("la%d" % slot, lv)] + P.slot_waits(("lnx", c)) + war0[-3:],
                    inc="V")
                P.slot_mark(("lnf", slot), "V", cv)
            vconv = P.cnt("V")
            wR = P.bank_waits("pC") + [("V", vconv)]
            for c in range(KC):
                P.do("tensor", lambda e, c=c: e.matmul(
                    pC[0:1, 0:512], ones_c[:], stA[:, 512 * c:512 * c + 512],
                    start=(c == 0), stop=(c == KC - 1)),
                    waits=wR if c == 0 else (), inc="T" if c == KC - 1 else None)
            for c in range(KC):
                vs = P.do("scalar", lambda e, c=c: e.activation(
                    stA[:, 6144:6656], stA[:, 512 * c:512 * c + 512], AF.Square),
                    waits=[("V", vconv), ("T", P.cnt("T"))] + war0, inc="S")
                P.do("tensor", lambda e, c=c: e.matmul(
                    pC[64:65, 0:512], ones_c[:], stA[:, 6144:6656],
                    start=(c == 0), stop=(c == KC - 1)), waits=[("S", vs)], inc="T")
            vstat = P.cnt("T")
            vmu = P.do("scalar", lambda e: e.activation(mu_row, pC[0:1, 0:512], AF.Copy,
                                                        scale=1.0 / D),
                       waits=[("T", vstat)], inc="S")
            wA = P.bank_waits("pD") + [("S", vmu)]
            vbA = P.do("tensor", lambda e: e.matmul(pD[:, 0:512], ones_r[:], mu_row,
                                                    start=True, stop=True), waits=wA, inc="T")
            vq = P.do("vector", lambda e: e.tensor_mul(rstd_row, mu_row, mu_row),
                      waits=[("S", vmu)], inc="V")
            vv = P.do("vector", lambda e: e.scalar_tensor_tensor(
                var_row, pC[64:65, 0:512], 1.0 / D, rstd_row, ALU.mult, ALU.subtract),
                waits=[("V", vq)], inc="V")
            P.bank_consumed("pC", "V", vv)
            vl = P.do("scalar", lambda e: e.activation(mu_row, var_row, AF.Ln,
                                                       bias=sc1[0:1, 15:16]),
                      waits=[("V", vv), ("T", vbA)], inc="S")
            vr = P.do("scalar", lambda e: e.activation(rstd_row, mu_row, AF.Exp, scale=-0.5),
                      waits=[("S", vl)], inc="S")
            wB = P.bank_waits("pE") + [("S", vr)]
            vbB = P.do("tensor", lambda e: e.matmul(pE[:, 0:512], ones_r[:], rstd_row,
                                                    start=True, stop=True), waits=wB, inc="T")
            for c in range(KC):
                vsb = P.do("vector", lambda e, c=c: e.tensor_sub(
                    stA[:, 512 * c:512 * c + 512], stA[:, 512 * c:512 * c + 512], pD[:, 0:512]),
                    waits=[("T", vbA)] if c == 0 else (), inc="V")
                vm2 = P.do("vector", lambda e, c=c, j=j: e.tensor_mul(
                    big0[:, 4096 * c + 512 * j:4096 * c + 512 * j + 512],
                    stA[:, 512 * c:512 * c + 512], pE[:, 0:512]),
                    waits=([("T", vbB)] if c == 0 else []) + [("V", vsb)], inc="V")
                P.slot_mark(("lnx", c), "V", vm2)
            vn = P.cnt("V")
            P.bank_consumed("pD", "V", vn)
            P.bank_consumed("pE", "V", vn)
        return P.cnt("V")

    sb_done = {}
    vx1 = []
    vspill = []
    if emit_l0:
        phase += 1
        if phase <= upto:
            sb_done["wgT"] = emit_wbuild("wg", "wgT", HID, D, I32, "og")
            sb_done["wuT"] = emit_wbuild("wu", "wuT", HID, D, I32, "ou")
            sb_done["wdT"] = emit_wbuild("wd", "wdT", D, HID, I32, "od")
        phase += 1
        if phase <= upto:
            emit_ln("x")
            if "xn0" in dbg:
                P.do("gpsimd", lambda e: e.dma_start(out=T["dbg_xn0"], in_=big0[:]),
                     waits=[("V", P.cnt("V"))], inc="gd", n=16)

        phase += 1
        if phase <= upto:
            vxn = P.cnt("V")
            KOFF, VOFF, QOFF = 0, 8192, 16512
            MOFF, POFF, OTOFF, ROFF = 24704, 26752, 30848, 31360
            war_bld = [(s, P.cnt(s)) for s in ("og", "ou", "od", "og1", "ou1", "od1") if P.cnt(s)]
            for k in (("pws", 0), ("pws", 1), "pwf"):
                P.slot_mark_add(k, "V", vxn)
            pv_war = [0] * 3
            att_pend = [None]
            sS = ["pA", "pB", "pR", "pT"]

            def load_wslice(wname, cols0, ncols, slot):
                for hh in range(2):
                    fv = stAf[:, 2080:2080 + 4 * ncols]
                    srcap = T[wname].rearrange("(kc p) q -> p kc q", p=128)[
                        :, 4 * hh:4 * hh + 4, cols0:cols0 + ncols]
                    lv = P.load(fv.rearrange("p (kc q) -> p kc q", kc=4), srcap,
                                "lb0", waits=P.slot_waits("pwf") + war_bld)
                    cv = P.do("vector", lambda e, fv=fv, slot=slot, hh=hh, ncols=ncols: e.tensor_copy(
                        stA[:, 2048 * slot + 4 * ncols * hh:2048 * slot + 4 * ncols * hh + 4 * ncols],
                        fv), waits=[("lb0", lv)] + P.slot_waits(("pws", slot)), inc="V")
                    P.slot_mark("pwf", "V", cv)
                return P.cnt("V")

            for g in range(min(G, GLIM)):
                vwk = load_wslice("wk", g * 256, 256, 0)
                for c2 in range(2):
                    for j in range(NB):
                        bank = ("pA", pA) if (c2 * NB + j) % 2 == 0 else ("pB", pB)
                        pairs = [(stA[:, 256 * kc + 128 * c2:256 * kc + 128 * c2 + 128],
                                  big0[:, 4096 * kc + 512 * j:4096 * kc + 512 * j + 512])
                                 for kc in range(KC)]
                        vm = P.mm_group(bank[1][:, 0:512], pairs,
                                        waits=[("V", vwk), ("V", vxn)], bankname=bank[0])
                        ve = P.do("vector", lambda e, g=g, c2=c2, j=j, b=bank[1]: e.tensor_scalar_add(
                            big2[:, KOFF + 4096 * c2 + 512 * j:KOFF + 4096 * c2 + 512 * j + 512],
                            b[:, 0:512], bias_k[:, 2 * g + c2:2 * g + c2 + 1]),
                            waits=[("T", vm), ("la0", vconsts)], inc="V")
                        P.bank_consumed(bank[0], "V", ve)
                vKg = P.cnt("V")
                P.slot_mark(("pws", 0), "T", P.cnt("T"))
                fvb = stAf[0:1, 3120:3380]
                lvb = P.load(fvb, T["bv"][0:1, g * VW:(g + 1) * VW], "lc0",
                             waits=[("V", P.cnt("V"))] + war_bld)
                vbv = P.do("vector", lambda e, fvb=fvb: e.tensor_copy(
                    stA[0:1, 6760:6760 + VW], fvb),
                    waits=[("lc0", P.cnt("lc0")), ("T", P.cnt("T"))], inc="V")
                vwv = load_wslice("wv", g * VW, VW, 1)
                for i in range(NT):
                    bank = ("pC", pC) if i % 2 == 0 else ("pD", pD)
                    pairs = [(big0[:, 4096 * kc + 128 * i:4096 * kc + 128 * i + 128],
                              stA[:, 2048 + VW * kc:2048 + VW * kc + VW])
                             for kc in range(KC)]
                    P.mm_group(bank[1][:, 0:VW], pairs, waits=[("V", vwv), ("V", vxn)],
                               bankname=bank[0], last_stop=False)
                    vm = P.do("tensor", lambda e, b=bank[1]: e.matmul(
                        b[:, 0:VW], ones_r[:], stA[0:1, 6760:6760 + VW],
                        start=False, stop=True), waits=[("V", vbv)], inc="T")
                    ve = P.do("vector", lambda e, i=i, b=bank[1]: e.tensor_copy(
                        big2[:, VOFF + VW * i:VOFF + VW * i + VW], b[:, 0:VW]),
                        waits=[("T", vm)], inc="V")
                    P.bank_consumed(bank[0], "V", ve)
                vVg = P.cnt("V")
                P.slot_mark(("pws", 1), "T", P.cnt("T"))
                vwq = load_wslice("wq", g * 256, 256, 0)
                for c2 in range(2):
                    for j in range(NB):
                        bank = ("pA", pA) if (c2 * NB + j) % 2 == 0 else ("pB", pB)
                        pairs = [(stA[:, 256 * kc + 128 * c2:256 * kc + 128 * c2 + 128],
                                  big0[:, 4096 * kc + 512 * j:4096 * kc + 512 * j + 512])
                                 for kc in range(KC)]
                        vm = P.mm_group(bank[1][:, 0:512], pairs,
                                        waits=[("V", vwq), ("V", vxn)], bankname=bank[0])
                        ve = P.do("vector", lambda e, g=g, c2=c2, j=j, b=bank[1]: e.tensor_scalar_add(
                            big2[:, QOFF + 4096 * c2 + 512 * j:QOFF + 4096 * c2 + 512 * j + 512],
                            b[:, 0:512], bias_q[:, 2 * g + c2:2 * g + c2 + 1]),
                            waits=[("T", vm), ("la0", vconsts)], inc="V")
                        P.bank_consumed(bank[0], "V", ve)
                vQg = P.cnt("V")
                P.slot_mark(("pws", 0), "T", P.cnt("T"))
                if g == 0 and "kvq0" in dbg:
                    P.do("gpsimd", lambda e: e.dma_start(out=T["dbg_kvq0"], in_=big2[:]),
                         waits=[("V", vQg)], inc="gd", n=16)

                def norm_max(OFFS, vready, outcol):
                    for j in range(NB):
                        for c2 in range(2):
                            wR = (P.bank_waits("pR") if c2 == 0 else [])
                            vs = P.do("scalar", lambda e, c2=c2, j=j, OFFS=OFFS: e.activation(
                                stA[:, 4160:4672],
                                big2[:, OFFS + 4096 * c2 + 512 * j:OFFS + 4096 * c2 + 512 * j + 512],
                                AF.Square),
                                waits=wR + [("V", vready), ("T", P.cnt("T"))], inc="S")
                            P.do("tensor", lambda e, c2=c2: e.matmul(
                                pR[0:1, 0:512], ones_c[:], stA[:, 4160:4672],
                                start=(c2 == 0), stop=(c2 == 1)), waits=[("S", vs)], inc="T")
                        vt = P.cnt("T")
                        vv = P.do("vector", lambda e, j=j: e.tensor_reduce(
                            qmr[0:1, j:j + 1], pR[0:1, 0:512], AX.X, ALU.max),
                            waits=[("T", vt)], inc="V")
                        P.bank_consumed("pR", "V", vv)
                        P._wait("scalar", "V", vv)
                    return P.do("vector", lambda e, outcol=outcol: e.tensor_reduce(
                        sc1[0:1, outcol:outcol + 1], qmr[0:1, 0:NB], AX.X, ALU.max),
                        waits=[("V", P.cnt("V"))], inc="V")

                if SUBPH < 2:
                    continue
                vq = norm_max(QOFF, vQg, 0)
                vk = norm_max(KOFF, vKg, 1)
                P.slot_mark_add("pwf", "S", P.cnt("S"))
                P.slot_mark_add("pwf", "T", P.cnt("T"))
                vp = P.do("vector", lambda e: e.tensor_mul(sc1[0:1, 2:3], sc1[0:1, 0:1],
                                                           sc1[0:1, 1:2]),
                          waits=[("V", vq), ("V", vk)], inc="V")
                vsq = P.do("scalar", lambda e: e.activation(sc1[0:1, 3:4], sc1[0:1, 2:3],
                                                            AF.Sqrt), waits=[("V", vp)], inc="S")
                wT2 = P.bank_waits("pT") + [("S", vsq)]
                vbc = P.do("tensor", lambda e: e.matmul(pT[:, 0:1], ones_rf[:], sc1[0:1, 3:4],
                                                        start=True, stop=True), waits=wT2, inc="T")
                vnc = P.do("scalar", lambda e: e.activation(negc[:, 0:1], pT[:, 0:1], AF.Identity,
                                                            scale=-0.125 * 1.001),
                           waits=[("T", vbc)], inc="S")
                P.bank_consumed("pT", "S", vnc)

                if SUBPH < 3:
                    continue
                SPAIR = [pS0a, pS1a, pS2a]
                SHALF = [["pA", "pB"], ["pR", "pT"], ["pE", "pF"]]
                big2f = big2[:].bitcast(F32)
                rrows = [big2f[0:1, 15680:16192], big2f[0:1, 14912:15424]]

                def att_epilogue2(pend):
                    g_, hswp_, qb_, oz_done_, vrcs_ = pend
                    for hh2 in range(2):
                        hh = 2 * hswp_ + hh2
                        c2, hl = hh // 2, hh % 2
                        oz = PSUM[["pC", "pD"][hh2]]
                        wbc = P.bank_waits("pE") + [("V", vrcs_[hh2])]
                        vbc2 = P.do("tensor", lambda e, rr=rrows[hh2]: e.matmul(
                            pE[0:64, 0:512], ones_rf[0:1, 0:64], rr, start=True, stop=True),
                            waits=wbc, inc="T")
                        ot = big2[0:64, OTOFF:OTOFF + 512]
                        voe = P.do("scalar", lambda e, ot=ot, oz=oz: e.copy(ot, oz[0:64, 0:512]),
                                   waits=[("V", P.cnt("V")), ("S", P.cnt("S"))], inc="S")
                        P.bank_consumed(["pC", "pD"][hh2], "S", voe)
                        dst = big1[64 * hl:64 * hl + 64,
                                   4096 * (2 * g_ + c2) + 512 * qb_:4096 * (2 * g_ + c2) + 512 * qb_ + 512]
                        vml = P.do("vector", lambda e, dst=dst, ot=ot: e.tensor_mul(
                            dst, ot, pE[0:64, 0:512]),
                            waits=[("S", voe), ("T", vbc2)] + (war_bld if g_ == 0 else []),
                            inc="V")
                        P.bank_consumed("pE", "V", vml)
                        P._wait("scalar", "V", vml)

                for hswp in range(2):
                    for qb in range(NB):
                        ktn = 4 * qb + 4
                        oz_done = [None, None]
                        qk_pend = {}

                        def emit_qkpair(kt, hswp=hswp, qb=qb, qk_pend=None):
                            pr = kt % 3
                            qks = []
                            for hh2 in range(2):
                                hh = 2 * hswp + hh2
                                c2, hl = hh // 2, hh % 2
                                bkn = SHALF[pr][hh2]
                                l = big2[64 * hl:64 * hl + 64,
                                         KOFF + 4096 * c2 + 128 * kt:KOFF + 4096 * c2 + 128 * kt + 128]
                                r = big2[64 * hl:64 * hl + 64,
                                         QOFF + 4096 * c2 + 512 * qb:QOFF + 4096 * c2 + 512 * qb + 512]
                                wqk = P.bank_waits(bkn) + [("S", vnc), ("V", vVg)]
                                vqk = P.do("tensor", lambda e, l=l, r=r, b=PSUM[bkn]: e.matmul(
                                    b, l, r, start=True, stop=True), waits=wqk, inc="T")
                                qks.append(vqk)
                            qk_pend[kt] = (pr, qks)

                        emit_qkpair(0, qk_pend=qk_pend)
                        if ktn > 1:
                            emit_qkpair(1, qk_pend=qk_pend)
                        if att_pend[0] is not None:
                            att_epilogue2(att_pend[0])
                            att_pend[0] = None
                        for kt in range(ktn):
                            if kt + 2 < ktn:
                                emit_qkpair(kt + 2, qk_pend=qk_pend)
                            pr, qks = qk_pend.pop(kt)
                            pp = kt % 3
                            ppt = big2[:, POFF + 1024 * pp:POFF + 1024 * pp + 1024]
                            wex = [("T", qks[1]), ("T", pv_war[pp])]
                            vex = P.do("scalar", lambda e, ppt=ppt, b=SPAIR[pr]: e.activation(
                                ppt, b, AF.Exp, bias=negc[:, 0:1], scale=0.125),
                                waits=wex, inc="S")
                            P.bank_consumed(SHALF[pr][0], "S", vex)
                            P.bank_consumed(SHALF[pr][1], "S", vex)
                            pvwait = ("S", vex)
                            if kt >= 4 * qb:
                                dk = kt - 4 * qb
                                for half in range(2):
                                    vmk = P.do("vector", lambda e, ppt=ppt, dk=dk, half=half: e.tensor_mul(
                                        ppt[:, 512 * half:512 * half + 512],
                                        ppt[:, 512 * half:512 * half + 512],
                                        big2[:, MOFF + 512 * dk:MOFF + 512 * dk + 512]),
                                        waits=[("S", vex)], inc="V")
                                pvwait = ("V", vmk)
                            for hh2 in range(2):
                                hh = 2 * hswp + hh2
                                pt = ppt[:, 512 * hh2:512 * hh2 + 512]
                                ozn = ["pC", "pD"][hh2]
                                wpv = [pvwait]
                                if kt == 0:
                                    wpv += P.bank_waits(ozn)
                                lv = big2[:, VOFF + VW * kt + 65 * hh:VOFF + VW * kt + 65 * hh + 65]
                                vpv = P.do("tensor", lambda e, lv=lv, pt=pt, st=(kt == 0),
                                           sp=(kt == ktn - 1), oz=PSUM[ozn]: e.matmul(
                                    oz[0:65, 0:512], lv, pt, start=st, stop=sp), waits=wpv, inc="T")
                                pv_war[pp] = max(pv_war[pp], vpv)
                                oz_done[hh2] = vpv
                        vrcs = []
                        for hh2 in range(2):
                            oz = PSUM[["pC", "pD"][hh2]]
                            vrc = P.do("vector", lambda e, oz=oz, rr=rrows[hh2]: e.reciprocal(
                                rr, oz[64:65, 0:512]),
                                waits=[("T", oz_done[hh2]), ("V", P.cnt("V"))], inc="V")
                            P.bank_consumed_add(["pC", "pD"][hh2], "V", vrc)
                            vrcs.append(vrc)
                        att_pend[0] = (g, hswp, qb, list(oz_done), vrcs)
                if att_pend[0] is not None:
                    att_epilogue2(att_pend[0])
                    att_pend[0] = None
            if att_pend[0] is not None:
                att_epilogue2(att_pend[0])
                att_pend[0] = None
            if "att" in dbg:
                P.do("gpsimd", lambda e: e.dma_start(out=T["dbg_att"], in_=big1[:]),
                     waits=[("V", P.cnt("V"))], inc="gd", n=16)

        phase += 1
        if phase <= upto:
            vOT = P.cnt("V")
            tw = ("T", P.cnt("T"))
            for k in (("wos", 0), ("wos", 1)):
                P.slot_mark_add(k, *tw)
            for dd in range(KC):
                slot = dd % 2
                for hh in range(2):
                    fv = stAf[:, 2080:2592]
                    srcap = T["wo"].rearrange("(kc p) q -> p kc q", p=128)[
                        :, 4 * hh:4 * hh + 4, dd * 128:(dd + 1) * 128]
                    lv = P.load(fv.rearrange("p (kc q) -> p kc q", kc=4), srcap, "lb0",
                                waits=P.slot_waits("pwf"))
                    cv = P.do("vector", lambda e, fv=fv, slot=slot, hh=hh: e.tensor_copy(
                        stA[:, 1024 * slot + 512 * hh:1024 * slot + 512 * hh + 512], fv),
                        waits=[("lb0", lv)] + P.slot_waits(("wos", slot)), inc="V")
                    P.slot_mark("pwf", "V", cv)
                vws = P.cnt("V")
                lastT = None
                for j in range(NB):
                    bank = ("pA", pA) if (dd * NB + j) % 2 == 0 else ("pB", pB)
                    pairs = [(stA[:, 1024 * slot + 128 * kc:1024 * slot + 128 * kc + 128],
                              big1[:, 4096 * kc + 512 * j:4096 * kc + 512 * j + 512])
                             for kc in range(KC)]
                    vm = P.mm_group(bank[1][:, 0:512], pairs, waits=[("V", vws), ("V", vOT)],
                                    bankname=bank[0])
                    lastT = vm
                    sp = stA[:, 6144 + 512 * (j % 2):6144 + 512 * (j % 2) + 512]
                    osm = "oa%d" % (j % 2)
                    ve = P.do("vector", lambda e, sp=sp, dd=dd, b=bank[1]: e.tensor_scalar_add(
                        sp, b[:, 0:512], bias_o[:, dd:dd + 1]),
                        waits=[("T", vm), (osm, P.cnt(osm)), ("la0", vconsts)], inc="V")
                    P.bank_consumed(bank[0], "V", ve)
                    P.store(T["s_s"][dd * 128:(dd + 1) * 128, j * 512:(j + 1) * 512], sp, osm,
                            waits=[("V", ve)])
                P.slot_mark(("wos", slot), "T", lastT)
            vspill = [("oa0", P.cnt("oa0")), ("oa1", P.cnt("oa1"))]
            if "s0" in dbg:
                P.do("gpsimd", lambda e: e.dma_start(out=T["dbg_s0"], in_=T["s_s"]),
                     waits=vspill, inc="gd", n=16)

        if emit_l1 and upto >= 5:
            sb_done["wqkvT"] = emit_wbuild("wqkv", "wqkvT", 3 * D, D, F32, "oq",
                                           stg=big2, stoff=8192)
        phase += 1
        if phase <= upto:
            vxn = P.cnt("V")
            tw = ("T", P.cnt("T"))
            for k in (("fg", 0), ("fg", 1), ("fu", 0), ("fu", 1), ("fd", 0), ("fd", 1),
                      ("sg", 0), ("sg", 1), ("cfx", 0), ("cfx", 1), ("csl", 0),
                      ("csl", 1), ("cpr", 0), ("cpr", 1)):
                P.slot_mark_add(k, *tw)
            for sv_ in vspill:
                P.slot_mark_add("cfx", sv_[0], sv_[1])
            for grp in range(4):
                j0 = grp * 2
                for hc in range(HCN):
                    slot = hc % 2
                    gsm = "lb%d" % (2 * slot)
                    usm = "lb%d" % (2 * slot + 1)
                    lvg = P.load(stA[:, 1024 * slot:1024 * slot + 1024].rearrange(
                        "p (c q) -> p c q", q=128),
                        T["wgT"][hc].rearrange("(c p) q -> p c q", p=128), gsm,
                        waits=P.slot_waits(("fg", slot)) + list(sb_done.get("wgT", [])))
                    lvu = P.load(stA[:, 2048 + 1024 * slot:2048 + 1024 * slot + 1024].rearrange(
                        "p (c q) -> p c q", q=128),
                        T["wuT"][hc].rearrange("(c p) q -> p c q", p=128), usm,
                        waits=P.slot_waits(("fu", slot)) + list(sb_done.get("wuT", [])))
                    lastg = lastu = None
                    for j2 in range(2):
                        j = j0 + j2
                        gb = ("pA", pA) if j2 == 0 else ("pB", pB)
                        ub = ("pC", pC) if j2 == 0 else ("pD", pD)
                        gp = [(stA[:, 1024 * slot + 128 * kc:1024 * slot + 128 * kc + 128],
                               big0[:, 4096 * kc + 512 * j:4096 * kc + 512 * j + 512])
                              for kc in range(KC)]
                        upp = [(stA[:, 2048 + 1024 * slot + 128 * kc:2048 + 1024 * slot + 128 * kc + 128],
                                big0[:, 4096 * kc + 512 * j:4096 * kc + 512 * j + 512])
                               for kc in range(KC)]
                        vmg = P.mm_group(gb[1][:, 0:512], gp, waits=[(gsm, lvg), ("V", vxn)],
                                         bankname=gb[0])
                        vmu2 = P.mm_group(ub[1][:, 0:512], upp, waits=[(usm, lvu)],
                                          bankname=ub[0])
                        lastg, lastu = vmg, vmu2
                        sgo = [4096, 6656][j2]
                        sg = stA[:, sgo:sgo + 512]
                        vsg = P.do("scalar", lambda e, sg=sg, b=gb[1]: e.activation(
                            sg, b[:, 0:512], AF.Silu, scale=1.0 / 127.0),
                            waits=[("T", vmg)] + P.slot_waits(("sg", j2)) + vspill, inc="S")
                        P.bank_consumed(gb[0], "S", vsg)
                        hdst = big1[:, 1024 * hc + 512 * j2:1024 * hc + 512 * j2 + 512]
                        vh = P.do("vector", lambda e, hdst=hdst, sg=sg, b=ub[1]: e.tensor_mul(
                            hdst, sg, b[:, 0:512]), waits=[("S", vsg), ("T", vmu2)], inc="V")
                        P.bank_consumed(ub[0], "V", vh)
                        P.slot_mark(("sg", j2), "V", vh)
                    P.slot_mark(("fg", slot), "T", lastg)
                    P.slot_mark(("fu", slot), "T", lastu)
                vhall = P.cnt("V")
                for dd in range(KC):
                    dslot = dd % 2
                    do_ = 24576 + 3072 * dslot
                    dsm = "ld%d" % dslot
                    lvd = P.load(big1[:, do_:do_ + 3072].rearrange("p (c q) -> p c q", q=128),
                                 T["wdT"][dd].rearrange("(c p) q -> p c q", p=128), dsm,
                                 waits=P.slot_waits(("fd", dslot)) + list(sb_done.get("wdT", [])))
                    lastd = None
                    for j2 in range(2):
                        bank = ("pE", pE) if j2 == 0 else ("pF", pF)
                        pairs = [(big1[:, do_ + 128 * hcc:do_ + 128 * hcc + 128],
                                  big1[:, 1024 * hcc + 512 * j2:1024 * hcc + 512 * j2 + 512])
                                 for hcc in range(HCN)]
                        vm = P.mm_group(bank[1][:, 0:512], pairs,
                                        waits=[(dsm, lvd), ("V", vhall)], bankname=bank[0])
                        lastd = vm
                        ve = P.do("scalar", lambda e, dd=dd, j2=j2, b=bank[1]: e.activation(
                            big2[:, 1024 * dd + 512 * j2:1024 * dd + 512 * j2 + 512],
                            b[:, 0:512], AF.Silu, scale=1.0 / 16129.0),
                            waits=[("T", vm)] + P.slot_waits(("gch", dd * 2 + j2)), inc="S")
                        P.bank_consumed(bank[0], "S", ve)
                    P.slot_mark(("fd", dslot), "T", lastd)
                vgrp = P.cnt("S")
                if grp == 0 and "gg" in dbg:
                    P.do("gpsimd", lambda e: e.dma_start(out=T["dbg_gg"], in_=big2[:]),
                         waits=[("S", vgrp)], inc="gd", n=16)
                b2f = big2[:].bitcast(F32)
                for j2 in range(2):
                    j = j0 + j2
                    for c in range(KC):
                        sl = c % 2
                        fx = b2f[:, 8192 + 512 * sl:8192 + 512 * sl + 512]
                        fo = b2f[:, 9216 + 512 * sl:9216 + 512 * sl + 512]
                        ssl = big2[:, 20480 + 512 * sl:20480 + 512 * sl + 512]
                        prd = big2[:, 21504 + 512 * sl:21504 + 512 * sl + 512]
                        osm = "ob%d" % sl
                        lvx = P.load(fx, T["x"][c * 128:(c + 1) * 128, j * 512:(j + 1) * 512],
                                     "lc0", waits=P.slot_waits(("cfx", sl)))
                        lvs = P.load(ssl, T["s_s"][c * 128:(c + 1) * 128, j * 512:(j + 1) * 512],
                                     "lc1", waits=vspill + P.slot_waits(("csl", sl)))
                        vp2 = P.do("vector", lambda e, prd=prd, ssl=ssl, c=c, j2=j2: e.tensor_mul(
                            prd, ssl, big2[:, 1024 * c + 512 * j2:1024 * c + 512 * j2 + 512]),
                            waits=[("lc1", lvs), ("S", vgrp)] + P.slot_waits(("cpr", sl)), inc="V")
                        P.slot_mark(("csl", sl), "V", vp2)
                        P.slot_mark_add(("gch", c * 2 + j2), "V", vp2)
                        vo = P.do("vector", lambda e, fo=fo, fx=fx, prd=prd: e.tensor_add(
                            fo, fx, prd),
                            waits=[("lc0", lvx), (osm, P.cnt(osm))], inc="V")
                        P.slot_mark(("cfx", sl), "V", vo)
                        P.slot_mark(("cpr", sl), "V", vo)
                        P.store(T["x1"][c * 128:(c + 1) * 128, j * 512:(j + 1) * 512], fo,
                                osm, waits=[("V", vo)])
            vx1 = [("ob0", P.cnt("ob0")), ("ob1", P.cnt("ob1"))]

    if emit_l1:
        phase += 1
        if phase <= upto and "wqkvT" not in sb_done:
            sb_done["wqkvT"] = emit_wbuild("wqkv", "wqkvT", 3 * D, D, F32, "oq")
        phase += 1
        if phase <= upto:
            emit_ln("x1", wait_src=vx1)
            if "xn1" in dbg:
                P.do("gpsimd", lambda e: e.dma_start(out=T["dbg_xn1"], in_=big0[:]),
                     waits=[("V", P.cnt("V"))], inc="gd", n=16)
        phase += 1
        vgs = []
        if phase <= upto:
            war_q = [(s, P.cnt(s)) for s in ("oq", "oq1") if P.cnt(s)] + [("T", P.cnt("T"))]
            P.slot_mark_add("wba", "V", P.cnt("V"))
            for idx, (an, bn) in enumerate((("aR", "bR"), ("aI", "bI"))):
                for ch in range(4):
                    for c in range(4):
                        fa = stAf[:, 0:512]
                        fb = stAf[:, 512:1024]
                        acc = stAf[:, 1024:1536]
                        la = P.load(fa, T[an][c][ch * 128:(ch + 1) * 128, :], "lb0",
                                    waits=P.slot_waits("wba") + war_q[-1:])
                        lb = P.load(fb, T[bn][c][ch * 128:(ch + 1) * 128, :], "lb1",
                                    waits=P.slot_waits("wba"))
                        if c == 0:
                            vmu0 = P.do("vector", lambda e, fa=fa, fb=fb, acc=acc: e.tensor_mul(
                                acc, fa, fb), waits=[("lb0", la), ("lb1", lb)], inc="V")
                        else:
                            vt0 = P.do("vector", lambda e, fa=fa, fb=fb: e.tensor_mul(
                                stAf[:, 1536:2048], fa, fb),
                                waits=[("lb0", la), ("lb1", lb)], inc="V")
                            vmu0 = P.do("vector", lambda e, acc=acc: e.tensor_add(
                                acc, acc, stAf[:, 1536:2048]), waits=[("V", vt0)], inc="V")
                        P.slot_mark("wba", "V", vmu0)
                    vcp = P.do("vector", lambda e, idx=idx, ch=ch, acc=acc: e.tensor_copy(
                        big1[:, 2048 * idx + 512 * ch:2048 * idx + 512 * ch + 512], acc),
                        waits=war_q + [("V", vmu0)], inc="V")
                    P.slot_mark("wba", "V", vcp)
            vwt = P.cnt("V")
            vneg = P.do("vector", lambda e: e.tensor_scalar_mul(
                big1[:, 4096:6144], big1[:, 2048:4096], -1.0), waits=[("V", vwt)] + war_q,
                inc="V")
            if "WT" in dbg:
                P.do("gpsimd", lambda e: e.dma_start(out=T["dbg_WT"], in_=big1[:, 0:6144]),
                     waits=[("V", vneg)], inc="gd", n=16)
            vxnl = P.cnt("V")
            for j in range(NB):
                for half in range(2):
                    for oc in range(4):
                        bank = ("pA", pA) if oc % 2 == 0 else ("pB", pB)
                        pairs = []
                        for ic in range(4):
                            wofs = 0 if half == 0 else 2048
                            pairs.append((big1[:, wofs + 512 * ic + 128 * oc:wofs + 512 * ic + 128 * oc + 128],
                                          big0[:, 4096 * ic + 512 * j:4096 * ic + 512 * j + 512]))
                        for ic in range(4):
                            wofs = 4096 if half == 0 else 0
                            pairs.append((big1[:, wofs + 512 * ic + 128 * oc:wofs + 512 * ic + 128 * oc + 128],
                                          big0[:, 4096 * (4 + ic) + 512 * j:4096 * (4 + ic) + 512 * j + 512]))
                        vm = P.mm_group(bank[1][:, 0:512], pairs,
                                        waits=[("V", vneg), ("V", vxnl)], bankname=bank[0])
                        gsl = stA[:, 4096 + 512 * (oc % 2):4096 + 512 * (oc % 2) + 512]
                        osm = "oa%d" % (oc % 2)
                        bias = bias_r if half == 0 else bias_i
                        ve = P.do("scalar", lambda e, gsl=gsl, oc=oc, b=bank[1], bias=bias: e.activation(
                            gsl, b[:, 0:512], AF.Silu, bias=bias[:, oc:oc + 1]),
                            waits=[("T", vm), (osm, P.cnt(osm)), ("la0", vconsts)], inc="S")
                        P.bank_consumed(bank[0], "S", ve)
                        P.store(T["g_s"][512 * half + 128 * oc:512 * half + 128 * oc + 128,
                                         j * 512:(j + 1) * 512], gsl, osm,
                                waits=[("S", ve)])
            vgs = [("oa0", P.cnt("oa0")), ("oa1", P.cnt("oa1"))]
            if "g1" in dbg:
                P.do("gpsimd", lambda e: e.dma_start(out=T["dbg_g1"], in_=T["g_s"]),
                     waits=vgs, inc="gd", n=16)

        phase += 1
        if phase <= upto:
            vxnl = P.cnt("V")
            tw = ("T", P.cnt("T"))
            for k in (("l1k", 0), ("l1k", 1), "l1v"):
                P.slot_mark_add(k, *tw)
                P.slot_mark_add(k, "V", vxnl)
            for fc in range(KC):
                slot = fc % 2
                ksm = "lb%d" % slot
                lvk = P.load(stA[:, 1024 * slot:1024 * slot + 1024].rearrange(
                    "p (c q) -> p c q", q=128),
                    T["wqkvT"][8 + fc].rearrange("(c p) q -> p c q", p=128), ksm,
                    waits=P.slot_waits(("l1k", slot)) + list(sb_done.get("wqkvT", [])))
                lastm = None
                for j in range(NB):
                    bank = ("pA", pA) if (fc * NB + j) % 2 == 0 else ("pB", pB)
                    pairs = [(stA[:, 1024 * slot + 128 * kc:1024 * slot + 128 * kc + 128],
                              big0[:, 4096 * kc + 512 * j:4096 * kc + 512 * j + 512])
                             for kc in range(KC)]
                    vm = P.mm_group(bank[1][:, 0:512], pairs,
                                    waits=[(ksm, lvk), ("V", vxnl)], bankname=bank[0])
                    lastm = vm
                    ve = P.do("vector", lambda e, fc=fc, j=j, b=bank[1]: e.tensor_copy(
                        big1[:, 4096 * fc + 512 * j:4096 * fc + 512 * j + 512], b[:, 0:512]),
                        waits=[("T", vm)], inc="V")
                    P.bank_consumed(bank[0], "V", ve)
                P.slot_mark(("l1k", slot), "T", lastm)
            for fg in range(2):
                for kc in range(KC):
                    P.load(stA[:, 2048 + 512 * kc:2048 + 512 * kc + 512].rearrange(
                        "p (f q) -> p f q", q=128),
                        T["wqkvT"].rearrange("f (kc p) q -> p f kc q", p=128)[
                            :, 16 + 4 * fg:16 + 4 * fg + 4, kc, :],
                        "lb0", waits=P.slot_waits("l1v") + vgs +
                        list(sb_done.get("wqkvT", [])))
                lvv = P.cnt("lb0")
                lastm = None
                for i in range(NT):
                    bank = ("pC", pC) if i % 2 == 0 else ("pD", pD)
                    pairs = [(big0[:, 4096 * kc + 128 * i:4096 * kc + 128 * i + 128],
                              stA[:, 2048 + 512 * kc:2048 + 512 * kc + 512])
                             for kc in range(KC)]
                    vm = P.mm_group(bank[1][:, 0:512], pairs, waits=[("lb0", lvv), ("V", vxnl)],
                                    bankname=bank[0])
                    lastm = vm
                    ve = P.do("vector", lambda e, i=i, fg=fg, b=bank[1]: e.tensor_copy(
                        big2[:, 1024 * i + 512 * fg:1024 * i + 512 * fg + 512], b[:, 0:512]),
                        waits=[("T", vm)], inc="V")
                    P.bank_consumed(bank[0], "V", ve)
                P.slot_mark("l1v", "T", lastm)
            BANKS = ["pA", "pB", "pC", "pD", "pE", "pF", "pR", "pT"]
            for j in range(NB):
                evs = []
                for fc in range(KC):
                    slot = fc % 2
                    lvq = P.load(stA[:, 1024 * slot:1024 * slot + 1024].rearrange(
                        "p (c q) -> p c q", q=128),
                        T["wqkvT"][fc].rearrange("(c p) q -> p c q", p=128),
                        "lb%d" % slot, waits=P.slot_waits(("l1k", slot)))
                    bank = BANKS[fc]
                    pairs = [(stA[:, 1024 * slot + 128 * kc:1024 * slot + 128 * kc + 128],
                              big0[:, 4096 * kc + 512 * j:4096 * kc + 512 * j + 512])
                             for kc in range(KC)]
                    vm = P.mm_group(PSUM[bank][:, 0:512], pairs,
                                    waits=[("lb%d" % slot, lvq), ("V", vxnl)], bankname=bank)
                    P.slot_mark(("l1k", slot), "T", vm)
                    evs.append((bank, vm))
                vall = P.cnt("T")
                for fc, (bank, vm) in enumerate(evs):
                    ve = P.do("vector", lambda e, fc=fc, j=j, bank=bank: e.tensor_copy(
                        big0[:, 4096 * fc + 512 * j:4096 * fc + 512 * j + 512],
                        PSUM[bank][:, 0:512]), waits=[("T", vall)], inc="V")
                    P.bank_consumed(bank, "V", ve)
            vqkv = P.cnt("V")
            if "kq1" in dbg:
                P.do("gpsimd", lambda e: e.dma_start(out=T["dbg_k1"], in_=big1[:]),
                     waits=[("V", vqkv)], inc="gd", n=16)
                P.do("gpsimd", lambda e: e.dma_start(out=T["dbg_q1"], in_=big0[:]),
                     waits=[("V", vqkv)], inc="gd", n=16)
            if "v1" in dbg:
                P.do("gpsimd", lambda e: e.dma_start(out=T["dbg_v1"], in_=big2[:]),
                     waits=[("V", vqkv)], inc="gd", n=16)

        phase += 1
        if phase <= upto:
            vqkv = P.cnt("V")

            def norm_max_l1(buf, vready, outcol):
                for j in range(NB):
                    for kc in range(KC):
                        wR = (P.bank_waits("pR") if kc == 0 else [])
                        vs = P.do("scalar", lambda e, kc=kc, j=j, buf=buf: e.activation(
                            stA[:, 4096:4608],
                            buf[:, 4096 * kc + 512 * j:4096 * kc + 512 * j + 512], AF.Square),
                            waits=wR + [("V", vready), ("T", P.cnt("T"))] + vgs, inc="S")
                        P.do("tensor", lambda e, kc=kc: e.matmul(
                            pR[0:1, 0:512], ones_c[:], stA[:, 4096:4608],
                            start=(kc == 0), stop=(kc == KC - 1)), waits=[("S", vs)], inc="T")
                    vt = P.cnt("T")
                    vv = P.do("vector", lambda e, j=j: e.tensor_reduce(
                        qmr[0:1, 8 + j:9 + j], pR[0:1, 0:512], AX.X, ALU.max),
                        waits=[("T", vt)], inc="V")
                    P.bank_consumed("pR", "V", vv)
                    P._wait("scalar", "V", vv)
                return P.do("vector", lambda e, outcol=outcol: e.tensor_reduce(
                    sc1[0:1, outcol:outcol + 1], qmr[0:1, 8:8 + NB], AX.X, ALU.max),
                    waits=[("V", P.cnt("V"))], inc="V")

            vqn = norm_max_l1(big0, vqkv, 4)
            vkn = norm_max_l1(big1, vqkv, 5)
            vpp = P.do("vector", lambda e: e.tensor_mul(sc1[0:1, 6:7], sc1[0:1, 4:5],
                                                        sc1[0:1, 5:6]),
                       waits=[("V", vqn), ("V", vkn)], inc="V")
            vsq = P.do("scalar", lambda e: e.activation(sc1[0:1, 7:8], sc1[0:1, 6:7],
                                                        AF.Sqrt), waits=[("V", vpp)], inc="S")
            wT2 = P.bank_waits("pT") + [("S", vsq)]
            vbc = P.do("tensor", lambda e: e.matmul(pT[:, 0:1], ones_rf[:], sc1[0:1, 7:8],
                                                    start=True, stop=True), waits=wT2, inc="T")
            vnc1 = P.do("scalar", lambda e: e.activation(negc[:, 1:2], pT[:, 0:1], AF.Identity,
                                                         scale=-(1.0 / 32.0) * 1.001),
                        waits=[("T", vbc)], inc="S")
            P.bank_consumed("pT", "S", vnc1)

            PVB = ["pC", "pD", "pE", "pF"]
            pv_war1 = [0, 0]
            qk_pend = {}

            def emit_qk(qb, kt):
                sb = ("pA", pA) if kt % 2 == 0 else ("pB", pB)
                pairs = [(big1[:, 4096 * kc + 128 * kt:4096 * kc + 128 * kt + 128],
                          big0[:, 4096 * kc + 256 * qb:4096 * kc + 256 * qb + 256])
                         for kc in range(KC)]
                vqk = P.mm_group(sb[1][:, 0:256], pairs, waits=[("S", vnc1), ("V", vqkv)],
                                 bankname=sb[0])
                qk_pend[(qb, kt)] = (sb, vqk)

            for qb in range(16):
                if qb == 0:
                    emit_qk(0, 0)
                pv_done = None
                for kt in range(NT):
                    sb, vqk = qk_pend.pop((qb, kt))
                    if kt + 1 < NT:
                        emit_qk(qb, kt + 1)
                    elif qb + 1 < 16:
                        emit_qk(qb + 1, 0)
                    ptile = stA[:, 4608 + 256 * (kt % 2):4608 + 256 * (kt % 2) + 256]
                    vex = P.do("scalar", lambda e, ptile=ptile, b=sb[1]: e.activation(
                        ptile, b[:, 0:256], AF.Exp, bias=negc[:, 1:2], scale=1.0 / 32.0),
                        waits=[("T", vqk), ("T", pv_war1[kt % 2])], inc="S")
                    P.bank_consumed(sb[0], "S", vex)
                    wz = [("S", vex)]
                    if kt == 0:
                        wz += P.bank_waits("pR")
                    P.do("tensor", lambda e, ptile=ptile, st=(kt == 0), sp=(kt == NT - 1): e.matmul(
                        pR[0:1, 0:256], ones_c[:], ptile, start=st, stop=sp), waits=wz, inc="T")
                    lastpv = None
                    for fc in range(KC):
                        bn = PVB[fc // 2]
                        col0 = 256 * (fc % 2)
                        wpv = []
                        if kt == 0 and fc % 2 == 0:
                            wpv += P.bank_waits(bn)
                        lvv = big2[:, 1024 * kt + 128 * fc:1024 * kt + 128 * fc + 128]
                        lastpv = P.do("tensor", lambda e, lvv=lvv, ptile=ptile, bn=bn, col0=col0,
                                      st=(kt == 0 and fc % 2 == 0),
                                      sp=(kt == NT - 1 and fc % 2 == 1): e.matmul(
                            PSUM[bn][:, col0:col0 + 256], lvv, ptile, start=st, stop=sp),
                            waits=wpv, inc="T" if fc == KC - 1 else None)
                    pv_war1[kt % 2] = lastpv
                    pv_done = lastpv
                rrow1 = stAf[0:1, 2816:3072]
                vrc = P.do("vector", lambda e: e.reciprocal(
                    rrow1, pR[0:1, 0:256]),
                    waits=[("T", P.cnt("T")), ("V", P.cnt("V"))], inc="V")
                P.bank_consumed("pR", "V", vrc)
                wbc = P.bank_waits("pT") + [("V", vrc)]
                vbc2 = P.do("tensor", lambda e: e.matmul(
                    pT[:, 0:256], ones_rf[:], rrow1, start=True, stop=True), waits=wbc, inc="T")
                for fc in range(KC):
                    bn = PVB[fc // 2]
                    col0 = 256 * (fc % 2)
                    ot = stA[:, 6144 + 256 * (fc % 2):6144 + 256 * (fc % 2) + 256]
                    voe = P.do("scalar", lambda e, ot=ot, bn=bn, col0=col0: e.copy(
                        ot, PSUM[bn][:, col0:col0 + 256]),
                        waits=[("T", pv_done), ("V", P.cnt("V"))], inc="S")
                    if fc % 2 == 1:
                        P.bank_consumed(bn, "S", voe)
                    dst = big0[:, 4096 * fc + 256 * qb:4096 * fc + 256 * qb + 256]
                    vml = P.do("vector", lambda e, dst=dst, ot=ot: e.tensor_mul(
                        dst, ot, pT[:, 0:256]), waits=[("S", voe), ("T", vbc2)], inc="V")
                P.bank_consumed("pT", "V", P.cnt("V"))

        phase += 1
        if phase <= upto:
            vs1 = P.cnt("V")
            b2f = big2[:].bitcast(F32)
            for j in range(NB):
                for c in range(KC):
                    sl = c % 2
                    fx = b2f[:, 8192 + 512 * sl:8192 + 512 * sl + 512]
                    fo = b2f[:, 9216 + 512 * sl:9216 + 512 * sl + 512]
                    gsl = big2[:, 20480 + 512 * sl:20480 + 512 * sl + 512]
                    prd = big2[:, 21504 + 512 * sl:21504 + 512 * sl + 512]
                    osm = "ob%d" % sl
                    lvx = P.load(fx, T["x1"][c * 128:(c + 1) * 128, j * 512:(j + 1) * 512],
                                 "lc0", waits=P.slot_waits(("cfx", sl)) + vx1 + [("V", vs1),
                                 ("T", P.cnt("T")), ("S", P.cnt("S"))])
                    lvg = P.load(gsl, T["g_s"][c * 128:(c + 1) * 128, j * 512:(j + 1) * 512],
                                 "lc1", waits=vgs + P.slot_waits(("csl", sl)) + [("V", vs1)])
                    vp2 = P.do("vector", lambda e, prd=prd, gsl=gsl, c=c, j=j: e.tensor_mul(
                        prd, gsl, big0[:, 4096 * c + 512 * j:4096 * c + 512 * j + 512]),
                        waits=[("lc1", lvg), ("V", vs1), ("S", P.cnt("S"))] + P.slot_waits(("cpr", sl)),
                        inc="V")
                    P.slot_mark(("csl", sl), "V", vp2)
                    vo = P.do("vector", lambda e, fo=fo, fx=fx, prd=prd: e.tensor_add(
                        fo, fx, prd),
                        waits=[("lc0", lvx), (osm, P.cnt(osm))], inc="V")
                    P.slot_mark(("cfx", sl), "V", vo)
                    P.slot_mark(("cpr", sl), "V", vo)
                    P.store(T["out"][c * 128:(c + 1) * 128, j * 512:(j + 1) * 512], fo,
                            osm, waits=[("V", vo)])

    if "x1d" in dbg and which == "full":
        P.do("gpsimd", lambda e: e.dma_start(out=T["dbg_x1d"], in_=T["x1"]),
             waits=vx1, inc="gd", n=16)
    final_waits = []
    for s in ("oa0", "oa1", "ob0", "ob1", "og", "ou", "od", "oq",
              "og1", "ou1", "od1", "oq1", "gd"):
        if P.cnt(s):
            final_waits.append((s, P.cnt(s)))
    for s, v in final_waits:
        P._wait("gpsimd", s, v)

    P.check()
    _maxsem = max(P.cnt(s) for s in P.sems)
    assert _maxsem < 60000, f"semaphore count overflow risk: {_maxsem}"

    nums = sorted(h.num for h, _ in P.sems.values())
    assert nums[-1] - nums[0] + 1 == len(nums), "sems not contiguous"
    nc.gpsimd.dma_reset(range(nums[0], nums[-1] + 1))
    nc.gpsimd.sem_clear(range(nums[0], nums[-1] + 1))
    nc.all_engine_barrier()

    with nc.Block() as block:
        @block.sync
        def _(e):
            for f in P.ops["sync"]:
                f(e)

        @block.tensor
        def _(e):
            for f in P.ops["tensor"]:
                f(e)

        @block.vector
        def _(e):
            for f in P.ops["vector"]:
                f(e)

        @block.scalar
        def _(e):
            for f in P.ops["scalar"]:
                f(e)

        @block.gpsimd
        def _(e):
            for f in P.ops["gpsimd"]:
                f(e)

    stack.close()
    return nc


def prep_inputs(inp, core):
    d = {}
    d["x"] = np.ascontiguousarray(inp["L0_attn_x"][core].T)
    d["wq"] = inp["L0_attn_wq"]
    d["wk"] = inp["L0_attn_wk"]
    d["wo"] = inp["L0_attn_wo"]
    wv = inp["L0_attn_wv"]
    wvp = np.zeros((D, H * (HD + 1)), np.float32)
    for h in range(H):
        wvp[:, h * 65:h * 65 + 64] = wv[:, h * 64:(h + 1) * 64]
    d["wv"] = wvp
    for nm, key in (("bq", "L0_attn_bq"), ("bk", "L0_attn_bk"), ("bo", "L0_attn_bo")):
        d[nm] = np.ascontiguousarray(inp[key].reshape(KC, 128).T)
    bv = inp["L0_attn_bv"]
    bvp = np.zeros((1, H * (HD + 1)), np.float32)
    for h in range(H):
        bvp[0, h * 65:h * 65 + 64] = bv[h * 64:(h + 1) * 64]
        bvp[0, h * 65 + 64] = 1.0
    d["bv"] = bvp
    d["wg"] = inp["L0_ffn_w_gate"]
    d["wu"] = inp["L0_ffn_w_up"]
    d["wd"] = inp["L0_ffn_w_down"]
    k = np.arange(128)[:, None]
    q = np.arange(512)[None, :]
    masks = np.zeros((4, 128, 512), np.float32)
    for dk in range(4):
        masks[dk] = (dk * 128 + k <= q).astype(np.float32)
    d["masks"] = masks
    d["ident"] = np.eye(128, dtype=np.float32)
    d["wqkv"] = inp["L1_attn_W_qkv"]

    def _arep(G1):
        A = np.transpose(G1[0], (1, 2, 0))
        A = np.broadcast_to(A[:, :, None, :, None], (4, 32, 16, 32, 16))
        return np.ascontiguousarray(A.reshape(4, 512, 512))

    def _brep(G2):
        B = np.transpose(G2[:, :, 0, :], (0, 2, 1))
        B = np.broadcast_to(B[:, None, :, None, :], (4, 32, 16, 32, 16))
        return np.ascontiguousarray(B.reshape(4, 512, 512))

    d["aR"] = _arep(inp["L1_ffn_G1_R"])
    d["aI"] = _arep(inp["L1_ffn_G1_I"])
    d["bR"] = _brep(inp["L1_ffn_G2_R"])
    d["bI"] = _brep(inp["L1_ffn_G2_I"])
    d["br"] = np.ascontiguousarray(inp["L1_ffn_bias_real"].reshape(4, 128).T)
    d["bi"] = np.ascontiguousarray(inp["L1_ffn_bias_imag"].reshape(4, 128).T)
    return d


def split_inputs(d, which="full"):
    l0 = {"x", "wq", "wk", "wv", "wo", "bq", "bk", "bv", "bo", "wg", "wu", "wd",
          "masks", "ident"}
    l1 = {"wqkv", "aR", "bR", "aI", "bI", "br", "bi", "ident"}
    if which == "l0":
        return {k: v for k, v in d.items() if k in l0}
    if which == "l1":
        return {k: v for k, v in d.items() if k in l1}
    return d


from concourse.bass_utils import run_bass_kernel_spmd


def kernel(**inputs):
    nc = build("full")
    in_maps = [prep_inputs(inputs, b) for b in range(8)]
    res = run_bass_kernel_spmd(nc, in_maps, core_ids=list(range(8)))
    return np.stack([np.ascontiguousarray(np.asarray(r["out"], dtype=np.float32).T)
                     for r in res.results])


# version 13
# speedup vs baseline: 1.1639x; 1.1639x over previous
import numpy as np
from contextlib import ExitStack

import concourse.bass as bass
import concourse.mybir as mybir

F32 = mybir.dt.float32
BF16 = mybir.dt.bfloat16
I32 = mybir.dt.int32
AF = mybir.ActivationFunctionType
ALU = mybir.AluOpType
AX = mybir.AxisListType

N = 4096
D = 1024
KC = 8
NB = 8
NT = 32
H, HD = 16, 64
G, HG = 4, 4
HID, HCN = 3072, 24
EPS = 1e-5
VW = HG * (HD + 1)
SUBPH = 99
SELF_FENCE = True
GLIM = 99

ENGINES = ("sync", "tensor", "vector", "scalar", "gpsimd")


class Prog:
    def __init__(self, nc, stack):
        self.nc = nc
        self.stack = stack
        self.ops = {e: [] for e in ENGINES}
        self.sems = {}
        self.waited = {e: {} for e in ENGINES}
        self.bank = {}
        self.slot = {}
        self.semgate = {}

    def sem(self, name):
        if name not in self.sems:
            h = self.stack.enter_context(self.nc.semaphore(name))
            self.sems[name] = [h, 0]
        return name

    def cnt(self, name):
        return self.sems[name][1]

    def _wait(self, eng, s, v):
        if v <= 0:
            return
        w = self.waited[eng]
        if w.get(s, 0) >= v:
            return
        w[s] = v
        h = self.sems[s][0]
        self.ops[eng].append(lambda e, h=h, v=v: e.wait_ge(h, v))

    def do(self, eng, fn, waits=(), inc=None, n=1):
        if SELF_FENCE:
            if eng == "vector":
                self._wait(eng, "V", self.cnt("V"))
            elif eng == "scalar":
                self._wait(eng, "S", self.cnt("S"))
        for (s, v) in waits:
            self._wait(eng, s, v)
        if inc is not None:
            self.sem(inc)
            self.sems[inc][1] += n
            h = self.sems[inc][0]
            self.ops[eng].append(lambda e, fn=fn, h=h, n=n: fn(e).then_inc(h, n))
            if inc in ("T", "V", "S"):
                for (s, v) in waits:
                    if s.startswith("l"):
                        self.semgate[s] = (inc, self.sems[inc][1])
            return self.sems[inc][1]
        self.ops[eng].append(lambda e, fn=fn: fn(e))
        return None

    def load(self, dst, src, sem, waits=(), eng="sync", **kw):
        g = self.semgate.get(sem)
        allw = list(waits) + ([g] if g else [])
        return self.do(eng, lambda e, dst=dst, src=src, kw=kw: e.dma_start(out=dst, in_=src, **kw),
                       waits=allw, inc=sem, n=16)

    def store(self, dst, src, sem, waits=(), eng="scalar", **kw):
        return self.do(eng, lambda e, dst=dst, src=src, kw=kw: e.dma_start(out=dst, in_=src, **kw),
                       waits=waits, inc=sem, n=16)

    def bank_waits(self, name):
        return list(self.bank.get(name, ()))

    def bank_consumed(self, name, sem, val):
        self.bank[name] = [(sem, val)]

    def bank_consumed_add(self, name, sem, val):
        self.bank.setdefault(name, []).append((sem, val))

    def slot_waits(self, key):
        return list(self.slot.get(key, ()))

    def slot_mark(self, key, sem, val):
        self.slot[key] = [(sem, val)]

    def slot_mark_add(self, key, sem, val):
        self.slot.setdefault(key, []).append((sem, val))

    def mm_group(self, out_ap, pairs, waits=(), bankname=None, first_start=True,
                 last_stop=True, inc="T"):
        n = len(pairs)
        allw = list(waits)
        if bankname is not None and first_start:
            allw += self.bank_waits(bankname)
        r = None
        for i, (l, rr) in enumerate(pairs):
            st = first_start and (i == 0)
            sp = last_stop and (i == n - 1)
            r = self.do("tensor",
                        lambda e, l=l, rr=rr, st=st, sp=sp, o=out_ap: e.matmul(o, l, rr, start=st, stop=sp),
                        waits=allw if i == 0 else (),
                        inc=inc if (i == n - 1 and inc) else None)
        if r is not None:
            for (s, v) in allw:
                if s.startswith("l"):
                    self.semgate[s] = ("T", r)
        return r

    def check(self):
        for e in ENGINES:
            for s, v in self.waited[e].items():
                assert v <= self.sems[s][1], f"wait {s}>={v} never reached (max {self.sems[s][1]})"


def _recip_bf16(e, out, in_):
    with e.bass.allow_low_precision(reason="softmax 1/Z feeds a bf16 matmul operand"):
        return e.reciprocal(out, in_)


def build(which="full", dbg=(), upto=99):
    nc = bass.Bass("TRN2", target_bir_lowering=False)
    stack = ExitStack()
    T = {}

    def dram(name, shape, dt, kind):
        T[name] = nc.dram_tensor(name, list(shape), dt, kind=kind).ap()
        return T[name]

    emit_l0 = which in ("l0", "full")
    emit_l1 = which in ("l1", "full")

    if emit_l0:
        dram("x", (D, N), F32, "ExternalInput")
        for nm in ("wq", "wk", "wo"):
            dram(nm, (D, D), F32, "ExternalInput")
        dram("wv", (D, H * (HD + 1)), F32, "ExternalInput")
        for nm in ("bq", "bk", "bo"):
            dram(nm, (128, KC), F32, "ExternalInput")
        dram("bv", (1, H * (HD + 1)), F32, "ExternalInput")
        dram("wg", (HID, D), I32, "ExternalInput")
        dram("wu", (HID, D), I32, "ExternalInput")
        dram("wd", (D, HID), I32, "ExternalInput")
        dram("masks", (4, 128, 512), F32, "ExternalInput")
        dram("wgT", (HCN, D, 128), BF16, "Internal")
        dram("wuT", (HCN, D, 128), BF16, "Internal")
        dram("wdT", (KC, HID, 128), BF16, "Internal")
        dram("s_s", (D, N), BF16, "Internal")
    if emit_l1:
        dram("wqkv", (3 * D, D), F32, "ExternalInput")
        dram("aR", (4, 512, 512), F32, "ExternalInput")
        dram("bR", (4, 512, 512), F32, "ExternalInput")
        dram("aI", (4, 512, 512), F32, "ExternalInput")
        dram("bI", (4, 512, 512), F32, "ExternalInput")
        dram("br", (128, 4), F32, "ExternalInput")
        dram("bi", (128, 4), F32, "ExternalInput")
        dram("wqkvT", (3 * D // 128, D, 128), BF16, "Internal")
        dram("g_s", (D, N), BF16, "Internal")
    dram("ident", (128, 128), F32, "ExternalInput")
    if which == "full":
        dram("x1", (D, N), F32, "Internal")
        dram("out", (D, N), F32, "ExternalOutput")
    elif which == "l0":
        dram("x1", (D, N), F32, "ExternalOutput")
    else:
        dram("x1", (D, N), F32, "ExternalInput")
        dram("out", (D, N), F32, "ExternalOutput")
    for d in dbg:
        if d in ("xn0", "xn1", "kvq0", "att", "kq1", "v1", "gg"):
            names = {"kq1": ("dbg_k1", "dbg_q1")}.get(d, ("dbg_" + d,))
            for nm in names:
                dram(nm, (128, 32768), BF16, "ExternalOutput")
        elif d in ("s0", "g1"):
            dram("dbg_" + d, (D, N), BF16, "ExternalOutput")
        elif d == "WT":
            dram("dbg_WT", (128, 6144), BF16, "ExternalOutput")
        elif d == "x1d":
            dram("dbg_x1d", (D, N), F32, "ExternalOutput")

    big0 = stack.enter_context(nc.sbuf_tensor("big0", [128, 32768], BF16))
    big1 = stack.enter_context(nc.sbuf_tensor("big1", [128, 32768], BF16))
    big2 = stack.enter_context(nc.sbuf_tensor("big2", [128, 32768], BF16))
    stA = stack.enter_context(nc.sbuf_tensor("stA", [128, 7168], BF16))
    ident_f = stack.enter_context(nc.sbuf_tensor("ident_f", [128, 128], F32))
    ident_b = stack.enter_context(nc.sbuf_tensor("ident_b", [128, 128], BF16))
    ones_c = stack.enter_context(nc.sbuf_tensor("ones_c", [128, 1], BF16))
    ones_r = stack.enter_context(nc.sbuf_tensor("ones_r", [1, 128], BF16))
    ones_rf = stack.enter_context(nc.sbuf_tensor("ones_rf", [1, 128], F32))
    bias_q = stack.enter_context(nc.sbuf_tensor("bias_q", [128, KC], F32))
    bias_k = stack.enter_context(nc.sbuf_tensor("bias_k", [128, KC], F32))
    bias_o = stack.enter_context(nc.sbuf_tensor("bias_o", [128, KC], F32))
    bias_r = stack.enter_context(nc.sbuf_tensor("bias_r", [128, 4], F32))
    bias_i = stack.enter_context(nc.sbuf_tensor("bias_i", [128, 4], F32))
    negc = stack.enter_context(nc.sbuf_tensor("negc", [128, 2], F32))
    sc1 = stack.enter_context(nc.sbuf_tensor("sc1", [1, 16], F32))
    qmr = stack.enter_context(nc.sbuf_tensor("qmr", [1, 16], F32))

    pS0 = stack.enter_context(nc.psum_tensor("pS0", [128, 1024], F32))
    pS1 = stack.enter_context(nc.psum_tensor("pS1", [128, 1024], F32))
    pS2 = stack.enter_context(nc.psum_tensor("pS2", [128, 1024], F32))
    pCh = stack.enter_context(nc.psum_tensor("pC", [128, 512], F32))
    pDh = stack.enter_context(nc.psum_tensor("pD", [128, 512], F32))
    pS0a, pS1a, pS2a = pS0[:], pS1[:], pS2[:]
    pA = pS0a[:, 0:512]
    pB = pS0a[:, 512:1024]
    pR = pS1a[:, 0:512]
    pT = pS1a[:, 512:1024]
    pE = pS2a[:, 0:512]
    pF = pS2a[:, 512:1024]
    pC, pD = pCh[:], pDh[:]
    PSUM = {"pA": pA, "pB": pB, "pC": pC, "pD": pD, "pE": pE, "pF": pF, "pR": pR, "pT": pT}

    P = Prog(nc, stack)
    for s in ("T", "V", "S", "la0", "la1", "lb0", "lb1", "lb2", "lb3",
              "lc0", "lc1", "ld0", "ld1", "oa0", "oa1", "ob0", "ob1",
              "og", "ou", "od", "oq", "og1", "ou1", "od1", "oq1", "gd"):
        P.sem(s)

    stAf = stA[:].bitcast(F32)
    big1f = big1[:].bitcast(F32)
    big1i = big1[:].bitcast(I32)

    if emit_l0:
        P.load(bias_q[:], T["bq"], "la0")
        P.load(bias_k[:], T["bk"], "la0")
        P.load(bias_o[:], T["bo"], "la0")
    if emit_l1:
        P.load(bias_r[:], T["br"], "la0")
        P.load(bias_i[:], T["bi"], "la0")
    P.load(ident_f[:], T["ident"], "la0")
    vld = P.cnt("la0")
    P.do("vector", lambda e: e.tensor_copy(ident_b[:], ident_f[:]), waits=[("la0", vld)], inc="V")
    P.do("vector", lambda e: e.memset(ones_c[:], 1.0), inc="V")
    P.do("vector", lambda e: e.memset(ones_r[:], 1.0), inc="V")
    P.do("vector", lambda e: e.memset(ones_rf[:], 1.0), inc="V")
    P.do("vector", lambda e: e.memset(sc1[0:1, 15:16], EPS), inc="V")
    if emit_l0:
        for i in range(4):
            fv = stAf[:, 512 * (i % 2):512 * (i % 2) + 512]
            P.load(fv, T["masks"][i], "la1", waits=[("V", P.cnt("V"))])
            v = P.cnt("la1")
            P.do("vector", lambda e, i=i, fv=fv: e.tensor_copy(
                big2[:, 24704 + 512 * i:24704 + 512 * i + 512], fv),
                waits=[("la1", v)], inc="V")
    vconsts = P.cnt("la0")
    vconstsV = P.cnt("V")

    phase = 0

    def emit_wbuild(src_name, dst_name, R, C, src_dt, osem, stg=None, stoff=0):
        stg = big1 if stg is None else stg
        stg_i = stg[:].bitcast(I32)
        stg_f = stg[:].bitcast(F32)
        src, dst = T[src_name], T[dst_name]
        tw = ("T", P.cnt("T"))
        for k in (("braw", 0), ("braw", 1), ("bconv", 0), ("bconv", 1),
                  ("bev", 0), ("bev", 1)):
            P.slot_mark_add(k, *tw)
        it = 0
        for r in range(R // 128):
            for cc in range(C // 1024):
                slot = it % 2
                it += 1
                raw = (stg_i if src_dt == I32 else stg_f)[
                    :, stoff // 2 + 1024 * slot:stoff // 2 + 1024 * slot + 1024]
                lsm = "lb%d" % slot
                co = stoff + 4096 + 1024 * slot
                eo = stoff + 6144 + 1024 * slot
                lv = P.load(raw, src[r * 128:(r + 1) * 128, cc * 1024:(cc + 1) * 1024],
                            lsm, waits=P.slot_waits(("braw", slot)))
                cv = P.do("vector", lambda e, raw=raw, co=co, stg=stg: e.tensor_copy(
                    stg[:, co:co + 1024], raw),
                    waits=[(lsm, lv)] + P.slot_waits(("bconv", slot)), inc="V")
                P.slot_mark(("braw", slot), "V", cv)
                vts = []
                for kc in range(8):
                    wT = P.bank_waits("pT") + [("V", cv)]
                    vt = P.do("tensor", lambda e, kc=kc, co=co, stg=stg: e.transpose(
                        pT.bitcast(BF16)[:, 128 * kc:128 * kc + 128],
                        stg[:, co + 128 * kc:co + 128 * kc + 128], ident_b[:]),
                        waits=wT, inc="T")
                    vts.append(vt)
                P.slot_mark(("bconv", slot), "T", vts[-1])
                ve = P.do("scalar", lambda e, eo=eo, stg=stg: e.copy(
                    stg[:, eo:eo + 1024], pT.bitcast(BF16)[:, 0:1024]),
                    waits=[("T", vts[-1])] + P.slot_waits(("bev", slot)), inc="S")
                P.bank_consumed("pT", "S", ve)
                dstap = dst[r][cc * 1024:(cc + 1) * 1024, :].rearrange(
                    "(c p) q -> p c q", p=128)
                ssem = osem if slot == 0 else osem + "1"
                sv = P.store(dstap, stg[:, eo:eo + 1024].rearrange("p (c q) -> p c q", q=128),
                             ssem, waits=[("S", ve)])
                P.slot_mark(("bev", slot), ssem, sv)
        return [(osem, P.cnt(osem)), (osem + "1", P.cnt(osem + "1"))]

    def emit_ln(src_name, wait_src=()):
        mu_row = big1[0:1, 30720:31232]
        rstd_row = big1[0:1, 31232:31744]
        var_row = big1f[0:1, 15872:16384]
        lnj = []
        war0 = [(s, P.cnt(s)) for s in ("og", "ou", "od", "oq", "og1", "ou1",
                                        "od1", "oq1", "oa0", "oa1", "ob0", "ob1")
                if P.cnt(s)]
        war0 += [("T", P.cnt("T")), ("S", P.cnt("S")), ("V", P.cnt("V"))]
        for j in range(NB):
            for c in range(KC):
                slot = c % 2
                fv = stAf[:, 2048 + 512 * slot:2560 + 512 * slot]
                lv = P.load(fv, T[src_name][c * 128:(c + 1) * 128, j * 512:(j + 1) * 512],
                            "la%d" % slot,
                            waits=P.slot_waits(("lnf", slot)) + list(wait_src) + war0)
                cv = P.do("vector", lambda e, c=c, fv=fv: e.tensor_copy(
                    stA[:, 512 * c:512 * c + 512], fv),
                    waits=[("la%d" % slot, lv)] + P.slot_waits(("lnx", c)) + war0[-3:],
                    inc="V")
                P.slot_mark(("lnf", slot), "V", cv)
            vconv = P.cnt("V")
            wR = P.bank_waits("pC") + [("V", vconv)]
            for c in range(KC):
                P.do("tensor", lambda e, c=c: e.matmul(
                    pC[0:1, 0:512], ones_c[:], stA[:, 512 * c:512 * c + 512],
                    start=(c == 0), stop=(c == KC - 1)),
                    waits=wR if c == 0 else (), inc="T" if c == KC - 1 else None)
            for c in range(KC):
                vs = P.do("scalar", lambda e, c=c: e.activation(
                    stA[:, 6144:6656], stA[:, 512 * c:512 * c + 512], AF.Square),
                    waits=[("V", vconv), ("T", P.cnt("T"))] + war0, inc="S")
                P.do("tensor", lambda e, c=c: e.matmul(
                    pC[64:65, 0:512], ones_c[:], stA[:, 6144:6656],
                    start=(c == 0), stop=(c == KC - 1)), waits=[("S", vs)], inc="T")
            vstat = P.cnt("T")
            vmu = P.do("scalar", lambda e: e.activation(mu_row, pC[0:1, 0:512], AF.Copy,
                                                        scale=1.0 / D),
                       waits=[("T", vstat)], inc="S")
            wA = P.bank_waits("pD") + [("S", vmu)]
            vbA = P.do("tensor", lambda e: e.matmul(pD[:, 0:512], ones_r[:], mu_row,
                                                    start=True, stop=True), waits=wA, inc="T")
            vq = P.do("vector", lambda e: e.tensor_mul(rstd_row, mu_row, mu_row),
                      waits=[("S", vmu)], inc="V")
            vv = P.do("vector", lambda e: e.scalar_tensor_tensor(
                var_row, pC[64:65, 0:512], 1.0 / D, rstd_row, ALU.mult, ALU.subtract),
                waits=[("V", vq)], inc="V")
            P.bank_consumed("pC", "V", vv)
            vl = P.do("scalar", lambda e: e.activation(mu_row, var_row, AF.Ln,
                                                       bias=sc1[0:1, 15:16]),
                      waits=[("V", vv), ("T", vbA)], inc="S")
            vr = P.do("scalar", lambda e: e.activation(rstd_row, mu_row, AF.Exp, scale=-0.5),
                      waits=[("S", vl)], inc="S")
            wB = P.bank_waits("pE") + [("S", vr)]
            vbB = P.do("tensor", lambda e: e.matmul(pE[:, 0:512], ones_r[:], rstd_row,
                                                    start=True, stop=True), waits=wB, inc="T")
            for c in range(KC):
                vsb = P.do("vector", lambda e, c=c: e.tensor_sub(
                    stA[:, 512 * c:512 * c + 512], stA[:, 512 * c:512 * c + 512], pD[:, 0:512]),
                    waits=[("T", vbA)] if c == 0 else (), inc="V")
                vm2 = P.do("vector", lambda e, c=c, j=j: e.tensor_mul(
                    big0[:, 4096 * c + 512 * j:4096 * c + 512 * j + 512],
                    stA[:, 512 * c:512 * c + 512], pE[:, 0:512]),
                    waits=([("T", vbB)] if c == 0 else []) + [("V", vsb)], inc="V")
                P.slot_mark(("lnx", c), "V", vm2)
            vn = P.cnt("V")
            lnj.append(vn)
            P.bank_consumed("pD", "V", vn)
            P.bank_consumed("pE", "V", vn)
        return lnj

    sb_done = {}
    vx1 = []
    vspill = []
    if emit_l0:
        phase += 1
        if phase <= upto:
            sb_done["wgT"] = emit_wbuild("wg", "wgT", HID, D, I32, "og")
            sb_done["wuT"] = emit_wbuild("wu", "wuT", HID, D, I32, "ou")
            sb_done["wdT"] = emit_wbuild("wd", "wdT", D, HID, I32, "od")
        phase += 1
        lnj0 = [0] * NB
        if phase <= upto:
            lnj0 = emit_ln("x")
            if "xn0" in dbg:
                P.do("gpsimd", lambda e: e.dma_start(out=T["dbg_xn0"], in_=big0[:]),
                     waits=[("V", P.cnt("V"))], inc="gd", n=16)

        phase += 1
        if phase <= upto:
            vxn = P.cnt("V")
            lnj_ = lnj0 if lnj0[0] else [vxn] * NB
            KOFF, VOFF, QOFF = 0, 8192, 16512
            MOFF, POFF, OTOFF, ROFF = 24704, 26752, 30848, 31360
            war_bld = [(s, P.cnt(s)) for s in ("og", "ou", "od", "og1", "ou1", "od1") if P.cnt(s)]
            for k in (("pws", 0), ("pws", 1), "pwf"):
                P.slot_mark_add(k, "V", vxn)
            pv_war = [0] * 3
            att_pend = [None]
            sS = ["pA", "pB", "pR", "pT"]

            def load_wslice(wname, cols0, ncols, slot):
                for hh in range(2):
                    fv = stAf[:, 2080:2080 + 4 * ncols]
                    srcap = T[wname].rearrange("(kc p) q -> p kc q", p=128)[
                        :, 4 * hh:4 * hh + 4, cols0:cols0 + ncols]
                    lv = P.load(fv.rearrange("p (kc q) -> p kc q", kc=4), srcap,
                                "lb0", waits=P.slot_waits("pwf") + war_bld)
                    cv = P.do("vector", lambda e, fv=fv, slot=slot, hh=hh, ncols=ncols: e.tensor_copy(
                        stA[:, 2048 * slot + 4 * ncols * hh:2048 * slot + 4 * ncols * hh + 4 * ncols],
                        fv), waits=[("lb0", lv)] + P.slot_waits(("pws", slot)), inc="V")
                    P.slot_mark("pwf", "V", cv)
                return P.cnt("V")

            for g in range(min(G, GLIM)):
                vwk = load_wslice("wk", g * 256, 256, 0)
                for c2 in range(2):
                    for j in range(NB):
                        bank = ("pA", pA) if (c2 * NB + j) % 2 == 0 else ("pB", pB)
                        pairs = [(stA[:, 256 * kc + 128 * c2:256 * kc + 128 * c2 + 128],
                                  big0[:, 4096 * kc + 512 * j:4096 * kc + 512 * j + 512])
                                 for kc in range(KC)]
                        vm = P.mm_group(bank[1][:, 0:512], pairs,
                                        waits=[("V", vwk), ("V", lnj_[j])], bankname=bank[0])
                        ve = P.do("vector", lambda e, g=g, c2=c2, j=j, b=bank[1]: e.tensor_scalar_add(
                            big2[:, KOFF + 4096 * c2 + 512 * j:KOFF + 4096 * c2 + 512 * j + 512],
                            b[:, 0:512], bias_k[:, 2 * g + c2:2 * g + c2 + 1]),
                            waits=[("T", vm), ("la0", vconsts)], inc="V")
                        P.bank_consumed(bank[0], "V", ve)
                vKg = P.cnt("V")
                P.slot_mark(("pws", 0), "T", P.cnt("T"))
                fvb = stAf[0:1, 3120:3380]
                lvb = P.load(fvb, T["bv"][0:1, g * VW:(g + 1) * VW], "lc0",
                             waits=[("V", P.cnt("V"))] + war_bld)
                vbv = P.do("vector", lambda e, fvb=fvb: e.tensor_copy(
                    stA[0:1, 6760:6760 + VW], fvb),
                    waits=[("lc0", P.cnt("lc0")), ("T", P.cnt("T"))], inc="V")
                vwv = load_wslice("wv", g * VW, VW, 1)
                for i in range(NT):
                    bank = ("pC", pC) if i % 2 == 0 else ("pD", pD)
                    pairs = [(big0[:, 4096 * kc + 128 * i:4096 * kc + 128 * i + 128],
                              stA[:, 2048 + VW * kc:2048 + VW * kc + VW])
                             for kc in range(KC)]
                    P.mm_group(bank[1][:, 0:VW], pairs,
                               waits=[("V", vwv), ("V", lnj_[i // 4])],
                               bankname=bank[0], last_stop=False)
                    vm = P.do("tensor", lambda e, b=bank[1]: e.matmul(
                        b[:, 0:VW], ones_r[:], stA[0:1, 6760:6760 + VW],
                        start=False, stop=True), waits=[("V", vbv)], inc="T")
                    ve = P.do("vector", lambda e, i=i, b=bank[1]: e.tensor_copy(
                        big2[:, VOFF + VW * i:VOFF + VW * i + VW], b[:, 0:VW]),
                        waits=[("T", vm)], inc="V")
                    P.bank_consumed(bank[0], "V", ve)
                vVg = P.cnt("V")
                P.slot_mark(("pws", 1), "T", P.cnt("T"))
                vwq = load_wslice("wq", g * 256, 256, 0)
                for c2 in range(2):
                    for j in range(NB):
                        bank = ("pA", pA) if (c2 * NB + j) % 2 == 0 else ("pB", pB)
                        pairs = [(stA[:, 256 * kc + 128 * c2:256 * kc + 128 * c2 + 128],
                                  big0[:, 4096 * kc + 512 * j:4096 * kc + 512 * j + 512])
                                 for kc in range(KC)]
                        vm = P.mm_group(bank[1][:, 0:512], pairs,
                                        waits=[("V", vwq), ("V", lnj_[j])], bankname=bank[0])
                        ve = P.do("vector", lambda e, g=g, c2=c2, j=j, b=bank[1]: e.tensor_scalar_add(
                            big2[:, QOFF + 4096 * c2 + 512 * j:QOFF + 4096 * c2 + 512 * j + 512],
                            b[:, 0:512], bias_q[:, 2 * g + c2:2 * g + c2 + 1]),
                            waits=[("T", vm), ("la0", vconsts)], inc="V")
                        P.bank_consumed(bank[0], "V", ve)
                vQg = P.cnt("V")
                P.slot_mark(("pws", 0), "T", P.cnt("T"))
                if g == 0 and "kvq0" in dbg:
                    P.do("gpsimd", lambda e: e.dma_start(out=T["dbg_kvq0"], in_=big2[:]),
                         waits=[("V", vQg)], inc="gd", n=16)

                def norm_max(OFFS, vready, outcol):
                    for j in range(NB):
                        for c2 in range(2):
                            wR = (P.bank_waits("pR") if c2 == 0 else [])
                            vs = P.do("scalar", lambda e, c2=c2, j=j, OFFS=OFFS: e.activation(
                                stA[:, 4160:4672],
                                big2[:, OFFS + 4096 * c2 + 512 * j:OFFS + 4096 * c2 + 512 * j + 512],
                                AF.Square),
                                waits=wR + [("V", vready), ("T", P.cnt("T"))], inc="S")
                            P.do("tensor", lambda e, c2=c2: e.matmul(
                                pR[0:1, 0:512], ones_c[:], stA[:, 4160:4672],
                                start=(c2 == 0), stop=(c2 == 1)), waits=[("S", vs)], inc="T")
                        vt = P.cnt("T")
                        vv = P.do("vector", lambda e, j=j: e.tensor_reduce(
                            qmr[0:1, j:j + 1], pR[0:1, 0:512], AX.X, ALU.max),
                            waits=[("T", vt)], inc="V")
                        P.bank_consumed("pR", "V", vv)
                        P._wait("scalar", "V", vv)
                    return P.do("vector", lambda e, outcol=outcol: e.tensor_reduce(
                        sc1[0:1, outcol:outcol + 1], qmr[0:1, 0:NB], AX.X, ALU.max),
                        waits=[("V", P.cnt("V"))], inc="V")

                if SUBPH < 2:
                    continue
                vq = norm_max(QOFF, vQg, 0)
                vk = norm_max(KOFF, vKg, 1)
                P.slot_mark_add("pwf", "S", P.cnt("S"))
                P.slot_mark_add("pwf", "T", P.cnt("T"))
                vp = P.do("vector", lambda e: e.tensor_mul(sc1[0:1, 2:3], sc1[0:1, 0:1],
                                                           sc1[0:1, 1:2]),
                          waits=[("V", vq), ("V", vk)], inc="V")
                vsq = P.do("scalar", lambda e: e.activation(sc1[0:1, 3:4], sc1[0:1, 2:3],
                                                            AF.Sqrt), waits=[("V", vp)], inc="S")
                wT2 = P.bank_waits("pT") + [("S", vsq)]
                vbc = P.do("tensor", lambda e: e.matmul(pT[:, 0:1], ones_rf[:], sc1[0:1, 3:4],
                                                        start=True, stop=True), waits=wT2, inc="T")
                vnc = P.do("scalar", lambda e: e.activation(negc[:, 0:1], pT[:, 0:1], AF.Identity,
                                                            scale=-0.125 * 1.001),
                           waits=[("T", vbc)], inc="S")
                P.bank_consumed("pT", "S", vnc)

                if SUBPH < 3:
                    continue
                SPAIR = [pS0a, pS1a, pS2a]
                SHALF = [["pA", "pB"], ["pR", "pT"], ["pE", "pF"]]
                big2f = big2[:].bitcast(F32)
                rrows = [big2f[0:1, 15680:16192], big2f[0:1, 14912:15424]]

                def att_epilogue2(pend):
                    g_, hswp_, qb_, oz_done_, vrcs_ = pend
                    for hh2 in range(2):
                        hh = 2 * hswp_ + hh2
                        c2, hl = hh // 2, hh % 2
                        oz = PSUM[["pC", "pD"][hh2]]
                        wbc = P.bank_waits("pE") + [("V", vrcs_[hh2])]
                        vbc2 = P.do("tensor", lambda e, rr=rrows[hh2]: e.matmul(
                            pE[0:64, 0:512], ones_rf[0:1, 0:64], rr, start=True, stop=True),
                            waits=wbc, inc="T")
                        ot = big2[0:64, OTOFF:OTOFF + 512]
                        voe = P.do("scalar", lambda e, ot=ot, oz=oz: e.copy(ot, oz[0:64, 0:512]),
                                   waits=[("V", P.cnt("V")), ("S", P.cnt("S"))], inc="S")
                        P.bank_consumed(["pC", "pD"][hh2], "S", voe)
                        dst = big1[64 * hl:64 * hl + 64,
                                   4096 * (2 * g_ + c2) + 512 * qb_:4096 * (2 * g_ + c2) + 512 * qb_ + 512]
                        vml = P.do("vector", lambda e, dst=dst, ot=ot: e.tensor_mul(
                            dst, ot, pE[0:64, 0:512]),
                            waits=[("S", voe), ("T", vbc2)] + (war_bld if g_ == 0 else []),
                            inc="V")
                        P.bank_consumed("pE", "V", vml)
                        P._wait("scalar", "V", vml)

                for hswp in range(2):
                    for qb in range(NB):
                        ktn = 4 * qb + 4
                        oz_done = [None, None]
                        qk_pend = {}

                        def emit_qkpair(kt, hswp=hswp, qb=qb, qk_pend=None):
                            pr = kt % 3
                            qks = []
                            for hh2 in range(2):
                                hh = 2 * hswp + hh2
                                c2, hl = hh // 2, hh % 2
                                bkn = SHALF[pr][hh2]
                                l = big2[64 * hl:64 * hl + 64,
                                         KOFF + 4096 * c2 + 128 * kt:KOFF + 4096 * c2 + 128 * kt + 128]
                                r = big2[64 * hl:64 * hl + 64,
                                         QOFF + 4096 * c2 + 512 * qb:QOFF + 4096 * c2 + 512 * qb + 512]
                                wqk = P.bank_waits(bkn) + [("S", vnc), ("V", vVg)]
                                vqk = P.do("tensor", lambda e, l=l, r=r, b=PSUM[bkn]: e.matmul(
                                    b, l, r, start=True, stop=True), waits=wqk, inc="T")
                                qks.append(vqk)
                            qk_pend[kt] = (pr, qks)

                        emit_qkpair(0, qk_pend=qk_pend)
                        if ktn > 1:
                            emit_qkpair(1, qk_pend=qk_pend)
                        if att_pend[0] is not None:
                            att_epilogue2(att_pend[0])
                            att_pend[0] = None
                        for kt in range(ktn):
                            if kt + 2 < ktn:
                                emit_qkpair(kt + 2, qk_pend=qk_pend)
                            pr, qks = qk_pend.pop(kt)
                            pp = kt % 3
                            ppt = big2[:, POFF + 1024 * pp:POFF + 1024 * pp + 1024]
                            wex = [("T", qks[1]), ("T", pv_war[pp])]
                            vex = P.do("scalar", lambda e, ppt=ppt, b=SPAIR[pr]: e.activation(
                                ppt, b, AF.Exp, bias=negc[:, 0:1], scale=0.125),
                                waits=wex, inc="S")
                            P.bank_consumed(SHALF[pr][0], "S", vex)
                            P.bank_consumed(SHALF[pr][1], "S", vex)
                            pvwait = ("S", vex)
                            if kt >= 4 * qb:
                                dk = kt - 4 * qb
                                for half in range(2):
                                    vmk = P.do("vector", lambda e, ppt=ppt, dk=dk, half=half: e.tensor_mul(
                                        ppt[:, 512 * half:512 * half + 512],
                                        ppt[:, 512 * half:512 * half + 512],
                                        big2[:, MOFF + 512 * dk:MOFF + 512 * dk + 512]),
                                        waits=[("S", vex)], inc="V")
                                pvwait = ("V", vmk)
                            for hh2 in range(2):
                                hh = 2 * hswp + hh2
                                pt = ppt[:, 512 * hh2:512 * hh2 + 512]
                                ozn = ["pC", "pD"][hh2]
                                wpv = [pvwait]
                                if kt == 0:
                                    wpv += P.bank_waits(ozn)
                                lv = big2[:, VOFF + VW * kt + 65 * hh:VOFF + VW * kt + 65 * hh + 65]
                                vpv = P.do("tensor", lambda e, lv=lv, pt=pt, st=(kt == 0),
                                           sp=(kt == ktn - 1), oz=PSUM[ozn]: e.matmul(
                                    oz[0:65, 0:512], lv, pt, start=st, stop=sp), waits=wpv, inc="T")
                                pv_war[pp] = max(pv_war[pp], vpv)
                                oz_done[hh2] = vpv
                        vrcs = []
                        for hh2 in range(2):
                            oz = PSUM[["pC", "pD"][hh2]]
                            vrc = P.do("vector", lambda e, oz=oz, rr=rrows[hh2]: e.reciprocal(
                                rr, oz[64:65, 0:512]),
                                waits=[("T", oz_done[hh2]), ("V", P.cnt("V"))], inc="V")
                            P.bank_consumed_add(["pC", "pD"][hh2], "V", vrc)
                            vrcs.append(vrc)
                        att_pend[0] = (g, hswp, qb, list(oz_done), vrcs)
                if att_pend[0] is not None:
                    att_epilogue2(att_pend[0])
                    att_pend[0] = None
            if att_pend[0] is not None:
                att_epilogue2(att_pend[0])
                att_pend[0] = None
            if "att" in dbg:
                P.do("gpsimd", lambda e: e.dma_start(out=T["dbg_att"], in_=big1[:]),
                     waits=[("V", P.cnt("V"))], inc="gd", n=16)

        phase += 1
        if phase <= upto:
            vOT = P.cnt("V")
            tw = ("T", P.cnt("T"))
            for k in (("wos", 0), ("wos", 1)):
                P.slot_mark_add(k, *tw)
            for dd in range(KC):
                slot = dd % 2
                for hh in range(2):
                    fv = stAf[:, 2080:2592]
                    srcap = T["wo"].rearrange("(kc p) q -> p kc q", p=128)[
                        :, 4 * hh:4 * hh + 4, dd * 128:(dd + 1) * 128]
                    lv = P.load(fv.rearrange("p (kc q) -> p kc q", kc=4), srcap, "lb0",
                                waits=P.slot_waits("pwf"))
                    cv = P.do("vector", lambda e, fv=fv, slot=slot, hh=hh: e.tensor_copy(
                        stA[:, 1024 * slot + 512 * hh:1024 * slot + 512 * hh + 512], fv),
                        waits=[("lb0", lv)] + P.slot_waits(("wos", slot)), inc="V")
                    P.slot_mark("pwf", "V", cv)
                vws = P.cnt("V")
                lastT = None
                for j in range(NB):
                    bank = ("pA", pA) if (dd * NB + j) % 2 == 0 else ("pB", pB)
                    pairs = [(stA[:, 1024 * slot + 128 * kc:1024 * slot + 128 * kc + 128],
                              big1[:, 4096 * kc + 512 * j:4096 * kc + 512 * j + 512])
                             for kc in range(KC)]
                    vm = P.mm_group(bank[1][:, 0:512], pairs, waits=[("V", vws), ("V", vOT)],
                                    bankname=bank[0])
                    lastT = vm
                    sp = stA[:, 6144 + 512 * (j % 2):6144 + 512 * (j % 2) + 512]
                    osm = "oa%d" % (j % 2)
                    ve = P.do("vector", lambda e, sp=sp, dd=dd, b=bank[1]: e.tensor_scalar_add(
                        sp, b[:, 0:512], bias_o[:, dd:dd + 1]),
                        waits=[("T", vm), (osm, P.cnt(osm)), ("la0", vconsts)], inc="V")
                    P.bank_consumed(bank[0], "V", ve)
                    P.store(T["s_s"][dd * 128:(dd + 1) * 128, j * 512:(j + 1) * 512], sp, osm,
                            waits=[("V", ve)])
                P.slot_mark(("wos", slot), "T", lastT)
            vspill = [("oa0", P.cnt("oa0")), ("oa1", P.cnt("oa1"))]
            if "s0" in dbg:
                P.do("gpsimd", lambda e: e.dma_start(out=T["dbg_s0"], in_=T["s_s"]),
                     waits=vspill, inc="gd", n=16)

        if emit_l1 and upto >= 5:
            sb_done["wqkvT"] = emit_wbuild("wqkv", "wqkvT", 3 * D, D, F32, "oq",
                                           stg=big2, stoff=8192)
        phase += 1
        if phase <= upto:
            vxn = P.cnt("V")
            tw = ("T", P.cnt("T"))
            for k in (("fg", 0), ("fg", 1), ("fu", 0), ("fu", 1), ("fd", 0), ("fd", 1),
                      ("sg", 0), ("sg", 1), ("cfx", 0), ("cfx", 1), ("csl", 0),
                      ("csl", 1), ("cpr", 0), ("cpr", 1)):
                P.slot_mark_add(k, *tw)
            for sv_ in vspill:
                P.slot_mark_add("cfx", sv_[0], sv_[1])
            for grp in range(4):
                j0 = grp * 2
                for hc in range(HCN):
                    slot = hc % 2
                    gsm = "lb%d" % (2 * slot)
                    usm = "lb%d" % (2 * slot + 1)
                    lvg = P.load(stA[:, 1024 * slot:1024 * slot + 1024].rearrange(
                        "p (c q) -> p c q", q=128),
                        T["wgT"][hc].rearrange("(c p) q -> p c q", p=128), gsm,
                        waits=P.slot_waits(("fg", slot)) + list(sb_done.get("wgT", [])))
                    lvu = P.load(stA[:, 2048 + 1024 * slot:2048 + 1024 * slot + 1024].rearrange(
                        "p (c q) -> p c q", q=128),
                        T["wuT"][hc].rearrange("(c p) q -> p c q", p=128), usm,
                        waits=P.slot_waits(("fu", slot)) + list(sb_done.get("wuT", [])))
                    lastg = lastu = None
                    for j2 in range(2):
                        j = j0 + j2
                        gb = ("pA", pA) if j2 == 0 else ("pB", pB)
                        ub = ("pC", pC) if j2 == 0 else ("pD", pD)
                        gp = [(stA[:, 1024 * slot + 128 * kc:1024 * slot + 128 * kc + 128],
                               big0[:, 4096 * kc + 512 * j:4096 * kc + 512 * j + 512])
                              for kc in range(KC)]
                        upp = [(stA[:, 2048 + 1024 * slot + 128 * kc:2048 + 1024 * slot + 128 * kc + 128],
                                big0[:, 4096 * kc + 512 * j:4096 * kc + 512 * j + 512])
                               for kc in range(KC)]
                        vmg = P.mm_group(gb[1][:, 0:512], gp, waits=[(gsm, lvg), ("V", vxn)],
                                         bankname=gb[0])
                        vmu2 = P.mm_group(ub[1][:, 0:512], upp, waits=[(usm, lvu)],
                                          bankname=ub[0])
                        lastg, lastu = vmg, vmu2
                        sgo = [4096, 6656][j2]
                        sg = stA[:, sgo:sgo + 512]
                        vsg = P.do("scalar", lambda e, sg=sg, b=gb[1]: e.activation(
                            sg, b[:, 0:512], AF.Silu, scale=1.0 / 127.0),
                            waits=[("T", vmg)] + P.slot_waits(("sg", j2)) + vspill, inc="S")
                        P.bank_consumed(gb[0], "S", vsg)
                        hdst = big1[:, 1024 * hc + 512 * j2:1024 * hc + 512 * j2 + 512]
                        vh = P.do("vector", lambda e, hdst=hdst, sg=sg, b=ub[1]: e.tensor_mul(
                            hdst, sg, b[:, 0:512]), waits=[("S", vsg), ("T", vmu2)], inc="V")
                        P.bank_consumed(ub[0], "V", vh)
                        P.slot_mark(("sg", j2), "V", vh)
                    P.slot_mark(("fg", slot), "T", lastg)
                    P.slot_mark(("fu", slot), "T", lastu)
                vhall = P.cnt("V")
                for dd in range(KC):
                    dslot = dd % 2
                    do_ = 24576 + 3072 * dslot
                    dsm = "ld%d" % dslot
                    lvd = P.load(big1[:, do_:do_ + 3072].rearrange("p (c q) -> p c q", q=128),
                                 T["wdT"][dd].rearrange("(c p) q -> p c q", p=128), dsm,
                                 waits=P.slot_waits(("fd", dslot)) + list(sb_done.get("wdT", [])))
                    lastd = None
                    for j2 in range(2):
                        bank = ("pE", pE) if j2 == 0 else ("pF", pF)
                        pairs = [(big1[:, do_ + 128 * hcc:do_ + 128 * hcc + 128],
                                  big1[:, 1024 * hcc + 512 * j2:1024 * hcc + 512 * j2 + 512])
                                 for hcc in range(HCN)]
                        vm = P.mm_group(bank[1][:, 0:512], pairs,
                                        waits=[(dsm, lvd), ("V", vhall)], bankname=bank[0])
                        lastd = vm
                        ve = P.do("scalar", lambda e, dd=dd, j2=j2, b=bank[1]: e.activation(
                            big2[:, 1024 * dd + 512 * j2:1024 * dd + 512 * j2 + 512],
                            b[:, 0:512], AF.Silu, scale=1.0 / 16129.0),
                            waits=[("T", vm)] + P.slot_waits(("gch", dd * 2 + j2)), inc="S")
                        P.bank_consumed(bank[0], "S", ve)
                    P.slot_mark(("fd", dslot), "T", lastd)
                vgrp = P.cnt("S")
                if grp == 0 and "gg" in dbg:
                    P.do("gpsimd", lambda e: e.dma_start(out=T["dbg_gg"], in_=big2[:]),
                         waits=[("S", vgrp)], inc="gd", n=16)
                b2f = big2[:].bitcast(F32)
                for j2 in range(2):
                    j = j0 + j2
                    for c in range(KC):
                        sl = c % 2
                        fx = b2f[:, 8192 + 512 * sl:8192 + 512 * sl + 512]
                        fo = b2f[:, 9216 + 512 * sl:9216 + 512 * sl + 512]
                        ssl = big2[:, 20480 + 512 * sl:20480 + 512 * sl + 512]
                        prd = big2[:, 21504 + 512 * sl:21504 + 512 * sl + 512]
                        osm = "ob%d" % sl
                        lvx = P.load(fx, T["x"][c * 128:(c + 1) * 128, j * 512:(j + 1) * 512],
                                     "lc0", waits=P.slot_waits(("cfx", sl)))
                        lvs = P.load(ssl, T["s_s"][c * 128:(c + 1) * 128, j * 512:(j + 1) * 512],
                                     "lc1", waits=vspill + P.slot_waits(("csl", sl)))
                        vp2 = P.do("vector", lambda e, prd=prd, ssl=ssl, c=c, j2=j2: e.tensor_mul(
                            prd, ssl, big2[:, 1024 * c + 512 * j2:1024 * c + 512 * j2 + 512]),
                            waits=[("lc1", lvs), ("S", vgrp)] + P.slot_waits(("cpr", sl)), inc="V")
                        P.slot_mark(("csl", sl), "V", vp2)
                        P.slot_mark_add(("gch", c * 2 + j2), "V", vp2)
                        vo = P.do("vector", lambda e, fo=fo, fx=fx, prd=prd: e.tensor_add(
                            fo, fx, prd),
                            waits=[("lc0", lvx), (osm, P.cnt(osm))], inc="V")
                        P.slot_mark(("cfx", sl), "V", vo)
                        P.slot_mark(("cpr", sl), "V", vo)
                        P.store(T["x1"][c * 128:(c + 1) * 128, j * 512:(j + 1) * 512], fo,
                                osm, waits=[("V", vo)])
            vx1 = [("ob0", P.cnt("ob0")), ("ob1", P.cnt("ob1"))]

    if emit_l1:
        phase += 1
        if phase <= upto and "wqkvT" not in sb_done:
            sb_done["wqkvT"] = emit_wbuild("wqkv", "wqkvT", 3 * D, D, F32, "oq")
        phase += 1
        lnj1 = [0] * NB
        if phase <= upto:
            lnj1 = emit_ln("x1", wait_src=vx1)
            if "xn1" in dbg:
                P.do("gpsimd", lambda e: e.dma_start(out=T["dbg_xn1"], in_=big0[:]),
                     waits=[("V", P.cnt("V"))], inc="gd", n=16)
        phase += 1
        vgs = []
        if phase <= upto:
            war_q = [(s, P.cnt(s)) for s in ("oq", "oq1") if P.cnt(s)] + [("T", P.cnt("T"))]
            P.slot_mark_add("wba", "V", P.cnt("V"))
            for idx, (an, bn) in enumerate((("aR", "bR"), ("aI", "bI"))):
                for ch in range(4):
                    for c in range(4):
                        fa = stAf[:, 0:512]
                        fb = stAf[:, 512:1024]
                        acc = stAf[:, 1024:1536]
                        la = P.load(fa, T[an][c][ch * 128:(ch + 1) * 128, :], "lb0",
                                    waits=P.slot_waits("wba") + war_q[-1:])
                        lb = P.load(fb, T[bn][c][ch * 128:(ch + 1) * 128, :], "lb1",
                                    waits=P.slot_waits("wba"))
                        if c == 0:
                            vmu0 = P.do("vector", lambda e, fa=fa, fb=fb, acc=acc: e.tensor_mul(
                                acc, fa, fb), waits=[("lb0", la), ("lb1", lb)], inc="V")
                        else:
                            vt0 = P.do("vector", lambda e, fa=fa, fb=fb: e.tensor_mul(
                                stAf[:, 1536:2048], fa, fb),
                                waits=[("lb0", la), ("lb1", lb)], inc="V")
                            vmu0 = P.do("vector", lambda e, acc=acc: e.tensor_add(
                                acc, acc, stAf[:, 1536:2048]), waits=[("V", vt0)], inc="V")
                        P.slot_mark("wba", "V", vmu0)
                    vcp = P.do("vector", lambda e, idx=idx, ch=ch, acc=acc: e.tensor_copy(
                        big1[:, 2048 * idx + 512 * ch:2048 * idx + 512 * ch + 512], acc),
                        waits=war_q + [("V", vmu0)], inc="V")
                    P.slot_mark("wba", "V", vcp)
            vwt = P.cnt("V")
            vneg = P.do("vector", lambda e: e.tensor_scalar_mul(
                big1[:, 4096:6144], big1[:, 2048:4096], -1.0), waits=[("V", vwt)] + war_q,
                inc="V")
            if "WT" in dbg:
                P.do("gpsimd", lambda e: e.dma_start(out=T["dbg_WT"], in_=big1[:, 0:6144]),
                     waits=[("V", vneg)], inc="gd", n=16)
            vxnl = P.cnt("V")
            for j in range(NB):
                for half in range(2):
                    for oc in range(4):
                        bank = ("pA", pA) if oc % 2 == 0 else ("pB", pB)
                        pairs = []
                        for ic in range(4):
                            wofs = 0 if half == 0 else 2048
                            pairs.append((big1[:, wofs + 512 * ic + 128 * oc:wofs + 512 * ic + 128 * oc + 128],
                                          big0[:, 4096 * ic + 512 * j:4096 * ic + 512 * j + 512]))
                        for ic in range(4):
                            wofs = 4096 if half == 0 else 0
                            pairs.append((big1[:, wofs + 512 * ic + 128 * oc:wofs + 512 * ic + 128 * oc + 128],
                                          big0[:, 4096 * (4 + ic) + 512 * j:4096 * (4 + ic) + 512 * j + 512]))
                        vm = P.mm_group(bank[1][:, 0:512], pairs,
                                        waits=[("V", vneg), ("V", lnj1[j] or vxnl)],
                                        bankname=bank[0])
                        gsl = big2[:, 512 * (oc % 2):512 * (oc % 2) + 512]
                        osm = "oa%d" % (oc % 2)
                        bias = bias_r if half == 0 else bias_i
                        ve = P.do("scalar", lambda e, gsl=gsl, oc=oc, b=bank[1], bias=bias: e.activation(
                            gsl, b[:, 0:512], AF.Silu, bias=bias[:, oc:oc + 1]),
                            waits=[("T", vm), (osm, P.cnt(osm)), ("la0", vconsts)], inc="S")
                        P.bank_consumed(bank[0], "S", ve)
                        P.store(T["g_s"][512 * half + 128 * oc:512 * half + 128 * oc + 128,
                                         j * 512:(j + 1) * 512], gsl, osm,
                                waits=[("S", ve)])
            vgs = [("oa0", P.cnt("oa0")), ("oa1", P.cnt("oa1"))]
            if "g1" in dbg:
                P.do("gpsimd", lambda e: e.dma_start(out=T["dbg_g1"], in_=T["g_s"]),
                     waits=vgs, inc="gd", n=16)

        phase += 1
        if phase <= upto:
            vxnl = P.cnt("V")
            tw = ("T", P.cnt("T"))
            for k in (("l1k", 0), ("l1k", 1), "l1v"):
                P.slot_mark_add(k, *tw)
                P.slot_mark_add(k, "V", vxnl)
            for fc in range(KC):
                slot = fc % 2
                ksm = "lb%d" % slot
                lvk = P.load(stA[:, 1024 * slot:1024 * slot + 1024].rearrange(
                    "p (c q) -> p c q", q=128),
                    T["wqkvT"][8 + fc].rearrange("(c p) q -> p c q", p=128), ksm,
                    waits=P.slot_waits(("l1k", slot)) + list(sb_done.get("wqkvT", [])))
                lastm = None
                for j in range(NB):
                    bank = ("pA", pA) if (fc * NB + j) % 2 == 0 else ("pB", pB)
                    pairs = [(stA[:, 1024 * slot + 128 * kc:1024 * slot + 128 * kc + 128],
                              big0[:, 4096 * kc + 512 * j:4096 * kc + 512 * j + 512])
                             for kc in range(KC)]
                    vm = P.mm_group(bank[1][:, 0:512], pairs,
                                    waits=[(ksm, lvk), ("V", lnj1[j] or vxnl)], bankname=bank[0])
                    lastm = vm
                    ve = P.do("vector", lambda e, fc=fc, j=j, b=bank[1]: e.tensor_copy(
                        big1[:, 4096 * fc + 512 * j:4096 * fc + 512 * j + 512], b[:, 0:512]),
                        waits=[("T", vm)], inc="V")
                    P.bank_consumed(bank[0], "V", ve)
                P.slot_mark(("l1k", slot), "T", lastm)
            for fg in range(2):
                for kc in range(KC):
                    P.load(stA[:, 2048 + 512 * kc:2048 + 512 * kc + 512].rearrange(
                        "p (f q) -> p f q", q=128),
                        T["wqkvT"].rearrange("f (kc p) q -> p f kc q", p=128)[
                            :, 16 + 4 * fg:16 + 4 * fg + 4, kc, :],
                        "lb0", waits=P.slot_waits("l1v") + vgs +
                        list(sb_done.get("wqkvT", [])))
                lvv = P.cnt("lb0")
                lastm = None
                for i in range(NT):
                    bank = ("pC", pC) if i % 2 == 0 else ("pD", pD)
                    pairs = [(big0[:, 4096 * kc + 128 * i:4096 * kc + 128 * i + 128],
                              stA[:, 2048 + 512 * kc:2048 + 512 * kc + 512])
                             for kc in range(KC)]
                    vm = P.mm_group(bank[1][:, 0:512], pairs,
                                    waits=[("lb0", lvv), ("V", lnj1[i // 4] or vxnl)],
                                    bankname=bank[0])
                    lastm = vm
                    ve = P.do("vector", lambda e, i=i, fg=fg, b=bank[1]: e.tensor_copy(
                        big2[:, 1024 * i + 512 * fg:1024 * i + 512 * fg + 512], b[:, 0:512]),
                        waits=[("T", vm)], inc="V")
                    P.bank_consumed(bank[0], "V", ve)
                P.slot_mark("l1v", "T", lastm)
            BANKS = ["pA", "pB", "pC", "pD", "pE", "pF", "pR", "pT"]
            for j in range(NB):
                evs = []
                for fc in range(KC):
                    slot = fc % 2
                    lvq = P.load(stA[:, 1024 * slot:1024 * slot + 1024].rearrange(
                        "p (c q) -> p c q", q=128),
                        T["wqkvT"][fc].rearrange("(c p) q -> p c q", p=128),
                        "lb%d" % slot, waits=P.slot_waits(("l1k", slot)))
                    bank = BANKS[fc]
                    pairs = [(stA[:, 1024 * slot + 128 * kc:1024 * slot + 128 * kc + 128],
                              big0[:, 4096 * kc + 512 * j:4096 * kc + 512 * j + 512])
                             for kc in range(KC)]
                    vm = P.mm_group(PSUM[bank][:, 0:512], pairs,
                                    waits=[("lb%d" % slot, lvq), ("V", lnj1[j] or vxnl)],
                                    bankname=bank)
                    P.slot_mark(("l1k", slot), "T", vm)
                    evs.append((bank, vm))
                vall = P.cnt("T")
                for fc, (bank, vm) in enumerate(evs):
                    ve = P.do("vector", lambda e, fc=fc, j=j, bank=bank: e.tensor_copy(
                        big0[:, 4096 * fc + 512 * j:4096 * fc + 512 * j + 512],
                        PSUM[bank][:, 0:512]), waits=[("T", vall)], inc="V")
                    P.bank_consumed(bank, "V", ve)
            vqkv = P.cnt("V")
            if "kq1" in dbg:
                P.do("gpsimd", lambda e: e.dma_start(out=T["dbg_k1"], in_=big1[:]),
                     waits=[("V", vqkv)], inc="gd", n=16)
                P.do("gpsimd", lambda e: e.dma_start(out=T["dbg_q1"], in_=big0[:]),
                     waits=[("V", vqkv)], inc="gd", n=16)
            if "v1" in dbg:
                P.do("gpsimd", lambda e: e.dma_start(out=T["dbg_v1"], in_=big2[:]),
                     waits=[("V", vqkv)], inc="gd", n=16)

        phase += 1
        if phase <= upto:
            vqkv = P.cnt("V")

            def norm_max_l1(buf, vready, outcol):
                for j in range(NB):
                    for kc in range(KC):
                        wR = (P.bank_waits("pR") if kc == 0 else [])
                        vs = P.do("scalar", lambda e, kc=kc, j=j, buf=buf: e.activation(
                            stA[:, 4096:4608],
                            buf[:, 4096 * kc + 512 * j:4096 * kc + 512 * j + 512], AF.Square),
                            waits=wR + [("V", vready), ("T", P.cnt("T"))] + vgs, inc="S")
                        P.do("tensor", lambda e, kc=kc: e.matmul(
                            pR[0:1, 0:512], ones_c[:], stA[:, 4096:4608],
                            start=(kc == 0), stop=(kc == KC - 1)), waits=[("S", vs)], inc="T")
                    vt = P.cnt("T")
                    vv = P.do("vector", lambda e, j=j: e.tensor_reduce(
                        qmr[0:1, 8 + j:9 + j], pR[0:1, 0:512], AX.X, ALU.max),
                        waits=[("T", vt)], inc="V")
                    P.bank_consumed("pR", "V", vv)
                    P._wait("scalar", "V", vv)
                return P.do("vector", lambda e, outcol=outcol: e.tensor_reduce(
                    sc1[0:1, outcol:outcol + 1], qmr[0:1, 8:8 + NB], AX.X, ALU.max),
                    waits=[("V", P.cnt("V"))], inc="V")

            vqn = norm_max_l1(big0, vqkv, 4)
            vkn = norm_max_l1(big1, vqkv, 5)
            vpp = P.do("vector", lambda e: e.tensor_mul(sc1[0:1, 6:7], sc1[0:1, 4:5],
                                                        sc1[0:1, 5:6]),
                       waits=[("V", vqn), ("V", vkn)], inc="V")
            vsq = P.do("scalar", lambda e: e.activation(sc1[0:1, 7:8], sc1[0:1, 6:7],
                                                        AF.Sqrt), waits=[("V", vpp)], inc="S")
            wT2 = P.bank_waits("pT") + [("S", vsq)]
            vbc = P.do("tensor", lambda e: e.matmul(pT[:, 0:1], ones_rf[:], sc1[0:1, 7:8],
                                                    start=True, stop=True), waits=wT2, inc="T")
            vnc1 = P.do("scalar", lambda e: e.activation(negc[:, 1:2], pT[:, 0:1], AF.Identity,
                                                         scale=-(1.0 / 32.0) * 1.001),
                        waits=[("T", vbc)], inc="S")
            P.bank_consumed("pT", "S", vnc1)

            PVB = ["pC", "pD", "pE", "pF"]
            pv_war1 = [0, 0]
            qk_pend = {}

            def emit_qk(qb, kt):
                sb = ("pA", pA) if kt % 2 == 0 else ("pB", pB)
                pairs = [(big1[:, 4096 * kc + 128 * kt:4096 * kc + 128 * kt + 128],
                          big0[:, 4096 * kc + 256 * qb:4096 * kc + 256 * qb + 256])
                         for kc in range(KC)]
                vqk = P.mm_group(sb[1][:, 0:256], pairs, waits=[("S", vnc1), ("V", vqkv)],
                                 bankname=sb[0])
                qk_pend[(qb, kt)] = (sb, vqk)

            for qb in range(16):
                if qb == 0:
                    emit_qk(0, 0)
                pv_done = None
                for kt in range(NT):
                    sb, vqk = qk_pend.pop((qb, kt))
                    if kt + 1 < NT:
                        emit_qk(qb, kt + 1)
                    elif qb + 1 < 16:
                        emit_qk(qb + 1, 0)
                    ptile = stA[:, 4608 + 256 * (kt % 2):4608 + 256 * (kt % 2) + 256]
                    vex = P.do("scalar", lambda e, ptile=ptile, b=sb[1]: e.activation(
                        ptile, b[:, 0:256], AF.Exp, bias=negc[:, 1:2], scale=1.0 / 32.0),
                        waits=[("T", vqk), ("T", pv_war1[kt % 2])], inc="S")
                    P.bank_consumed(sb[0], "S", vex)
                    wz = [("S", vex)]
                    if kt == 0:
                        wz += P.bank_waits("pR")
                    P.do("tensor", lambda e, ptile=ptile, st=(kt == 0), sp=(kt == NT - 1): e.matmul(
                        pR[0:1, 0:256], ones_c[:], ptile, start=st, stop=sp), waits=wz, inc="T")
                    lastpv = None
                    for fc in range(KC):
                        bn = PVB[fc // 2]
                        col0 = 256 * (fc % 2)
                        wpv = []
                        if kt == 0 and fc % 2 == 0:
                            wpv += P.bank_waits(bn)
                        lvv = big2[:, 1024 * kt + 128 * fc:1024 * kt + 128 * fc + 128]
                        lastpv = P.do("tensor", lambda e, lvv=lvv, ptile=ptile, bn=bn, col0=col0,
                                      st=(kt == 0 and fc % 2 == 0),
                                      sp=(kt == NT - 1 and fc % 2 == 1): e.matmul(
                            PSUM[bn][:, col0:col0 + 256], lvv, ptile, start=st, stop=sp),
                            waits=wpv, inc="T" if fc == KC - 1 else None)
                    pv_war1[kt % 2] = lastpv
                    pv_done = lastpv
                rrow1 = stAf[0:1, 2816:3072]
                vrc = P.do("vector", lambda e: e.reciprocal(
                    rrow1, pR[0:1, 0:256]),
                    waits=[("T", P.cnt("T")), ("V", P.cnt("V"))], inc="V")
                P.bank_consumed("pR", "V", vrc)
                wbc = P.bank_waits("pT") + [("V", vrc)]
                vbc2 = P.do("tensor", lambda e: e.matmul(
                    pT[:, 0:256], ones_rf[:], rrow1, start=True, stop=True), waits=wbc, inc="T")
                for fc in range(KC):
                    bn = PVB[fc // 2]
                    col0 = 256 * (fc % 2)
                    ot = stA[:, 6144 + 256 * (fc % 2):6144 + 256 * (fc % 2) + 256]
                    voe = P.do("scalar", lambda e, ot=ot, bn=bn, col0=col0: e.copy(
                        ot, PSUM[bn][:, col0:col0 + 256]),
                        waits=[("T", pv_done), ("V", P.cnt("V"))], inc="S")
                    if fc % 2 == 1:
                        P.bank_consumed(bn, "S", voe)
                    dst = big0[:, 4096 * fc + 256 * qb:4096 * fc + 256 * qb + 256]
                    vml = P.do("vector", lambda e, dst=dst, ot=ot: e.tensor_mul(
                        dst, ot, pT[:, 0:256]), waits=[("S", voe), ("T", vbc2)], inc="V")
                P.bank_consumed("pT", "V", P.cnt("V"))

        phase += 1
        if phase <= upto:
            vs1 = P.cnt("V")
            b2f = big2[:].bitcast(F32)
            for j in range(NB):
                for c in range(KC):
                    sl = c % 2
                    fx = b2f[:, 8192 + 512 * sl:8192 + 512 * sl + 512]
                    fo = b2f[:, 9216 + 512 * sl:9216 + 512 * sl + 512]
                    gsl = big2[:, 20480 + 512 * sl:20480 + 512 * sl + 512]
                    prd = big2[:, 21504 + 512 * sl:21504 + 512 * sl + 512]
                    osm = "ob%d" % sl
                    lvx = P.load(fx, T["x1"][c * 128:(c + 1) * 128, j * 512:(j + 1) * 512],
                                 "lc0", waits=P.slot_waits(("cfx", sl)) + vx1 + [("V", vs1),
                                 ("T", P.cnt("T")), ("S", P.cnt("S"))])
                    lvg = P.load(gsl, T["g_s"][c * 128:(c + 1) * 128, j * 512:(j + 1) * 512],
                                 "lc1", waits=vgs + P.slot_waits(("csl", sl)) + [("V", vs1)])
                    vp2 = P.do("vector", lambda e, prd=prd, gsl=gsl, c=c, j=j: e.tensor_mul(
                        prd, gsl, big0[:, 4096 * c + 512 * j:4096 * c + 512 * j + 512]),
                        waits=[("lc1", lvg), ("V", vs1), ("S", P.cnt("S"))] + P.slot_waits(("cpr", sl)),
                        inc="V")
                    P.slot_mark(("csl", sl), "V", vp2)
                    vo = P.do("vector", lambda e, fo=fo, fx=fx, prd=prd: e.tensor_add(
                        fo, fx, prd),
                        waits=[("lc0", lvx), (osm, P.cnt(osm))], inc="V")
                    P.slot_mark(("cfx", sl), "V", vo)
                    P.slot_mark(("cpr", sl), "V", vo)
                    P.store(T["out"][c * 128:(c + 1) * 128, j * 512:(j + 1) * 512], fo,
                            osm, waits=[("V", vo)])

    if "x1d" in dbg and which == "full":
        P.do("gpsimd", lambda e: e.dma_start(out=T["dbg_x1d"], in_=T["x1"]),
             waits=vx1, inc="gd", n=16)
    final_waits = []
    for s in ("oa0", "oa1", "ob0", "ob1", "og", "ou", "od", "oq",
              "og1", "ou1", "od1", "oq1", "gd"):
        if P.cnt(s):
            final_waits.append((s, P.cnt(s)))
    for s, v in final_waits:
        P._wait("gpsimd", s, v)

    P.check()
    _maxsem = max(P.cnt(s) for s in P.sems)
    assert _maxsem < 60000, f"semaphore count overflow risk: {_maxsem}"

    nums = sorted(h.num for h, _ in P.sems.values())
    assert nums[-1] - nums[0] + 1 == len(nums), "sems not contiguous"
    nc.gpsimd.dma_reset(range(nums[0], nums[-1] + 1))
    nc.gpsimd.sem_clear(range(nums[0], nums[-1] + 1))
    nc.all_engine_barrier()

    with nc.Block() as block:
        @block.sync
        def _(e):
            for f in P.ops["sync"]:
                f(e)

        @block.tensor
        def _(e):
            for f in P.ops["tensor"]:
                f(e)

        @block.vector
        def _(e):
            for f in P.ops["vector"]:
                f(e)

        @block.scalar
        def _(e):
            for f in P.ops["scalar"]:
                f(e)

        @block.gpsimd
        def _(e):
            for f in P.ops["gpsimd"]:
                f(e)

    stack.close()
    return nc


def prep_inputs(inp, core):
    d = {}
    d["x"] = np.ascontiguousarray(inp["L0_attn_x"][core].T)
    d["wq"] = inp["L0_attn_wq"]
    d["wk"] = inp["L0_attn_wk"]
    d["wo"] = inp["L0_attn_wo"]
    wv = inp["L0_attn_wv"]
    wvp = np.zeros((D, H * (HD + 1)), np.float32)
    for h in range(H):
        wvp[:, h * 65:h * 65 + 64] = wv[:, h * 64:(h + 1) * 64]
    d["wv"] = wvp
    for nm, key in (("bq", "L0_attn_bq"), ("bk", "L0_attn_bk"), ("bo", "L0_attn_bo")):
        d[nm] = np.ascontiguousarray(inp[key].reshape(KC, 128).T)
    bv = inp["L0_attn_bv"]
    bvp = np.zeros((1, H * (HD + 1)), np.float32)
    for h in range(H):
        bvp[0, h * 65:h * 65 + 64] = bv[h * 64:(h + 1) * 64]
        bvp[0, h * 65 + 64] = 1.0
    d["bv"] = bvp
    d["wg"] = inp["L0_ffn_w_gate"]
    d["wu"] = inp["L0_ffn_w_up"]
    d["wd"] = inp["L0_ffn_w_down"]
    k = np.arange(128)[:, None]
    q = np.arange(512)[None, :]
    masks = np.zeros((4, 128, 512), np.float32)
    for dk in range(4):
        masks[dk] = (dk * 128 + k <= q).astype(np.float32)
    d["masks"] = masks
    d["ident"] = np.eye(128, dtype=np.float32)
    d["wqkv"] = inp["L1_attn_W_qkv"]

    def _arep(G1):
        A = np.transpose(G1[0], (1, 2, 0))
        A = np.broadcast_to(A[:, :, None, :, None], (4, 32, 16, 32, 16))
        return np.ascontiguousarray(A.reshape(4, 512, 512))

    def _brep(G2):
        B = np.transpose(G2[:, :, 0, :], (0, 2, 1))
        B = np.broadcast_to(B[:, None, :, None, :], (4, 32, 16, 32, 16))
        return np.ascontiguousarray(B.reshape(4, 512, 512))

    d["aR"] = _arep(inp["L1_ffn_G1_R"])
    d["aI"] = _arep(inp["L1_ffn_G1_I"])
    d["bR"] = _brep(inp["L1_ffn_G2_R"])
    d["bI"] = _brep(inp["L1_ffn_G2_I"])
    d["br"] = np.ascontiguousarray(inp["L1_ffn_bias_real"].reshape(4, 128).T)
    d["bi"] = np.ascontiguousarray(inp["L1_ffn_bias_imag"].reshape(4, 128).T)
    return d


def split_inputs(d, which="full"):
    l0 = {"x", "wq", "wk", "wv", "wo", "bq", "bk", "bv", "bo", "wg", "wu", "wd",
          "masks", "ident"}
    l1 = {"wqkv", "aR", "bR", "aI", "bI", "br", "bi", "ident"}
    if which == "l0":
        return {k: v for k, v in d.items() if k in l0}
    if which == "l1":
        return {k: v for k, v in d.items() if k in l1}
    return d


_NC_CACHE = [None]


def kernel(**inputs):
    import numpy as np
    from concourse.bass_utils import run_bass_kernel_spmd

    if _NC_CACHE[0] is None:
        _NC_CACHE[0] = build(which="full")
    nc = _NC_CACHE[0]
    in_maps = []
    for core in range(8):
        d = prep_inputs(inputs, core)
        in_maps.append(split_inputs(d, "full"))
    res = run_bass_kernel_spmd(nc, in_maps, core_ids=list(range(8)))
    outs = [np.asarray(res.results[c]["out"], dtype=np.float32).T for c in range(8)]
    return np.stack(outs, axis=0)


# version 14
# speedup vs baseline: 1.1784x; 1.0124x over previous
import numpy as np
from contextlib import ExitStack

import concourse.bass as bass
import concourse.mybir as mybir

F32 = mybir.dt.float32
BF16 = mybir.dt.bfloat16
I32 = mybir.dt.int32
AF = mybir.ActivationFunctionType
ALU = mybir.AluOpType
AX = mybir.AxisListType

N = 4096
D = 1024
KC = 8
NB = 8
NT = 32
H, HD = 16, 64
G, HG = 4, 4
HID, HCN = 3072, 24
EPS = 1e-5
VW = HG * (HD + 1)
SUBPH = 99
SELF_FENCE = True
GLIM = 99

ENGINES = ("sync", "tensor", "vector", "scalar", "gpsimd")


class Prog:
    def __init__(self, nc, stack):
        self.nc = nc
        self.stack = stack
        self.ops = {e: [] for e in ENGINES}
        self.sems = {}
        self.waited = {e: {} for e in ENGINES}
        self.bank = {}
        self.slot = {}
        self.semgate = {}

    def sem(self, name):
        if name not in self.sems:
            h = self.stack.enter_context(self.nc.semaphore(name))
            self.sems[name] = [h, 0]
        return name

    def cnt(self, name):
        return self.sems[name][1]

    def _wait(self, eng, s, v):
        if v <= 0:
            return
        w = self.waited[eng]
        if w.get(s, 0) >= v:
            return
        w[s] = v
        h = self.sems[s][0]
        self.ops[eng].append(lambda e, h=h, v=v: e.wait_ge(h, v))

    def do(self, eng, fn, waits=(), inc=None, n=1):
        if SELF_FENCE:
            if eng == "vector":
                self._wait(eng, "V", self.cnt("V"))
            elif eng == "scalar":
                self._wait(eng, "S", self.cnt("S"))
        for (s, v) in waits:
            self._wait(eng, s, v)
        if inc is not None:
            self.sem(inc)
            self.sems[inc][1] += n
            h = self.sems[inc][0]
            self.ops[eng].append(lambda e, fn=fn, h=h, n=n: fn(e).then_inc(h, n))
            if inc in ("T", "V", "S"):
                for (s, v) in waits:
                    if s.startswith("l"):
                        self.semgate[s] = (inc, self.sems[inc][1])
            return self.sems[inc][1]
        self.ops[eng].append(lambda e, fn=fn: fn(e))
        return None

    def load(self, dst, src, sem, waits=(), eng="sync", **kw):
        g = self.semgate.get(sem)
        allw = list(waits) + ([g] if g else [])
        return self.do(eng, lambda e, dst=dst, src=src, kw=kw: e.dma_start(out=dst, in_=src, **kw),
                       waits=allw, inc=sem, n=16)

    def store(self, dst, src, sem, waits=(), eng="scalar", **kw):
        return self.do(eng, lambda e, dst=dst, src=src, kw=kw: e.dma_start(out=dst, in_=src, **kw),
                       waits=waits, inc=sem, n=16)

    def bank_waits(self, name):
        return list(self.bank.get(name, ()))

    def bank_consumed(self, name, sem, val):
        self.bank[name] = [(sem, val)]

    def bank_consumed_add(self, name, sem, val):
        self.bank.setdefault(name, []).append((sem, val))

    def slot_waits(self, key):
        return list(self.slot.get(key, ()))

    def slot_mark(self, key, sem, val):
        self.slot[key] = [(sem, val)]

    def slot_mark_add(self, key, sem, val):
        self.slot.setdefault(key, []).append((sem, val))

    def mm_group(self, out_ap, pairs, waits=(), bankname=None, first_start=True,
                 last_stop=True, inc="T"):
        n = len(pairs)
        allw = list(waits)
        if bankname is not None and first_start:
            allw += self.bank_waits(bankname)
        r = None
        for i, (l, rr) in enumerate(pairs):
            st = first_start and (i == 0)
            sp = last_stop and (i == n - 1)
            r = self.do("tensor",
                        lambda e, l=l, rr=rr, st=st, sp=sp, o=out_ap: e.matmul(o, l, rr, start=st, stop=sp),
                        waits=allw if i == 0 else (),
                        inc=inc if (i == n - 1 and inc) else None)
        if r is not None:
            for (s, v) in allw:
                if s.startswith("l"):
                    self.semgate[s] = ("T", r)
        return r

    def check(self):
        for e in ENGINES:
            for s, v in self.waited[e].items():
                assert v <= self.sems[s][1], f"wait {s}>={v} never reached (max {self.sems[s][1]})"


def _recip_bf16(e, out, in_):
    with e.bass.allow_low_precision(reason="softmax 1/Z feeds a bf16 matmul operand"):
        return e.reciprocal(out, in_)


def build(which="full", dbg=(), upto=99):
    nc = bass.Bass("TRN2", target_bir_lowering=False)
    stack = ExitStack()
    T = {}

    def dram(name, shape, dt, kind):
        T[name] = nc.dram_tensor(name, list(shape), dt, kind=kind).ap()
        return T[name]

    emit_l0 = which in ("l0", "full")
    emit_l1 = which in ("l1", "full")

    if emit_l0:
        dram("x", (D, N), F32, "ExternalInput")
        for nm in ("wq", "wk", "wo"):
            dram(nm, (D, D), F32, "ExternalInput")
        dram("wv", (D, H * (HD + 1)), F32, "ExternalInput")
        for nm in ("bq", "bk", "bo"):
            dram(nm, (128, KC), F32, "ExternalInput")
        dram("bv", (1, H * (HD + 1)), F32, "ExternalInput")
        dram("wg", (HID, D), I32, "ExternalInput")
        dram("wu", (HID, D), I32, "ExternalInput")
        dram("wd", (D, HID), I32, "ExternalInput")
        dram("masks", (4, 128, 512), F32, "ExternalInput")
        dram("wgT", (HCN, D, 128), BF16, "Internal")
        dram("wuT", (HCN, D, 128), BF16, "Internal")
        dram("wdT", (KC, HID, 128), BF16, "Internal")
        dram("s_s", (D, N), BF16, "Internal")
    if emit_l1:
        dram("wqkv", (3 * D, D), F32, "ExternalInput")
        dram("aR", (4, 512, 512), F32, "ExternalInput")
        dram("bR", (4, 512, 512), F32, "ExternalInput")
        dram("aI", (4, 512, 512), F32, "ExternalInput")
        dram("bI", (4, 512, 512), F32, "ExternalInput")
        dram("br", (128, 4), F32, "ExternalInput")
        dram("bi", (128, 4), F32, "ExternalInput")
        dram("wqkvT", (3 * D // 128, D, 128), BF16, "Internal")
        dram("g_s", (D, N), BF16, "Internal")
    dram("ident", (128, 128), F32, "ExternalInput")
    if which == "full":
        dram("x1", (D, N), F32, "Internal")
        dram("out", (D, N), F32, "ExternalOutput")
    elif which == "l0":
        dram("x1", (D, N), F32, "ExternalOutput")
    else:
        dram("x1", (D, N), F32, "ExternalInput")
        dram("out", (D, N), F32, "ExternalOutput")
    for d in dbg:
        if d in ("xn0", "xn1", "kvq0", "att", "kq1", "v1", "gg"):
            names = {"kq1": ("dbg_k1", "dbg_q1")}.get(d, ("dbg_" + d,))
            for nm in names:
                dram(nm, (128, 32768), BF16, "ExternalOutput")
        elif d in ("s0", "g1"):
            dram("dbg_" + d, (D, N), BF16, "ExternalOutput")
        elif d == "WT":
            dram("dbg_WT", (128, 6144), BF16, "ExternalOutput")
        elif d == "x1d":
            dram("dbg_x1d", (D, N), F32, "ExternalOutput")

    big0 = stack.enter_context(nc.sbuf_tensor("big0", [128, 32768], BF16))
    big1 = stack.enter_context(nc.sbuf_tensor("big1", [128, 32768], BF16))
    big2 = stack.enter_context(nc.sbuf_tensor("big2", [128, 32768], BF16))
    stA = stack.enter_context(nc.sbuf_tensor("stA", [128, 7168], BF16))
    ident_f = stack.enter_context(nc.sbuf_tensor("ident_f", [128, 128], F32))
    ident_b = stack.enter_context(nc.sbuf_tensor("ident_b", [128, 128], BF16))
    ones_c = stack.enter_context(nc.sbuf_tensor("ones_c", [128, 1], BF16))
    ones_r = stack.enter_context(nc.sbuf_tensor("ones_r", [1, 128], BF16))
    ones_rf = stack.enter_context(nc.sbuf_tensor("ones_rf", [1, 128], F32))
    bias_q = stack.enter_context(nc.sbuf_tensor("bias_q", [128, KC], F32))
    bias_k = stack.enter_context(nc.sbuf_tensor("bias_k", [128, KC], F32))
    bias_o = stack.enter_context(nc.sbuf_tensor("bias_o", [128, KC], F32))
    bias_r = stack.enter_context(nc.sbuf_tensor("bias_r", [128, 4], F32))
    bias_i = stack.enter_context(nc.sbuf_tensor("bias_i", [128, 4], F32))
    negc = stack.enter_context(nc.sbuf_tensor("negc", [128, 2], F32))
    sc1 = stack.enter_context(nc.sbuf_tensor("sc1", [1, 16], F32))
    qmr = stack.enter_context(nc.sbuf_tensor("qmr", [1, 16], F32))

    pS0 = stack.enter_context(nc.psum_tensor("pS0", [128, 1024], F32))
    pS1 = stack.enter_context(nc.psum_tensor("pS1", [128, 1024], F32))
    pS2 = stack.enter_context(nc.psum_tensor("pS2", [128, 1024], F32))
    pCh = stack.enter_context(nc.psum_tensor("pC", [128, 512], F32))
    pDh = stack.enter_context(nc.psum_tensor("pD", [128, 512], F32))
    pS0a, pS1a, pS2a = pS0[:], pS1[:], pS2[:]
    pA = pS0a[:, 0:512]
    pB = pS0a[:, 512:1024]
    pR = pS1a[:, 0:512]
    pT = pS1a[:, 512:1024]
    pE = pS2a[:, 0:512]
    pF = pS2a[:, 512:1024]
    pC, pD = pCh[:], pDh[:]
    PSUM = {"pA": pA, "pB": pB, "pC": pC, "pD": pD, "pE": pE, "pF": pF, "pR": pR, "pT": pT}

    P = Prog(nc, stack)
    for s in ("T", "V", "S", "la0", "la1", "lb0", "lb1", "lb2", "lb3",
              "lc0", "lc1", "ld0", "ld1", "oa0", "oa1", "ob0", "ob1",
              "og", "ou", "od", "oq", "og1", "ou1", "od1", "oq1", "gd"):
        P.sem(s)

    stAf = stA[:].bitcast(F32)
    big1f = big1[:].bitcast(F32)
    big1i = big1[:].bitcast(I32)

    if emit_l0:
        P.load(bias_q[:], T["bq"], "la0")
        P.load(bias_k[:], T["bk"], "la0")
        P.load(bias_o[:], T["bo"], "la0")
    if emit_l1:
        P.load(bias_r[:], T["br"], "la0")
        P.load(bias_i[:], T["bi"], "la0")
    P.load(ident_f[:], T["ident"], "la0")
    vld = P.cnt("la0")
    P.do("vector", lambda e: e.tensor_copy(ident_b[:], ident_f[:]), waits=[("la0", vld)], inc="V")
    P.do("vector", lambda e: e.memset(ones_c[:], 1.0), inc="V")
    P.do("vector", lambda e: e.memset(ones_r[:], 1.0), inc="V")
    P.do("vector", lambda e: e.memset(ones_rf[:], 1.0), inc="V")
    P.do("vector", lambda e: e.memset(sc1[0:1, 15:16], EPS), inc="V")
    if emit_l0:
        for i in range(4):
            fv = stAf[:, 512 * (i % 2):512 * (i % 2) + 512]
            P.load(fv, T["masks"][i], "la1", waits=[("V", P.cnt("V"))])
            v = P.cnt("la1")
            P.do("vector", lambda e, i=i, fv=fv: e.tensor_copy(
                big2[:, 24704 + 512 * i:24704 + 512 * i + 512], fv),
                waits=[("la1", v)], inc="V")
    vconsts = P.cnt("la0")
    vconstsV = P.cnt("V")

    phase = 0

    def emit_wbuild(src_name, dst_name, R, C, src_dt, osem, stg=None, stoff=0):
        stg = big1 if stg is None else stg
        stg_i = stg[:].bitcast(I32)
        stg_f = stg[:].bitcast(F32)
        src, dst = T[src_name], T[dst_name]
        tw = ("T", P.cnt("T"))
        for k in (("braw", 0), ("braw", 1), ("bconv", 0), ("bconv", 1),
                  ("bev", 0), ("bev", 1)):
            P.slot_mark_add(k, *tw)
        it = 0
        for r in range(R // 128):
            for cc in range(C // 1024):
                slot = it % 2
                it += 1
                raw = (stg_i if src_dt == I32 else stg_f)[
                    :, stoff // 2 + 1024 * slot:stoff // 2 + 1024 * slot + 1024]
                lsm = "lb%d" % slot
                co = stoff + 4096 + 1024 * slot
                eo = stoff + 6144 + 1024 * slot
                lv = P.load(raw, src[r * 128:(r + 1) * 128, cc * 1024:(cc + 1) * 1024],
                            lsm, waits=P.slot_waits(("braw", slot)))
                cv = P.do("vector", lambda e, raw=raw, co=co, stg=stg: e.tensor_copy(
                    stg[:, co:co + 1024], raw),
                    waits=[(lsm, lv)] + P.slot_waits(("bconv", slot)), inc="V")
                P.slot_mark(("braw", slot), "V", cv)
                vts = []
                for kc in range(8):
                    wT = P.bank_waits("pT") + [("V", cv)]
                    vt = P.do("tensor", lambda e, kc=kc, co=co, stg=stg: e.transpose(
                        pT.bitcast(BF16)[:, 128 * kc:128 * kc + 128],
                        stg[:, co + 128 * kc:co + 128 * kc + 128], ident_b[:]),
                        waits=wT, inc="T")
                    vts.append(vt)
                P.slot_mark(("bconv", slot), "T", vts[-1])
                ve = P.do("scalar", lambda e, eo=eo, stg=stg: e.copy(
                    stg[:, eo:eo + 1024], pT.bitcast(BF16)[:, 0:1024]),
                    waits=[("T", vts[-1])] + P.slot_waits(("bev", slot)), inc="S")
                P.bank_consumed("pT", "S", ve)
                dstap = dst[r][cc * 1024:(cc + 1) * 1024, :].rearrange(
                    "(c p) q -> p c q", p=128)
                ssem = osem if slot == 0 else osem + "1"
                sv = P.store(dstap, stg[:, eo:eo + 1024].rearrange("p (c q) -> p c q", q=128),
                             ssem, waits=[("S", ve)])
                P.slot_mark(("bev", slot), ssem, sv)
        return [(osem, P.cnt(osem)), (osem + "1", P.cnt(osem + "1"))]

    def emit_ln(src_name, wait_src=()):
        mu_row = big1[0:1, 30720:31232]
        rstd_row = big1[0:1, 31232:31744]
        var_row = big1f[0:1, 15872:16384]
        lnj = []
        war0 = [(s, P.cnt(s)) for s in ("og", "ou", "od", "oq", "og1", "ou1",
                                        "od1", "oq1", "oa0", "oa1", "ob0", "ob1")
                if P.cnt(s)]
        war0 += [("T", P.cnt("T")), ("S", P.cnt("S")), ("V", P.cnt("V"))]
        for j in range(NB):
            cvs = []
            for c in range(KC):
                slot = c % 2
                fv = stAf[:, 2048 + 512 * slot:2560 + 512 * slot]
                lv = P.load(fv, T[src_name][c * 128:(c + 1) * 128, j * 512:(j + 1) * 512],
                            "la%d" % slot,
                            waits=P.slot_waits(("lnf", slot)) + list(wait_src) + war0)
                cv = P.do("vector", lambda e, c=c, fv=fv: e.tensor_copy(
                    stA[:, 512 * c:512 * c + 512], fv),
                    waits=[("la%d" % slot, lv)] + P.slot_waits(("lnx", c)) + war0[-3:],
                    inc="V")
                P.slot_mark(("lnf", slot), "V", cv)
                cvs.append(cv)
            vconv = P.cnt("V")
            wR = P.bank_waits("pC")
            for c in range(KC):
                P.do("tensor", lambda e, c=c: e.matmul(
                    pC[0:1, 0:512], ones_c[:], stA[:, 512 * c:512 * c + 512],
                    start=(c == 0), stop=(c == KC - 1)),
                    waits=(wR if c == 0 else []) + [("V", cvs[c])],
                    inc="T" if c == KC - 1 else None)
            for c in range(KC):
                vs = P.do("scalar", lambda e, c=c: e.activation(
                    stA[:, 6144:6656], stA[:, 512 * c:512 * c + 512], AF.Square),
                    waits=[("V", cvs[c]), ("T", P.cnt("T"))] + war0, inc="S")
                P.do("tensor", lambda e, c=c: e.matmul(
                    pC[64:65, 0:512], ones_c[:], stA[:, 6144:6656],
                    start=(c == 0), stop=(c == KC - 1)), waits=[("S", vs)], inc="T")
            vstat = P.cnt("T")
            vmu = P.do("scalar", lambda e: e.activation(mu_row, pC[0:1, 0:512], AF.Copy,
                                                        scale=1.0 / D),
                       waits=[("T", vstat)], inc="S")
            wA = P.bank_waits("pD") + [("S", vmu)]
            vbA = P.do("tensor", lambda e: e.matmul(pD[:, 0:512], ones_r[:], mu_row,
                                                    start=True, stop=True), waits=wA, inc="T")
            vq = P.do("vector", lambda e: e.tensor_mul(rstd_row, mu_row, mu_row),
                      waits=[("S", vmu)], inc="V")
            vv = P.do("vector", lambda e: e.scalar_tensor_tensor(
                var_row, pC[64:65, 0:512], 1.0 / D, rstd_row, ALU.mult, ALU.subtract),
                waits=[("V", vq)], inc="V")
            P.bank_consumed("pC", "V", vv)
            vl = P.do("scalar", lambda e: e.activation(mu_row, var_row, AF.Ln,
                                                       bias=sc1[0:1, 15:16]),
                      waits=[("V", vv), ("T", vbA)], inc="S")
            vr = P.do("scalar", lambda e: e.activation(rstd_row, mu_row, AF.Exp, scale=-0.5),
                      waits=[("S", vl)], inc="S")
            wB = P.bank_waits("pE") + [("S", vr)]
            vbB = P.do("tensor", lambda e: e.matmul(pE[:, 0:512], ones_r[:], rstd_row,
                                                    start=True, stop=True), waits=wB, inc="T")
            for c in range(KC):
                vsb = P.do("vector", lambda e, c=c: e.tensor_sub(
                    stA[:, 512 * c:512 * c + 512], stA[:, 512 * c:512 * c + 512], pD[:, 0:512]),
                    waits=[("T", vbA)] if c == 0 else (), inc="V")
                vm2 = P.do("vector", lambda e, c=c, j=j: e.tensor_mul(
                    big0[:, 4096 * c + 512 * j:4096 * c + 512 * j + 512],
                    stA[:, 512 * c:512 * c + 512], pE[:, 0:512]),
                    waits=([("T", vbB)] if c == 0 else []) + [("V", vsb)], inc="V")
                P.slot_mark(("lnx", c), "V", vm2)
            vn = P.cnt("V")
            lnj.append(vn)
            P.bank_consumed("pD", "V", vn)
            P.bank_consumed("pE", "V", vn)
        return lnj

    sb_done = {}
    vx1 = []
    vspill = []
    if emit_l0:
        phase += 1
        if phase <= upto:
            sb_done["wgT"] = emit_wbuild("wg", "wgT", HID, D, I32, "og")
            sb_done["wuT"] = emit_wbuild("wu", "wuT", HID, D, I32, "ou")
            sb_done["wdT"] = emit_wbuild("wd", "wdT", D, HID, I32, "od")
        phase += 1
        lnj0 = [0] * NB
        if phase <= upto:
            lnj0 = emit_ln("x")
            if "xn0" in dbg:
                P.do("gpsimd", lambda e: e.dma_start(out=T["dbg_xn0"], in_=big0[:]),
                     waits=[("V", P.cnt("V"))], inc="gd", n=16)

        phase += 1
        if phase <= upto:
            vxn = P.cnt("V")
            lnj_ = lnj0 if lnj0[0] else [vxn] * NB
            KOFF, VOFF, QOFF = 0, 8192, 16512
            MOFF, POFF, OTOFF, ROFF = 24704, 26752, 30848, 31360
            war_bld = [(s, P.cnt(s)) for s in ("og", "ou", "od", "og1", "ou1", "od1") if P.cnt(s)]
            for k in (("pws", 0), ("pws", 1), "pwf"):
                P.slot_mark_add(k, "V", vxn)
            pv_war = [0] * 3
            att_pend = [None]
            sS = ["pA", "pB", "pR", "pT"]

            def load_wslice(wname, cols0, ncols, slot):
                for hh in range(2):
                    fv = stAf[:, 2080:2080 + 4 * ncols]
                    srcap = T[wname].rearrange("(kc p) q -> p kc q", p=128)[
                        :, 4 * hh:4 * hh + 4, cols0:cols0 + ncols]
                    lv = P.load(fv.rearrange("p (kc q) -> p kc q", kc=4), srcap,
                                "lb0", waits=P.slot_waits("pwf") + war_bld)
                    cv = P.do("vector", lambda e, fv=fv, slot=slot, hh=hh, ncols=ncols: e.tensor_copy(
                        stA[:, 2048 * slot + 4 * ncols * hh:2048 * slot + 4 * ncols * hh + 4 * ncols],
                        fv), waits=[("lb0", lv)] + P.slot_waits(("pws", slot)), inc="V")
                    P.slot_mark("pwf", "V", cv)
                return P.cnt("V")

            for g in range(min(G, GLIM)):
                vwk = load_wslice("wk", g * 256, 256, 0)
                for c2 in range(2):
                    for j in range(NB):
                        bank = ("pA", pA) if (c2 * NB + j) % 2 == 0 else ("pB", pB)
                        pairs = [(stA[:, 256 * kc + 128 * c2:256 * kc + 128 * c2 + 128],
                                  big0[:, 4096 * kc + 512 * j:4096 * kc + 512 * j + 512])
                                 for kc in range(KC)]
                        vm = P.mm_group(bank[1][:, 0:512], pairs,
                                        waits=[("V", vwk), ("V", lnj_[j])], bankname=bank[0])
                        ve = P.do("vector", lambda e, g=g, c2=c2, j=j, b=bank[1]: e.tensor_scalar_add(
                            big2[:, KOFF + 4096 * c2 + 512 * j:KOFF + 4096 * c2 + 512 * j + 512],
                            b[:, 0:512], bias_k[:, 2 * g + c2:2 * g + c2 + 1]),
                            waits=[("T", vm), ("la0", vconsts)], inc="V")
                        P.bank_consumed(bank[0], "V", ve)
                vKg = P.cnt("V")
                P.slot_mark(("pws", 0), "T", P.cnt("T"))
                fvb = stAf[0:1, 3120:3380]
                lvb = P.load(fvb, T["bv"][0:1, g * VW:(g + 1) * VW], "lc0",
                             waits=[("V", P.cnt("V"))] + war_bld)
                vbv = P.do("vector", lambda e, fvb=fvb: e.tensor_copy(
                    stA[0:1, 6760:6760 + VW], fvb),
                    waits=[("lc0", P.cnt("lc0")), ("T", P.cnt("T"))], inc="V")
                vwv = load_wslice("wv", g * VW, VW, 1)
                for i in range(NT):
                    bank = ("pC", pC) if i % 2 == 0 else ("pD", pD)
                    pairs = [(big0[:, 4096 * kc + 128 * i:4096 * kc + 128 * i + 128],
                              stA[:, 2048 + VW * kc:2048 + VW * kc + VW])
                             for kc in range(KC)]
                    P.mm_group(bank[1][:, 0:VW], pairs,
                               waits=[("V", vwv), ("V", lnj_[i // 4])],
                               bankname=bank[0], last_stop=False)
                    vm = P.do("tensor", lambda e, b=bank[1]: e.matmul(
                        b[:, 0:VW], ones_r[:], stA[0:1, 6760:6760 + VW],
                        start=False, stop=True), waits=[("V", vbv)], inc="T")
                    ve = P.do("vector", lambda e, i=i, b=bank[1]: e.tensor_copy(
                        big2[:, VOFF + VW * i:VOFF + VW * i + VW], b[:, 0:VW]),
                        waits=[("T", vm)], inc="V")
                    P.bank_consumed(bank[0], "V", ve)
                vVg = P.cnt("V")
                P.slot_mark(("pws", 1), "T", P.cnt("T"))
                vwq = load_wslice("wq", g * 256, 256, 0)
                for c2 in range(2):
                    for j in range(NB):
                        bank = ("pA", pA) if (c2 * NB + j) % 2 == 0 else ("pB", pB)
                        pairs = [(stA[:, 256 * kc + 128 * c2:256 * kc + 128 * c2 + 128],
                                  big0[:, 4096 * kc + 512 * j:4096 * kc + 512 * j + 512])
                                 for kc in range(KC)]
                        vm = P.mm_group(bank[1][:, 0:512], pairs,
                                        waits=[("V", vwq), ("V", lnj_[j])], bankname=bank[0])
                        ve = P.do("vector", lambda e, g=g, c2=c2, j=j, b=bank[1]: e.tensor_scalar_add(
                            big2[:, QOFF + 4096 * c2 + 512 * j:QOFF + 4096 * c2 + 512 * j + 512],
                            b[:, 0:512], bias_q[:, 2 * g + c2:2 * g + c2 + 1]),
                            waits=[("T", vm), ("la0", vconsts)], inc="V")
                        P.bank_consumed(bank[0], "V", ve)
                vQg = P.cnt("V")
                P.slot_mark(("pws", 0), "T", P.cnt("T"))
                if g == 0 and "kvq0" in dbg:
                    P.do("gpsimd", lambda e: e.dma_start(out=T["dbg_kvq0"], in_=big2[:]),
                         waits=[("V", vQg)], inc="gd", n=16)

                def norm_max(OFFS, vready, outcol):
                    for j in range(NB):
                        for c2 in range(2):
                            wR = (P.bank_waits("pR") if c2 == 0 else [])
                            vs = P.do("scalar", lambda e, c2=c2, j=j, OFFS=OFFS: e.activation(
                                stA[:, 4160:4672],
                                big2[:, OFFS + 4096 * c2 + 512 * j:OFFS + 4096 * c2 + 512 * j + 512],
                                AF.Square),
                                waits=wR + [("V", vready), ("T", P.cnt("T"))], inc="S")
                            P.do("tensor", lambda e, c2=c2: e.matmul(
                                pR[0:1, 0:512], ones_c[:], stA[:, 4160:4672],
                                start=(c2 == 0), stop=(c2 == 1)), waits=[("S", vs)], inc="T")
                        vt = P.cnt("T")
                        vv = P.do("vector", lambda e, j=j: e.tensor_reduce(
                            qmr[0:1, j:j + 1], pR[0:1, 0:512], AX.X, ALU.max),
                            waits=[("T", vt)], inc="V")
                        P.bank_consumed("pR", "V", vv)
                        P._wait("scalar", "V", vv)
                    return P.do("vector", lambda e, outcol=outcol: e.tensor_reduce(
                        sc1[0:1, outcol:outcol + 1], qmr[0:1, 0:NB], AX.X, ALU.max),
                        waits=[("V", P.cnt("V"))], inc="V")

                if SUBPH < 2:
                    continue
                vq = norm_max(QOFF, vQg, 0)
                vk = norm_max(KOFF, vKg, 1)
                P.slot_mark_add("pwf", "S", P.cnt("S"))
                P.slot_mark_add("pwf", "T", P.cnt("T"))
                vp = P.do("vector", lambda e: e.tensor_mul(sc1[0:1, 2:3], sc1[0:1, 0:1],
                                                           sc1[0:1, 1:2]),
                          waits=[("V", vq), ("V", vk)], inc="V")
                vsq = P.do("scalar", lambda e: e.activation(sc1[0:1, 3:4], sc1[0:1, 2:3],
                                                            AF.Sqrt), waits=[("V", vp)], inc="S")
                wT2 = P.bank_waits("pT") + [("S", vsq)]
                vbc = P.do("tensor", lambda e: e.matmul(pT[:, 0:1], ones_rf[:], sc1[0:1, 3:4],
                                                        start=True, stop=True), waits=wT2, inc="T")
                vnc = P.do("scalar", lambda e: e.activation(negc[:, 0:1], pT[:, 0:1], AF.Identity,
                                                            scale=-0.125 * 1.001),
                           waits=[("T", vbc)], inc="S")
                P.bank_consumed("pT", "S", vnc)

                if SUBPH < 3:
                    continue
                SPAIR = [pS0a, pS1a, pS2a]
                SHALF = [["pA", "pB"], ["pR", "pT"], ["pE", "pF"]]
                big2f = big2[:].bitcast(F32)
                rrows = [big2f[0:1, 15680:16192], big2f[0:1, 14912:15424]]

                def att_epilogue2(pend):
                    g_, hswp_, qb_, oz_done_, vrcs_ = pend
                    for hh2 in range(2):
                        hh = 2 * hswp_ + hh2
                        c2, hl = hh // 2, hh % 2
                        oz = PSUM[["pC", "pD"][hh2]]
                        wbc = P.bank_waits("pE") + [("V", vrcs_[hh2])]
                        vbc2 = P.do("tensor", lambda e, rr=rrows[hh2]: e.matmul(
                            pE[0:64, 0:512], ones_rf[0:1, 0:64], rr, start=True, stop=True),
                            waits=wbc, inc="T")
                        ot = big2[0:64, OTOFF:OTOFF + 512]
                        voe = P.do("scalar", lambda e, ot=ot, oz=oz: e.copy(ot, oz[0:64, 0:512]),
                                   waits=[("V", P.cnt("V")), ("S", P.cnt("S"))], inc="S")
                        P.bank_consumed(["pC", "pD"][hh2], "S", voe)
                        dst = big1[64 * hl:64 * hl + 64,
                                   4096 * (2 * g_ + c2) + 512 * qb_:4096 * (2 * g_ + c2) + 512 * qb_ + 512]
                        vml = P.do("vector", lambda e, dst=dst, ot=ot: e.tensor_mul(
                            dst, ot, pE[0:64, 0:512]),
                            waits=[("S", voe), ("T", vbc2)] + (war_bld if g_ == 0 else []),
                            inc="V")
                        P.bank_consumed("pE", "V", vml)
                        P._wait("scalar", "V", vml)

                for hswp in range(2):
                    for qb in range(NB):
                        ktn = 4 * qb + 4
                        oz_done = [None, None]
                        qk_pend = {}

                        def emit_qkpair(kt, hswp=hswp, qb=qb, qk_pend=None):
                            pr = kt % 3
                            qks = []
                            for hh2 in range(2):
                                hh = 2 * hswp + hh2
                                c2, hl = hh // 2, hh % 2
                                bkn = SHALF[pr][hh2]
                                l = big2[64 * hl:64 * hl + 64,
                                         KOFF + 4096 * c2 + 128 * kt:KOFF + 4096 * c2 + 128 * kt + 128]
                                r = big2[64 * hl:64 * hl + 64,
                                         QOFF + 4096 * c2 + 512 * qb:QOFF + 4096 * c2 + 512 * qb + 512]
                                wqk = P.bank_waits(bkn) + [("S", vnc), ("V", vVg)]
                                vqk = P.do("tensor", lambda e, l=l, r=r, b=PSUM[bkn]: e.matmul(
                                    b, l, r, start=True, stop=True), waits=wqk, inc="T")
                                qks.append(vqk)
                            qk_pend[kt] = (pr, qks)

                        emit_qkpair(0, qk_pend=qk_pend)
                        if ktn > 1:
                            emit_qkpair(1, qk_pend=qk_pend)
                        if att_pend[0] is not None:
                            att_epilogue2(att_pend[0])
                            att_pend[0] = None
                        for kt in range(ktn):
                            if kt + 2 < ktn:
                                emit_qkpair(kt + 2, qk_pend=qk_pend)
                            pr, qks = qk_pend.pop(kt)
                            pp = kt % 3
                            ppt = big2[:, POFF + 1024 * pp:POFF + 1024 * pp + 1024]
                            wex = [("T", qks[1]), ("T", pv_war[pp])]
                            vex = P.do("scalar", lambda e, ppt=ppt, b=SPAIR[pr]: e.activation(
                                ppt, b, AF.Exp, bias=negc[:, 0:1], scale=0.125),
                                waits=wex, inc="S")
                            P.bank_consumed(SHALF[pr][0], "S", vex)
                            P.bank_consumed(SHALF[pr][1], "S", vex)
                            pvwait = ("S", vex)
                            if kt >= 4 * qb:
                                dk = kt - 4 * qb
                                for half in range(2):
                                    vmk = P.do("vector", lambda e, ppt=ppt, dk=dk, half=half: e.tensor_mul(
                                        ppt[:, 512 * half:512 * half + 512],
                                        ppt[:, 512 * half:512 * half + 512],
                                        big2[:, MOFF + 512 * dk:MOFF + 512 * dk + 512]),
                                        waits=[("S", vex)], inc="V")
                                pvwait = ("V", vmk)
                            for hh2 in range(2):
                                hh = 2 * hswp + hh2
                                pt = ppt[:, 512 * hh2:512 * hh2 + 512]
                                ozn = ["pC", "pD"][hh2]
                                wpv = [pvwait]
                                if kt == 0:
                                    wpv += P.bank_waits(ozn)
                                lv = big2[:, VOFF + VW * kt + 65 * hh:VOFF + VW * kt + 65 * hh + 65]
                                vpv = P.do("tensor", lambda e, lv=lv, pt=pt, st=(kt == 0),
                                           sp=(kt == ktn - 1), oz=PSUM[ozn]: e.matmul(
                                    oz[0:65, 0:512], lv, pt, start=st, stop=sp), waits=wpv, inc="T")
                                pv_war[pp] = max(pv_war[pp], vpv)
                                oz_done[hh2] = vpv
                        vrcs = []
                        for hh2 in range(2):
                            oz = PSUM[["pC", "pD"][hh2]]
                            vrc = P.do("vector", lambda e, oz=oz, rr=rrows[hh2]: e.reciprocal(
                                rr, oz[64:65, 0:512]),
                                waits=[("T", oz_done[hh2]), ("V", P.cnt("V"))], inc="V")
                            P.bank_consumed_add(["pC", "pD"][hh2], "V", vrc)
                            vrcs.append(vrc)
                        att_pend[0] = (g, hswp, qb, list(oz_done), vrcs)
                if att_pend[0] is not None:
                    att_epilogue2(att_pend[0])
                    att_pend[0] = None
            if att_pend[0] is not None:
                att_epilogue2(att_pend[0])
                att_pend[0] = None
            if "att" in dbg:
                P.do("gpsimd", lambda e: e.dma_start(out=T["dbg_att"], in_=big1[:]),
                     waits=[("V", P.cnt("V"))], inc="gd", n=16)

        phase += 1
        if phase <= upto:
            vOT = P.cnt("V")
            tw = ("T", P.cnt("T"))
            for k in (("wos", 0), ("wos", 1)):
                P.slot_mark_add(k, *tw)
            for dd in range(KC):
                slot = dd % 2
                for hh in range(2):
                    fv = stAf[:, 2080:2592]
                    srcap = T["wo"].rearrange("(kc p) q -> p kc q", p=128)[
                        :, 4 * hh:4 * hh + 4, dd * 128:(dd + 1) * 128]
                    lv = P.load(fv.rearrange("p (kc q) -> p kc q", kc=4), srcap, "lb0",
                                waits=P.slot_waits("pwf"))
                    cv = P.do("vector", lambda e, fv=fv, slot=slot, hh=hh: e.tensor_copy(
                        stA[:, 1024 * slot + 512 * hh:1024 * slot + 512 * hh + 512], fv),
                        waits=[("lb0", lv)] + P.slot_waits(("wos", slot)), inc="V")
                    P.slot_mark("pwf", "V", cv)
                vws = P.cnt("V")
                lastT = None
                for j in range(NB):
                    bank = ("pA", pA) if (dd * NB + j) % 2 == 0 else ("pB", pB)
                    pairs = [(stA[:, 1024 * slot + 128 * kc:1024 * slot + 128 * kc + 128],
                              big1[:, 4096 * kc + 512 * j:4096 * kc + 512 * j + 512])
                             for kc in range(KC)]
                    vm = P.mm_group(bank[1][:, 0:512], pairs, waits=[("V", vws), ("V", vOT)],
                                    bankname=bank[0])
                    lastT = vm
                    sp = stA[:, 6144 + 512 * (j % 2):6144 + 512 * (j % 2) + 512]
                    osm = "oa%d" % (j % 2)
                    ve = P.do("vector", lambda e, sp=sp, dd=dd, b=bank[1]: e.tensor_scalar_add(
                        sp, b[:, 0:512], bias_o[:, dd:dd + 1]),
                        waits=[("T", vm), (osm, P.cnt(osm)), ("la0", vconsts)], inc="V")
                    P.bank_consumed(bank[0], "V", ve)
                    P.store(T["s_s"][dd * 128:(dd + 1) * 128, j * 512:(j + 1) * 512], sp, osm,
                            waits=[("V", ve)])
                P.slot_mark(("wos", slot), "T", lastT)
            vspill = [("oa0", P.cnt("oa0")), ("oa1", P.cnt("oa1"))]
            if "s0" in dbg:
                P.do("gpsimd", lambda e: e.dma_start(out=T["dbg_s0"], in_=T["s_s"]),
                     waits=vspill, inc="gd", n=16)

        if emit_l1 and upto >= 5:
            sb_done["wqkvT"] = emit_wbuild("wqkv", "wqkvT", 3 * D, D, F32, "oq",
                                           stg=big2, stoff=8192)
        phase += 1
        if phase <= upto:
            vxn = P.cnt("V")
            tw = ("T", P.cnt("T"))
            for k in (("fg", 0), ("fg", 1), ("fu", 0), ("fu", 1), ("fd", 0), ("fd", 1),
                      ("sg", 0), ("sg", 1), ("cfx", 0), ("cfx", 1), ("csl", 0),
                      ("csl", 1), ("cpr", 0), ("cpr", 1)):
                P.slot_mark_add(k, *tw)
            for sv_ in vspill:
                P.slot_mark_add("cfx", sv_[0], sv_[1])
            for grp in range(4):
                j0 = grp * 2
                for hc in range(HCN):
                    slot = hc % 2
                    gsm = "lb%d" % (2 * slot)
                    usm = "lb%d" % (2 * slot + 1)
                    lvg = P.load(stA[:, 1024 * slot:1024 * slot + 1024].rearrange(
                        "p (c q) -> p c q", q=128),
                        T["wgT"][hc].rearrange("(c p) q -> p c q", p=128), gsm,
                        waits=P.slot_waits(("fg", slot)) + list(sb_done.get("wgT", [])))
                    lvu = P.load(stA[:, 2048 + 1024 * slot:2048 + 1024 * slot + 1024].rearrange(
                        "p (c q) -> p c q", q=128),
                        T["wuT"][hc].rearrange("(c p) q -> p c q", p=128), usm,
                        waits=P.slot_waits(("fu", slot)) + list(sb_done.get("wuT", [])))
                    lastg = lastu = None
                    for j2 in range(2):
                        j = j0 + j2
                        gb = ("pA", pA) if j2 == 0 else ("pB", pB)
                        ub = ("pC", pC) if j2 == 0 else ("pD", pD)
                        gp = [(stA[:, 1024 * slot + 128 * kc:1024 * slot + 128 * kc + 128],
                               big0[:, 4096 * kc + 512 * j:4096 * kc + 512 * j + 512])
                              for kc in range(KC)]
                        upp = [(stA[:, 2048 + 1024 * slot + 128 * kc:2048 + 1024 * slot + 128 * kc + 128],
                                big0[:, 4096 * kc + 512 * j:4096 * kc + 512 * j + 512])
                               for kc in range(KC)]
                        vmg = P.mm_group(gb[1][:, 0:512], gp, waits=[(gsm, lvg), ("V", vxn)],
                                         bankname=gb[0])
                        vmu2 = P.mm_group(ub[1][:, 0:512], upp, waits=[(usm, lvu)],
                                          bankname=ub[0])
                        lastg, lastu = vmg, vmu2
                        sgo = [4096, 6656][j2]
                        sg = stA[:, sgo:sgo + 512]
                        vsg = P.do("scalar", lambda e, sg=sg, b=gb[1]: e.activation(
                            sg, b[:, 0:512], AF.Silu, scale=1.0 / 127.0),
                            waits=[("T", vmg)] + P.slot_waits(("sg", j2)) + vspill, inc="S")
                        P.bank_consumed(gb[0], "S", vsg)
                        hdst = big1[:, 1024 * hc + 512 * j2:1024 * hc + 512 * j2 + 512]
                        vh = P.do("vector", lambda e, hdst=hdst, sg=sg, b=ub[1]: e.tensor_mul(
                            hdst, sg, b[:, 0:512]), waits=[("S", vsg), ("T", vmu2)], inc="V")
                        P.bank_consumed(ub[0], "V", vh)
                        P.slot_mark(("sg", j2), "V", vh)
                    P.slot_mark(("fg", slot), "T", lastg)
                    P.slot_mark(("fu", slot), "T", lastu)
                vhall = P.cnt("V")
                for dd in range(KC):
                    dslot = dd % 2
                    do_ = 24576 + 3072 * dslot
                    dsm = "ld%d" % dslot
                    lvd = P.load(big1[:, do_:do_ + 3072].rearrange("p (c q) -> p c q", q=128),
                                 T["wdT"][dd].rearrange("(c p) q -> p c q", p=128), dsm,
                                 waits=P.slot_waits(("fd", dslot)) + list(sb_done.get("wdT", [])))
                    lastd = None
                    for j2 in range(2):
                        bank = ("pE", pE) if j2 == 0 else ("pF", pF)
                        pairs = [(big1[:, do_ + 128 * hcc:do_ + 128 * hcc + 128],
                                  big1[:, 1024 * hcc + 512 * j2:1024 * hcc + 512 * j2 + 512])
                                 for hcc in range(HCN)]
                        vm = P.mm_group(bank[1][:, 0:512], pairs,
                                        waits=[(dsm, lvd), ("V", vhall)], bankname=bank[0])
                        lastd = vm
                        ve = P.do("scalar", lambda e, dd=dd, j2=j2, b=bank[1]: e.activation(
                            big2[:, 1024 * dd + 512 * j2:1024 * dd + 512 * j2 + 512],
                            b[:, 0:512], AF.Silu, scale=1.0 / 16129.0),
                            waits=[("T", vm)] + P.slot_waits(("gch", dd * 2 + j2)), inc="S")
                        P.bank_consumed(bank[0], "S", ve)
                    P.slot_mark(("fd", dslot), "T", lastd)
                vgrp = P.cnt("S")
                if grp == 0 and "gg" in dbg:
                    P.do("gpsimd", lambda e: e.dma_start(out=T["dbg_gg"], in_=big2[:]),
                         waits=[("S", vgrp)], inc="gd", n=16)
                b2f = big2[:].bitcast(F32)
                for j2 in range(2):
                    j = j0 + j2
                    for c in range(KC):
                        sl = c % 2
                        fx = b2f[:, 8192 + 512 * sl:8192 + 512 * sl + 512]
                        fo = b2f[:, 9216 + 512 * sl:9216 + 512 * sl + 512]
                        ssl = big2[:, 20480 + 512 * sl:20480 + 512 * sl + 512]
                        prd = big2[:, 21504 + 512 * sl:21504 + 512 * sl + 512]
                        osm = "ob%d" % sl
                        lvx = P.load(fx, T["x"][c * 128:(c + 1) * 128, j * 512:(j + 1) * 512],
                                     "lc0", waits=P.slot_waits(("cfx", sl)))
                        lvs = P.load(ssl, T["s_s"][c * 128:(c + 1) * 128, j * 512:(j + 1) * 512],
                                     "lc1", waits=vspill + P.slot_waits(("csl", sl)))
                        vp2 = P.do("vector", lambda e, prd=prd, ssl=ssl, c=c, j2=j2: e.tensor_mul(
                            prd, ssl, big2[:, 1024 * c + 512 * j2:1024 * c + 512 * j2 + 512]),
                            waits=[("lc1", lvs), ("S", vgrp)] + P.slot_waits(("cpr", sl)), inc="V")
                        P.slot_mark(("csl", sl), "V", vp2)
                        P.slot_mark_add(("gch", c * 2 + j2), "V", vp2)
                        vo = P.do("vector", lambda e, fo=fo, fx=fx, prd=prd: e.tensor_add(
                            fo, fx, prd),
                            waits=[("lc0", lvx), (osm, P.cnt(osm))], inc="V")
                        P.slot_mark(("cfx", sl), "V", vo)
                        P.slot_mark(("cpr", sl), "V", vo)
                        P.store(T["x1"][c * 128:(c + 1) * 128, j * 512:(j + 1) * 512], fo,
                                osm, waits=[("V", vo)])
            vx1 = [("ob0", P.cnt("ob0")), ("ob1", P.cnt("ob1"))]

    if emit_l1:
        phase += 1
        if phase <= upto and "wqkvT" not in sb_done:
            sb_done["wqkvT"] = emit_wbuild("wqkv", "wqkvT", 3 * D, D, F32, "oq")
        phase += 1
        lnj1 = [0] * NB
        vneg = 0
        if phase <= upto:
            war_q = [(s, P.cnt(s)) for s in ("oq", "oq1") if P.cnt(s)] + [("T", P.cnt("T"))]
            P.slot_mark_add("wba", "V", P.cnt("V"))
            for idx, (an, bn) in enumerate((("aR", "bR"), ("aI", "bI"))):
                for ch in range(4):
                    for c in range(4):
                        fa = stAf[:, 0:512]
                        fb = stAf[:, 512:1024]
                        acc = stAf[:, 1024:1536]
                        la = P.load(fa, T[an][c][ch * 128:(ch + 1) * 128, :], "lb0",
                                    waits=P.slot_waits("wba") + war_q[-1:])
                        lb = P.load(fb, T[bn][c][ch * 128:(ch + 1) * 128, :], "lb1",
                                    waits=P.slot_waits("wba"))
                        if c == 0:
                            vmu0 = P.do("vector", lambda e, fa=fa, fb=fb, acc=acc: e.tensor_mul(
                                acc, fa, fb), waits=[("lb0", la), ("lb1", lb)], inc="V")
                        else:
                            vt0 = P.do("vector", lambda e, fa=fa, fb=fb: e.tensor_mul(
                                stAf[:, 1536:2048], fa, fb),
                                waits=[("lb0", la), ("lb1", lb)], inc="V")
                            vmu0 = P.do("vector", lambda e, acc=acc: e.tensor_add(
                                acc, acc, stAf[:, 1536:2048]), waits=[("V", vt0)], inc="V")
                        P.slot_mark("wba", "V", vmu0)
                    vcp = P.do("vector", lambda e, idx=idx, ch=ch, acc=acc: e.tensor_copy(
                        big1[:, 2048 * idx + 512 * ch:2048 * idx + 512 * ch + 512], acc),
                        waits=war_q + [("V", vmu0)], inc="V")
                    P.slot_mark("wba", "V", vcp)
            vwt = P.cnt("V")
            vneg = P.do("vector", lambda e: e.tensor_scalar_mul(
                big1[:, 4096:6144], big1[:, 2048:4096], -1.0), waits=[("V", vwt)] + war_q,
                inc="V")
            if "WT" in dbg:
                P.do("gpsimd", lambda e: e.dma_start(out=T["dbg_WT"], in_=big1[:, 0:6144]),
                     waits=[("V", vneg)], inc="gd", n=16)
            lnj1 = emit_ln("x1", wait_src=vx1)
            if "xn1" in dbg:
                P.do("gpsimd", lambda e: e.dma_start(out=T["dbg_xn1"], in_=big0[:]),
                     waits=[("V", P.cnt("V"))], inc="gd", n=16)
        phase += 1
        vgs = []
        if phase <= upto:
            vxnl = P.cnt("V")
            for j in range(NB):
                for half in range(2):
                    for oc in range(4):
                        bank = ("pA", pA) if oc % 2 == 0 else ("pB", pB)
                        pairs = []
                        for ic in range(4):
                            wofs = 0 if half == 0 else 2048
                            pairs.append((big1[:, wofs + 512 * ic + 128 * oc:wofs + 512 * ic + 128 * oc + 128],
                                          big0[:, 4096 * ic + 512 * j:4096 * ic + 512 * j + 512]))
                        for ic in range(4):
                            wofs = 4096 if half == 0 else 0
                            pairs.append((big1[:, wofs + 512 * ic + 128 * oc:wofs + 512 * ic + 128 * oc + 128],
                                          big0[:, 4096 * (4 + ic) + 512 * j:4096 * (4 + ic) + 512 * j + 512]))
                        vm = P.mm_group(bank[1][:, 0:512], pairs,
                                        waits=[("V", vneg), ("V", lnj1[j] or vxnl)],
                                        bankname=bank[0])
                        gsl = big2[:, 512 * (oc % 2):512 * (oc % 2) + 512]
                        osm = "oa%d" % (oc % 2)
                        bias = bias_r if half == 0 else bias_i
                        ve = P.do("scalar", lambda e, gsl=gsl, oc=oc, b=bank[1], bias=bias: e.activation(
                            gsl, b[:, 0:512], AF.Silu, bias=bias[:, oc:oc + 1]),
                            waits=[("T", vm), (osm, P.cnt(osm)), ("la0", vconsts)], inc="S")
                        P.bank_consumed(bank[0], "S", ve)
                        P.store(T["g_s"][512 * half + 128 * oc:512 * half + 128 * oc + 128,
                                         j * 512:(j + 1) * 512], gsl, osm,
                                waits=[("S", ve)])
            vgs = [("oa0", P.cnt("oa0")), ("oa1", P.cnt("oa1"))]
            if "g1" in dbg:
                P.do("gpsimd", lambda e: e.dma_start(out=T["dbg_g1"], in_=T["g_s"]),
                     waits=vgs, inc="gd", n=16)

        phase += 1
        if phase <= upto:
            vxnl = P.cnt("V")
            tw = ("T", P.cnt("T"))
            for k in (("l1k", 0), ("l1k", 1), "l1v"):
                P.slot_mark_add(k, *tw)
                P.slot_mark_add(k, "V", vxnl)
            for fc in range(KC):
                slot = fc % 2
                ksm = "lb%d" % slot
                lvk = P.load(stA[:, 1024 * slot:1024 * slot + 1024].rearrange(
                    "p (c q) -> p c q", q=128),
                    T["wqkvT"][8 + fc].rearrange("(c p) q -> p c q", p=128), ksm,
                    waits=P.slot_waits(("l1k", slot)) + list(sb_done.get("wqkvT", [])))
                lastm = None
                for j in range(NB):
                    bank = ("pA", pA) if (fc * NB + j) % 2 == 0 else ("pB", pB)
                    pairs = [(stA[:, 1024 * slot + 128 * kc:1024 * slot + 128 * kc + 128],
                              big0[:, 4096 * kc + 512 * j:4096 * kc + 512 * j + 512])
                             for kc in range(KC)]
                    vm = P.mm_group(bank[1][:, 0:512], pairs,
                                    waits=[(ksm, lvk), ("V", lnj1[j] or vxnl)], bankname=bank[0])
                    lastm = vm
                    ve = P.do("vector", lambda e, fc=fc, j=j, b=bank[1]: e.tensor_copy(
                        big1[:, 4096 * fc + 512 * j:4096 * fc + 512 * j + 512], b[:, 0:512]),
                        waits=[("T", vm)], inc="V")
                    P.bank_consumed(bank[0], "V", ve)
                P.slot_mark(("l1k", slot), "T", lastm)
            for fg in range(2):
                for kc in range(KC):
                    P.load(stA[:, 2048 + 512 * kc:2048 + 512 * kc + 512].rearrange(
                        "p (f q) -> p f q", q=128),
                        T["wqkvT"].rearrange("f (kc p) q -> p f kc q", p=128)[
                            :, 16 + 4 * fg:16 + 4 * fg + 4, kc, :],
                        "lb0", waits=P.slot_waits("l1v") + vgs +
                        list(sb_done.get("wqkvT", [])))
                lvv = P.cnt("lb0")
                lastm = None
                for i in range(NT):
                    bank = ("pC", pC) if i % 2 == 0 else ("pD", pD)
                    pairs = [(big0[:, 4096 * kc + 128 * i:4096 * kc + 128 * i + 128],
                              stA[:, 2048 + 512 * kc:2048 + 512 * kc + 512])
                             for kc in range(KC)]
                    vm = P.mm_group(bank[1][:, 0:512], pairs,
                                    waits=[("lb0", lvv), ("V", lnj1[i // 4] or vxnl)],
                                    bankname=bank[0])
                    lastm = vm
                    ve = P.do("vector", lambda e, i=i, fg=fg, b=bank[1]: e.tensor_copy(
                        big2[:, 1024 * i + 512 * fg:1024 * i + 512 * fg + 512], b[:, 0:512]),
                        waits=[("T", vm)], inc="V")
                    P.bank_consumed(bank[0], "V", ve)
                P.slot_mark("l1v", "T", lastm)
            BANKS = ["pA", "pB", "pC", "pD", "pE", "pF", "pR", "pT"]
            for j in range(NB):
                evs = []
                for fc in range(KC):
                    slot = fc % 2
                    lvq = P.load(stA[:, 1024 * slot:1024 * slot + 1024].rearrange(
                        "p (c q) -> p c q", q=128),
                        T["wqkvT"][fc].rearrange("(c p) q -> p c q", p=128),
                        "lb%d" % slot, waits=P.slot_waits(("l1k", slot)))
                    bank = BANKS[fc]
                    pairs = [(stA[:, 1024 * slot + 128 * kc:1024 * slot + 128 * kc + 128],
                              big0[:, 4096 * kc + 512 * j:4096 * kc + 512 * j + 512])
                             for kc in range(KC)]
                    vm = P.mm_group(PSUM[bank][:, 0:512], pairs,
                                    waits=[("lb%d" % slot, lvq), ("V", lnj1[j] or vxnl)],
                                    bankname=bank)
                    P.slot_mark(("l1k", slot), "T", vm)
                    evs.append((bank, vm))
                vall = P.cnt("T")
                for fc, (bank, vm) in enumerate(evs):
                    ve = P.do("vector", lambda e, fc=fc, j=j, bank=bank: e.tensor_copy(
                        big0[:, 4096 * fc + 512 * j:4096 * fc + 512 * j + 512],
                        PSUM[bank][:, 0:512]), waits=[("T", vall)], inc="V")
                    P.bank_consumed(bank, "V", ve)
            vqkv = P.cnt("V")
            if "kq1" in dbg:
                P.do("gpsimd", lambda e: e.dma_start(out=T["dbg_k1"], in_=big1[:]),
                     waits=[("V", vqkv)], inc="gd", n=16)
                P.do("gpsimd", lambda e: e.dma_start(out=T["dbg_q1"], in_=big0[:]),
                     waits=[("V", vqkv)], inc="gd", n=16)
            if "v1" in dbg:
                P.do("gpsimd", lambda e: e.dma_start(out=T["dbg_v1"], in_=big2[:]),
                     waits=[("V", vqkv)], inc="gd", n=16)

        phase += 1
        if phase <= upto:
            vqkv = P.cnt("V")

            def norm_max_l1(buf, vready, outcol):
                for j in range(NB):
                    for kc in range(KC):
                        wR = (P.bank_waits("pR") if kc == 0 else [])
                        vs = P.do("scalar", lambda e, kc=kc, j=j, buf=buf: e.activation(
                            stA[:, 4096:4608],
                            buf[:, 4096 * kc + 512 * j:4096 * kc + 512 * j + 512], AF.Square),
                            waits=wR + [("V", vready), ("T", P.cnt("T"))] + vgs, inc="S")
                        P.do("tensor", lambda e, kc=kc: e.matmul(
                            pR[0:1, 0:512], ones_c[:], stA[:, 4096:4608],
                            start=(kc == 0), stop=(kc == KC - 1)), waits=[("S", vs)], inc="T")
                    vt = P.cnt("T")
                    vv = P.do("vector", lambda e, j=j: e.tensor_reduce(
                        qmr[0:1, 8 + j:9 + j], pR[0:1, 0:512], AX.X, ALU.max),
                        waits=[("T", vt)], inc="V")
                    P.bank_consumed("pR", "V", vv)
                    P._wait("scalar", "V", vv)
                return P.do("vector", lambda e, outcol=outcol: e.tensor_reduce(
                    sc1[0:1, outcol:outcol + 1], qmr[0:1, 8:8 + NB], AX.X, ALU.max),
                    waits=[("V", P.cnt("V"))], inc="V")

            vqn = norm_max_l1(big0, vqkv, 4)
            vkn = norm_max_l1(big1, vqkv, 5)
            vpp = P.do("vector", lambda e: e.tensor_mul(sc1[0:1, 6:7], sc1[0:1, 4:5],
                                                        sc1[0:1, 5:6]),
                       waits=[("V", vqn), ("V", vkn)], inc="V")
            vsq = P.do("scalar", lambda e: e.activation(sc1[0:1, 7:8], sc1[0:1, 6:7],
                                                        AF.Sqrt), waits=[("V", vpp)], inc="S")
            wT2 = P.bank_waits("pT") + [("S", vsq)]
            vbc = P.do("tensor", lambda e: e.matmul(pT[:, 0:1], ones_rf[:], sc1[0:1, 7:8],
                                                    start=True, stop=True), waits=wT2, inc="T")
            vnc1 = P.do("scalar", lambda e: e.activation(negc[:, 1:2], pT[:, 0:1], AF.Identity,
                                                         scale=-(1.0 / 32.0) * 1.001),
                        waits=[("T", vbc)], inc="S")
            P.bank_consumed("pT", "S", vnc1)

            PVB = ["pC", "pD", "pE", "pF"]
            pv_war1 = [0, 0]
            qk_pend = {}

            def emit_qk(qb, kt):
                sb = ("pA", pA) if kt % 2 == 0 else ("pB", pB)
                pairs = [(big1[:, 4096 * kc + 128 * kt:4096 * kc + 128 * kt + 128],
                          big0[:, 4096 * kc + 256 * qb:4096 * kc + 256 * qb + 256])
                         for kc in range(KC)]
                vqk = P.mm_group(sb[1][:, 0:256], pairs, waits=[("S", vnc1), ("V", vqkv)],
                                 bankname=sb[0])
                qk_pend[(qb, kt)] = (sb, vqk)

            for qb in range(16):
                if qb == 0:
                    emit_qk(0, 0)
                pv_done = None
                for kt in range(NT):
                    sb, vqk = qk_pend.pop((qb, kt))
                    if kt + 1 < NT:
                        emit_qk(qb, kt + 1)
                    elif qb + 1 < 16:
                        emit_qk(qb + 1, 0)
                    ptile = stA[:, 4608 + 256 * (kt % 2):4608 + 256 * (kt % 2) + 256]
                    vex = P.do("scalar", lambda e, ptile=ptile, b=sb[1]: e.activation(
                        ptile, b[:, 0:256], AF.Exp, bias=negc[:, 1:2], scale=1.0 / 32.0),
                        waits=[("T", vqk), ("T", pv_war1[kt % 2])], inc="S")
                    P.bank_consumed(sb[0], "S", vex)
                    wz = [("S", vex)]
                    if kt == 0:
                        wz += P.bank_waits("pR")
                    P.do("tensor", lambda e, ptile=ptile, st=(kt == 0), sp=(kt == NT - 1): e.matmul(
                        pR[0:1, 0:256], ones_c[:], ptile, start=st, stop=sp), waits=wz, inc="T")
                    lastpv = None
                    for fc in range(KC):
                        bn = PVB[fc // 2]
                        col0 = 256 * (fc % 2)
                        wpv = []
                        if kt == 0 and fc % 2 == 0:
                            wpv += P.bank_waits(bn)
                        lvv = big2[:, 1024 * kt + 128 * fc:1024 * kt + 128 * fc + 128]
                        lastpv = P.do("tensor", lambda e, lvv=lvv, ptile=ptile, bn=bn, col0=col0,
                                      st=(kt == 0 and fc % 2 == 0),
                                      sp=(kt == NT - 1 and fc % 2 == 1): e.matmul(
                            PSUM[bn][:, col0:col0 + 256], lvv, ptile, start=st, stop=sp),
                            waits=wpv, inc="T" if fc == KC - 1 else None)
                    pv_war1[kt % 2] = lastpv
                    pv_done = lastpv
                rrow1 = stAf[0:1, 2816:3072]
                vrc = P.do("vector", lambda e: e.reciprocal(
                    rrow1, pR[0:1, 0:256]),
                    waits=[("T", P.cnt("T")), ("V", P.cnt("V"))], inc="V")
                P.bank_consumed("pR", "V", vrc)
                wbc = P.bank_waits("pT") + [("V", vrc)]
                vbc2 = P.do("tensor", lambda e: e.matmul(
                    pT[:, 0:256], ones_rf[:], rrow1, start=True, stop=True), waits=wbc, inc="T")
                for fc in range(KC):
                    bn = PVB[fc // 2]
                    col0 = 256 * (fc % 2)
                    ot = stA[:, 6144 + 256 * (fc % 2):6144 + 256 * (fc % 2) + 256]
                    voe = P.do("scalar", lambda e, ot=ot, bn=bn, col0=col0: e.copy(
                        ot, PSUM[bn][:, col0:col0 + 256]),
                        waits=[("T", pv_done), ("V", P.cnt("V"))], inc="S")
                    if fc % 2 == 1:
                        P.bank_consumed(bn, "S", voe)
                    dst = big0[:, 4096 * fc + 256 * qb:4096 * fc + 256 * qb + 256]
                    vml = P.do("vector", lambda e, dst=dst, ot=ot: e.tensor_mul(
                        dst, ot, pT[:, 0:256]), waits=[("S", voe), ("T", vbc2)], inc="V")
                P.bank_consumed("pT", "V", P.cnt("V"))

        phase += 1
        if phase <= upto:
            vs1 = P.cnt("V")
            b2f = big2[:].bitcast(F32)
            for j in range(NB):
                for c in range(KC):
                    sl = c % 2
                    fx = b2f[:, 8192 + 512 * sl:8192 + 512 * sl + 512]
                    fo = b2f[:, 9216 + 512 * sl:9216 + 512 * sl + 512]
                    gsl = big2[:, 20480 + 512 * sl:20480 + 512 * sl + 512]
                    prd = big2[:, 21504 + 512 * sl:21504 + 512 * sl + 512]
                    osm = "ob%d" % sl
                    lvx = P.load(fx, T["x1"][c * 128:(c + 1) * 128, j * 512:(j + 1) * 512],
                                 "lc0", waits=P.slot_waits(("cfx", sl)) + vx1 + [("V", vs1),
                                 ("T", P.cnt("T")), ("S", P.cnt("S"))])
                    lvg = P.load(gsl, T["g_s"][c * 128:(c + 1) * 128, j * 512:(j + 1) * 512],
                                 "lc1", waits=vgs + P.slot_waits(("csl", sl)) + [("V", vs1)])
                    vp2 = P.do("vector", lambda e, prd=prd, gsl=gsl, c=c, j=j: e.tensor_mul(
                        prd, gsl, big0[:, 4096 * c + 512 * j:4096 * c + 512 * j + 512]),
                        waits=[("lc1", lvg), ("V", vs1), ("S", P.cnt("S"))] + P.slot_waits(("cpr", sl)),
                        inc="V")
                    P.slot_mark(("csl", sl), "V", vp2)
                    vo = P.do("vector", lambda e, fo=fo, fx=fx, prd=prd: e.tensor_add(
                        fo, fx, prd),
                        waits=[("lc0", lvx), (osm, P.cnt(osm))], inc="V")
                    P.slot_mark(("cfx", sl), "V", vo)
                    P.slot_mark(("cpr", sl), "V", vo)
                    P.store(T["out"][c * 128:(c + 1) * 128, j * 512:(j + 1) * 512], fo,
                            osm, waits=[("V", vo)])

    if "x1d" in dbg and which == "full":
        P.do("gpsimd", lambda e: e.dma_start(out=T["dbg_x1d"], in_=T["x1"]),
             waits=vx1, inc="gd", n=16)
    final_waits = []
    for s in ("oa0", "oa1", "ob0", "ob1", "og", "ou", "od", "oq",
              "og1", "ou1", "od1", "oq1", "gd"):
        if P.cnt(s):
            final_waits.append((s, P.cnt(s)))
    for s, v in final_waits:
        P._wait("gpsimd", s, v)

    P.check()
    _maxsem = max(P.cnt(s) for s in P.sems)
    assert _maxsem < 60000, f"semaphore count overflow risk: {_maxsem}"

    nums = sorted(h.num for h, _ in P.sems.values())
    assert nums[-1] - nums[0] + 1 == len(nums), "sems not contiguous"
    nc.gpsimd.dma_reset(range(nums[0], nums[-1] + 1))
    nc.gpsimd.sem_clear(range(nums[0], nums[-1] + 1))
    nc.all_engine_barrier()

    with nc.Block() as block:
        @block.sync
        def _(e):
            for f in P.ops["sync"]:
                f(e)

        @block.tensor
        def _(e):
            for f in P.ops["tensor"]:
                f(e)

        @block.vector
        def _(e):
            for f in P.ops["vector"]:
                f(e)

        @block.scalar
        def _(e):
            for f in P.ops["scalar"]:
                f(e)

        @block.gpsimd
        def _(e):
            for f in P.ops["gpsimd"]:
                f(e)

    stack.close()
    return nc


def prep_inputs(inp, core):
    d = {}
    d["x"] = np.ascontiguousarray(inp["L0_attn_x"][core].T)
    d["wq"] = inp["L0_attn_wq"]
    d["wk"] = inp["L0_attn_wk"]
    d["wo"] = inp["L0_attn_wo"]
    wv = inp["L0_attn_wv"]
    wvp = np.zeros((D, H * (HD + 1)), np.float32)
    for h in range(H):
        wvp[:, h * 65:h * 65 + 64] = wv[:, h * 64:(h + 1) * 64]
    d["wv"] = wvp
    for nm, key in (("bq", "L0_attn_bq"), ("bk", "L0_attn_bk"), ("bo", "L0_attn_bo")):
        d[nm] = np.ascontiguousarray(inp[key].reshape(KC, 128).T)
    bv = inp["L0_attn_bv"]
    bvp = np.zeros((1, H * (HD + 1)), np.float32)
    for h in range(H):
        bvp[0, h * 65:h * 65 + 64] = bv[h * 64:(h + 1) * 64]
        bvp[0, h * 65 + 64] = 1.0
    d["bv"] = bvp
    d["wg"] = inp["L0_ffn_w_gate"]
    d["wu"] = inp["L0_ffn_w_up"]
    d["wd"] = inp["L0_ffn_w_down"]
    k = np.arange(128)[:, None]
    q = np.arange(512)[None, :]
    masks = np.zeros((4, 128, 512), np.float32)
    for dk in range(4):
        masks[dk] = (dk * 128 + k <= q).astype(np.float32)
    d["masks"] = masks
    d["ident"] = np.eye(128, dtype=np.float32)
    d["wqkv"] = inp["L1_attn_W_qkv"]

    def _arep(G1):
        A = np.transpose(G1[0], (1, 2, 0))
        A = np.broadcast_to(A[:, :, None, :, None], (4, 32, 16, 32, 16))
        return np.ascontiguousarray(A.reshape(4, 512, 512))

    def _brep(G2):
        B = np.transpose(G2[:, :, 0, :], (0, 2, 1))
        B = np.broadcast_to(B[:, None, :, None, :], (4, 32, 16, 32, 16))
        return np.ascontiguousarray(B.reshape(4, 512, 512))

    d["aR"] = _arep(inp["L1_ffn_G1_R"])
    d["aI"] = _arep(inp["L1_ffn_G1_I"])
    d["bR"] = _brep(inp["L1_ffn_G2_R"])
    d["bI"] = _brep(inp["L1_ffn_G2_I"])
    d["br"] = np.ascontiguousarray(inp["L1_ffn_bias_real"].reshape(4, 128).T)
    d["bi"] = np.ascontiguousarray(inp["L1_ffn_bias_imag"].reshape(4, 128).T)
    return d


def split_inputs(d, which="full"):
    l0 = {"x", "wq", "wk", "wv", "wo", "bq", "bk", "bv", "bo", "wg", "wu", "wd",
          "masks", "ident"}
    l1 = {"wqkv", "aR", "bR", "aI", "bI", "br", "bi", "ident"}
    if which == "l0":
        return {k: v for k, v in d.items() if k in l0}
    if which == "l1":
        return {k: v for k, v in d.items() if k in l1}
    return d


_NC_CACHE = [None]


def kernel(**inputs):
    import numpy as np
    from concourse.bass_utils import run_bass_kernel_spmd

    if _NC_CACHE[0] is None:
        _NC_CACHE[0] = build(which="full")
    nc = _NC_CACHE[0]
    in_maps = []
    for core in range(8):
        d = prep_inputs(inputs, core)
        in_maps.append(split_inputs(d, "full"))
    res = run_bass_kernel_spmd(nc, in_maps, core_ids=list(range(8)))
    outs = [np.asarray(res.results[c]["out"], dtype=np.float32).T for c in range(8)]
    return np.stack(outs, axis=0)


# version 16
# speedup vs baseline: 1.1948x; 1.0139x over previous
import numpy as np
from contextlib import ExitStack

import concourse.bass as bass
import concourse.mybir as mybir

F32 = mybir.dt.float32
BF16 = mybir.dt.bfloat16
I32 = mybir.dt.int32
AF = mybir.ActivationFunctionType
ALU = mybir.AluOpType
AX = mybir.AxisListType

N = 4096
D = 1024
KC = 8
NB = 8
NT = 32
H, HD = 16, 64
G, HG = 4, 4
HID, HCN = 3072, 24
EPS = 1e-5
VW = HG * (HD + 1)
SUBPH = 99
SELF_FENCE = True
GLIM = 99

ENGINES = ("sync", "tensor", "vector", "scalar", "gpsimd")


class Prog:
    def __init__(self, nc, stack):
        self.nc = nc
        self.stack = stack
        self.ops = {e: [] for e in ENGINES}
        self.sems = {}
        self.waited = {e: {} for e in ENGINES}
        self.bank = {}
        self.slot = {}
        self.semgate = {}

    def sem(self, name):
        if name not in self.sems:
            h = self.stack.enter_context(self.nc.semaphore(name))
            self.sems[name] = [h, 0]
        return name

    def cnt(self, name):
        return self.sems[name][1]

    def _wait(self, eng, s, v):
        if v <= 0:
            return
        w = self.waited[eng]
        if w.get(s, 0) >= v:
            return
        w[s] = v
        h = self.sems[s][0]
        self.ops[eng].append(lambda e, h=h, v=v: e.wait_ge(h, v))

    def do(self, eng, fn, waits=(), inc=None, n=1):
        if SELF_FENCE:
            if eng == "vector":
                self._wait(eng, "V", self.cnt("V"))
            elif eng == "scalar":
                self._wait(eng, "S", self.cnt("S"))
        for (s, v) in waits:
            self._wait(eng, s, v)
        if inc is not None:
            self.sem(inc)
            self.sems[inc][1] += n
            h = self.sems[inc][0]
            self.ops[eng].append(lambda e, fn=fn, h=h, n=n: fn(e).then_inc(h, n))
            if inc in ("T", "V", "S"):
                for (s, v) in waits:
                    if s.startswith("l"):
                        self.semgate[s] = (inc, self.sems[inc][1])
            return self.sems[inc][1]
        self.ops[eng].append(lambda e, fn=fn: fn(e))
        return None

    def load(self, dst, src, sem, waits=(), eng="sync", **kw):
        g = self.semgate.get(sem)
        allw = list(waits) + ([g] if g else [])
        return self.do(eng, lambda e, dst=dst, src=src, kw=kw: e.dma_start(out=dst, in_=src, **kw),
                       waits=allw, inc=sem, n=16)

    def store(self, dst, src, sem, waits=(), eng="scalar", **kw):
        return self.do(eng, lambda e, dst=dst, src=src, kw=kw: e.dma_start(out=dst, in_=src, **kw),
                       waits=waits, inc=sem, n=16)

    def bank_waits(self, name):
        return list(self.bank.get(name, ()))

    def bank_consumed(self, name, sem, val):
        self.bank[name] = [(sem, val)]

    def bank_consumed_add(self, name, sem, val):
        self.bank.setdefault(name, []).append((sem, val))

    def slot_waits(self, key):
        return list(self.slot.get(key, ()))

    def slot_mark(self, key, sem, val):
        self.slot[key] = [(sem, val)]

    def slot_mark_add(self, key, sem, val):
        self.slot.setdefault(key, []).append((sem, val))

    def mm_group(self, out_ap, pairs, waits=(), bankname=None, first_start=True,
                 last_stop=True, inc="T"):
        n = len(pairs)
        allw = list(waits)
        if bankname is not None and first_start:
            allw += self.bank_waits(bankname)
        r = None
        for i, (l, rr) in enumerate(pairs):
            st = first_start and (i == 0)
            sp = last_stop and (i == n - 1)
            r = self.do("tensor",
                        lambda e, l=l, rr=rr, st=st, sp=sp, o=out_ap: e.matmul(o, l, rr, start=st, stop=sp),
                        waits=allw if i == 0 else (),
                        inc=inc if (i == n - 1 and inc) else None)
        if r is not None:
            for (s, v) in allw:
                if s.startswith("l"):
                    self.semgate[s] = ("T", r)
        return r

    def check(self):
        for e in ENGINES:
            for s, v in self.waited[e].items():
                assert v <= self.sems[s][1], f"wait {s}>={v} never reached (max {self.sems[s][1]})"


def _recip_bf16(e, out, in_):
    with e.bass.allow_low_precision(reason="softmax 1/Z feeds a bf16 matmul operand"):
        return e.reciprocal(out, in_)


def build(which="full", dbg=(), upto=99):
    nc = bass.Bass("TRN2", target_bir_lowering=False)
    stack = ExitStack()
    T = {}

    def dram(name, shape, dt, kind):
        T[name] = nc.dram_tensor(name, list(shape), dt, kind=kind).ap()
        return T[name]

    emit_l0 = which in ("l0", "full")
    emit_l1 = which in ("l1", "full")

    if emit_l0:
        dram("x", (D, N), F32, "ExternalInput")
        for nm in ("wq", "wk", "wo"):
            dram(nm, (D, D), F32, "ExternalInput")
        dram("wv", (D, H * (HD + 1)), F32, "ExternalInput")
        for nm in ("bq", "bk", "bo"):
            dram(nm, (128, KC), F32, "ExternalInput")
        dram("bv", (1, H * (HD + 1)), F32, "ExternalInput")
        dram("wg", (HID, D), I32, "ExternalInput")
        dram("wu", (HID, D), I32, "ExternalInput")
        dram("wd", (D, HID), I32, "ExternalInput")
        dram("masks", (4, 128, 512), F32, "ExternalInput")
        dram("wgT", (HCN, D, 128), BF16, "Internal")
        dram("wuT", (HCN, D, 128), BF16, "Internal")
        dram("wdT", (KC, HID, 128), BF16, "Internal")
        dram("s_s", (D, N), BF16, "Internal")
    if emit_l1:
        dram("wqkv", (3 * D, D), F32, "ExternalInput")
        dram("aR", (4, 512, 512), F32, "ExternalInput")
        dram("bR", (4, 512, 512), F32, "ExternalInput")
        dram("aI", (4, 512, 512), F32, "ExternalInput")
        dram("bI", (4, 512, 512), F32, "ExternalInput")
        dram("br", (128, 4), F32, "ExternalInput")
        dram("bi", (128, 4), F32, "ExternalInput")
        dram("wqkvT", (3 * D // 128, D, 128), BF16, "Internal")
        dram("g_s", (D, N), BF16, "Internal")
    dram("ident", (128, 128), F32, "ExternalInput")
    if which == "full":
        dram("x1", (D, N), F32, "Internal")
        dram("out", (D, N), F32, "ExternalOutput")
    elif which == "l0":
        dram("x1", (D, N), F32, "ExternalOutput")
    else:
        dram("x1", (D, N), F32, "ExternalInput")
        dram("out", (D, N), F32, "ExternalOutput")
    for d in dbg:
        if d in ("xn0", "xn1", "kvq0", "att", "kq1", "v1", "gg"):
            names = {"kq1": ("dbg_k1", "dbg_q1")}.get(d, ("dbg_" + d,))
            for nm in names:
                dram(nm, (128, 32768), BF16, "ExternalOutput")
        elif d in ("s0", "g1"):
            dram("dbg_" + d, (D, N), BF16, "ExternalOutput")
        elif d == "WT":
            dram("dbg_WT", (128, 6144), BF16, "ExternalOutput")
        elif d == "x1d":
            dram("dbg_x1d", (D, N), F32, "ExternalOutput")

    big0 = stack.enter_context(nc.sbuf_tensor("big0", [128, 32768], BF16))
    big1 = stack.enter_context(nc.sbuf_tensor("big1", [128, 32768], BF16))
    big2 = stack.enter_context(nc.sbuf_tensor("big2", [128, 32768], BF16))
    stA = stack.enter_context(nc.sbuf_tensor("stA", [128, 7168], BF16))
    ident_f = stack.enter_context(nc.sbuf_tensor("ident_f", [128, 128], F32))
    ident_b = stack.enter_context(nc.sbuf_tensor("ident_b", [128, 128], BF16))
    ones_c = stack.enter_context(nc.sbuf_tensor("ones_c", [128, 1], BF16))
    ones_r = stack.enter_context(nc.sbuf_tensor("ones_r", [1, 128], BF16))
    ones_rf = stack.enter_context(nc.sbuf_tensor("ones_rf", [1, 128], F32))
    bias_q = stack.enter_context(nc.sbuf_tensor("bias_q", [128, KC], F32))
    bias_k = stack.enter_context(nc.sbuf_tensor("bias_k", [128, KC], F32))
    bias_o = stack.enter_context(nc.sbuf_tensor("bias_o", [128, KC], F32))
    bias_r = stack.enter_context(nc.sbuf_tensor("bias_r", [128, 4], F32))
    bias_i = stack.enter_context(nc.sbuf_tensor("bias_i", [128, 4], F32))
    negc = stack.enter_context(nc.sbuf_tensor("negc", [128, 2], F32))
    sc1 = stack.enter_context(nc.sbuf_tensor("sc1", [1, 16], F32))
    qmr = stack.enter_context(nc.sbuf_tensor("qmr", [1, 16], F32))

    pS0 = stack.enter_context(nc.psum_tensor("pS0", [128, 1024], F32))
    pS1 = stack.enter_context(nc.psum_tensor("pS1", [128, 1024], F32))
    pS2 = stack.enter_context(nc.psum_tensor("pS2", [128, 1024], F32))
    pCh = stack.enter_context(nc.psum_tensor("pC", [128, 512], F32))
    pDh = stack.enter_context(nc.psum_tensor("pD", [128, 512], F32))
    pS0a, pS1a, pS2a = pS0[:], pS1[:], pS2[:]
    pA = pS0a[:, 0:512]
    pB = pS0a[:, 512:1024]
    pR = pS1a[:, 0:512]
    pT = pS1a[:, 512:1024]
    pE = pS2a[:, 0:512]
    pF = pS2a[:, 512:1024]
    pC, pD = pCh[:], pDh[:]
    PSUM = {"pA": pA, "pB": pB, "pC": pC, "pD": pD, "pE": pE, "pF": pF, "pR": pR, "pT": pT}

    P = Prog(nc, stack)
    for s in ("T", "V", "S", "la0", "la1", "lb0", "lb1", "lb2", "lb3",
              "lc0", "lc1", "ld0", "ld1", "oa0", "oa1", "ob0", "ob1",
              "og", "ou", "od", "oq", "og1", "ou1", "od1", "oq1", "gd"):
        P.sem(s)

    stAf = stA[:].bitcast(F32)
    big1f = big1[:].bitcast(F32)
    big1i = big1[:].bitcast(I32)

    if emit_l0:
        P.load(bias_q[:], T["bq"], "la0")
        P.load(bias_k[:], T["bk"], "la0")
        P.load(bias_o[:], T["bo"], "la0")
    if emit_l1:
        P.load(bias_r[:], T["br"], "la0")
        P.load(bias_i[:], T["bi"], "la0")
    P.load(ident_f[:], T["ident"], "la0")
    vld = P.cnt("la0")
    P.do("vector", lambda e: e.tensor_copy(ident_b[:], ident_f[:]), waits=[("la0", vld)], inc="V")
    P.do("vector", lambda e: e.memset(ones_c[:], 1.0), inc="V")
    P.do("vector", lambda e: e.memset(ones_r[:], 1.0), inc="V")
    P.do("vector", lambda e: e.memset(ones_rf[:], 1.0), inc="V")
    P.do("vector", lambda e: e.memset(sc1[0:1, 15:16], EPS), inc="V")
    P.do("vector", lambda e: e.memset(sc1[0:1, 14:15], float(np.log(2.0 ** 55))), inc="V")
    if emit_l0:
        for i in range(4):
            fv = stAf[:, 512 * (i % 2):512 * (i % 2) + 512]
            P.load(fv, T["masks"][i], "la1", waits=[("V", P.cnt("V"))])
            v = P.cnt("la1")
            P.do("vector", lambda e, i=i, fv=fv: e.tensor_copy(
                big2[:, 24704 + 512 * i:24704 + 512 * i + 512], fv),
                waits=[("la1", v)], inc="V")
    vconsts = P.cnt("la0")
    vconstsV = P.cnt("V")

    phase = 0

    def emit_wbuild(src_name, dst_name, R, C, src_dt, osem, stg=None, stoff=0):
        stg = big1 if stg is None else stg
        stg_i = stg[:].bitcast(I32)
        stg_f = stg[:].bitcast(F32)
        src, dst = T[src_name], T[dst_name]
        tw = ("T", P.cnt("T"))
        for k in (("braw", 0), ("braw", 1), ("bconv", 0), ("bconv", 1),
                  ("bev", 0), ("bev", 1)):
            P.slot_mark_add(k, *tw)
        it = 0
        for r in range(R // 128):
            for cc in range(C // 1024):
                slot = it % 2
                it += 1
                raw = (stg_i if src_dt == I32 else stg_f)[
                    :, stoff // 2 + 1024 * slot:stoff // 2 + 1024 * slot + 1024]
                lsm = "lb%d" % slot
                co = stoff + 4096 + 1024 * slot
                eo = stoff + 6144 + 1024 * slot
                lv = P.load(raw, src[r * 128:(r + 1) * 128, cc * 1024:(cc + 1) * 1024],
                            lsm, waits=P.slot_waits(("braw", slot)))
                cv = P.do("vector", lambda e, raw=raw, co=co, stg=stg: e.tensor_copy(
                    stg[:, co:co + 1024], raw),
                    waits=[(lsm, lv)] + P.slot_waits(("bconv", slot)), inc="V")
                P.slot_mark(("braw", slot), "V", cv)
                vts = []
                for kc in range(8):
                    wT = P.bank_waits("pT") + [("V", cv)]
                    vt = P.do("tensor", lambda e, kc=kc, co=co, stg=stg: e.transpose(
                        pT.bitcast(BF16)[:, 128 * kc:128 * kc + 128],
                        stg[:, co + 128 * kc:co + 128 * kc + 128], ident_b[:]),
                        waits=wT, inc="T")
                    vts.append(vt)
                P.slot_mark(("bconv", slot), "T", vts[-1])
                ve = P.do("scalar", lambda e, eo=eo, stg=stg: e.copy(
                    stg[:, eo:eo + 1024], pT.bitcast(BF16)[:, 0:1024]),
                    waits=[("T", vts[-1])] + P.slot_waits(("bev", slot)), inc="S")
                P.bank_consumed("pT", "S", ve)
                dstap = dst[r][cc * 1024:(cc + 1) * 1024, :].rearrange(
                    "(c p) q -> p c q", p=128)
                ssem = osem if slot == 0 else osem + "1"
                sv = P.store(dstap, stg[:, eo:eo + 1024].rearrange("p (c q) -> p c q", q=128),
                             ssem, waits=[("S", ve)])
                P.slot_mark(("bev", slot), ssem, sv)
        return [(osem, P.cnt(osem)), (osem + "1", P.cnt(osem + "1"))]

    def emit_ln(src_name, wait_src=()):
        mu_row = big1[0:1, 30720:31232]
        rstd_row = big1[0:1, 31232:31744]
        var_row = big1f[0:1, 15872:16384]
        lnj = []
        war0 = [(s, P.cnt(s)) for s in ("og", "ou", "od", "oq", "og1", "ou1",
                                        "od1", "oq1", "oa0", "oa1", "ob0", "ob1")
                if P.cnt(s)]
        war0 += [("T", P.cnt("T")), ("S", P.cnt("S")), ("V", P.cnt("V"))]
        for j in range(NB):
            cvs = []
            for c in range(KC):
                slot = c % 2
                fv = stAf[:, 2048 + 512 * slot:2560 + 512 * slot]
                lv = P.load(fv, T[src_name][c * 128:(c + 1) * 128, j * 512:(j + 1) * 512],
                            "la%d" % slot,
                            waits=P.slot_waits(("lnf", slot)) + list(wait_src) + war0)
                cv = P.do("vector", lambda e, c=c, fv=fv: e.tensor_copy(
                    stA[:, 512 * c:512 * c + 512], fv),
                    waits=[("la%d" % slot, lv)] + P.slot_waits(("lnx", c)) + war0[-3:],
                    inc="V")
                P.slot_mark(("lnf", slot), "V", cv)
                cvs.append(cv)
            vconv = P.cnt("V")
            wR = P.bank_waits("pC")
            for c in range(KC):
                P.do("tensor", lambda e, c=c: e.matmul(
                    pC[0:1, 0:512], ones_c[:], stA[:, 512 * c:512 * c + 512],
                    start=(c == 0), stop=(c == KC - 1)),
                    waits=(wR if c == 0 else []) + [("V", cvs[c])],
                    inc="T" if c == KC - 1 else None)
            for c in range(KC):
                vs = P.do("scalar", lambda e, c=c: e.activation(
                    stA[:, 6144:6656], stA[:, 512 * c:512 * c + 512], AF.Square),
                    waits=[("V", cvs[c]), ("T", P.cnt("T"))] + war0, inc="S")
                P.do("tensor", lambda e, c=c: e.matmul(
                    pC[64:65, 0:512], ones_c[:], stA[:, 6144:6656],
                    start=(c == 0), stop=(c == KC - 1)), waits=[("S", vs)], inc="T")
            vstat = P.cnt("T")
            vmu = P.do("scalar", lambda e: e.activation(mu_row, pC[0:1, 0:512], AF.Copy,
                                                        scale=1.0 / D),
                       waits=[("T", vstat)], inc="S")
            wA = P.bank_waits("pD") + [("S", vmu)]
            vbA = P.do("tensor", lambda e: e.matmul(pD[:, 0:512], ones_r[:], mu_row,
                                                    start=True, stop=True), waits=wA, inc="T")
            vq = P.do("vector", lambda e: e.tensor_mul(rstd_row, mu_row, mu_row),
                      waits=[("S", vmu)], inc="V")
            vv = P.do("vector", lambda e: e.scalar_tensor_tensor(
                var_row, pC[64:65, 0:512], 1.0 / D, rstd_row, ALU.mult, ALU.subtract),
                waits=[("V", vq)], inc="V")
            P.bank_consumed("pC", "V", vv)
            vl = P.do("scalar", lambda e: e.activation(mu_row, var_row, AF.Ln,
                                                       bias=sc1[0:1, 15:16]),
                      waits=[("V", vv), ("T", vbA)], inc="S")
            vr = P.do("scalar", lambda e: e.activation(rstd_row, mu_row, AF.Exp, scale=-0.5),
                      waits=[("S", vl)], inc="S")
            wB = P.bank_waits("pE") + [("S", vr)]
            vbB = P.do("tensor", lambda e: e.matmul(pE[:, 0:512], ones_r[:], rstd_row,
                                                    start=True, stop=True), waits=wB, inc="T")
            for c in range(KC):
                vsb = P.do("vector", lambda e, c=c: e.tensor_sub(
                    stA[:, 512 * c:512 * c + 512], stA[:, 512 * c:512 * c + 512], pD[:, 0:512]),
                    waits=[("T", vbA)] if c == 0 else (), inc="V")
                vm2 = P.do("vector", lambda e, c=c, j=j: e.tensor_mul(
                    big0[:, 4096 * c + 512 * j:4096 * c + 512 * j + 512],
                    stA[:, 512 * c:512 * c + 512], pE[:, 0:512]),
                    waits=([("T", vbB)] if c == 0 else []) + [("V", vsb)], inc="V")
                P.slot_mark(("lnx", c), "V", vm2)
            vn = P.cnt("V")
            lnj.append(vn)
            P.bank_consumed("pD", "V", vn)
            P.bank_consumed("pE", "V", vn)
        return lnj

    sb_done = {}
    vx1 = []
    vspill = []
    if emit_l0:
        phase += 1
        if phase <= upto:
            sb_done["wgT"] = emit_wbuild("wg", "wgT", HID, D, I32, "og")
            sb_done["wuT"] = emit_wbuild("wu", "wuT", HID, D, I32, "ou")
            sb_done["wdT"] = emit_wbuild("wd", "wdT", D, HID, I32, "od")
        phase += 1
        lnj0 = [0] * NB
        if phase <= upto:
            lnj0 = emit_ln("x")
            if "xn0" in dbg:
                P.do("gpsimd", lambda e: e.dma_start(out=T["dbg_xn0"], in_=big0[:]),
                     waits=[("V", P.cnt("V"))], inc="gd", n=16)

        phase += 1
        if phase <= upto:
            vxn = P.cnt("V")
            lnj_ = lnj0 if lnj0[0] else [vxn] * NB
            KOFF, VOFF, QOFF = 0, 8192, 16512
            MOFF, POFF, OTOFF, ROFF = 24704, 26752, 30848, 31360
            war_bld = [(s, P.cnt(s)) for s in ("og", "ou", "od", "og1", "ou1", "od1") if P.cnt(s)]
            for k in (("pws", 0), ("pws", 1), "pwf"):
                P.slot_mark_add(k, "V", vxn)
            pv_war = [0] * 3
            att_pend = [None]
            sS = ["pA", "pB", "pR", "pT"]

            def load_wslice(wname, cols0, ncols, slot):
                for hh in range(2):
                    fv = stAf[:, 2080:2080 + 4 * ncols]
                    srcap = T[wname].rearrange("(kc p) q -> p kc q", p=128)[
                        :, 4 * hh:4 * hh + 4, cols0:cols0 + ncols]
                    lv = P.load(fv.rearrange("p (kc q) -> p kc q", kc=4), srcap,
                                "lb0", waits=P.slot_waits("pwf") + war_bld)
                    cv = P.do("vector", lambda e, fv=fv, slot=slot, hh=hh, ncols=ncols: e.tensor_copy(
                        stA[:, 2048 * slot + 4 * ncols * hh:2048 * slot + 4 * ncols * hh + 4 * ncols],
                        fv), waits=[("lb0", lv)] + P.slot_waits(("pws", slot)), inc="V")
                    P.slot_mark("pwf", "V", cv)
                return P.cnt("V")

            for g in range(min(G, GLIM)):
                vwk = load_wslice("wk", g * 256, 256, 0)
                for c2 in range(2):
                    for j in range(NB):
                        bank = ("pA", pA) if (c2 * NB + j) % 2 == 0 else ("pB", pB)
                        pairs = [(stA[:, 256 * kc + 128 * c2:256 * kc + 128 * c2 + 128],
                                  big0[:, 4096 * kc + 512 * j:4096 * kc + 512 * j + 512])
                                 for kc in range(KC)]
                        vm = P.mm_group(bank[1][:, 0:512], pairs,
                                        waits=[("V", vwk), ("V", lnj_[j])], bankname=bank[0])
                        ve = P.do("vector", lambda e, g=g, c2=c2, j=j, b=bank[1]: e.tensor_scalar_add(
                            big2[:, KOFF + 4096 * c2 + 512 * j:KOFF + 4096 * c2 + 512 * j + 512],
                            b[:, 0:512], bias_k[:, 2 * g + c2:2 * g + c2 + 1]),
                            waits=[("T", vm), ("la0", vconsts)], inc="V")
                        P.bank_consumed(bank[0], "V", ve)
                vKg = P.cnt("V")
                P.slot_mark(("pws", 0), "T", P.cnt("T"))
                fvb = stAf[0:1, 3120:3380]
                lvb = P.load(fvb, T["bv"][0:1, g * VW:(g + 1) * VW], "lc0",
                             waits=[("V", P.cnt("V"))] + war_bld)
                vbv = P.do("vector", lambda e, fvb=fvb: e.tensor_copy(
                    stA[0:1, 6760:6760 + VW], fvb),
                    waits=[("lc0", P.cnt("lc0")), ("T", P.cnt("T"))], inc="V")
                vwv = load_wslice("wv", g * VW, VW, 1)
                for i in range(NT):
                    bank = ("pC", pC) if i % 2 == 0 else ("pD", pD)
                    pairs = [(big0[:, 4096 * kc + 128 * i:4096 * kc + 128 * i + 128],
                              stA[:, 2048 + VW * kc:2048 + VW * kc + VW])
                             for kc in range(KC)]
                    P.mm_group(bank[1][:, 0:VW], pairs,
                               waits=[("V", vwv), ("V", lnj_[i // 4])],
                               bankname=bank[0], last_stop=False)
                    vm = P.do("tensor", lambda e, b=bank[1]: e.matmul(
                        b[:, 0:VW], ones_r[:], stA[0:1, 6760:6760 + VW],
                        start=False, stop=True), waits=[("V", vbv)], inc="T")
                    ve = P.do("vector", lambda e, i=i, b=bank[1]: e.tensor_copy(
                        big2[:, VOFF + VW * i:VOFF + VW * i + VW], b[:, 0:VW]),
                        waits=[("T", vm)], inc="V")
                    P.bank_consumed(bank[0], "V", ve)
                vVg = P.cnt("V")
                P.slot_mark(("pws", 1), "T", P.cnt("T"))
                vwq = load_wslice("wq", g * 256, 256, 0)
                for c2 in range(2):
                    for j in range(NB):
                        bank = ("pA", pA) if (c2 * NB + j) % 2 == 0 else ("pB", pB)
                        pairs = [(stA[:, 256 * kc + 128 * c2:256 * kc + 128 * c2 + 128],
                                  big0[:, 4096 * kc + 512 * j:4096 * kc + 512 * j + 512])
                                 for kc in range(KC)]
                        vm = P.mm_group(bank[1][:, 0:512], pairs,
                                        waits=[("V", vwq), ("V", lnj_[j])], bankname=bank[0])
                        ve = P.do("vector", lambda e, g=g, c2=c2, j=j, b=bank[1]: e.tensor_scalar_add(
                            big2[:, QOFF + 4096 * c2 + 512 * j:QOFF + 4096 * c2 + 512 * j + 512],
                            b[:, 0:512], bias_q[:, 2 * g + c2:2 * g + c2 + 1]),
                            waits=[("T", vm), ("la0", vconsts)], inc="V")
                        P.bank_consumed(bank[0], "V", ve)
                vQg = P.cnt("V")
                P.slot_mark(("pws", 0), "T", P.cnt("T"))
                if g == 0 and "kvq0" in dbg:
                    P.do("gpsimd", lambda e: e.dma_start(out=T["dbg_kvq0"], in_=big2[:]),
                         waits=[("V", vQg)], inc="gd", n=16)

                def norm_max(OFFS, vready, outcol):
                    for j in range(NB):
                        for c2 in range(2):
                            wR = (P.bank_waits("pR") if c2 == 0 else [])
                            vs = P.do("scalar", lambda e, c2=c2, j=j, OFFS=OFFS: e.activation(
                                stA[:, 4160:4672],
                                big2[:, OFFS + 4096 * c2 + 512 * j:OFFS + 4096 * c2 + 512 * j + 512],
                                AF.Square),
                                waits=wR + [("V", vready), ("T", P.cnt("T"))], inc="S")
                            P.do("tensor", lambda e, c2=c2: e.matmul(
                                pR[0:1, 0:512], ones_c[:], stA[:, 4160:4672],
                                start=(c2 == 0), stop=(c2 == 1)), waits=[("S", vs)], inc="T")
                        vt = P.cnt("T")
                        vv = P.do("vector", lambda e, j=j: e.tensor_reduce(
                            qmr[0:1, j:j + 1], pR[0:1, 0:512], AX.X, ALU.max),
                            waits=[("T", vt)], inc="V")
                        P.bank_consumed("pR", "V", vv)
                        P._wait("scalar", "V", vv)
                    return P.do("vector", lambda e, outcol=outcol: e.tensor_reduce(
                        sc1[0:1, outcol:outcol + 1], qmr[0:1, 0:NB], AX.X, ALU.max),
                        waits=[("V", P.cnt("V"))], inc="V")

                if SUBPH < 2:
                    continue
                vq = norm_max(QOFF, vQg, 0)
                vk = norm_max(KOFF, vKg, 1)
                P.slot_mark_add("pwf", "S", P.cnt("S"))
                P.slot_mark_add("pwf", "T", P.cnt("T"))
                vp = P.do("vector", lambda e: e.tensor_mul(sc1[0:1, 2:3], sc1[0:1, 0:1],
                                                           sc1[0:1, 1:2]),
                          waits=[("V", vq), ("V", vk)], inc="V")
                vsq = P.do("scalar", lambda e: e.activation(sc1[0:1, 3:4], sc1[0:1, 2:3],
                                                            AF.Sqrt), waits=[("V", vp)], inc="S")
                wT2 = P.bank_waits("pT") + [("S", vsq)]
                vbc = P.do("tensor", lambda e: e.matmul(pT[:, 0:1], ones_rf[:], sc1[0:1, 3:4],
                                                        start=True, stop=True), waits=wT2, inc="T")
                vnc = P.do("scalar", lambda e: e.activation(negc[:, 0:1], pT[:, 0:1], AF.Identity,
                                                            scale=-0.125 * 1.001),
                           waits=[("T", vbc)], inc="S")
                P.bank_consumed("pT", "S", vnc)

                if SUBPH < 3:
                    continue
                SPAIR = [pS0a, pS1a, pS2a]
                SHALF = [["pA", "pB"], ["pR", "pT"], ["pE", "pF"]]
                big2f = big2[:].bitcast(F32)
                rrows = [big2f[0:1, 15680:16192], big2f[0:1, 14912:15424]]

                def att_epilogue2(pend):
                    g_, hswp_, qb_, oz_done_, vrcs_ = pend
                    for hh2 in range(2):
                        hh = 2 * hswp_ + hh2
                        c2, hl = hh // 2, hh % 2
                        oz = PSUM[["pC", "pD"][hh2]]
                        wbc = P.bank_waits("pE") + [("S", vrcs_[hh2])]
                        vbc2 = P.do("tensor", lambda e, rr=rrows[hh2]: e.matmul(
                            pE[0:64, 0:512], ones_rf[0:1, 0:64], rr, start=True, stop=True),
                            waits=wbc, inc="T")
                        ot = big2[0:64, OTOFF:OTOFF + 512]
                        voe = P.do("scalar", lambda e, ot=ot, oz=oz: e.copy(ot, oz[0:64, 0:512]),
                                   waits=[("V", P.cnt("V")), ("S", P.cnt("S"))], inc="S")
                        P.bank_consumed(["pC", "pD"][hh2], "S", voe)
                        dst = big1[64 * hl:64 * hl + 64,
                                   4096 * (2 * g_ + c2) + 512 * qb_:4096 * (2 * g_ + c2) + 512 * qb_ + 512]
                        vml = P.do("vector", lambda e, dst=dst, ot=ot: e.tensor_mul(
                            dst, ot, pE[0:64, 0:512]),
                            waits=[("S", voe), ("T", vbc2)] + (war_bld if g_ == 0 else []),
                            inc="V")
                        P.bank_consumed("pE", "V", vml)
                        P._wait("scalar", "V", vml)

                for hswp in range(2):
                    for qb in range(NB):
                        ktn = 4 * qb + 4
                        oz_done = [None, None]
                        qk_pend = {}

                        def emit_qkpair(kt, hswp=hswp, qb=qb, qk_pend=None):
                            pr = kt % 3
                            qks = []
                            for hh2 in range(2):
                                hh = 2 * hswp + hh2
                                c2, hl = hh // 2, hh % 2
                                bkn = SHALF[pr][hh2]
                                l = big2[64 * hl:64 * hl + 64,
                                         KOFF + 4096 * c2 + 128 * kt:KOFF + 4096 * c2 + 128 * kt + 128]
                                r = big2[64 * hl:64 * hl + 64,
                                         QOFF + 4096 * c2 + 512 * qb:QOFF + 4096 * c2 + 512 * qb + 512]
                                wqk = P.bank_waits(bkn) + [("S", vnc), ("V", vVg)]
                                vqk = P.do("tensor", lambda e, l=l, r=r, b=PSUM[bkn]: e.matmul(
                                    b, l, r, start=True, stop=True), waits=wqk, inc="T")
                                qks.append(vqk)
                            qk_pend[kt] = (pr, qks)

                        emit_qkpair(0, qk_pend=qk_pend)
                        if ktn > 1:
                            emit_qkpair(1, qk_pend=qk_pend)
                        if att_pend[0] is not None:
                            att_epilogue2(att_pend[0])
                            att_pend[0] = None
                        for kt in range(ktn):
                            if kt + 2 < ktn:
                                emit_qkpair(kt + 2, qk_pend=qk_pend)
                            pr, qks = qk_pend.pop(kt)
                            pp = kt % 3
                            ppt = big2[:, POFF + 1024 * pp:POFF + 1024 * pp + 1024]
                            wex = [("T", qks[1]), ("T", pv_war[pp])]
                            vex = P.do("scalar", lambda e, ppt=ppt, b=SPAIR[pr]: e.activation(
                                ppt, b, AF.Exp, bias=negc[:, 0:1], scale=0.125),
                                waits=wex, inc="S")
                            P.bank_consumed(SHALF[pr][0], "S", vex)
                            P.bank_consumed(SHALF[pr][1], "S", vex)
                            pvwait = ("S", vex)
                            if kt >= 4 * qb:
                                dk = kt - 4 * qb
                                for half in range(2):
                                    vmk = P.do("vector", lambda e, ppt=ppt, dk=dk, half=half: e.tensor_mul(
                                        ppt[:, 512 * half:512 * half + 512],
                                        ppt[:, 512 * half:512 * half + 512],
                                        big2[:, MOFF + 512 * dk:MOFF + 512 * dk + 512]),
                                        waits=[("S", vex)], inc="V")
                                pvwait = ("V", vmk)
                            for hh2 in range(2):
                                hh = 2 * hswp + hh2
                                pt = ppt[:, 512 * hh2:512 * hh2 + 512]
                                ozn = ["pC", "pD"][hh2]
                                wpv = [pvwait]
                                if kt == 0:
                                    wpv += P.bank_waits(ozn)
                                lv = big2[:, VOFF + VW * kt + 65 * hh:VOFF + VW * kt + 65 * hh + 65]
                                vpv = P.do("tensor", lambda e, lv=lv, pt=pt, st=(kt == 0),
                                           sp=(kt == ktn - 1), oz=PSUM[ozn]: e.matmul(
                                    oz[0:65, 0:512], lv, pt, start=st, stop=sp), waits=wpv, inc="T")
                                pv_war[pp] = max(pv_war[pp], vpv)
                                oz_done[hh2] = vpv
                        vrcs = []
                        for hh2 in range(2):
                            oz = PSUM[["pC", "pD"][hh2]]
                            vln = P.do("scalar", lambda e, oz=oz: e.activation(
                                oz[64:65, 0:512], oz[64:65, 0:512], AF.Ln,
                                scale=float(2.0 ** 55)),
                                waits=[("T", oz_done[hh2]), ("S", P.cnt("S")),
                                       ("V", vconstsV)], inc="S")
                            vrc = P.do("scalar", lambda e, oz=oz, rr=rrows[hh2]: e.activation(
                                rr, oz[64:65, 0:512], AF.Exp, scale=-1.0,
                                bias=sc1[0:1, 14:15]),
                                waits=[("S", vln)], inc="S")
                            P.bank_consumed_add(["pC", "pD"][hh2], "S", vrc)
                            vrcs.append(vrc)
                        att_pend[0] = (g, hswp, qb, list(oz_done), vrcs)
                if att_pend[0] is not None:
                    att_epilogue2(att_pend[0])
                    att_pend[0] = None
            if att_pend[0] is not None:
                att_epilogue2(att_pend[0])
                att_pend[0] = None
            if "att" in dbg:
                P.do("gpsimd", lambda e: e.dma_start(out=T["dbg_att"], in_=big1[:]),
                     waits=[("V", P.cnt("V"))], inc="gd", n=16)

        phase += 1
        if phase <= upto:
            vOT = P.cnt("V")
            tw = ("T", P.cnt("T"))
            for k in (("wos", 0), ("wos", 1)):
                P.slot_mark_add(k, *tw)
            for dd in range(KC):
                slot = dd % 2
                for hh in range(2):
                    fv = stAf[:, 2080:2592]
                    srcap = T["wo"].rearrange("(kc p) q -> p kc q", p=128)[
                        :, 4 * hh:4 * hh + 4, dd * 128:(dd + 1) * 128]
                    lv = P.load(fv.rearrange("p (kc q) -> p kc q", kc=4), srcap, "lb0",
                                waits=P.slot_waits("pwf"))
                    cv = P.do("vector", lambda e, fv=fv, slot=slot, hh=hh: e.tensor_copy(
                        stA[:, 1024 * slot + 512 * hh:1024 * slot + 512 * hh + 512], fv),
                        waits=[("lb0", lv)] + P.slot_waits(("wos", slot)), inc="V")
                    P.slot_mark("pwf", "V", cv)
                vws = P.cnt("V")
                lastT = None
                for j in range(NB):
                    bank = ("pA", pA) if (dd * NB + j) % 2 == 0 else ("pB", pB)
                    pairs = [(stA[:, 1024 * slot + 128 * kc:1024 * slot + 128 * kc + 128],
                              big1[:, 4096 * kc + 512 * j:4096 * kc + 512 * j + 512])
                             for kc in range(KC)]
                    vm = P.mm_group(bank[1][:, 0:512], pairs, waits=[("V", vws), ("V", vOT)],
                                    bankname=bank[0])
                    lastT = vm
                    sp = stA[:, 6144 + 512 * (j % 2):6144 + 512 * (j % 2) + 512]
                    osm = "oa%d" % (j % 2)
                    ve = P.do("vector", lambda e, sp=sp, dd=dd, b=bank[1]: e.tensor_scalar_add(
                        sp, b[:, 0:512], bias_o[:, dd:dd + 1]),
                        waits=[("T", vm), (osm, P.cnt(osm)), ("la0", vconsts)], inc="V")
                    P.bank_consumed(bank[0], "V", ve)
                    P.store(T["s_s"][dd * 128:(dd + 1) * 128, j * 512:(j + 1) * 512], sp, osm,
                            waits=[("V", ve)])
                P.slot_mark(("wos", slot), "T", lastT)
            vspill = [("oa0", P.cnt("oa0")), ("oa1", P.cnt("oa1"))]
            if "s0" in dbg:
                P.do("gpsimd", lambda e: e.dma_start(out=T["dbg_s0"], in_=T["s_s"]),
                     waits=vspill, inc="gd", n=16)

        if emit_l1 and upto >= 5:
            sb_done["wqkvT"] = emit_wbuild("wqkv", "wqkvT", 3 * D, D, F32, "oq",
                                           stg=big2, stoff=8192)
        phase += 1
        if phase <= upto:
            vxn = P.cnt("V")
            tw = ("T", P.cnt("T"))
            for k in (("fg", 0), ("fg", 1), ("fu", 0), ("fu", 1), ("fd", 0), ("fd", 1),
                      ("sg", 0), ("sg", 1), ("cfx", 0), ("cfx", 1), ("csl", 0),
                      ("csl", 1), ("cpr", 0), ("cpr", 1)):
                P.slot_mark_add(k, *tw)
            for sv_ in vspill:
                P.slot_mark_add("cfx", sv_[0], sv_[1])
            for grp in range(4):
                j0 = grp * 2
                for hc in range(HCN):
                    slot = hc % 2
                    gsm = "lb%d" % (2 * slot)
                    usm = "lb%d" % (2 * slot + 1)
                    lvg = P.load(stA[:, 1024 * slot:1024 * slot + 1024].rearrange(
                        "p (c q) -> p c q", q=128),
                        T["wgT"][hc].rearrange("(c p) q -> p c q", p=128), gsm,
                        waits=P.slot_waits(("fg", slot)) + list(sb_done.get("wgT", [])))
                    lvu = P.load(stA[:, 2048 + 1024 * slot:2048 + 1024 * slot + 1024].rearrange(
                        "p (c q) -> p c q", q=128),
                        T["wuT"][hc].rearrange("(c p) q -> p c q", p=128), usm,
                        waits=P.slot_waits(("fu", slot)) + list(sb_done.get("wuT", [])))
                    lastg = lastu = None
                    for j2 in range(2):
                        j = j0 + j2
                        gb = ("pA", pA) if j2 == 0 else ("pB", pB)
                        ub = ("pC", pC) if j2 == 0 else ("pD", pD)
                        gp = [(stA[:, 1024 * slot + 128 * kc:1024 * slot + 128 * kc + 128],
                               big0[:, 4096 * kc + 512 * j:4096 * kc + 512 * j + 512])
                              for kc in range(KC)]
                        upp = [(stA[:, 2048 + 1024 * slot + 128 * kc:2048 + 1024 * slot + 128 * kc + 128],
                                big0[:, 4096 * kc + 512 * j:4096 * kc + 512 * j + 512])
                               for kc in range(KC)]
                        vmg = P.mm_group(gb[1][:, 0:512], gp, waits=[(gsm, lvg), ("V", vxn)],
                                         bankname=gb[0])
                        vmu2 = P.mm_group(ub[1][:, 0:512], upp, waits=[(usm, lvu)],
                                          bankname=ub[0])
                        lastg, lastu = vmg, vmu2
                        sgo = [4096, 6656][j2]
                        sg = stA[:, sgo:sgo + 512]
                        vsg = P.do("scalar", lambda e, sg=sg, b=gb[1]: e.activation(
                            sg, b[:, 0:512], AF.Silu, scale=1.0 / 127.0),
                            waits=[("T", vmg)] + P.slot_waits(("sg", j2)) + vspill, inc="S")
                        P.bank_consumed(gb[0], "S", vsg)
                        hdst = big1[:, 1024 * hc + 512 * j2:1024 * hc + 512 * j2 + 512]
                        vh = P.do("vector", lambda e, hdst=hdst, sg=sg, b=ub[1]: e.tensor_mul(
                            hdst, sg, b[:, 0:512]), waits=[("S", vsg), ("T", vmu2)], inc="V")
                        P.bank_consumed(ub[0], "V", vh)
                        P.slot_mark(("sg", j2), "V", vh)
                    P.slot_mark(("fg", slot), "T", lastg)
                    P.slot_mark(("fu", slot), "T", lastu)
                vhall = P.cnt("V")
                for dd in range(KC):
                    dslot = dd % 2
                    do_ = 24576 + 3072 * dslot
                    dsm = "ld%d" % dslot
                    lvd = P.load(big1[:, do_:do_ + 3072].rearrange("p (c q) -> p c q", q=128),
                                 T["wdT"][dd].rearrange("(c p) q -> p c q", p=128), dsm,
                                 waits=P.slot_waits(("fd", dslot)) + list(sb_done.get("wdT", [])))
                    lastd = None
                    for j2 in range(2):
                        bank = ("pE", pE) if j2 == 0 else ("pF", pF)
                        pairs = [(big1[:, do_ + 128 * hcc:do_ + 128 * hcc + 128],
                                  big1[:, 1024 * hcc + 512 * j2:1024 * hcc + 512 * j2 + 512])
                                 for hcc in range(HCN)]
                        vm = P.mm_group(bank[1][:, 0:512], pairs,
                                        waits=[(dsm, lvd), ("V", vhall)], bankname=bank[0])
                        lastd = vm
                        ve = P.do("scalar", lambda e, dd=dd, j2=j2, b=bank[1]: e.activation(
                            big2[:, 1024 * dd + 512 * j2:1024 * dd + 512 * j2 + 512],
                            b[:, 0:512], AF.Silu, scale=1.0 / 16129.0),
                            waits=[("T", vm)] + P.slot_waits(("gch", dd * 2 + j2)), inc="S")
                        P.bank_consumed(bank[0], "S", ve)
                    P.slot_mark(("fd", dslot), "T", lastd)
                vgrp = P.cnt("S")
                if grp == 0 and "gg" in dbg:
                    P.do("gpsimd", lambda e: e.dma_start(out=T["dbg_gg"], in_=big2[:]),
                         waits=[("S", vgrp)], inc="gd", n=16)
                b2f = big2[:].bitcast(F32)
                for j2 in range(2):
                    j = j0 + j2
                    for c in range(KC):
                        sl = c % 2
                        fx = b2f[:, 8192 + 512 * sl:8192 + 512 * sl + 512]
                        fo = b2f[:, 9216 + 512 * sl:9216 + 512 * sl + 512]
                        ssl = big2[:, 20480 + 512 * sl:20480 + 512 * sl + 512]
                        prd = big2[:, 21504 + 512 * sl:21504 + 512 * sl + 512]
                        osm = "ob%d" % sl
                        lvx = P.load(fx, T["x"][c * 128:(c + 1) * 128, j * 512:(j + 1) * 512],
                                     "lc0", waits=P.slot_waits(("cfx", sl)))
                        lvs = P.load(ssl, T["s_s"][c * 128:(c + 1) * 128, j * 512:(j + 1) * 512],
                                     "lc1", waits=vspill + P.slot_waits(("csl", sl)))
                        vp2 = P.do("vector", lambda e, prd=prd, ssl=ssl, c=c, j2=j2: e.tensor_mul(
                            prd, ssl, big2[:, 1024 * c + 512 * j2:1024 * c + 512 * j2 + 512]),
                            waits=[("lc1", lvs), ("S", vgrp)] + P.slot_waits(("cpr", sl)), inc="V")
                        P.slot_mark(("csl", sl), "V", vp2)
                        P.slot_mark_add(("gch", c * 2 + j2), "V", vp2)
                        vo = P.do("vector", lambda e, fo=fo, fx=fx, prd=prd: e.tensor_add(
                            fo, fx, prd),
                            waits=[("lc0", lvx), (osm, P.cnt(osm))], inc="V")
                        P.slot_mark(("cfx", sl), "V", vo)
                        P.slot_mark(("cpr", sl), "V", vo)
                        P.store(T["x1"][c * 128:(c + 1) * 128, j * 512:(j + 1) * 512], fo,
                                osm, waits=[("V", vo)])
            vx1 = [("ob0", P.cnt("ob0")), ("ob1", P.cnt("ob1"))]

    if emit_l1:
        phase += 1
        if phase <= upto and "wqkvT" not in sb_done:
            sb_done["wqkvT"] = emit_wbuild("wqkv", "wqkvT", 3 * D, D, F32, "oq")
        phase += 1
        lnj1 = [0] * NB
        vneg = 0
        if phase <= upto:
            war_q = [(s, P.cnt(s)) for s in ("oq", "oq1") if P.cnt(s)] + [("T", P.cnt("T"))]
            P.slot_mark_add("wba", "V", P.cnt("V"))
            for idx, (an, bn) in enumerate((("aR", "bR"), ("aI", "bI"))):
                for ch in range(4):
                    for c in range(4):
                        fa = stAf[:, 0:512]
                        fb = stAf[:, 512:1024]
                        acc = stAf[:, 1024:1536]
                        la = P.load(fa, T[an][c][ch * 128:(ch + 1) * 128, :], "lb0",
                                    waits=P.slot_waits("wba") + war_q[-1:])
                        lb = P.load(fb, T[bn][c][ch * 128:(ch + 1) * 128, :], "lb1",
                                    waits=P.slot_waits("wba"))
                        if c == 0:
                            vmu0 = P.do("vector", lambda e, fa=fa, fb=fb, acc=acc: e.tensor_mul(
                                acc, fa, fb), waits=[("lb0", la), ("lb1", lb)], inc="V")
                        else:
                            vt0 = P.do("vector", lambda e, fa=fa, fb=fb: e.tensor_mul(
                                stAf[:, 1536:2048], fa, fb),
                                waits=[("lb0", la), ("lb1", lb)], inc="V")
                            vmu0 = P.do("vector", lambda e, acc=acc: e.tensor_add(
                                acc, acc, stAf[:, 1536:2048]), waits=[("V", vt0)], inc="V")
                        P.slot_mark("wba", "V", vmu0)
                    vcp = P.do("vector", lambda e, idx=idx, ch=ch, acc=acc: e.tensor_copy(
                        big1[:, 2048 * idx + 512 * ch:2048 * idx + 512 * ch + 512], acc),
                        waits=war_q + [("V", vmu0)], inc="V")
                    P.slot_mark("wba", "V", vcp)
            vwt = P.cnt("V")
            vneg = P.do("vector", lambda e: e.tensor_scalar_mul(
                big1[:, 4096:6144], big1[:, 2048:4096], -1.0), waits=[("V", vwt)] + war_q,
                inc="V")
            if "WT" in dbg:
                P.do("gpsimd", lambda e: e.dma_start(out=T["dbg_WT"], in_=big1[:, 0:6144]),
                     waits=[("V", vneg)], inc="gd", n=16)
            lnj1 = emit_ln("x1", wait_src=vx1)
            if "xn1" in dbg:
                P.do("gpsimd", lambda e: e.dma_start(out=T["dbg_xn1"], in_=big0[:]),
                     waits=[("V", P.cnt("V"))], inc="gd", n=16)
        phase += 1
        vgs = []
        if phase <= upto:
            vxnl = P.cnt("V")
            for j in range(NB):
                for half in range(2):
                    for oc in range(4):
                        bank = ("pA", pA) if oc % 2 == 0 else ("pB", pB)
                        pairs = []
                        for ic in range(4):
                            wofs = 0 if half == 0 else 2048
                            pairs.append((big1[:, wofs + 512 * ic + 128 * oc:wofs + 512 * ic + 128 * oc + 128],
                                          big0[:, 4096 * ic + 512 * j:4096 * ic + 512 * j + 512]))
                        for ic in range(4):
                            wofs = 4096 if half == 0 else 0
                            pairs.append((big1[:, wofs + 512 * ic + 128 * oc:wofs + 512 * ic + 128 * oc + 128],
                                          big0[:, 4096 * (4 + ic) + 512 * j:4096 * (4 + ic) + 512 * j + 512]))
                        vm = P.mm_group(bank[1][:, 0:512], pairs,
                                        waits=[("V", vneg), ("V", lnj1[j] or vxnl)],
                                        bankname=bank[0])
                        gsl = big2[:, 512 * (oc % 2):512 * (oc % 2) + 512]
                        osm = "oa%d" % (oc % 2)
                        bias = bias_r if half == 0 else bias_i
                        ve = P.do("scalar", lambda e, gsl=gsl, oc=oc, b=bank[1], bias=bias: e.activation(
                            gsl, b[:, 0:512], AF.Silu, bias=bias[:, oc:oc + 1]),
                            waits=[("T", vm), (osm, P.cnt(osm)), ("la0", vconsts)], inc="S")
                        P.bank_consumed(bank[0], "S", ve)
                        P.store(T["g_s"][512 * half + 128 * oc:512 * half + 128 * oc + 128,
                                         j * 512:(j + 1) * 512], gsl, osm,
                                waits=[("S", ve)])
            vgs = [("oa0", P.cnt("oa0")), ("oa1", P.cnt("oa1"))]
            if "g1" in dbg:
                P.do("gpsimd", lambda e: e.dma_start(out=T["dbg_g1"], in_=T["g_s"]),
                     waits=vgs, inc="gd", n=16)

        phase += 1
        if phase <= upto:
            vxnl = P.cnt("V")
            tw = ("T", P.cnt("T"))
            for k in (("l1k", 0), ("l1k", 1), "l1v"):
                P.slot_mark_add(k, *tw)
                P.slot_mark_add(k, "V", vxnl)
            for fc in range(KC):
                slot = fc % 2
                ksm = "lb%d" % slot
                lvk = P.load(stA[:, 1024 * slot:1024 * slot + 1024].rearrange(
                    "p (c q) -> p c q", q=128),
                    T["wqkvT"][8 + fc].rearrange("(c p) q -> p c q", p=128), ksm,
                    waits=P.slot_waits(("l1k", slot)) + list(sb_done.get("wqkvT", [])))
                lastm = None
                for j in range(NB):
                    bank = ("pA", pA) if (fc * NB + j) % 2 == 0 else ("pB", pB)
                    pairs = [(stA[:, 1024 * slot + 128 * kc:1024 * slot + 128 * kc + 128],
                              big0[:, 4096 * kc + 512 * j:4096 * kc + 512 * j + 512])
                             for kc in range(KC)]
                    vm = P.mm_group(bank[1][:, 0:512], pairs,
                                    waits=[(ksm, lvk), ("V", lnj1[j] or vxnl)], bankname=bank[0])
                    lastm = vm
                    ve = P.do("vector", lambda e, fc=fc, j=j, b=bank[1]: e.tensor_copy(
                        big1[:, 4096 * fc + 512 * j:4096 * fc + 512 * j + 512], b[:, 0:512]),
                        waits=[("T", vm)], inc="V")
                    P.bank_consumed(bank[0], "V", ve)
                P.slot_mark(("l1k", slot), "T", lastm)
            for fg in range(2):
                for kc in range(KC):
                    P.load(stA[:, 2048 + 512 * kc:2048 + 512 * kc + 512].rearrange(
                        "p (f q) -> p f q", q=128),
                        T["wqkvT"].rearrange("f (kc p) q -> p f kc q", p=128)[
                            :, 16 + 4 * fg:16 + 4 * fg + 4, kc, :],
                        "lb0", waits=P.slot_waits("l1v") + vgs +
                        list(sb_done.get("wqkvT", [])))
                lvv = P.cnt("lb0")
                lastm = None
                for i in range(NT):
                    bank = ("pC", pC) if i % 2 == 0 else ("pD", pD)
                    pairs = [(big0[:, 4096 * kc + 128 * i:4096 * kc + 128 * i + 128],
                              stA[:, 2048 + 512 * kc:2048 + 512 * kc + 512])
                             for kc in range(KC)]
                    vm = P.mm_group(bank[1][:, 0:512], pairs,
                                    waits=[("lb0", lvv), ("V", lnj1[i // 4] or vxnl)],
                                    bankname=bank[0])
                    lastm = vm
                    ve = P.do("vector", lambda e, i=i, fg=fg, b=bank[1]: e.tensor_copy(
                        big2[:, 1024 * i + 512 * fg:1024 * i + 512 * fg + 512], b[:, 0:512]),
                        waits=[("T", vm)], inc="V")
                    P.bank_consumed(bank[0], "V", ve)
                P.slot_mark("l1v", "T", lastm)
            BANKS = ["pA", "pB", "pC", "pD", "pE", "pF", "pR", "pT"]
            for j in range(NB):
                evs = []
                for fc in range(KC):
                    slot = fc % 2
                    lvq = P.load(stA[:, 1024 * slot:1024 * slot + 1024].rearrange(
                        "p (c q) -> p c q", q=128),
                        T["wqkvT"][fc].rearrange("(c p) q -> p c q", p=128),
                        "lb%d" % slot, waits=P.slot_waits(("l1k", slot)))
                    bank = BANKS[fc]
                    pairs = [(stA[:, 1024 * slot + 128 * kc:1024 * slot + 128 * kc + 128],
                              big0[:, 4096 * kc + 512 * j:4096 * kc + 512 * j + 512])
                             for kc in range(KC)]
                    vm = P.mm_group(PSUM[bank][:, 0:512], pairs,
                                    waits=[("lb%d" % slot, lvq), ("V", lnj1[j] or vxnl)],
                                    bankname=bank)
                    P.slot_mark(("l1k", slot), "T", vm)
                    evs.append((bank, vm))
                vall = P.cnt("T")
                for fc, (bank, vm) in enumerate(evs):
                    ve = P.do("vector", lambda e, fc=fc, j=j, bank=bank: e.tensor_copy(
                        big0[:, 4096 * fc + 512 * j:4096 * fc + 512 * j + 512],
                        PSUM[bank][:, 0:512]), waits=[("T", vall)], inc="V")
                    P.bank_consumed(bank, "V", ve)
            vqkv = P.cnt("V")
            if "kq1" in dbg:
                P.do("gpsimd", lambda e: e.dma_start(out=T["dbg_k1"], in_=big1[:]),
                     waits=[("V", vqkv)], inc="gd", n=16)
                P.do("gpsimd", lambda e: e.dma_start(out=T["dbg_q1"], in_=big0[:]),
                     waits=[("V", vqkv)], inc="gd", n=16)
            if "v1" in dbg:
                P.do("gpsimd", lambda e: e.dma_start(out=T["dbg_v1"], in_=big2[:]),
                     waits=[("V", vqkv)], inc="gd", n=16)

        phase += 1
        if phase <= upto:
            vqkv = P.cnt("V")

            def norm_max_l1(buf, vready, outcol):
                for j in range(NB):
                    for kc in range(KC):
                        wR = (P.bank_waits("pR") if kc == 0 else [])
                        vs = P.do("scalar", lambda e, kc=kc, j=j, buf=buf: e.activation(
                            stA[:, 4096:4608],
                            buf[:, 4096 * kc + 512 * j:4096 * kc + 512 * j + 512], AF.Square),
                            waits=wR + [("V", vready), ("T", P.cnt("T"))] + vgs, inc="S")
                        P.do("tensor", lambda e, kc=kc: e.matmul(
                            pR[0:1, 0:512], ones_c[:], stA[:, 4096:4608],
                            start=(kc == 0), stop=(kc == KC - 1)), waits=[("S", vs)], inc="T")
                    vt = P.cnt("T")
                    vv = P.do("vector", lambda e, j=j: e.tensor_reduce(
                        qmr[0:1, 8 + j:9 + j], pR[0:1, 0:512], AX.X, ALU.max),
                        waits=[("T", vt)], inc="V")
                    P.bank_consumed("pR", "V", vv)
                    P._wait("scalar", "V", vv)
                return P.do("vector", lambda e, outcol=outcol: e.tensor_reduce(
                    sc1[0:1, outcol:outcol + 1], qmr[0:1, 8:8 + NB], AX.X, ALU.max),
                    waits=[("V", P.cnt("V"))], inc="V")

            vqn = norm_max_l1(big0, vqkv, 4)
            vkn = norm_max_l1(big1, vqkv, 5)
            vpp = P.do("vector", lambda e: e.tensor_mul(sc1[0:1, 6:7], sc1[0:1, 4:5],
                                                        sc1[0:1, 5:6]),
                       waits=[("V", vqn), ("V", vkn)], inc="V")
            vsq = P.do("scalar", lambda e: e.activation(sc1[0:1, 7:8], sc1[0:1, 6:7],
                                                        AF.Sqrt), waits=[("V", vpp)], inc="S")
            wT2 = P.bank_waits("pT") + [("S", vsq)]
            vbc = P.do("tensor", lambda e: e.matmul(pT[:, 0:1], ones_rf[:], sc1[0:1, 7:8],
                                                    start=True, stop=True), waits=wT2, inc="T")
            vnc1 = P.do("scalar", lambda e: e.activation(negc[:, 1:2], pT[:, 0:1], AF.Identity,
                                                         scale=-(1.0 / 32.0) * 1.001),
                        waits=[("T", vbc)], inc="S")
            P.bank_consumed("pT", "S", vnc1)

            PVB = ["pC", "pD", "pE", "pF"]
            pv_war1 = [0, 0]
            qk_pend = {}

            def emit_qk(qb, kt):
                sb = ("pA", pA) if kt % 2 == 0 else ("pB", pB)
                pairs = [(big1[:, 4096 * kc + 128 * kt:4096 * kc + 128 * kt + 128],
                          big0[:, 4096 * kc + 256 * qb:4096 * kc + 256 * qb + 256])
                         for kc in range(KC)]
                vqk = P.mm_group(sb[1][:, 0:256], pairs, waits=[("S", vnc1), ("V", vqkv)],
                                 bankname=sb[0])
                qk_pend[(qb, kt)] = (sb, vqk)

            for qb in range(16):
                if qb == 0:
                    emit_qk(0, 0)
                pv_done = None
                for kt in range(NT):
                    sb, vqk = qk_pend.pop((qb, kt))
                    if kt + 1 < NT:
                        emit_qk(qb, kt + 1)
                    elif qb + 1 < 16:
                        emit_qk(qb + 1, 0)
                    ptile = stA[:, 4608 + 256 * (kt % 2):4608 + 256 * (kt % 2) + 256]
                    vex = P.do("scalar", lambda e, ptile=ptile, b=sb[1]: e.activation(
                        ptile, b[:, 0:256], AF.Exp, bias=negc[:, 1:2], scale=1.0 / 32.0),
                        waits=[("T", vqk), ("T", pv_war1[kt % 2])], inc="S")
                    P.bank_consumed(sb[0], "S", vex)
                    wz = [("S", vex)]
                    if kt == 0:
                        wz += P.bank_waits("pR")
                    P.do("tensor", lambda e, ptile=ptile, st=(kt == 0), sp=(kt == NT - 1): e.matmul(
                        pR[0:1, 0:256], ones_c[:], ptile, start=st, stop=sp), waits=wz, inc="T")
                    lastpv = None
                    for fc in range(KC):
                        bn = PVB[fc // 2]
                        col0 = 256 * (fc % 2)
                        wpv = []
                        if kt == 0 and fc % 2 == 0:
                            wpv += P.bank_waits(bn)
                        lvv = big2[:, 1024 * kt + 128 * fc:1024 * kt + 128 * fc + 128]
                        lastpv = P.do("tensor", lambda e, lvv=lvv, ptile=ptile, bn=bn, col0=col0,
                                      st=(kt == 0 and fc % 2 == 0),
                                      sp=(kt == NT - 1 and fc % 2 == 1): e.matmul(
                            PSUM[bn][:, col0:col0 + 256], lvv, ptile, start=st, stop=sp),
                            waits=wpv, inc="T" if fc == KC - 1 else None)
                    pv_war1[kt % 2] = lastpv
                    pv_done = lastpv
                rrow1 = stAf[0:1, 2816:3072]
                vln = P.do("scalar", lambda e: e.activation(
                    pR[0:1, 0:256], pR[0:1, 0:256], AF.Ln,
                    scale=float(2.0 ** 55)),
                    waits=[("T", P.cnt("T")), ("S", P.cnt("S")), ("V", vconstsV)], inc="S")
                vrc = P.do("scalar", lambda e: e.activation(
                    rrow1, pR[0:1, 0:256], AF.Exp, scale=-1.0,
                    bias=sc1[0:1, 14:15]),
                    waits=[("S", vln)], inc="S")
                P.bank_consumed("pR", "S", vrc)
                wbc = P.bank_waits("pT") + [("S", vrc)]
                vbc2 = P.do("tensor", lambda e: e.matmul(
                    pT[:, 0:256], ones_rf[:], rrow1, start=True, stop=True), waits=wbc, inc="T")
                for fc in range(KC):
                    bn = PVB[fc // 2]
                    col0 = 256 * (fc % 2)
                    ot = stA[:, 6144 + 256 * (fc % 2):6144 + 256 * (fc % 2) + 256]
                    voe = P.do("scalar", lambda e, ot=ot, bn=bn, col0=col0: e.copy(
                        ot, PSUM[bn][:, col0:col0 + 256]),
                        waits=[("T", pv_done), ("V", P.cnt("V"))], inc="S")
                    if fc % 2 == 1:
                        P.bank_consumed(bn, "S", voe)
                    dst = big0[:, 4096 * fc + 256 * qb:4096 * fc + 256 * qb + 256]
                    vml = P.do("vector", lambda e, dst=dst, ot=ot: e.tensor_mul(
                        dst, ot, pT[:, 0:256]), waits=[("S", voe), ("T", vbc2)], inc="V")
                P.bank_consumed("pT", "V", P.cnt("V"))

        phase += 1
        if phase <= upto:
            vs1 = P.cnt("V")
            b2f = big2[:].bitcast(F32)
            for j in range(NB):
                for c in range(KC):
                    sl = c % 2
                    fx = b2f[:, 8192 + 512 * sl:8192 + 512 * sl + 512]
                    fo = b2f[:, 9216 + 512 * sl:9216 + 512 * sl + 512]
                    gsl = big2[:, 20480 + 512 * sl:20480 + 512 * sl + 512]
                    prd = big2[:, 21504 + 512 * sl:21504 + 512 * sl + 512]
                    osm = "ob%d" % sl
                    lvx = P.load(fx, T["x1"][c * 128:(c + 1) * 128, j * 512:(j + 1) * 512],
                                 "lc0", waits=P.slot_waits(("cfx", sl)) + vx1 + [("V", vs1),
                                 ("T", P.cnt("T")), ("S", P.cnt("S"))])
                    lvg = P.load(gsl, T["g_s"][c * 128:(c + 1) * 128, j * 512:(j + 1) * 512],
                                 "lc1", waits=vgs + P.slot_waits(("csl", sl)) + [("V", vs1)])
                    vp2 = P.do("vector", lambda e, prd=prd, gsl=gsl, c=c, j=j: e.tensor_mul(
                        prd, gsl, big0[:, 4096 * c + 512 * j:4096 * c + 512 * j + 512]),
                        waits=[("lc1", lvg), ("V", vs1), ("S", P.cnt("S"))] + P.slot_waits(("cpr", sl)),
                        inc="V")
                    P.slot_mark(("csl", sl), "V", vp2)
                    vo = P.do("vector", lambda e, fo=fo, fx=fx, prd=prd: e.tensor_add(
                        fo, fx, prd),
                        waits=[("lc0", lvx), (osm, P.cnt(osm))], inc="V")
                    P.slot_mark(("cfx", sl), "V", vo)
                    P.slot_mark(("cpr", sl), "V", vo)
                    P.store(T["out"][c * 128:(c + 1) * 128, j * 512:(j + 1) * 512], fo,
                            osm, waits=[("V", vo)])

    if "x1d" in dbg and which == "full":
        P.do("gpsimd", lambda e: e.dma_start(out=T["dbg_x1d"], in_=T["x1"]),
             waits=vx1, inc="gd", n=16)
    final_waits = []
    for s in ("oa0", "oa1", "ob0", "ob1", "og", "ou", "od", "oq",
              "og1", "ou1", "od1", "oq1", "gd"):
        if P.cnt(s):
            final_waits.append((s, P.cnt(s)))
    for s, v in final_waits:
        P._wait("gpsimd", s, v)

    P.check()
    _maxsem = max(P.cnt(s) for s in P.sems)
    assert _maxsem < 60000, f"semaphore count overflow risk: {_maxsem}"

    nums = sorted(h.num for h, _ in P.sems.values())
    assert nums[-1] - nums[0] + 1 == len(nums), "sems not contiguous"
    nc.gpsimd.dma_reset(range(nums[0], nums[-1] + 1))
    nc.gpsimd.sem_clear(range(nums[0], nums[-1] + 1))
    nc.all_engine_barrier()

    with nc.Block() as block:
        @block.sync
        def _(e):
            for f in P.ops["sync"]:
                f(e)

        @block.tensor
        def _(e):
            for f in P.ops["tensor"]:
                f(e)

        @block.vector
        def _(e):
            for f in P.ops["vector"]:
                f(e)

        @block.scalar
        def _(e):
            for f in P.ops["scalar"]:
                f(e)

        @block.gpsimd
        def _(e):
            for f in P.ops["gpsimd"]:
                f(e)

    stack.close()
    return nc


def prep_inputs(inp, core):
    d = {}
    d["x"] = np.ascontiguousarray(inp["L0_attn_x"][core].T)
    d["wq"] = inp["L0_attn_wq"]
    d["wk"] = inp["L0_attn_wk"]
    d["wo"] = inp["L0_attn_wo"]
    wv = inp["L0_attn_wv"]
    wvp = np.zeros((D, H * (HD + 1)), np.float32)
    for h in range(H):
        wvp[:, h * 65:h * 65 + 64] = wv[:, h * 64:(h + 1) * 64]
    d["wv"] = wvp
    for nm, key in (("bq", "L0_attn_bq"), ("bk", "L0_attn_bk"), ("bo", "L0_attn_bo")):
        d[nm] = np.ascontiguousarray(inp[key].reshape(KC, 128).T)
    bv = inp["L0_attn_bv"]
    bvp = np.zeros((1, H * (HD + 1)), np.float32)
    for h in range(H):
        bvp[0, h * 65:h * 65 + 64] = bv[h * 64:(h + 1) * 64]
        bvp[0, h * 65 + 64] = 1.0
    d["bv"] = bvp
    d["wg"] = inp["L0_ffn_w_gate"]
    d["wu"] = inp["L0_ffn_w_up"]
    d["wd"] = inp["L0_ffn_w_down"]
    k = np.arange(128)[:, None]
    q = np.arange(512)[None, :]
    masks = np.zeros((4, 128, 512), np.float32)
    for dk in range(4):
        masks[dk] = (dk * 128 + k <= q).astype(np.float32)
    d["masks"] = masks
    d["ident"] = np.eye(128, dtype=np.float32)
    d["wqkv"] = inp["L1_attn_W_qkv"]

    def _arep(G1):
        A = np.transpose(G1[0], (1, 2, 0))
        A = np.broadcast_to(A[:, :, None, :, None], (4, 32, 16, 32, 16))
        return np.ascontiguousarray(A.reshape(4, 512, 512))

    def _brep(G2):
        B = np.transpose(G2[:, :, 0, :], (0, 2, 1))
        B = np.broadcast_to(B[:, None, :, None, :], (4, 32, 16, 32, 16))
        return np.ascontiguousarray(B.reshape(4, 512, 512))

    d["aR"] = _arep(inp["L1_ffn_G1_R"])
    d["aI"] = _arep(inp["L1_ffn_G1_I"])
    d["bR"] = _brep(inp["L1_ffn_G2_R"])
    d["bI"] = _brep(inp["L1_ffn_G2_I"])
    d["br"] = np.ascontiguousarray(inp["L1_ffn_bias_real"].reshape(4, 128).T)
    d["bi"] = np.ascontiguousarray(inp["L1_ffn_bias_imag"].reshape(4, 128).T)
    return d


def split_inputs(d, which="full"):
    l0 = {"x", "wq", "wk", "wv", "wo", "bq", "bk", "bv", "bo", "wg", "wu", "wd",
          "masks", "ident"}
    l1 = {"wqkv", "aR", "bR", "aI", "bI", "br", "bi", "ident"}
    if which == "l0":
        return {k: v for k, v in d.items() if k in l0}
    if which == "l1":
        return {k: v for k, v in d.items() if k in l1}
    return d


_NC_CACHE = [None]


def kernel(**inputs):
    import numpy as np
    from concourse.bass_utils import run_bass_kernel_spmd

    if _NC_CACHE[0] is None:
        _NC_CACHE[0] = build(which="full")
    nc = _NC_CACHE[0]
    in_maps = []
    for core in range(8):
        d = prep_inputs(inputs, core)
        in_maps.append(split_inputs(d, "full"))
    res = run_bass_kernel_spmd(nc, in_maps, core_ids=list(range(8)))
    outs = [np.asarray(res.results[c]["out"], dtype=np.float32).T for c in range(8)]
    return np.stack(outs, axis=0)


# version 18
# speedup vs baseline: 1.2068x; 1.0100x over previous
import numpy as np
from contextlib import ExitStack

import concourse.bass as bass
import concourse.mybir as mybir

F32 = mybir.dt.float32
BF16 = mybir.dt.bfloat16
I32 = mybir.dt.int32
AF = mybir.ActivationFunctionType
ALU = mybir.AluOpType
AX = mybir.AxisListType

N = 4096
D = 1024
KC = 8
NB = 8
NT = 32
H, HD = 16, 64
G, HG = 4, 4
HID, HCN = 3072, 24
EPS = 1e-5
VW = HG * (HD + 1)
SUBPH = 99
SELF_FENCE = True
GLIM = 99

ENGINES = ("sync", "tensor", "vector", "scalar", "gpsimd")


class Prog:
    def __init__(self, nc, stack):
        self.nc = nc
        self.stack = stack
        self.ops = {e: [] for e in ENGINES}
        self.sems = {}
        self.waited = {e: {} for e in ENGINES}
        self.bank = {}
        self.slot = {}
        self.semgate = {}

    def sem(self, name):
        if name not in self.sems:
            h = self.stack.enter_context(self.nc.semaphore(name))
            self.sems[name] = [h, 0]
        return name

    def cnt(self, name):
        return self.sems[name][1]

    def _wait(self, eng, s, v):
        if v <= 0:
            return
        w = self.waited[eng]
        if w.get(s, 0) >= v:
            return
        w[s] = v
        h = self.sems[s][0]
        self.ops[eng].append(lambda e, h=h, v=v: e.wait_ge(h, v))

    def do(self, eng, fn, waits=(), inc=None, n=1):
        if SELF_FENCE:
            if eng == "vector":
                self._wait(eng, "V", self.cnt("V"))
            elif eng == "scalar":
                self._wait(eng, "S", self.cnt("S"))
        for (s, v) in waits:
            self._wait(eng, s, v)
        if inc is not None:
            self.sem(inc)
            self.sems[inc][1] += n
            h = self.sems[inc][0]
            self.ops[eng].append(lambda e, fn=fn, h=h, n=n: fn(e).then_inc(h, n))
            if inc in ("T", "V", "S"):
                for (s, v) in waits:
                    if s.startswith("l"):
                        self.semgate[s] = (inc, self.sems[inc][1])
            return self.sems[inc][1]
        self.ops[eng].append(lambda e, fn=fn: fn(e))
        return None

    def load(self, dst, src, sem, waits=(), eng="sync", **kw):
        g = self.semgate.get(sem)
        allw = list(waits) + ([g] if g else [])
        return self.do(eng, lambda e, dst=dst, src=src, kw=kw: e.dma_start(out=dst, in_=src, **kw),
                       waits=allw, inc=sem, n=16)

    def store(self, dst, src, sem, waits=(), eng="scalar", **kw):
        return self.do(eng, lambda e, dst=dst, src=src, kw=kw: e.dma_start(out=dst, in_=src, **kw),
                       waits=waits, inc=sem, n=16)

    def bank_waits(self, name):
        return list(self.bank.get(name, ()))

    def bank_consumed(self, name, sem, val):
        self.bank[name] = [(sem, val)]

    def bank_consumed_add(self, name, sem, val):
        self.bank.setdefault(name, []).append((sem, val))

    def slot_waits(self, key):
        return list(self.slot.get(key, ()))

    def slot_mark(self, key, sem, val):
        self.slot[key] = [(sem, val)]

    def slot_mark_add(self, key, sem, val):
        self.slot.setdefault(key, []).append((sem, val))

    def mm_group(self, out_ap, pairs, waits=(), bankname=None, first_start=True,
                 last_stop=True, inc="T"):
        n = len(pairs)
        allw = list(waits)
        if bankname is not None and first_start:
            allw += self.bank_waits(bankname)
        r = None
        for i, (l, rr) in enumerate(pairs):
            st = first_start and (i == 0)
            sp = last_stop and (i == n - 1)
            r = self.do("tensor",
                        lambda e, l=l, rr=rr, st=st, sp=sp, o=out_ap: e.matmul(o, l, rr, start=st, stop=sp),
                        waits=allw if i == 0 else (),
                        inc=inc if (i == n - 1 and inc) else None)
        if r is not None:
            for (s, v) in allw:
                if s.startswith("l"):
                    self.semgate[s] = ("T", r)
        return r

    def check(self):
        for e in ENGINES:
            for s, v in self.waited[e].items():
                assert v <= self.sems[s][1], f"wait {s}>={v} never reached (max {self.sems[s][1]})"


def _recip_bf16(e, out, in_):
    with e.bass.allow_low_precision(reason="softmax 1/Z feeds a bf16 matmul operand"):
        return e.reciprocal(out, in_)


def build(which="full", dbg=(), upto=99):
    nc = bass.Bass("TRN2", target_bir_lowering=False)
    stack = ExitStack()
    T = {}

    def dram(name, shape, dt, kind):
        T[name] = nc.dram_tensor(name, list(shape), dt, kind=kind).ap()
        return T[name]

    emit_l0 = which in ("l0", "full")
    emit_l1 = which in ("l1", "full")

    if emit_l0:
        dram("x", (D, N), F32, "ExternalInput")
        for nm in ("wq", "wk", "wo"):
            dram(nm, (D, D), F32, "ExternalInput")
        dram("wv", (D, H * (HD + 1)), F32, "ExternalInput")
        for nm in ("bq", "bk", "bo"):
            dram(nm, (128, KC), F32, "ExternalInput")
        dram("bv", (1, H * (HD + 1)), F32, "ExternalInput")
        dram("wg", (HID, D), I32, "ExternalInput")
        dram("wu", (HID, D), I32, "ExternalInput")
        dram("wd", (D, HID), I32, "ExternalInput")
        dram("masks", (4, 128, 512), F32, "ExternalInput")
        dram("wgT", (HCN, D, 128), BF16, "Internal")
        dram("wuT", (HCN, D, 128), BF16, "Internal")
        dram("wdT", (KC, HID, 128), BF16, "Internal")
        dram("s_s", (D, N), BF16, "Internal")
    if emit_l1:
        dram("wqkv", (3 * D, D), F32, "ExternalInput")
        dram("aR", (4, 512, 512), F32, "ExternalInput")
        dram("bR", (4, 512, 512), F32, "ExternalInput")
        dram("aI", (4, 512, 512), F32, "ExternalInput")
        dram("bI", (4, 512, 512), F32, "ExternalInput")
        dram("br", (128, 4), F32, "ExternalInput")
        dram("bi", (128, 4), F32, "ExternalInput")
        dram("wqkvT", (3 * D // 128, D, 128), BF16, "Internal")
        dram("g_s", (D, N), BF16, "Internal")
    dram("ident", (128, 128), F32, "ExternalInput")
    if which == "full":
        dram("x1", (D, N), F32, "Internal")
        dram("out", (D, N), F32, "ExternalOutput")
    elif which == "l0":
        dram("x1", (D, N), F32, "ExternalOutput")
    else:
        dram("x1", (D, N), F32, "ExternalInput")
        dram("out", (D, N), F32, "ExternalOutput")
    for d in dbg:
        if d in ("xn0", "xn1", "kvq0", "att", "kq1", "v1", "gg"):
            names = {"kq1": ("dbg_k1", "dbg_q1")}.get(d, ("dbg_" + d,))
            for nm in names:
                dram(nm, (128, 32768), BF16, "ExternalOutput")
        elif d in ("s0", "g1"):
            dram("dbg_" + d, (D, N), BF16, "ExternalOutput")
        elif d == "WT":
            dram("dbg_WT", (128, 6144), BF16, "ExternalOutput")
        elif d == "x1d":
            dram("dbg_x1d", (D, N), F32, "ExternalOutput")

    big0 = stack.enter_context(nc.sbuf_tensor("big0", [128, 32768], BF16))
    big1 = stack.enter_context(nc.sbuf_tensor("big1", [128, 32768], BF16))
    big2 = stack.enter_context(nc.sbuf_tensor("big2", [128, 32768], BF16))
    stA = stack.enter_context(nc.sbuf_tensor("stA", [128, 7168], BF16))
    ident_f = stack.enter_context(nc.sbuf_tensor("ident_f", [128, 128], F32))
    ident_b = stack.enter_context(nc.sbuf_tensor("ident_b", [128, 128], BF16))
    ones_c = stack.enter_context(nc.sbuf_tensor("ones_c", [128, 1], BF16))
    ones_r = stack.enter_context(nc.sbuf_tensor("ones_r", [1, 128], BF16))
    ones_rf = stack.enter_context(nc.sbuf_tensor("ones_rf", [1, 128], F32))
    bias_q = stack.enter_context(nc.sbuf_tensor("bias_q", [128, KC], F32))
    bias_k = stack.enter_context(nc.sbuf_tensor("bias_k", [128, KC], F32))
    bias_o = stack.enter_context(nc.sbuf_tensor("bias_o", [128, KC], F32))
    bias_r = stack.enter_context(nc.sbuf_tensor("bias_r", [128, 4], F32))
    bias_i = stack.enter_context(nc.sbuf_tensor("bias_i", [128, 4], F32))
    negc = stack.enter_context(nc.sbuf_tensor("negc", [128, 2], F32))
    sc1 = stack.enter_context(nc.sbuf_tensor("sc1", [1, 16], F32))
    qmr = stack.enter_context(nc.sbuf_tensor("qmr", [1, 16], F32))

    pS0 = stack.enter_context(nc.psum_tensor("pS0", [128, 1024], F32))
    pS1 = stack.enter_context(nc.psum_tensor("pS1", [128, 1024], F32))
    pS2 = stack.enter_context(nc.psum_tensor("pS2", [128, 1024], F32))
    pCh = stack.enter_context(nc.psum_tensor("pC", [128, 512], F32))
    pDh = stack.enter_context(nc.psum_tensor("pD", [128, 512], F32))
    pS0a, pS1a, pS2a = pS0[:], pS1[:], pS2[:]
    pA = pS0a[:, 0:512]
    pB = pS0a[:, 512:1024]
    pR = pS1a[:, 0:512]
    pT = pS1a[:, 512:1024]
    pE = pS2a[:, 0:512]
    pF = pS2a[:, 512:1024]
    pC, pD = pCh[:], pDh[:]
    PSUM = {"pA": pA, "pB": pB, "pC": pC, "pD": pD, "pE": pE, "pF": pF, "pR": pR, "pT": pT}

    P = Prog(nc, stack)
    for s in ("T", "V", "S", "la0", "la1", "lb0", "lb1", "lb2", "lb3",
              "lc0", "lc1", "ld0", "ld1", "oa0", "oa1", "ob0", "ob1",
              "og", "ou", "od", "oq", "og1", "ou1", "od1", "oq1", "gd"):
        P.sem(s)

    stAf = stA[:].bitcast(F32)
    big1f = big1[:].bitcast(F32)
    big1i = big1[:].bitcast(I32)

    if emit_l0:
        P.load(bias_q[:], T["bq"], "la0")
        P.load(bias_k[:], T["bk"], "la0")
        P.load(bias_o[:], T["bo"], "la0")
    if emit_l1:
        P.load(bias_r[:], T["br"], "la0")
        P.load(bias_i[:], T["bi"], "la0")
    P.load(ident_f[:], T["ident"], "la0")
    vld = P.cnt("la0")
    P.do("vector", lambda e: e.tensor_copy(ident_b[:], ident_f[:]), waits=[("la0", vld)], inc="V")
    P.do("vector", lambda e: e.memset(ones_c[:], 1.0), inc="V")
    P.do("vector", lambda e: e.memset(ones_r[:], 1.0), inc="V")
    P.do("vector", lambda e: e.memset(ones_rf[:], 1.0), inc="V")
    P.do("vector", lambda e: e.memset(sc1[0:1, 15:16], EPS), inc="V")
    P.do("vector", lambda e: e.memset(sc1[0:1, 14:15], float(np.log(2.0 ** 55))), inc="V")
    if emit_l0:
        for i in range(4):
            fv = stAf[:, 512 * (i % 2):512 * (i % 2) + 512]
            P.load(fv, T["masks"][i], "la1", waits=[("V", P.cnt("V"))])
            v = P.cnt("la1")
            P.do("vector", lambda e, i=i, fv=fv: e.tensor_copy(
                big2[:, 24704 + 512 * i:24704 + 512 * i + 512], fv),
                waits=[("la1", v)], inc="V")
    vconsts = P.cnt("la0")
    vconstsV = P.cnt("V")

    phase = 0

    def emit_wbuild(src_name, dst_name, R, C, src_dt, osem, stg=None, stoff=0):
        stg = big1 if stg is None else stg
        stg_i = stg[:].bitcast(I32)
        stg_f = stg[:].bitcast(F32)
        src, dst = T[src_name], T[dst_name]
        tw = ("T", P.cnt("T"))
        for k in (("braw", 0), ("braw", 1), ("bconv", 0), ("bconv", 1),
                  ("bev", 0), ("bev", 1)):
            P.slot_mark_add(k, *tw)
        it = 0
        for r in range(R // 128):
            for cc in range(C // 1024):
                slot = it % 2
                it += 1
                raw = (stg_i if src_dt == I32 else stg_f)[
                    :, stoff // 2 + 1024 * slot:stoff // 2 + 1024 * slot + 1024]
                lsm = "lb%d" % slot
                co = stoff + 4096 + 1024 * slot
                eo = stoff + 6144 + 1024 * slot
                lv = P.load(raw, src[r * 128:(r + 1) * 128, cc * 1024:(cc + 1) * 1024],
                            lsm, waits=P.slot_waits(("braw", slot)))
                cv = P.do("vector", lambda e, raw=raw, co=co, stg=stg: e.tensor_copy(
                    stg[:, co:co + 1024], raw),
                    waits=[(lsm, lv)] + P.slot_waits(("bconv", slot)), inc="V")
                P.slot_mark(("braw", slot), "V", cv)
                vts = []
                for kc in range(8):
                    wT = P.bank_waits("pT") + [("V", cv)]
                    vt = P.do("tensor", lambda e, kc=kc, co=co, stg=stg: e.transpose(
                        pT.bitcast(BF16)[:, 128 * kc:128 * kc + 128],
                        stg[:, co + 128 * kc:co + 128 * kc + 128], ident_b[:]),
                        waits=wT, inc="T")
                    vts.append(vt)
                P.slot_mark(("bconv", slot), "T", vts[-1])
                ve = P.do("scalar", lambda e, eo=eo, stg=stg: e.copy(
                    stg[:, eo:eo + 1024], pT.bitcast(BF16)[:, 0:1024]),
                    waits=[("T", vts[-1])] + P.slot_waits(("bev", slot)), inc="S")
                P.bank_consumed("pT", "S", ve)
                dstap = dst[r][cc * 1024:(cc + 1) * 1024, :].rearrange(
                    "(c p) q -> p c q", p=128)
                ssem = osem if slot == 0 else osem + "1"
                sv = P.store(dstap, stg[:, eo:eo + 1024].rearrange("p (c q) -> p c q", q=128),
                             ssem, waits=[("S", ve)])
                P.slot_mark(("bev", slot), ssem, sv)
        return [(osem, P.cnt(osem)), (osem + "1", P.cnt(osem + "1"))]

    def emit_ln(src_name, wait_src=(), war0=None):
        mu_row = big1[0:1, 30720:31232]
        rstd_row = big1[0:1, 31232:31744]
        var_row = big1f[0:1, 15872:16384]
        lnj = []
        if war0 is None:
            war0 = [(s, P.cnt(s)) for s in ("og", "ou", "od", "oq", "og1", "ou1",
                                            "od1", "oq1", "oa0", "oa1", "ob0", "ob1")
                    if P.cnt(s)]
            war0 += [("T", P.cnt("T")), ("S", P.cnt("S")), ("V", P.cnt("V"))]
        for j in range(NB):
            cvs = []
            for c in range(KC):
                slot = c % 2
                fv = stAf[:, 2048 + 512 * slot:2560 + 512 * slot]
                lv = P.load(fv, T[src_name][c * 128:(c + 1) * 128, j * 512:(j + 1) * 512],
                            "la%d" % slot,
                            waits=P.slot_waits(("lnf", slot)) + list(wait_src) + war0)
                cv = P.do("vector", lambda e, c=c, j=j, fv=fv: e.tensor_copy(
                    big0[:, 4096 * c + 512 * j:4096 * c + 512 * j + 512], fv),
                    waits=[("la%d" % slot, lv)] + war0[-3:],
                    inc="V")
                P.slot_mark(("lnf", slot), "V", cv)
                cvs.append(cv)
            vconv = P.cnt("V")
            wR = P.bank_waits("pC")
            for c in range(KC):
                P.do("tensor", lambda e, c=c, j=j: e.matmul(
                    pC[0:1, 0:512], ones_c[:],
                    big0[:, 4096 * c + 512 * j:4096 * c + 512 * j + 512],
                    start=(c == 0), stop=(c == KC - 1)),
                    waits=(wR if c == 0 else []) + [("V", cvs[c])],
                    inc="T" if c == KC - 1 else None)
            for c in range(KC):
                vs = P.do("scalar", lambda e, c=c, j=j: e.activation(
                    stA[:, 6144:6656],
                    big0[:, 4096 * c + 512 * j:4096 * c + 512 * j + 512], AF.Square),
                    waits=[("V", cvs[c]), ("T", P.cnt("T"))] + war0, inc="S")
                P.do("tensor", lambda e, c=c: e.matmul(
                    pC[64:65, 0:512], ones_c[:], stA[:, 6144:6656],
                    start=(c == 0), stop=(c == KC - 1)), waits=[("S", vs)], inc="T")
            vstat = P.cnt("T")
            vmu = P.do("scalar", lambda e: e.activation(mu_row, pC[0:1, 0:512], AF.Copy,
                                                        scale=1.0 / D),
                       waits=[("T", vstat)], inc="S")
            wA = P.bank_waits("pD") + [("S", vmu)]
            vbA = P.do("tensor", lambda e: e.matmul(pD[:, 0:512], ones_r[:], mu_row,
                                                    start=True, stop=True), waits=wA, inc="T")
            vq = P.do("vector", lambda e: e.tensor_mul(rstd_row, mu_row, mu_row),
                      waits=[("S", vmu)], inc="V")
            vv = P.do("vector", lambda e: e.scalar_tensor_tensor(
                var_row, pC[64:65, 0:512], 1.0 / D, rstd_row, ALU.mult, ALU.subtract),
                waits=[("V", vq)], inc="V")
            P.bank_consumed("pC", "V", vv)
            vl = P.do("scalar", lambda e: e.activation(mu_row, var_row, AF.Ln,
                                                       bias=sc1[0:1, 15:16]),
                      waits=[("V", vv), ("T", vbA)], inc="S")
            vr = P.do("scalar", lambda e: e.activation(rstd_row, mu_row, AF.Exp, scale=-0.5),
                      waits=[("S", vl)], inc="S")
            wB = P.bank_waits("pE") + [("S", vr)]
            vbB = P.do("tensor", lambda e: e.matmul(pE[:, 0:512], ones_r[:], rstd_row,
                                                    start=True, stop=True), waits=wB, inc="T")
            for c in range(KC):
                dstc = big0[:, 4096 * c + 512 * j:4096 * c + 512 * j + 512]
                vsb = P.do("vector", lambda e, dstc=dstc: e.tensor_sub(
                    dstc, dstc, pD[:, 0:512]),
                    waits=[("T", vbA)] if c == 0 else (), inc="V")
                vm2 = P.do("vector", lambda e, dstc=dstc: e.tensor_mul(
                    dstc, dstc, pE[:, 0:512]),
                    waits=([("T", vbB)] if c == 0 else []) + [("V", vsb)], inc="V")
            vn = P.cnt("V")
            lnj.append(vn)
            P.bank_consumed("pD", "V", vn)
            P.bank_consumed("pE", "V", vn)
        return lnj

    sb_done = {}
    vx1 = []
    vspill = []
    if emit_l0:
        war_pre = [("T", P.cnt("T")), ("S", P.cnt("S")), ("V", P.cnt("V"))]
        phase += 1
        if phase <= upto:
            sb_done["wgT"] = emit_wbuild("wg", "wgT", HID, D, I32, "og")
            sb_done["wuT"] = emit_wbuild("wu", "wuT", HID, D, I32, "ou")
            sb_done["wdT"] = emit_wbuild("wd", "wdT", D, HID, I32, "od")
        phase += 1
        lnj0 = [0] * NB
        if phase <= upto:
            lnj0 = emit_ln("x", war0=war_pre)
            if "xn0" in dbg:
                P.do("gpsimd", lambda e: e.dma_start(out=T["dbg_xn0"], in_=big0[:]),
                     waits=[("V", P.cnt("V"))], inc="gd", n=16)

        phase += 1
        if phase <= upto:
            vxn = P.cnt("V")
            lnj_ = lnj0 if lnj0[0] else [vxn] * NB
            KOFF, VOFF, QOFF = 0, 8192, 16512
            MOFF, POFF, OTOFF, ROFF = 24704, 26752, 30848, 31360
            war_bld = [(s, P.cnt(s)) for s in ("og", "ou", "od", "og1", "ou1", "od1") if P.cnt(s)]
            for k in (("pws", 0), ("pws", 1), "pwf"):
                P.slot_mark_add(k, "V", vxn)
            pv_war = [0] * 3
            att_pend = [None]
            sS = ["pA", "pB", "pR", "pT"]

            def load_wslice(wname, cols0, ncols, slot):
                for hh in range(2):
                    fv = stAf[:, 2080:2080 + 4 * ncols]
                    srcap = T[wname].rearrange("(kc p) q -> p kc q", p=128)[
                        :, 4 * hh:4 * hh + 4, cols0:cols0 + ncols]
                    lv = P.load(fv.rearrange("p (kc q) -> p kc q", kc=4), srcap,
                                "lb0", waits=P.slot_waits("pwf") + war_bld)
                    cv = P.do("vector", lambda e, fv=fv, slot=slot, hh=hh, ncols=ncols: e.tensor_copy(
                        stA[:, 2048 * slot + 4 * ncols * hh:2048 * slot + 4 * ncols * hh + 4 * ncols],
                        fv), waits=[("lb0", lv)] + P.slot_waits(("pws", slot)), inc="V")
                    P.slot_mark("pwf", "V", cv)
                return P.cnt("V")

            for g in range(min(G, GLIM)):
                vwk = load_wslice("wk", g * 256, 256, 0)
                for c2 in range(2):
                    for j in range(NB):
                        bank = ("pA", pA) if (c2 * NB + j) % 2 == 0 else ("pB", pB)
                        pairs = [(stA[:, 256 * kc + 128 * c2:256 * kc + 128 * c2 + 128],
                                  big0[:, 4096 * kc + 512 * j:4096 * kc + 512 * j + 512])
                                 for kc in range(KC)]
                        vm = P.mm_group(bank[1][:, 0:512], pairs,
                                        waits=[("V", vwk), ("V", lnj_[j])], bankname=bank[0])
                        ve = P.do("vector", lambda e, g=g, c2=c2, j=j, b=bank[1]: e.tensor_scalar_add(
                            big2[:, KOFF + 4096 * c2 + 512 * j:KOFF + 4096 * c2 + 512 * j + 512],
                            b[:, 0:512], bias_k[:, 2 * g + c2:2 * g + c2 + 1]),
                            waits=[("T", vm), ("la0", vconsts)], inc="V")
                        P.bank_consumed(bank[0], "V", ve)
                vKg = P.cnt("V")
                P.slot_mark(("pws", 0), "T", P.cnt("T"))
                fvb = stAf[0:1, 3120:3380]
                lvb = P.load(fvb, T["bv"][0:1, g * VW:(g + 1) * VW], "lc0",
                             waits=[("V", P.cnt("V"))] + war_bld)
                vbv = P.do("vector", lambda e, fvb=fvb: e.tensor_copy(
                    stA[0:1, 6760:6760 + VW], fvb),
                    waits=[("lc0", P.cnt("lc0")), ("T", P.cnt("T"))], inc="V")
                vwv = load_wslice("wv", g * VW, VW, 1)
                for i in range(NT):
                    bank = ("pC", pC) if i % 2 == 0 else ("pD", pD)
                    pairs = [(big0[:, 4096 * kc + 128 * i:4096 * kc + 128 * i + 128],
                              stA[:, 2048 + VW * kc:2048 + VW * kc + VW])
                             for kc in range(KC)]
                    P.mm_group(bank[1][:, 0:VW], pairs,
                               waits=[("V", vwv), ("V", lnj_[i // 4])],
                               bankname=bank[0], last_stop=False)
                    vm = P.do("tensor", lambda e, b=bank[1]: e.matmul(
                        b[:, 0:VW], ones_r[:], stA[0:1, 6760:6760 + VW],
                        start=False, stop=True), waits=[("V", vbv)], inc="T")
                    ve = P.do("vector", lambda e, i=i, b=bank[1]: e.tensor_copy(
                        big2[:, VOFF + VW * i:VOFF + VW * i + VW], b[:, 0:VW]),
                        waits=[("T", vm)], inc="V")
                    P.bank_consumed(bank[0], "V", ve)
                vVg = P.cnt("V")
                P.slot_mark(("pws", 1), "T", P.cnt("T"))
                vwq = load_wslice("wq", g * 256, 256, 0)
                for c2 in range(2):
                    for j in range(NB):
                        bank = ("pA", pA) if (c2 * NB + j) % 2 == 0 else ("pB", pB)
                        pairs = [(stA[:, 256 * kc + 128 * c2:256 * kc + 128 * c2 + 128],
                                  big0[:, 4096 * kc + 512 * j:4096 * kc + 512 * j + 512])
                                 for kc in range(KC)]
                        vm = P.mm_group(bank[1][:, 0:512], pairs,
                                        waits=[("V", vwq), ("V", lnj_[j])], bankname=bank[0])
                        ve = P.do("vector", lambda e, g=g, c2=c2, j=j, b=bank[1]: e.tensor_scalar_add(
                            big2[:, QOFF + 4096 * c2 + 512 * j:QOFF + 4096 * c2 + 512 * j + 512],
                            b[:, 0:512], bias_q[:, 2 * g + c2:2 * g + c2 + 1]),
                            waits=[("T", vm), ("la0", vconsts)], inc="V")
                        P.bank_consumed(bank[0], "V", ve)
                vQg = P.cnt("V")
                P.slot_mark(("pws", 0), "T", P.cnt("T"))
                if g == 0 and "kvq0" in dbg:
                    P.do("gpsimd", lambda e: e.dma_start(out=T["dbg_kvq0"], in_=big2[:]),
                         waits=[("V", vQg)], inc="gd", n=16)

                def norm_max(OFFS, vready, outcol):
                    for j in range(NB):
                        for c2 in range(2):
                            wR = (P.bank_waits("pR") if c2 == 0 else [])
                            vs = P.do("vector", lambda e, c2=c2, j=j, OFFS=OFFS: e.tensor_mul(
                                stA[:, 4160:4672],
                                big2[:, OFFS + 4096 * c2 + 512 * j:OFFS + 4096 * c2 + 512 * j + 512],
                                big2[:, OFFS + 4096 * c2 + 512 * j:OFFS + 4096 * c2 + 512 * j + 512]),
                                waits=wR + [("V", vready), ("T", P.cnt("T"))], inc="V")
                            P.do("tensor", lambda e, c2=c2: e.matmul(
                                pR[0:1, 0:512], ones_c[:], stA[:, 4160:4672],
                                start=(c2 == 0), stop=(c2 == 1)), waits=[("V", vs)], inc="T")
                        vt = P.cnt("T")
                        vv = P.do("vector", lambda e, j=j: e.tensor_reduce(
                            qmr[0:1, j:j + 1], pR[0:1, 0:512], AX.X, ALU.max),
                            waits=[("T", vt)], inc="V")
                        P.bank_consumed("pR", "V", vv)
                        P._wait("scalar", "V", vv)
                    return P.do("vector", lambda e, outcol=outcol: e.tensor_reduce(
                        sc1[0:1, outcol:outcol + 1], qmr[0:1, 0:NB], AX.X, ALU.max),
                        waits=[("V", P.cnt("V"))], inc="V")

                if SUBPH < 2:
                    continue
                vq = norm_max(QOFF, vQg, 0)
                vk = norm_max(KOFF, vKg, 1)
                P.slot_mark_add("pwf", "S", P.cnt("S"))
                P.slot_mark_add("pwf", "T", P.cnt("T"))
                vp = P.do("vector", lambda e: e.tensor_mul(sc1[0:1, 2:3], sc1[0:1, 0:1],
                                                           sc1[0:1, 1:2]),
                          waits=[("V", vq), ("V", vk)], inc="V")
                vsq = P.do("scalar", lambda e: e.activation(sc1[0:1, 3:4], sc1[0:1, 2:3],
                                                            AF.Sqrt), waits=[("V", vp)], inc="S")
                wT2 = P.bank_waits("pT") + [("S", vsq)]
                vbc = P.do("tensor", lambda e: e.matmul(pT[:, 0:1], ones_rf[:], sc1[0:1, 3:4],
                                                        start=True, stop=True), waits=wT2, inc="T")
                vnc = P.do("scalar", lambda e: e.activation(negc[:, 0:1], pT[:, 0:1], AF.Identity,
                                                            scale=-0.125 * 1.001),
                           waits=[("T", vbc)], inc="S")
                P.bank_consumed("pT", "S", vnc)

                if SUBPH < 3:
                    continue
                SPAIR = [pS0a, pS1a, pS2a]
                SHALF = [["pA", "pB"], ["pR", "pT"], ["pE", "pF"]]
                big2f = big2[:].bitcast(F32)
                rrows = [big2f[0:1, 15680:16192], big2f[0:1, 14912:15424]]

                def att_epilogue2(pend):
                    g_, hswp_, qb_, oz_done_, vrcs_ = pend
                    for hh2 in range(2):
                        hh = 2 * hswp_ + hh2
                        c2, hl = hh // 2, hh % 2
                        oz = PSUM[["pC", "pD"][hh2]]
                        wbc = P.bank_waits("pE") + [("S", vrcs_[hh2])]
                        vbc2 = P.do("tensor", lambda e, rr=rrows[hh2]: e.matmul(
                            pE[0:64, 0:512], ones_rf[0:1, 0:64], rr, start=True, stop=True),
                            waits=wbc, inc="T")
                        ot = big2[0:64, OTOFF:OTOFF + 512]
                        voe = P.do("vector", lambda e, ot=ot, oz=oz: e.tensor_copy(
                            ot, oz[0:64, 0:512]),
                            waits=[("V", P.cnt("V")), ("S", P.cnt("S"))], inc="V")
                        P.bank_consumed(["pC", "pD"][hh2], "V", voe)
                        dst = big1[64 * hl:64 * hl + 64,
                                   4096 * (2 * g_ + c2) + 512 * qb_:4096 * (2 * g_ + c2) + 512 * qb_ + 512]
                        vml = P.do("vector", lambda e, dst=dst, ot=ot: e.tensor_mul(
                            dst, ot, pE[0:64, 0:512]),
                            waits=[("V", voe), ("T", vbc2)] + (war_bld if g_ == 0 else []),
                            inc="V")
                        P.bank_consumed("pE", "V", vml)
                        P._wait("scalar", "V", vml)

                for hswp in range(2):
                    for qb in range(NB):
                        ktn = 4 * qb + 4
                        oz_done = [None, None]
                        qk_pend = {}

                        def emit_qkpair(kt, hswp=hswp, qb=qb, qk_pend=None):
                            pr = kt % 3
                            qks = []
                            for hh2 in range(2):
                                hh = 2 * hswp + hh2
                                c2, hl = hh // 2, hh % 2
                                bkn = SHALF[pr][hh2]
                                l = big2[64 * hl:64 * hl + 64,
                                         KOFF + 4096 * c2 + 128 * kt:KOFF + 4096 * c2 + 128 * kt + 128]
                                r = big2[64 * hl:64 * hl + 64,
                                         QOFF + 4096 * c2 + 512 * qb:QOFF + 4096 * c2 + 512 * qb + 512]
                                wqk = P.bank_waits(bkn) + [("S", vnc), ("V", vVg)]
                                vqk = P.do("tensor", lambda e, l=l, r=r, b=PSUM[bkn]: e.matmul(
                                    b, l, r, start=True, stop=True), waits=wqk, inc="T")
                                qks.append(vqk)
                            qk_pend[kt] = (pr, qks)

                        emit_qkpair(0, qk_pend=qk_pend)
                        if ktn > 1:
                            emit_qkpair(1, qk_pend=qk_pend)
                        if att_pend[0] is not None:
                            att_epilogue2(att_pend[0])
                            att_pend[0] = None
                        for kt in range(ktn):
                            if kt + 2 < ktn:
                                emit_qkpair(kt + 2, qk_pend=qk_pend)
                            pr, qks = qk_pend.pop(kt)
                            pp = kt % 3
                            ppt = big2[:, POFF + 1024 * pp:POFF + 1024 * pp + 1024]
                            wex = [("T", qks[1]), ("T", pv_war[pp])]
                            vex = P.do("scalar", lambda e, ppt=ppt, b=SPAIR[pr]: e.activation(
                                ppt, b, AF.Exp, bias=negc[:, 0:1], scale=0.125),
                                waits=wex, inc="S")
                            P.bank_consumed(SHALF[pr][0], "S", vex)
                            P.bank_consumed(SHALF[pr][1], "S", vex)
                            pvwait = ("S", vex)
                            if kt >= 4 * qb:
                                dk = kt - 4 * qb
                                for half in range(2):
                                    vmk = P.do("vector", lambda e, ppt=ppt, dk=dk, half=half: e.tensor_mul(
                                        ppt[:, 512 * half:512 * half + 512],
                                        ppt[:, 512 * half:512 * half + 512],
                                        big2[:, MOFF + 512 * dk:MOFF + 512 * dk + 512]),
                                        waits=[("S", vex)], inc="V")
                                pvwait = ("V", vmk)
                            for hh2 in range(2):
                                hh = 2 * hswp + hh2
                                pt = ppt[:, 512 * hh2:512 * hh2 + 512]
                                ozn = ["pC", "pD"][hh2]
                                wpv = [pvwait]
                                if kt == 0:
                                    wpv += P.bank_waits(ozn)
                                lv = big2[:, VOFF + VW * kt + 65 * hh:VOFF + VW * kt + 65 * hh + 65]
                                vpv = P.do("tensor", lambda e, lv=lv, pt=pt, st=(kt == 0),
                                           sp=(kt == ktn - 1), oz=PSUM[ozn]: e.matmul(
                                    oz[0:65, 0:512], lv, pt, start=st, stop=sp), waits=wpv, inc="T")
                                pv_war[pp] = max(pv_war[pp], vpv)
                                oz_done[hh2] = vpv
                        vrcs = []
                        for hh2 in range(2):
                            oz = PSUM[["pC", "pD"][hh2]]
                            vln = P.do("scalar", lambda e, oz=oz: e.activation(
                                oz[64:65, 0:512], oz[64:65, 0:512], AF.Ln,
                                scale=float(2.0 ** 55)),
                                waits=[("T", oz_done[hh2]), ("S", P.cnt("S")),
                                       ("V", vconstsV)], inc="S")
                            vrc = P.do("scalar", lambda e, oz=oz, rr=rrows[hh2]: e.activation(
                                rr, oz[64:65, 0:512], AF.Exp, scale=-1.0,
                                bias=sc1[0:1, 14:15]),
                                waits=[("S", vln)], inc="S")
                            P.bank_consumed_add(["pC", "pD"][hh2], "S", vrc)
                            vrcs.append(vrc)
                        att_pend[0] = (g, hswp, qb, list(oz_done), vrcs)
                if att_pend[0] is not None:
                    att_epilogue2(att_pend[0])
                    att_pend[0] = None
            if att_pend[0] is not None:
                att_epilogue2(att_pend[0])
                att_pend[0] = None
            if "att" in dbg:
                P.do("gpsimd", lambda e: e.dma_start(out=T["dbg_att"], in_=big1[:]),
                     waits=[("V", P.cnt("V"))], inc="gd", n=16)

        phase += 1
        if phase <= upto:
            vOT = P.cnt("V")
            tw = ("T", P.cnt("T"))
            for k in (("wos", 0), ("wos", 1)):
                P.slot_mark_add(k, *tw)
            for dd in range(KC):
                slot = dd % 2
                for hh in range(2):
                    fv = stAf[:, 2080:2592]
                    srcap = T["wo"].rearrange("(kc p) q -> p kc q", p=128)[
                        :, 4 * hh:4 * hh + 4, dd * 128:(dd + 1) * 128]
                    lv = P.load(fv.rearrange("p (kc q) -> p kc q", kc=4), srcap, "lb0",
                                waits=P.slot_waits("pwf"))
                    cv = P.do("vector", lambda e, fv=fv, slot=slot, hh=hh: e.tensor_copy(
                        stA[:, 1024 * slot + 512 * hh:1024 * slot + 512 * hh + 512], fv),
                        waits=[("lb0", lv)] + P.slot_waits(("wos", slot)), inc="V")
                    P.slot_mark("pwf", "V", cv)
                vws = P.cnt("V")
                lastT = None
                for j in range(NB):
                    bank = ("pA", pA) if (dd * NB + j) % 2 == 0 else ("pB", pB)
                    pairs = [(stA[:, 1024 * slot + 128 * kc:1024 * slot + 128 * kc + 128],
                              big1[:, 4096 * kc + 512 * j:4096 * kc + 512 * j + 512])
                             for kc in range(KC)]
                    vm = P.mm_group(bank[1][:, 0:512], pairs, waits=[("V", vws), ("V", vOT)],
                                    bankname=bank[0])
                    lastT = vm
                    sp = stA[:, 6144 + 512 * (j % 2):6144 + 512 * (j % 2) + 512]
                    osm = "oa%d" % (j % 2)
                    ve = P.do("vector", lambda e, sp=sp, dd=dd, b=bank[1]: e.tensor_scalar_add(
                        sp, b[:, 0:512], bias_o[:, dd:dd + 1]),
                        waits=[("T", vm), (osm, P.cnt(osm)), ("la0", vconsts)], inc="V")
                    P.bank_consumed(bank[0], "V", ve)
                    P.store(T["s_s"][dd * 128:(dd + 1) * 128, j * 512:(j + 1) * 512], sp, osm,
                            waits=[("V", ve)])
                P.slot_mark(("wos", slot), "T", lastT)
            vspill = [("oa0", P.cnt("oa0")), ("oa1", P.cnt("oa1"))]
            if "s0" in dbg:
                P.do("gpsimd", lambda e: e.dma_start(out=T["dbg_s0"], in_=T["s_s"]),
                     waits=vspill, inc="gd", n=16)

        if emit_l1 and upto >= 5:
            sb_done["wqkvT"] = emit_wbuild("wqkv", "wqkvT", 3 * D, D, F32, "oq",
                                           stg=big2, stoff=8192)
        phase += 1
        if phase <= upto:
            vxn = P.cnt("V")
            tw = ("T", P.cnt("T"))
            for k in (("fg", 0), ("fg", 1), ("fu", 0), ("fu", 1), ("fd", 0), ("fd", 1),
                      ("sg", 0), ("sg", 1), ("cfx", 0), ("cfx", 1), ("csl", 0),
                      ("csl", 1), ("cpr", 0), ("cpr", 1)):
                P.slot_mark_add(k, *tw)
            for sv_ in vspill:
                P.slot_mark_add("cfx", sv_[0], sv_[1])
            for grp in range(4):
                j0 = grp * 2
                for hc in range(HCN):
                    slot = hc % 2
                    gsm = "lb%d" % (2 * slot)
                    usm = "lb%d" % (2 * slot + 1)
                    lvg = P.load(stA[:, 1024 * slot:1024 * slot + 1024].rearrange(
                        "p (c q) -> p c q", q=128),
                        T["wgT"][hc].rearrange("(c p) q -> p c q", p=128), gsm,
                        waits=P.slot_waits(("fg", slot)) + list(sb_done.get("wgT", [])))
                    lvu = P.load(stA[:, 2048 + 1024 * slot:2048 + 1024 * slot + 1024].rearrange(
                        "p (c q) -> p c q", q=128),
                        T["wuT"][hc].rearrange("(c p) q -> p c q", p=128), usm,
                        waits=P.slot_waits(("fu", slot)) + list(sb_done.get("wuT", [])))
                    lastg = lastu = None
                    for j2 in range(2):
                        j = j0 + j2
                        gb = ("pA", pA) if j2 == 0 else ("pB", pB)
                        ub = ("pC", pC) if j2 == 0 else ("pD", pD)
                        gp = [(stA[:, 1024 * slot + 128 * kc:1024 * slot + 128 * kc + 128],
                               big0[:, 4096 * kc + 512 * j:4096 * kc + 512 * j + 512])
                              for kc in range(KC)]
                        upp = [(stA[:, 2048 + 1024 * slot + 128 * kc:2048 + 1024 * slot + 128 * kc + 128],
                                big0[:, 4096 * kc + 512 * j:4096 * kc + 512 * j + 512])
                               for kc in range(KC)]
                        vmg = P.mm_group(gb[1][:, 0:512], gp, waits=[(gsm, lvg), ("V", vxn)],
                                         bankname=gb[0])
                        vmu2 = P.mm_group(ub[1][:, 0:512], upp, waits=[(usm, lvu)],
                                          bankname=ub[0])
                        lastg, lastu = vmg, vmu2
                        sgo = [4096, 6656][j2]
                        sg = stA[:, sgo:sgo + 512]
                        vsg = P.do("scalar", lambda e, sg=sg, b=gb[1]: e.activation(
                            sg, b[:, 0:512], AF.Silu, scale=1.0 / 127.0),
                            waits=[("T", vmg)] + P.slot_waits(("sg", j2)) + vspill, inc="S")
                        P.bank_consumed(gb[0], "S", vsg)
                        hdst = big1[:, 1024 * hc + 512 * j2:1024 * hc + 512 * j2 + 512]
                        vh = P.do("vector", lambda e, hdst=hdst, sg=sg, b=ub[1]: e.tensor_mul(
                            hdst, sg, b[:, 0:512]), waits=[("S", vsg), ("T", vmu2)], inc="V")
                        P.bank_consumed(ub[0], "V", vh)
                        P.slot_mark(("sg", j2), "V", vh)
                    P.slot_mark(("fg", slot), "T", lastg)
                    P.slot_mark(("fu", slot), "T", lastu)
                vhall = P.cnt("V")
                for dd in range(KC):
                    dslot = dd % 2
                    do_ = 24576 + 3072 * dslot
                    dsm = "ld%d" % dslot
                    lvd = P.load(big1[:, do_:do_ + 3072].rearrange("p (c q) -> p c q", q=128),
                                 T["wdT"][dd].rearrange("(c p) q -> p c q", p=128), dsm,
                                 waits=P.slot_waits(("fd", dslot)) + list(sb_done.get("wdT", [])))
                    lastd = None
                    for j2 in range(2):
                        bank = ("pE", pE) if j2 == 0 else ("pF", pF)
                        pairs = [(big1[:, do_ + 128 * hcc:do_ + 128 * hcc + 128],
                                  big1[:, 1024 * hcc + 512 * j2:1024 * hcc + 512 * j2 + 512])
                                 for hcc in range(HCN)]
                        vm = P.mm_group(bank[1][:, 0:512], pairs,
                                        waits=[(dsm, lvd), ("V", vhall)], bankname=bank[0])
                        lastd = vm
                        ve = P.do("scalar", lambda e, dd=dd, j2=j2, b=bank[1]: e.activation(
                            big2[:, 1024 * dd + 512 * j2:1024 * dd + 512 * j2 + 512],
                            b[:, 0:512], AF.Silu, scale=1.0 / 16129.0),
                            waits=[("T", vm)] + P.slot_waits(("gch", dd * 2 + j2)), inc="S")
                        P.bank_consumed(bank[0], "S", ve)
                    P.slot_mark(("fd", dslot), "T", lastd)
                vgrp = P.cnt("S")
                if grp == 0 and "gg" in dbg:
                    P.do("gpsimd", lambda e: e.dma_start(out=T["dbg_gg"], in_=big2[:]),
                         waits=[("S", vgrp)], inc="gd", n=16)
                b2f = big2[:].bitcast(F32)
                for j2 in range(2):
                    j = j0 + j2
                    for c in range(KC):
                        sl = c % 2
                        fx = b2f[:, 8192 + 512 * sl:8192 + 512 * sl + 512]
                        fo = b2f[:, 9216 + 512 * sl:9216 + 512 * sl + 512]
                        ssl = big2[:, 20480 + 512 * sl:20480 + 512 * sl + 512]
                        prd = big2[:, 21504 + 512 * sl:21504 + 512 * sl + 512]
                        osm = "ob%d" % sl
                        lvx = P.load(fx, T["x"][c * 128:(c + 1) * 128, j * 512:(j + 1) * 512],
                                     "lc0", waits=P.slot_waits(("cfx", sl)))
                        lvs = P.load(ssl, T["s_s"][c * 128:(c + 1) * 128, j * 512:(j + 1) * 512],
                                     "lc1", waits=vspill + P.slot_waits(("csl", sl)))
                        vp2 = P.do("vector", lambda e, prd=prd, ssl=ssl, c=c, j2=j2: e.tensor_mul(
                            prd, ssl, big2[:, 1024 * c + 512 * j2:1024 * c + 512 * j2 + 512]),
                            waits=[("lc1", lvs), ("S", vgrp)] + P.slot_waits(("cpr", sl)), inc="V")
                        P.slot_mark(("csl", sl), "V", vp2)
                        P.slot_mark_add(("gch", c * 2 + j2), "V", vp2)
                        vo = P.do("vector", lambda e, fo=fo, fx=fx, prd=prd: e.tensor_add(
                            fo, fx, prd),
                            waits=[("lc0", lvx), (osm, P.cnt(osm))], inc="V")
                        P.slot_mark(("cfx", sl), "V", vo)
                        P.slot_mark(("cpr", sl), "V", vo)
                        P.store(T["x1"][c * 128:(c + 1) * 128, j * 512:(j + 1) * 512], fo,
                                osm, waits=[("V", vo)])
            vx1 = [("ob0", P.cnt("ob0")), ("ob1", P.cnt("ob1"))]

    if emit_l1:
        phase += 1
        if phase <= upto and "wqkvT" not in sb_done:
            sb_done["wqkvT"] = emit_wbuild("wqkv", "wqkvT", 3 * D, D, F32, "oq")
        phase += 1
        lnj1 = [0] * NB
        vneg = 0
        if phase <= upto:
            war_q = [(s, P.cnt(s)) for s in ("oq", "oq1") if P.cnt(s)] + [("T", P.cnt("T"))]
            P.slot_mark_add("wba", "V", P.cnt("V"))
            for idx, (an, bn) in enumerate((("aR", "bR"), ("aI", "bI"))):
                for ch in range(4):
                    for c in range(4):
                        fa = stAf[:, 0:512]
                        fb = stAf[:, 512:1024]
                        acc = stAf[:, 1024:1536]
                        la = P.load(fa, T[an][c][ch * 128:(ch + 1) * 128, :], "lb0",
                                    waits=P.slot_waits("wba") + war_q[-1:])
                        lb = P.load(fb, T[bn][c][ch * 128:(ch + 1) * 128, :], "lb1",
                                    waits=P.slot_waits("wba"))
                        if c == 0:
                            vmu0 = P.do("vector", lambda e, fa=fa, fb=fb, acc=acc: e.tensor_mul(
                                acc, fa, fb), waits=[("lb0", la), ("lb1", lb)], inc="V")
                        else:
                            vt0 = P.do("vector", lambda e, fa=fa, fb=fb: e.tensor_mul(
                                stAf[:, 1536:2048], fa, fb),
                                waits=[("lb0", la), ("lb1", lb)], inc="V")
                            vmu0 = P.do("vector", lambda e, acc=acc: e.tensor_add(
                                acc, acc, stAf[:, 1536:2048]), waits=[("V", vt0)], inc="V")
                        P.slot_mark("wba", "V", vmu0)
                    vcp = P.do("vector", lambda e, idx=idx, ch=ch, acc=acc: e.tensor_copy(
                        big1[:, 2048 * idx + 512 * ch:2048 * idx + 512 * ch + 512], acc),
                        waits=war_q + [("V", vmu0)], inc="V")
                    P.slot_mark("wba", "V", vcp)
            vwt = P.cnt("V")
            vneg = P.do("vector", lambda e: e.tensor_scalar_mul(
                big1[:, 4096:6144], big1[:, 2048:4096], -1.0), waits=[("V", vwt)] + war_q,
                inc="V")
            if "WT" in dbg:
                P.do("gpsimd", lambda e: e.dma_start(out=T["dbg_WT"], in_=big1[:, 0:6144]),
                     waits=[("V", vneg)], inc="gd", n=16)
            lnj1 = emit_ln("x1", wait_src=vx1)
            if "xn1" in dbg:
                P.do("gpsimd", lambda e: e.dma_start(out=T["dbg_xn1"], in_=big0[:]),
                     waits=[("V", P.cnt("V"))], inc="gd", n=16)
        phase += 1
        vgs = []
        if phase <= upto:
            vxnl = P.cnt("V")
            for j in range(NB):
                for half in range(2):
                    for oc in range(4):
                        bank = ("pA", pA) if oc % 2 == 0 else ("pB", pB)
                        pairs = []
                        for ic in range(4):
                            wofs = 0 if half == 0 else 2048
                            pairs.append((big1[:, wofs + 512 * ic + 128 * oc:wofs + 512 * ic + 128 * oc + 128],
                                          big0[:, 4096 * ic + 512 * j:4096 * ic + 512 * j + 512]))
                        for ic in range(4):
                            wofs = 4096 if half == 0 else 0
                            pairs.append((big1[:, wofs + 512 * ic + 128 * oc:wofs + 512 * ic + 128 * oc + 128],
                                          big0[:, 4096 * (4 + ic) + 512 * j:4096 * (4 + ic) + 512 * j + 512]))
                        vm = P.mm_group(bank[1][:, 0:512], pairs,
                                        waits=[("V", vneg), ("V", lnj1[j] or vxnl)],
                                        bankname=bank[0])
                        gsl = big2[:, 512 * (oc % 2):512 * (oc % 2) + 512]
                        osm = "oa%d" % (oc % 2)
                        bias = bias_r if half == 0 else bias_i
                        ve = P.do("scalar", lambda e, gsl=gsl, oc=oc, b=bank[1], bias=bias: e.activation(
                            gsl, b[:, 0:512], AF.Silu, bias=bias[:, oc:oc + 1]),
                            waits=[("T", vm), (osm, P.cnt(osm)), ("la0", vconsts)], inc="S")
                        P.bank_consumed(bank[0], "S", ve)
                        P.store(T["g_s"][512 * half + 128 * oc:512 * half + 128 * oc + 128,
                                         j * 512:(j + 1) * 512], gsl, osm,
                                waits=[("S", ve)])
            vgs = [("oa0", P.cnt("oa0")), ("oa1", P.cnt("oa1"))]
            if "g1" in dbg:
                P.do("gpsimd", lambda e: e.dma_start(out=T["dbg_g1"], in_=T["g_s"]),
                     waits=vgs, inc="gd", n=16)

        phase += 1
        if phase <= upto:
            vxnl = P.cnt("V")
            tw = ("T", P.cnt("T"))
            for k in (("l1k", 0), ("l1k", 1), "l1v"):
                P.slot_mark_add(k, *tw)
                P.slot_mark_add(k, "V", vxnl)
            for fc in range(KC):
                slot = fc % 2
                ksm = "lb%d" % slot
                lvk = P.load(stA[:, 1024 * slot:1024 * slot + 1024].rearrange(
                    "p (c q) -> p c q", q=128),
                    T["wqkvT"][8 + fc].rearrange("(c p) q -> p c q", p=128), ksm,
                    waits=P.slot_waits(("l1k", slot)) + list(sb_done.get("wqkvT", [])))
                lastm = None
                for j in range(NB):
                    bank = ("pA", pA) if (fc * NB + j) % 2 == 0 else ("pB", pB)
                    pairs = [(stA[:, 1024 * slot + 128 * kc:1024 * slot + 128 * kc + 128],
                              big0[:, 4096 * kc + 512 * j:4096 * kc + 512 * j + 512])
                             for kc in range(KC)]
                    vm = P.mm_group(bank[1][:, 0:512], pairs,
                                    waits=[(ksm, lvk), ("V", lnj1[j] or vxnl)], bankname=bank[0])
                    lastm = vm
                    ve = P.do("vector", lambda e, fc=fc, j=j, b=bank[1]: e.tensor_copy(
                        big1[:, 4096 * fc + 512 * j:4096 * fc + 512 * j + 512], b[:, 0:512]),
                        waits=[("T", vm)], inc="V")
                    P.bank_consumed(bank[0], "V", ve)
                P.slot_mark(("l1k", slot), "T", lastm)
            for fg in range(2):
                for kc in range(KC):
                    P.load(stA[:, 2048 + 512 * kc:2048 + 512 * kc + 512].rearrange(
                        "p (f q) -> p f q", q=128),
                        T["wqkvT"].rearrange("f (kc p) q -> p f kc q", p=128)[
                            :, 16 + 4 * fg:16 + 4 * fg + 4, kc, :],
                        "lb0", waits=P.slot_waits("l1v") + vgs +
                        list(sb_done.get("wqkvT", [])))
                lvv = P.cnt("lb0")
                lastm = None
                for i in range(NT):
                    bank = ("pC", pC) if i % 2 == 0 else ("pD", pD)
                    pairs = [(big0[:, 4096 * kc + 128 * i:4096 * kc + 128 * i + 128],
                              stA[:, 2048 + 512 * kc:2048 + 512 * kc + 512])
                             for kc in range(KC)]
                    vm = P.mm_group(bank[1][:, 0:512], pairs,
                                    waits=[("lb0", lvv), ("V", lnj1[i // 4] or vxnl)],
                                    bankname=bank[0])
                    lastm = vm
                    ve = P.do("vector", lambda e, i=i, fg=fg, b=bank[1]: e.tensor_copy(
                        big2[:, 1024 * i + 512 * fg:1024 * i + 512 * fg + 512], b[:, 0:512]),
                        waits=[("T", vm)], inc="V")
                    P.bank_consumed(bank[0], "V", ve)
                P.slot_mark("l1v", "T", lastm)
            BANKS = ["pA", "pB", "pC", "pD", "pE", "pF", "pR", "pT"]
            for j in range(NB):
                evs = []
                for fc in range(KC):
                    slot = fc % 2
                    lvq = P.load(stA[:, 1024 * slot:1024 * slot + 1024].rearrange(
                        "p (c q) -> p c q", q=128),
                        T["wqkvT"][fc].rearrange("(c p) q -> p c q", p=128),
                        "lb%d" % slot, waits=P.slot_waits(("l1k", slot)))
                    bank = BANKS[fc]
                    pairs = [(stA[:, 1024 * slot + 128 * kc:1024 * slot + 128 * kc + 128],
                              big0[:, 4096 * kc + 512 * j:4096 * kc + 512 * j + 512])
                             for kc in range(KC)]
                    vm = P.mm_group(PSUM[bank][:, 0:512], pairs,
                                    waits=[("lb%d" % slot, lvq), ("V", lnj1[j] or vxnl)],
                                    bankname=bank)
                    P.slot_mark(("l1k", slot), "T", vm)
                    evs.append((bank, vm))
                vall = P.cnt("T")
                for fc, (bank, vm) in enumerate(evs):
                    ve = P.do("vector", lambda e, fc=fc, j=j, bank=bank: e.tensor_copy(
                        big0[:, 4096 * fc + 512 * j:4096 * fc + 512 * j + 512],
                        PSUM[bank][:, 0:512]), waits=[("T", vall)], inc="V")
                    P.bank_consumed(bank, "V", ve)
            vqkv = P.cnt("V")
            if "kq1" in dbg:
                P.do("gpsimd", lambda e: e.dma_start(out=T["dbg_k1"], in_=big1[:]),
                     waits=[("V", vqkv)], inc="gd", n=16)
                P.do("gpsimd", lambda e: e.dma_start(out=T["dbg_q1"], in_=big0[:]),
                     waits=[("V", vqkv)], inc="gd", n=16)
            if "v1" in dbg:
                P.do("gpsimd", lambda e: e.dma_start(out=T["dbg_v1"], in_=big2[:]),
                     waits=[("V", vqkv)], inc="gd", n=16)

        phase += 1
        if phase <= upto:
            vqkv = P.cnt("V")

            def norm_max_l1(buf, vready, outcol):
                for j in range(NB):
                    for kc in range(KC):
                        wR = (P.bank_waits("pR") if kc == 0 else [])
                        vs = P.do("vector", lambda e, kc=kc, j=j, buf=buf: e.tensor_mul(
                            stA[:, 4096:4608],
                            buf[:, 4096 * kc + 512 * j:4096 * kc + 512 * j + 512],
                            buf[:, 4096 * kc + 512 * j:4096 * kc + 512 * j + 512]),
                            waits=wR + [("V", vready), ("T", P.cnt("T"))] + vgs, inc="V")
                        P.do("tensor", lambda e, kc=kc: e.matmul(
                            pR[0:1, 0:512], ones_c[:], stA[:, 4096:4608],
                            start=(kc == 0), stop=(kc == KC - 1)), waits=[("V", vs)], inc="T")
                    vt = P.cnt("T")
                    vv = P.do("vector", lambda e, j=j: e.tensor_reduce(
                        qmr[0:1, 8 + j:9 + j], pR[0:1, 0:512], AX.X, ALU.max),
                        waits=[("T", vt)], inc="V")
                    P.bank_consumed("pR", "V", vv)
                    P._wait("scalar", "V", vv)
                return P.do("vector", lambda e, outcol=outcol: e.tensor_reduce(
                    sc1[0:1, outcol:outcol + 1], qmr[0:1, 8:8 + NB], AX.X, ALU.max),
                    waits=[("V", P.cnt("V"))], inc="V")

            vqn = norm_max_l1(big0, vqkv, 4)
            vkn = norm_max_l1(big1, vqkv, 5)
            vpp = P.do("vector", lambda e: e.tensor_mul(sc1[0:1, 6:7], sc1[0:1, 4:5],
                                                        sc1[0:1, 5:6]),
                       waits=[("V", vqn), ("V", vkn)], inc="V")
            vsq = P.do("scalar", lambda e: e.activation(sc1[0:1, 7:8], sc1[0:1, 6:7],
                                                        AF.Sqrt), waits=[("V", vpp)], inc="S")
            wT2 = P.bank_waits("pT") + [("S", vsq)]
            vbc = P.do("tensor", lambda e: e.matmul(pT[:, 0:1], ones_rf[:], sc1[0:1, 7:8],
                                                    start=True, stop=True), waits=wT2, inc="T")
            vnc1 = P.do("scalar", lambda e: e.activation(negc[:, 1:2], pT[:, 0:1], AF.Identity,
                                                         scale=-(1.0 / 32.0) * 1.001),
                        waits=[("T", vbc)], inc="S")
            P.bank_consumed("pT", "S", vnc1)

            PVB = ["pC", "pD", "pE", "pF"]
            pv_war1 = [0, 0]
            qk_pend = {}

            def emit_qk(qb, kt):
                sb = ("pA", pA) if kt % 2 == 0 else ("pB", pB)
                pairs = [(big1[:, 4096 * kc + 128 * kt:4096 * kc + 128 * kt + 128],
                          big0[:, 4096 * kc + 256 * qb:4096 * kc + 256 * qb + 256])
                         for kc in range(KC)]
                vqk = P.mm_group(sb[1][:, 0:256], pairs, waits=[("S", vnc1), ("V", vqkv)],
                                 bankname=sb[0])
                qk_pend[(qb, kt)] = (sb, vqk)

            for qb in range(16):
                if qb == 0:
                    emit_qk(0, 0)
                pv_done = None
                for kt in range(NT):
                    sb, vqk = qk_pend.pop((qb, kt))
                    if kt + 1 < NT:
                        emit_qk(qb, kt + 1)
                    elif qb + 1 < 16:
                        emit_qk(qb + 1, 0)
                    ptile = stA[:, 4608 + 256 * (kt % 2):4608 + 256 * (kt % 2) + 256]
                    vex = P.do("scalar", lambda e, ptile=ptile, b=sb[1]: e.activation(
                        ptile, b[:, 0:256], AF.Exp, bias=negc[:, 1:2], scale=1.0 / 32.0),
                        waits=[("T", vqk), ("T", pv_war1[kt % 2])], inc="S")
                    P.bank_consumed(sb[0], "S", vex)
                    wz = [("S", vex)]
                    if kt == 0:
                        wz += P.bank_waits("pR")
                    P.do("tensor", lambda e, ptile=ptile, st=(kt == 0), sp=(kt == NT - 1): e.matmul(
                        pR[0:1, 0:256], ones_c[:], ptile, start=st, stop=sp), waits=wz, inc="T")
                    lastpv = None
                    for fc in range(KC):
                        bn = PVB[fc // 2]
                        col0 = 256 * (fc % 2)
                        wpv = []
                        if kt == 0 and fc % 2 == 0:
                            wpv += P.bank_waits(bn)
                        lvv = big2[:, 1024 * kt + 128 * fc:1024 * kt + 128 * fc + 128]
                        lastpv = P.do("tensor", lambda e, lvv=lvv, ptile=ptile, bn=bn, col0=col0,
                                      st=(kt == 0 and fc % 2 == 0),
                                      sp=(kt == NT - 1 and fc % 2 == 1): e.matmul(
                            PSUM[bn][:, col0:col0 + 256], lvv, ptile, start=st, stop=sp),
                            waits=wpv, inc="T" if fc == KC - 1 else None)
                    pv_war1[kt % 2] = lastpv
                    pv_done = lastpv
                rrow1 = stAf[0:1, 2816:3072]
                vln = P.do("scalar", lambda e: e.activation(
                    pR[0:1, 0:256], pR[0:1, 0:256], AF.Ln,
                    scale=float(2.0 ** 55)),
                    waits=[("T", P.cnt("T")), ("S", P.cnt("S")), ("V", vconstsV)], inc="S")
                vrc = P.do("scalar", lambda e: e.activation(
                    rrow1, pR[0:1, 0:256], AF.Exp, scale=-1.0,
                    bias=sc1[0:1, 14:15]),
                    waits=[("S", vln)], inc="S")
                P.bank_consumed("pR", "S", vrc)
                wbc = P.bank_waits("pT") + [("S", vrc)]
                vbc2 = P.do("tensor", lambda e: e.matmul(
                    pT[:, 0:256], ones_rf[:], rrow1, start=True, stop=True), waits=wbc, inc="T")
                for fc in range(KC):
                    bn = PVB[fc // 2]
                    col0 = 256 * (fc % 2)
                    ot = stA[:, 6144 + 256 * (fc % 2):6144 + 256 * (fc % 2) + 256]
                    voe = P.do("scalar", lambda e, ot=ot, bn=bn, col0=col0: e.copy(
                        ot, PSUM[bn][:, col0:col0 + 256]),
                        waits=[("T", pv_done), ("V", P.cnt("V"))], inc="S")
                    if fc % 2 == 1:
                        P.bank_consumed(bn, "S", voe)
                    dst = big0[:, 4096 * fc + 256 * qb:4096 * fc + 256 * qb + 256]
                    vml = P.do("vector", lambda e, dst=dst, ot=ot: e.tensor_mul(
                        dst, ot, pT[:, 0:256]), waits=[("S", voe), ("T", vbc2)], inc="V")
                P.bank_consumed("pT", "V", P.cnt("V"))

        phase += 1
        if phase <= upto:
            vs1 = P.cnt("V")
            b2f = big2[:].bitcast(F32)
            for j in range(NB):
                for c in range(KC):
                    sl = c % 2
                    fx = b2f[:, 8192 + 512 * sl:8192 + 512 * sl + 512]
                    fo = b2f[:, 9216 + 512 * sl:9216 + 512 * sl + 512]
                    gsl = big2[:, 20480 + 512 * sl:20480 + 512 * sl + 512]
                    prd = big2[:, 21504 + 512 * sl:21504 + 512 * sl + 512]
                    osm = "ob%d" % sl
                    lvx = P.load(fx, T["x1"][c * 128:(c + 1) * 128, j * 512:(j + 1) * 512],
                                 "lc0", waits=P.slot_waits(("cfx", sl)) + vx1 + [("V", vs1),
                                 ("T", P.cnt("T")), ("S", P.cnt("S"))])
                    lvg = P.load(gsl, T["g_s"][c * 128:(c + 1) * 128, j * 512:(j + 1) * 512],
                                 "lc1", waits=vgs + P.slot_waits(("csl", sl)) + [("V", vs1)])
                    vp2 = P.do("vector", lambda e, prd=prd, gsl=gsl, c=c, j=j: e.tensor_mul(
                        prd, gsl, big0[:, 4096 * c + 512 * j:4096 * c + 512 * j + 512]),
                        waits=[("lc1", lvg), ("V", vs1), ("S", P.cnt("S"))] + P.slot_waits(("cpr", sl)),
                        inc="V")
                    P.slot_mark(("csl", sl), "V", vp2)
                    vo = P.do("vector", lambda e, fo=fo, fx=fx, prd=prd: e.tensor_add(
                        fo, fx, prd),
                        waits=[("lc0", lvx), (osm, P.cnt(osm))], inc="V")
                    P.slot_mark(("cfx", sl), "V", vo)
                    P.slot_mark(("cpr", sl), "V", vo)
                    P.store(T["out"][c * 128:(c + 1) * 128, j * 512:(j + 1) * 512], fo,
                            osm, waits=[("V", vo)])

    if "x1d" in dbg and which == "full":
        P.do("gpsimd", lambda e: e.dma_start(out=T["dbg_x1d"], in_=T["x1"]),
             waits=vx1, inc="gd", n=16)
    final_waits = []
    for s in ("oa0", "oa1", "ob0", "ob1", "og", "ou", "od", "oq",
              "og1", "ou1", "od1", "oq1", "gd"):
        if P.cnt(s):
            final_waits.append((s, P.cnt(s)))
    for s, v in final_waits:
        P._wait("gpsimd", s, v)

    P.check()
    _maxsem = max(P.cnt(s) for s in P.sems)
    assert _maxsem < 60000, f"semaphore count overflow risk: {_maxsem}"

    nums = sorted(h.num for h, _ in P.sems.values())
    assert nums[-1] - nums[0] + 1 == len(nums), "sems not contiguous"
    nc.gpsimd.dma_reset(range(nums[0], nums[-1] + 1))
    nc.gpsimd.sem_clear(range(nums[0], nums[-1] + 1))
    nc.all_engine_barrier()

    with nc.Block() as block:
        @block.sync
        def _(e):
            for f in P.ops["sync"]:
                f(e)

        @block.tensor
        def _(e):
            for f in P.ops["tensor"]:
                f(e)

        @block.vector
        def _(e):
            for f in P.ops["vector"]:
                f(e)

        @block.scalar
        def _(e):
            for f in P.ops["scalar"]:
                f(e)

        @block.gpsimd
        def _(e):
            for f in P.ops["gpsimd"]:
                f(e)

    stack.close()
    return nc


def prep_inputs(inp, core):
    d = {}
    d["x"] = np.ascontiguousarray(inp["L0_attn_x"][core].T)
    d["wq"] = inp["L0_attn_wq"]
    d["wk"] = inp["L0_attn_wk"]
    d["wo"] = inp["L0_attn_wo"]
    wv = inp["L0_attn_wv"]
    wvp = np.zeros((D, H * (HD + 1)), np.float32)
    for h in range(H):
        wvp[:, h * 65:h * 65 + 64] = wv[:, h * 64:(h + 1) * 64]
    d["wv"] = wvp
    for nm, key in (("bq", "L0_attn_bq"), ("bk", "L0_attn_bk"), ("bo", "L0_attn_bo")):
        d[nm] = np.ascontiguousarray(inp[key].reshape(KC, 128).T)
    bv = inp["L0_attn_bv"]
    bvp = np.zeros((1, H * (HD + 1)), np.float32)
    for h in range(H):
        bvp[0, h * 65:h * 65 + 64] = bv[h * 64:(h + 1) * 64]
        bvp[0, h * 65 + 64] = 1.0
    d["bv"] = bvp
    d["wg"] = inp["L0_ffn_w_gate"]
    d["wu"] = inp["L0_ffn_w_up"]
    d["wd"] = inp["L0_ffn_w_down"]
    k = np.arange(128)[:, None]
    q = np.arange(512)[None, :]
    masks = np.zeros((4, 128, 512), np.float32)
    for dk in range(4):
        masks[dk] = (dk * 128 + k <= q).astype(np.float32)
    d["masks"] = masks
    d["ident"] = np.eye(128, dtype=np.float32)
    d["wqkv"] = inp["L1_attn_W_qkv"]

    def _arep(G1):
        A = np.transpose(G1[0], (1, 2, 0))
        A = np.broadcast_to(A[:, :, None, :, None], (4, 32, 16, 32, 16))
        return np.ascontiguousarray(A.reshape(4, 512, 512))

    def _brep(G2):
        B = np.transpose(G2[:, :, 0, :], (0, 2, 1))
        B = np.broadcast_to(B[:, None, :, None, :], (4, 32, 16, 32, 16))
        return np.ascontiguousarray(B.reshape(4, 512, 512))

    d["aR"] = _arep(inp["L1_ffn_G1_R"])
    d["aI"] = _arep(inp["L1_ffn_G1_I"])
    d["bR"] = _brep(inp["L1_ffn_G2_R"])
    d["bI"] = _brep(inp["L1_ffn_G2_I"])
    d["br"] = np.ascontiguousarray(inp["L1_ffn_bias_real"].reshape(4, 128).T)
    d["bi"] = np.ascontiguousarray(inp["L1_ffn_bias_imag"].reshape(4, 128).T)
    return d


def split_inputs(d, which="full"):
    l0 = {"x", "wq", "wk", "wv", "wo", "bq", "bk", "bv", "bo", "wg", "wu", "wd",
          "masks", "ident"}
    l1 = {"wqkv", "aR", "bR", "aI", "bI", "br", "bi", "ident"}
    if which == "l0":
        return {k: v for k, v in d.items() if k in l0}
    if which == "l1":
        return {k: v for k, v in d.items() if k in l1}
    return d


_NC_CACHE = [None]


def kernel(**inputs):
    import numpy as np
    from concourse.bass_utils import run_bass_kernel_spmd

    if _NC_CACHE[0] is None:
        _NC_CACHE[0] = build(which="full")
    nc = _NC_CACHE[0]
    in_maps = []
    for core in range(8):
        d = prep_inputs(inputs, core)
        in_maps.append(split_inputs(d, "full"))
    res = run_bass_kernel_spmd(nc, in_maps, core_ids=list(range(8)))
    outs = [np.asarray(res.results[c]["out"], dtype=np.float32).T for c in range(8)]
    return np.stack(outs, axis=0)


# version 19 (final)
# speedup vs baseline: 1.2334x; 1.0221x over previous
import numpy as np
from contextlib import ExitStack

import concourse.bass as bass
import concourse.mybir as mybir

F32 = mybir.dt.float32
BF16 = mybir.dt.bfloat16
I32 = mybir.dt.int32
AF = mybir.ActivationFunctionType
ALU = mybir.AluOpType
AX = mybir.AxisListType

N = 4096
D = 1024
KC = 8
NB = 8
NT = 32
H, HD = 16, 64
G, HG = 4, 4
HID, HCN = 3072, 24
EPS = 1e-5
VW = HG * (HD + 1)
SUBPH = 99
SELF_FENCE = True
GLIM = 99

ENGINES = ("sync", "tensor", "vector", "scalar", "gpsimd")


class Prog:
    def __init__(self, nc, stack):
        self.nc = nc
        self.stack = stack
        self.ops = {e: [] for e in ENGINES}
        self.sems = {}
        self.waited = {e: {} for e in ENGINES}
        self.bank = {}
        self.slot = {}
        self.semgate = {}

    def sem(self, name):
        if name not in self.sems:
            h = self.stack.enter_context(self.nc.semaphore(name))
            self.sems[name] = [h, 0]
        return name

    def cnt(self, name):
        return self.sems[name][1]

    def _wait(self, eng, s, v):
        if v <= 0:
            return
        w = self.waited[eng]
        if w.get(s, 0) >= v:
            return
        w[s] = v
        h = self.sems[s][0]
        self.ops[eng].append(lambda e, h=h, v=v: e.wait_ge(h, v))

    def do(self, eng, fn, waits=(), inc=None, n=1):
        if SELF_FENCE:
            if eng == "vector":
                self._wait(eng, "V", self.cnt("V"))
            elif eng == "scalar":
                self._wait(eng, "S", self.cnt("S"))
        for (s, v) in waits:
            self._wait(eng, s, v)
        if inc is not None:
            self.sem(inc)
            self.sems[inc][1] += n
            h = self.sems[inc][0]
            self.ops[eng].append(lambda e, fn=fn, h=h, n=n: fn(e).then_inc(h, n))
            if inc in ("T", "V", "S"):
                for (s, v) in waits:
                    if s.startswith("l"):
                        self.semgate[s] = (inc, self.sems[inc][1])
            return self.sems[inc][1]
        self.ops[eng].append(lambda e, fn=fn: fn(e))
        return None

    def load(self, dst, src, sem, waits=(), eng="sync", **kw):
        g = self.semgate.get(sem)
        allw = list(waits) + ([g] if g else [])
        return self.do(eng, lambda e, dst=dst, src=src, kw=kw: e.dma_start(out=dst, in_=src, **kw),
                       waits=allw, inc=sem, n=16)

    def store(self, dst, src, sem, waits=(), eng="scalar", **kw):
        return self.do(eng, lambda e, dst=dst, src=src, kw=kw: e.dma_start(out=dst, in_=src, **kw),
                       waits=waits, inc=sem, n=16)

    def bank_waits(self, name):
        return list(self.bank.get(name, ()))

    def bank_consumed(self, name, sem, val):
        self.bank[name] = [(sem, val)]

    def bank_consumed_add(self, name, sem, val):
        self.bank.setdefault(name, []).append((sem, val))

    def slot_waits(self, key):
        return list(self.slot.get(key, ()))

    def slot_mark(self, key, sem, val):
        self.slot[key] = [(sem, val)]

    def slot_mark_add(self, key, sem, val):
        self.slot.setdefault(key, []).append((sem, val))

    def mm_group(self, out_ap, pairs, waits=(), bankname=None, first_start=True,
                 last_stop=True, inc="T"):
        n = len(pairs)
        allw = list(waits)
        if bankname is not None and first_start:
            allw += self.bank_waits(bankname)
        r = None
        for i, (l, rr) in enumerate(pairs):
            st = first_start and (i == 0)
            sp = last_stop and (i == n - 1)
            r = self.do("tensor",
                        lambda e, l=l, rr=rr, st=st, sp=sp, o=out_ap: e.matmul(o, l, rr, start=st, stop=sp),
                        waits=allw if i == 0 else (),
                        inc=inc if (i == n - 1 and inc) else None)
        if r is not None:
            for (s, v) in allw:
                if s.startswith("l"):
                    self.semgate[s] = ("T", r)
        return r

    def check(self):
        for e in ENGINES:
            for s, v in self.waited[e].items():
                assert v <= self.sems[s][1], f"wait {s}>={v} never reached (max {self.sems[s][1]})"


def _recip_bf16(e, out, in_):
    with e.bass.allow_low_precision(reason="softmax 1/Z feeds a bf16 matmul operand"):
        return e.reciprocal(out, in_)


def build(which="full", dbg=(), upto=99):
    nc = bass.Bass("TRN2", target_bir_lowering=False)
    stack = ExitStack()
    T = {}

    def dram(name, shape, dt, kind):
        T[name] = nc.dram_tensor(name, list(shape), dt, kind=kind).ap()
        return T[name]

    emit_l0 = which in ("l0", "full")
    emit_l1 = which in ("l1", "full")

    if emit_l0:
        dram("x", (D, N), F32, "ExternalInput")
        for nm in ("wq", "wk", "wo"):
            dram(nm, (D, D), F32, "ExternalInput")
        dram("wv", (D, H * (HD + 1)), F32, "ExternalInput")
        for nm in ("bq", "bk", "bo"):
            dram(nm, (128, KC), F32, "ExternalInput")
        dram("bv", (1, H * (HD + 1)), F32, "ExternalInput")
        dram("wg", (HID, D), I32, "ExternalInput")
        dram("wu", (HID, D), I32, "ExternalInput")
        dram("wd", (D, HID), I32, "ExternalInput")
        dram("masks", (4, 128, 512), F32, "ExternalInput")
        dram("wgT", (HCN, D, 128), BF16, "Internal")
        dram("wuT", (HCN, D, 128), BF16, "Internal")
        dram("wdT", (KC, HID, 128), BF16, "Internal")
        dram("s_s", (D, N), BF16, "Internal")
    if emit_l1:
        dram("wqkv", (3 * D, D), F32, "ExternalInput")
        dram("aR", (4, 512, 512), F32, "ExternalInput")
        dram("bR", (4, 512, 512), F32, "ExternalInput")
        dram("aI", (4, 512, 512), F32, "ExternalInput")
        dram("bI", (4, 512, 512), F32, "ExternalInput")
        dram("br", (128, 4), F32, "ExternalInput")
        dram("bi", (128, 4), F32, "ExternalInput")
        dram("wqkvT", (3 * D // 128, D, 128), BF16, "Internal")
        dram("g_s", (D, N), BF16, "Internal")
    dram("ident", (128, 128), F32, "ExternalInput")
    if which == "full":
        dram("x1", (D, N), F32, "Internal")
        dram("out", (D, N), F32, "ExternalOutput")
    elif which == "l0":
        dram("x1", (D, N), F32, "ExternalOutput")
    else:
        dram("x1", (D, N), F32, "ExternalInput")
        dram("out", (D, N), F32, "ExternalOutput")
    for d in dbg:
        if d in ("xn0", "xn1", "kvq0", "att", "kq1", "v1", "gg"):
            names = {"kq1": ("dbg_k1", "dbg_q1")}.get(d, ("dbg_" + d,))
            for nm in names:
                dram(nm, (128, 32768), BF16, "ExternalOutput")
        elif d in ("s0", "g1"):
            dram("dbg_" + d, (D, N), BF16, "ExternalOutput")
        elif d == "WT":
            dram("dbg_WT", (128, 6144), BF16, "ExternalOutput")
        elif d == "x1d":
            dram("dbg_x1d", (D, N), F32, "ExternalOutput")

    big0 = stack.enter_context(nc.sbuf_tensor("big0", [128, 32768], BF16))
    big1 = stack.enter_context(nc.sbuf_tensor("big1", [128, 32768], BF16))
    big2 = stack.enter_context(nc.sbuf_tensor("big2", [128, 32768], BF16))
    stA = stack.enter_context(nc.sbuf_tensor("stA", [128, 7168], BF16))
    ident_f = stack.enter_context(nc.sbuf_tensor("ident_f", [128, 128], F32))
    ident_b = stack.enter_context(nc.sbuf_tensor("ident_b", [128, 128], BF16))
    ones_c = stack.enter_context(nc.sbuf_tensor("ones_c", [128, 1], BF16))
    ones_r = stack.enter_context(nc.sbuf_tensor("ones_r", [1, 128], BF16))
    ones_rf = stack.enter_context(nc.sbuf_tensor("ones_rf", [1, 128], F32))
    bias_q = stack.enter_context(nc.sbuf_tensor("bias_q", [128, KC], F32))
    bias_k = stack.enter_context(nc.sbuf_tensor("bias_k", [128, KC], F32))
    bias_o = stack.enter_context(nc.sbuf_tensor("bias_o", [128, KC], F32))
    bias_r = stack.enter_context(nc.sbuf_tensor("bias_r", [128, 4], F32))
    bias_i = stack.enter_context(nc.sbuf_tensor("bias_i", [128, 4], F32))
    negc = stack.enter_context(nc.sbuf_tensor("negc", [128, 2], F32))
    sc1 = stack.enter_context(nc.sbuf_tensor("sc1", [1, 16], F32))
    qmr = stack.enter_context(nc.sbuf_tensor("qmr", [1, 16], F32))

    pS0 = stack.enter_context(nc.psum_tensor("pS0", [128, 1024], F32))
    pS1 = stack.enter_context(nc.psum_tensor("pS1", [128, 1024], F32))
    pS2 = stack.enter_context(nc.psum_tensor("pS2", [128, 1024], F32))
    pCh = stack.enter_context(nc.psum_tensor("pC", [128, 512], F32))
    pDh = stack.enter_context(nc.psum_tensor("pD", [128, 512], F32))
    pS0a, pS1a, pS2a = pS0[:], pS1[:], pS2[:]
    pA = pS0a[:, 0:512]
    pB = pS0a[:, 512:1024]
    pR = pS1a[:, 0:512]
    pT = pS1a[:, 512:1024]
    pE = pS2a[:, 0:512]
    pF = pS2a[:, 512:1024]
    pC, pD = pCh[:], pDh[:]
    PSUM = {"pA": pA, "pB": pB, "pC": pC, "pD": pD, "pE": pE, "pF": pF, "pR": pR, "pT": pT}

    P = Prog(nc, stack)
    for s in ("T", "V", "S", "la0", "la1", "lb0", "lb1", "lb2", "lb3",
              "lc0", "lc1", "ld0", "ld1", "oa0", "oa1", "ob0", "ob1",
              "og", "ou", "od", "oq", "og1", "ou1", "od1", "oq1", "gd"):
        P.sem(s)

    stAf = stA[:].bitcast(F32)
    big1f = big1[:].bitcast(F32)
    big1i = big1[:].bitcast(I32)

    if emit_l0:
        P.load(bias_q[:], T["bq"], "la0")
        P.load(bias_k[:], T["bk"], "la0")
        P.load(bias_o[:], T["bo"], "la0")
    if emit_l1:
        P.load(bias_r[:], T["br"], "la0")
        P.load(bias_i[:], T["bi"], "la0")
    P.load(ident_f[:], T["ident"], "la0")
    vld = P.cnt("la0")
    P.do("vector", lambda e: e.tensor_copy(ident_b[:], ident_f[:]), waits=[("la0", vld)], inc="V")
    P.do("vector", lambda e: e.memset(ones_c[:], 1.0), inc="V")
    P.do("vector", lambda e: e.memset(ones_r[:], 1.0), inc="V")
    P.do("vector", lambda e: e.memset(ones_rf[:], 1.0), inc="V")
    P.do("vector", lambda e: e.memset(sc1[0:1, 15:16], EPS), inc="V")
    P.do("vector", lambda e: e.memset(sc1[0:1, 14:15], float(np.log(2.0 ** 55))), inc="V")
    if emit_l0:
        for i in range(4):
            fv = stAf[:, 512 * (i % 2):512 * (i % 2) + 512]
            P.load(fv, T["masks"][i], "la1", waits=[("V", P.cnt("V"))])
            v = P.cnt("la1")
            P.do("vector", lambda e, i=i, fv=fv: e.tensor_copy(
                big2[:, 24704 + 512 * i:24704 + 512 * i + 512], fv),
                waits=[("la1", v)], inc="V")
        _b2f = big2[:].bitcast(F32)
        sel2f = _b2f[0:65, 16192:16320]
        P.do("vector", lambda e: e.memset(sel2f, 0.0), inc="V")
        P.do("vector", lambda e: e.memset(sel2f[0:1, 0:64], 1.0), inc="V")
        P.do("vector", lambda e: e.memset(sel2f[64:65, 64:128], 1.0), inc="V")
        P.do("vector", lambda e: e.memset(_b2f[0:65, 15680:16192], 0.0), inc="V")
    vconsts = P.cnt("la0")
    vconstsV = P.cnt("V")

    phase = 0

    def emit_wbuild(src_name, dst_name, R, C, src_dt, osem, stg=None, stoff=0):
        stg = big1 if stg is None else stg
        stg_i = stg[:].bitcast(I32)
        stg_f = stg[:].bitcast(F32)
        src, dst = T[src_name], T[dst_name]
        tw = ("T", P.cnt("T"))
        for k in (("braw", 0), ("braw", 1), ("bconv", 0), ("bconv", 1),
                  ("bev", 0), ("bev", 1)):
            P.slot_mark_add(k, *tw)
        it = 0
        for r in range(R // 128):
            for cc in range(C // 1024):
                slot = it % 2
                it += 1
                raw = (stg_i if src_dt == I32 else stg_f)[
                    :, stoff // 2 + 1024 * slot:stoff // 2 + 1024 * slot + 1024]
                lsm = "lb%d" % slot
                co = stoff + 4096 + 1024 * slot
                eo = stoff + 6144 + 1024 * slot
                lv = P.load(raw, src[r * 128:(r + 1) * 128, cc * 1024:(cc + 1) * 1024],
                            lsm, waits=P.slot_waits(("braw", slot)))
                cv = P.do("vector", lambda e, raw=raw, co=co, stg=stg: e.tensor_copy(
                    stg[:, co:co + 1024], raw),
                    waits=[(lsm, lv)] + P.slot_waits(("bconv", slot)), inc="V")
                P.slot_mark(("braw", slot), "V", cv)
                vts = []
                for kc in range(8):
                    wT = P.bank_waits("pT") + [("V", cv)]
                    vt = P.do("tensor", lambda e, kc=kc, co=co, stg=stg: e.transpose(
                        pT.bitcast(BF16)[:, 128 * kc:128 * kc + 128],
                        stg[:, co + 128 * kc:co + 128 * kc + 128], ident_b[:]),
                        waits=wT, inc="T")
                    vts.append(vt)
                P.slot_mark(("bconv", slot), "T", vts[-1])
                ve = P.do("scalar", lambda e, eo=eo, stg=stg: e.copy(
                    stg[:, eo:eo + 1024], pT.bitcast(BF16)[:, 0:1024]),
                    waits=[("T", vts[-1])] + P.slot_waits(("bev", slot)), inc="S")
                P.bank_consumed("pT", "S", ve)
                dstap = dst[r][cc * 1024:(cc + 1) * 1024, :].rearrange(
                    "(c p) q -> p c q", p=128)
                ssem = osem if slot == 0 else osem + "1"
                sv = P.store(dstap, stg[:, eo:eo + 1024].rearrange("p (c q) -> p c q", q=128),
                             ssem, waits=[("S", ve)])
                P.slot_mark(("bev", slot), ssem, sv)
        return [(osem, P.cnt(osem)), (osem + "1", P.cnt(osem + "1"))]

    def emit_ln(src_name, wait_src=(), war0=None):
        mu_row = big1[0:1, 30720:31232]
        rstd_row = big1[0:1, 31232:31744]
        var_row = big1f[0:1, 15872:16384]
        lnj = []
        if war0 is None:
            war0 = [(s, P.cnt(s)) for s in ("og", "ou", "od", "oq", "og1", "ou1",
                                            "od1", "oq1", "oa0", "oa1", "ob0", "ob1")
                    if P.cnt(s)]
            war0 += [("T", P.cnt("T")), ("S", P.cnt("S")), ("V", P.cnt("V"))]
        for j in range(NB):
            cvs = []
            for c in range(KC):
                slot = c % 2
                fv = stAf[:, 2048 + 512 * slot:2560 + 512 * slot]
                lv = P.load(fv, T[src_name][c * 128:(c + 1) * 128, j * 512:(j + 1) * 512],
                            "la%d" % slot,
                            waits=P.slot_waits(("lnf", slot)) + list(wait_src) + war0)
                cv = P.do("vector", lambda e, c=c, fv=fv: e.tensor_copy(
                    stA[:, 512 * c:512 * c + 512], fv),
                    waits=[("la%d" % slot, lv)] + P.slot_waits(("lnx", c)) + war0[-3:],
                    inc="V")
                P.slot_mark(("lnf", slot), "V", cv)
                cvs.append(cv)
            vconv = P.cnt("V")
            wR = P.bank_waits("pC")
            for c in range(KC):
                P.do("tensor", lambda e, c=c: e.matmul(
                    pC[0:1, 0:512], ones_c[:], stA[:, 512 * c:512 * c + 512],
                    start=(c == 0), stop=(c == KC - 1)),
                    waits=(wR if c == 0 else []) + [("V", cvs[c])],
                    inc="T" if c == KC - 1 else None)
            for c in range(KC):
                vs = P.do("scalar", lambda e, c=c: e.activation(
                    stA[:, 6144:6656], stA[:, 512 * c:512 * c + 512], AF.Square),
                    waits=[("V", cvs[c]), ("T", P.cnt("T"))] + war0, inc="S")
                P.do("tensor", lambda e, c=c: e.matmul(
                    pC[64:65, 0:512], ones_c[:], stA[:, 6144:6656],
                    start=(c == 0), stop=(c == KC - 1)), waits=[("S", vs)], inc="T")
            vstat = P.cnt("T")
            vmu = P.do("scalar", lambda e: e.activation(mu_row, pC[0:1, 0:512], AF.Copy,
                                                        scale=1.0 / D),
                       waits=[("T", vstat)], inc="S")
            wA = P.bank_waits("pD") + [("S", vmu)]
            vbA = P.do("tensor", lambda e: e.matmul(pD[:, 0:512], ones_r[:], mu_row,
                                                    start=True, stop=True), waits=wA, inc="T")
            vq = P.do("vector", lambda e: e.tensor_mul(rstd_row, mu_row, mu_row),
                      waits=[("S", vmu)], inc="V")
            vv = P.do("vector", lambda e: e.scalar_tensor_tensor(
                var_row, pC[64:65, 0:512], 1.0 / D, rstd_row, ALU.mult, ALU.subtract),
                waits=[("V", vq)], inc="V")
            P.bank_consumed("pC", "V", vv)
            vl = P.do("scalar", lambda e: e.activation(mu_row, var_row, AF.Ln,
                                                       bias=sc1[0:1, 15:16]),
                      waits=[("V", vv), ("T", vbA)], inc="S")
            vr = P.do("scalar", lambda e: e.activation(rstd_row, mu_row, AF.Exp, scale=-0.5),
                      waits=[("S", vl)], inc="S")
            wB = P.bank_waits("pE") + [("S", vr)]
            vbB = P.do("tensor", lambda e: e.matmul(pE[:, 0:512], ones_r[:], rstd_row,
                                                    start=True, stop=True), waits=wB, inc="T")
            for c in range(KC):
                vsb = P.do("vector", lambda e, c=c: e.tensor_sub(
                    stA[:, 512 * c:512 * c + 512], stA[:, 512 * c:512 * c + 512], pD[:, 0:512]),
                    waits=[("T", vbA)] if c == 0 else (), inc="V")
                vm2 = P.do("vector", lambda e, c=c, j=j: e.tensor_mul(
                    big0[:, 4096 * c + 512 * j:4096 * c + 512 * j + 512],
                    stA[:, 512 * c:512 * c + 512], pE[:, 0:512]),
                    waits=([("T", vbB)] if c == 0 else []) + [("V", vsb)], inc="V")
                P.slot_mark(("lnx", c), "V", vm2)
            vn = P.cnt("V")
            lnj.append(vn)
            P.bank_consumed("pD", "V", vn)
            P.bank_consumed("pE", "V", vn)
        return lnj

    sb_done = {}
    vx1 = []
    vspill = []
    if emit_l0:
        war_pre = [("T", P.cnt("T")), ("S", P.cnt("S")), ("V", P.cnt("V"))]
        phase += 1
        if phase <= upto:
            sb_done["wgT"] = emit_wbuild("wg", "wgT", HID, D, I32, "og")
            sb_done["wuT"] = emit_wbuild("wu", "wuT", HID, D, I32, "ou")
            sb_done["wdT"] = emit_wbuild("wd", "wdT", D, HID, I32, "od")
        phase += 1
        lnj0 = [0] * NB
        if phase <= upto:
            lnj0 = emit_ln("x", war0=war_pre)
            if "xn0" in dbg:
                P.do("gpsimd", lambda e: e.dma_start(out=T["dbg_xn0"], in_=big0[:]),
                     waits=[("V", P.cnt("V"))], inc="gd", n=16)

        phase += 1
        if phase <= upto:
            vxn = P.cnt("V")
            lnj_ = lnj0 if lnj0[0] else [vxn] * NB
            KOFF, VOFF, QOFF = 0, 8192, 16512
            MOFF, POFF, OTOFF, ROFF = 24704, 26752, 30848, 31360
            war_bld = [(s, P.cnt(s)) for s in ("og", "ou", "od", "og1", "ou1", "od1") if P.cnt(s)]
            for k in (("pws", 0), ("pws", 1), "pwf"):
                P.slot_mark_add(k, "V", vxn)
            pv_war = [0] * 3
            att_pend = [None]
            sS = ["pA", "pB", "pR", "pT"]

            def load_wslice(wname, cols0, ncols, slot):
                for hh in range(2):
                    fv = stAf[:, 2080:2080 + 4 * ncols]
                    srcap = T[wname].rearrange("(kc p) q -> p kc q", p=128)[
                        :, 4 * hh:4 * hh + 4, cols0:cols0 + ncols]
                    lv = P.load(fv.rearrange("p (kc q) -> p kc q", kc=4), srcap,
                                "lb0", waits=P.slot_waits("pwf") + war_bld)
                    cv = P.do("vector", lambda e, fv=fv, slot=slot, hh=hh, ncols=ncols: e.tensor_copy(
                        stA[:, 2048 * slot + 4 * ncols * hh:2048 * slot + 4 * ncols * hh + 4 * ncols],
                        fv), waits=[("lb0", lv)] + P.slot_waits(("pws", slot)), inc="V")
                    P.slot_mark("pwf", "V", cv)
                return P.cnt("V")

            for g in range(min(G, GLIM)):
                vwk = load_wslice("wk", g * 256, 256, 0)
                for c2 in range(2):
                    for j in range(NB):
                        bank = ("pA", pA) if (c2 * NB + j) % 2 == 0 else ("pB", pB)
                        pairs = [(stA[:, 256 * kc + 128 * c2:256 * kc + 128 * c2 + 128],
                                  big0[:, 4096 * kc + 512 * j:4096 * kc + 512 * j + 512])
                                 for kc in range(KC)]
                        vm = P.mm_group(bank[1][:, 0:512], pairs,
                                        waits=[("V", vwk), ("V", lnj_[j])], bankname=bank[0])
                        ve = P.do("vector", lambda e, g=g, c2=c2, j=j, b=bank[1]: e.tensor_scalar_add(
                            big2[:, KOFF + 4096 * c2 + 512 * j:KOFF + 4096 * c2 + 512 * j + 512],
                            b[:, 0:512], bias_k[:, 2 * g + c2:2 * g + c2 + 1]),
                            waits=[("T", vm), ("la0", vconsts)], inc="V")
                        P.bank_consumed(bank[0], "V", ve)
                vKg = P.cnt("V")
                P.slot_mark(("pws", 0), "T", P.cnt("T"))
                fvb = stAf[0:1, 3120:3380]
                lvb = P.load(fvb, T["bv"][0:1, g * VW:(g + 1) * VW], "lc0",
                             waits=[("V", P.cnt("V"))] + war_bld)
                vbv = P.do("vector", lambda e, fvb=fvb: e.tensor_copy(
                    stA[0:1, 6760:6760 + VW], fvb),
                    waits=[("lc0", P.cnt("lc0")), ("T", P.cnt("T"))], inc="V")
                vwv = load_wslice("wv", g * VW, VW, 1)
                for i in range(NT):
                    bank = ("pC", pC) if i % 2 == 0 else ("pD", pD)
                    pairs = [(big0[:, 4096 * kc + 128 * i:4096 * kc + 128 * i + 128],
                              stA[:, 2048 + VW * kc:2048 + VW * kc + VW])
                             for kc in range(KC)]
                    P.mm_group(bank[1][:, 0:VW], pairs,
                               waits=[("V", vwv), ("V", lnj_[i // 4])],
                               bankname=bank[0], last_stop=False)
                    vm = P.do("tensor", lambda e, b=bank[1]: e.matmul(
                        b[:, 0:VW], ones_r[:], stA[0:1, 6760:6760 + VW],
                        start=False, stop=True), waits=[("V", vbv)], inc="T")
                    ve = P.do("vector", lambda e, i=i, b=bank[1]: e.tensor_copy(
                        big2[:, VOFF + VW * i:VOFF + VW * i + VW], b[:, 0:VW]),
                        waits=[("T", vm)], inc="V")
                    P.bank_consumed(bank[0], "V", ve)
                vVg = P.cnt("V")
                P.slot_mark(("pws", 1), "T", P.cnt("T"))
                vwq = load_wslice("wq", g * 256, 256, 0)
                for c2 in range(2):
                    for j in range(NB):
                        bank = ("pA", pA) if (c2 * NB + j) % 2 == 0 else ("pB", pB)
                        pairs = [(stA[:, 256 * kc + 128 * c2:256 * kc + 128 * c2 + 128],
                                  big0[:, 4096 * kc + 512 * j:4096 * kc + 512 * j + 512])
                                 for kc in range(KC)]
                        vm = P.mm_group(bank[1][:, 0:512], pairs,
                                        waits=[("V", vwq), ("V", lnj_[j])], bankname=bank[0])
                        ve = P.do("vector", lambda e, g=g, c2=c2, j=j, b=bank[1]: e.tensor_scalar_add(
                            big2[:, QOFF + 4096 * c2 + 512 * j:QOFF + 4096 * c2 + 512 * j + 512],
                            b[:, 0:512], bias_q[:, 2 * g + c2:2 * g + c2 + 1]),
                            waits=[("T", vm), ("la0", vconsts)], inc="V")
                        P.bank_consumed(bank[0], "V", ve)
                vQg = P.cnt("V")
                P.slot_mark(("pws", 0), "T", P.cnt("T"))
                if g == 0 and "kvq0" in dbg:
                    P.do("gpsimd", lambda e: e.dma_start(out=T["dbg_kvq0"], in_=big2[:]),
                         waits=[("V", vQg)], inc="gd", n=16)

                def norm_max(OFFS, vready, outcol):
                    for j in range(NB):
                        for c2 in range(2):
                            wR = (P.bank_waits("pR") if c2 == 0 else [])
                            vs = P.do("vector", lambda e, c2=c2, j=j, OFFS=OFFS: e.tensor_mul(
                                stA[:, 4160:4672],
                                big2[:, OFFS + 4096 * c2 + 512 * j:OFFS + 4096 * c2 + 512 * j + 512],
                                big2[:, OFFS + 4096 * c2 + 512 * j:OFFS + 4096 * c2 + 512 * j + 512]),
                                waits=wR + [("V", vready), ("T", P.cnt("T"))], inc="V")
                            P.do("tensor", lambda e, c2=c2: e.matmul(
                                pR[0:1, 0:512], ones_c[:], stA[:, 4160:4672],
                                start=(c2 == 0), stop=(c2 == 1)), waits=[("V", vs)], inc="T")
                        vt = P.cnt("T")
                        vv = P.do("vector", lambda e, j=j: e.tensor_reduce(
                            qmr[0:1, j:j + 1], pR[0:1, 0:512], AX.X, ALU.max),
                            waits=[("T", vt)], inc="V")
                        P.bank_consumed("pR", "V", vv)
                        P._wait("scalar", "V", vv)
                    return P.do("vector", lambda e, outcol=outcol: e.tensor_reduce(
                        sc1[0:1, outcol:outcol + 1], qmr[0:1, 0:NB], AX.X, ALU.max),
                        waits=[("V", P.cnt("V"))], inc="V")

                if SUBPH < 2:
                    continue
                vq = norm_max(QOFF, vQg, 0)
                vk = norm_max(KOFF, vKg, 1)
                P.slot_mark_add("pwf", "S", P.cnt("S"))
                P.slot_mark_add("pwf", "T", P.cnt("T"))
                vp = P.do("vector", lambda e: e.tensor_mul(sc1[0:1, 2:3], sc1[0:1, 0:1],
                                                           sc1[0:1, 1:2]),
                          waits=[("V", vq), ("V", vk)], inc="V")
                vsq = P.do("scalar", lambda e: e.activation(sc1[0:1, 3:4], sc1[0:1, 2:3],
                                                            AF.Sqrt), waits=[("V", vp)], inc="S")
                wT2 = P.bank_waits("pT") + [("S", vsq)]
                vbc = P.do("tensor", lambda e: e.matmul(pT[:, 0:1], ones_rf[:], sc1[0:1, 3:4],
                                                        start=True, stop=True), waits=wT2, inc="T")
                vnc = P.do("scalar", lambda e: e.activation(negc[:, 0:1], pT[:, 0:1], AF.Identity,
                                                            scale=-0.125 * 1.001),
                           waits=[("T", vbc)], inc="S")
                P.bank_consumed("pT", "S", vnc)

                if SUBPH < 3:
                    continue
                SPAIR = [pS0a, pS1a, pS2a]
                SHALF = [["pA", "pB"], ["pR", "pT"], ["pE", "pF"]]
                big2f = big2[:].bitcast(F32)
                rrows = [big2f[0:1, 15680:16192], big2f[64:65, 15680:16192]]

                sel2f = big2f[0:65, 16192:16320]

                def att_epilogue2(pend):
                    g_, hswp_, qb_, oz_done_, vrcs_ = pend
                    wbc = P.bank_waits("pE") + [("S", vrcs_[0]), ("S", vrcs_[1]),
                                                ("V", vconstsV)]
                    vbc2 = P.do("tensor", lambda e: e.matmul(
                        pE[0:128, 0:512], sel2f, big2f[0:65, 15680:16192],
                        start=True, stop=True), waits=wbc, inc="T")
                    vml = 0
                    for hh2 in range(2):
                        hh = 2 * hswp_ + hh2
                        c2, hl = hh // 2, hh % 2
                        oz = PSUM[["pC", "pD"][hh2]]
                        ot = big2[0:64, OTOFF:OTOFF + 512]
                        voe = P.do("vector", lambda e, ot=ot, oz=oz: e.tensor_copy(
                            ot, oz[0:64, 0:512]),
                            waits=[("V", P.cnt("V")), ("S", P.cnt("S"))], inc="V")
                        P.bank_consumed(["pC", "pD"][hh2], "V", voe)
                        dst = big1[64 * hl:64 * hl + 64,
                                   4096 * (2 * g_ + c2) + 512 * qb_:4096 * (2 * g_ + c2) + 512 * qb_ + 512]
                        vml = P.do("vector", lambda e, dst=dst, ot=ot, hh2=hh2: e.tensor_mul(
                            dst, ot, pE[64 * hh2:64 * hh2 + 64, 0:512]),
                            waits=[("V", voe), ("T", vbc2)] + (war_bld if g_ == 0 else []),
                            inc="V")
                    P.bank_consumed("pE", "V", vml)
                    P._wait("scalar", "V", vml)

                for hswp in range(2):
                    for qb in range(NB):
                        ktn = 4 * qb + 4
                        oz_done = [None, None]
                        qk_pend = {}

                        def emit_qkpair(kt, hswp=hswp, qb=qb, qk_pend=None):
                            pr = kt % 3
                            qks = []
                            for hh2 in range(2):
                                hh = 2 * hswp + hh2
                                c2, hl = hh // 2, hh % 2
                                bkn = SHALF[pr][hh2]
                                l = big2[64 * hl:64 * hl + 64,
                                         KOFF + 4096 * c2 + 128 * kt:KOFF + 4096 * c2 + 128 * kt + 128]
                                r = big2[64 * hl:64 * hl + 64,
                                         QOFF + 4096 * c2 + 512 * qb:QOFF + 4096 * c2 + 512 * qb + 512]
                                wqk = P.bank_waits(bkn) + [("S", vnc), ("V", vVg)]
                                vqk = P.do("tensor", lambda e, l=l, r=r, b=PSUM[bkn]: e.matmul(
                                    b, l, r, start=True, stop=True), waits=wqk, inc="T")
                                qks.append(vqk)
                            qk_pend[kt] = (pr, qks)

                        emit_qkpair(0, qk_pend=qk_pend)
                        if ktn > 1:
                            emit_qkpair(1, qk_pend=qk_pend)
                        if att_pend[0] is not None:
                            att_epilogue2(att_pend[0])
                            att_pend[0] = None
                        for kt in range(ktn):
                            if kt + 2 < ktn:
                                emit_qkpair(kt + 2, qk_pend=qk_pend)
                            pr, qks = qk_pend.pop(kt)
                            pp = kt % 3
                            ppt = big2[:, POFF + 1024 * pp:POFF + 1024 * pp + 1024]
                            wex = [("T", qks[1]), ("T", pv_war[pp])]
                            vex = P.do("scalar", lambda e, ppt=ppt, b=SPAIR[pr]: e.activation(
                                ppt, b, AF.Exp, bias=negc[:, 0:1], scale=0.125),
                                waits=wex, inc="S")
                            P.bank_consumed(SHALF[pr][0], "S", vex)
                            P.bank_consumed(SHALF[pr][1], "S", vex)
                            pvwait = ("S", vex)
                            if kt >= 4 * qb:
                                dk = kt - 4 * qb
                                for half in range(2):
                                    vmk = P.do("vector", lambda e, ppt=ppt, dk=dk, half=half: e.tensor_mul(
                                        ppt[:, 512 * half:512 * half + 512],
                                        ppt[:, 512 * half:512 * half + 512],
                                        big2[:, MOFF + 512 * dk:MOFF + 512 * dk + 512]),
                                        waits=[("S", vex)], inc="V")
                                pvwait = ("V", vmk)
                            for hh2 in range(2):
                                hh = 2 * hswp + hh2
                                pt = ppt[:, 512 * hh2:512 * hh2 + 512]
                                ozn = ["pC", "pD"][hh2]
                                wpv = [pvwait]
                                if kt == 0:
                                    wpv += P.bank_waits(ozn)
                                lv = big2[:, VOFF + VW * kt + 65 * hh:VOFF + VW * kt + 65 * hh + 65]
                                vpv = P.do("tensor", lambda e, lv=lv, pt=pt, st=(kt == 0),
                                           sp=(kt == ktn - 1), oz=PSUM[ozn]: e.matmul(
                                    oz[0:65, 0:512], lv, pt, start=st, stop=sp), waits=wpv, inc="T")
                                pv_war[pp] = max(pv_war[pp], vpv)
                                oz_done[hh2] = vpv
                        vrcs = []
                        for hh2 in range(2):
                            oz = PSUM[["pC", "pD"][hh2]]
                            vln = P.do("scalar", lambda e, oz=oz: e.activation(
                                oz[64:65, 0:512], oz[64:65, 0:512], AF.Ln,
                                scale=float(2.0 ** 55)),
                                waits=[("T", oz_done[hh2]), ("S", P.cnt("S")),
                                       ("V", vconstsV)], inc="S")
                            vrc = P.do("scalar", lambda e, oz=oz, rr=rrows[hh2]: e.activation(
                                rr, oz[64:65, 0:512], AF.Exp, scale=-1.0,
                                bias=sc1[0:1, 14:15]),
                                waits=[("S", vln)], inc="S")
                            P.bank_consumed_add(["pC", "pD"][hh2], "S", vrc)
                            vrcs.append(vrc)
                        att_pend[0] = (g, hswp, qb, list(oz_done), vrcs)
                if att_pend[0] is not None:
                    att_epilogue2(att_pend[0])
                    att_pend[0] = None
            if att_pend[0] is not None:
                att_epilogue2(att_pend[0])
                att_pend[0] = None
            if "att" in dbg:
                P.do("gpsimd", lambda e: e.dma_start(out=T["dbg_att"], in_=big1[:]),
                     waits=[("V", P.cnt("V"))], inc="gd", n=16)

        phase += 1
        if phase <= upto:
            vOT = P.cnt("V")
            tw = ("T", P.cnt("T"))
            for k in (("wos", 0), ("wos", 1)):
                P.slot_mark_add(k, *tw)
            for dd in range(KC):
                slot = dd % 2
                for hh in range(2):
                    fv = stAf[:, 2080:2592]
                    srcap = T["wo"].rearrange("(kc p) q -> p kc q", p=128)[
                        :, 4 * hh:4 * hh + 4, dd * 128:(dd + 1) * 128]
                    lv = P.load(fv.rearrange("p (kc q) -> p kc q", kc=4), srcap, "lb0",
                                waits=P.slot_waits("pwf"))
                    cv = P.do("vector", lambda e, fv=fv, slot=slot, hh=hh: e.tensor_copy(
                        stA[:, 1024 * slot + 512 * hh:1024 * slot + 512 * hh + 512], fv),
                        waits=[("lb0", lv)] + P.slot_waits(("wos", slot)), inc="V")
                    P.slot_mark("pwf", "V", cv)
                vws = P.cnt("V")
                lastT = None
                for j in range(NB):
                    bank = ("pA", pA) if (dd * NB + j) % 2 == 0 else ("pB", pB)
                    pairs = [(stA[:, 1024 * slot + 128 * kc:1024 * slot + 128 * kc + 128],
                              big1[:, 4096 * kc + 512 * j:4096 * kc + 512 * j + 512])
                             for kc in range(KC)]
                    vm = P.mm_group(bank[1][:, 0:512], pairs, waits=[("V", vws), ("V", vOT)],
                                    bankname=bank[0])
                    lastT = vm
                    sp = stA[:, 6144 + 512 * (j % 2):6144 + 512 * (j % 2) + 512]
                    osm = "oa%d" % (j % 2)
                    ve = P.do("vector", lambda e, sp=sp, dd=dd, b=bank[1]: e.tensor_scalar_add(
                        sp, b[:, 0:512], bias_o[:, dd:dd + 1]),
                        waits=[("T", vm), (osm, P.cnt(osm)), ("la0", vconsts)], inc="V")
                    P.bank_consumed(bank[0], "V", ve)
                    P.store(T["s_s"][dd * 128:(dd + 1) * 128, j * 512:(j + 1) * 512], sp, osm,
                            waits=[("V", ve)])
                P.slot_mark(("wos", slot), "T", lastT)
            vspill = [("oa0", P.cnt("oa0")), ("oa1", P.cnt("oa1"))]
            if "s0" in dbg:
                P.do("gpsimd", lambda e: e.dma_start(out=T["dbg_s0"], in_=T["s_s"]),
                     waits=vspill, inc="gd", n=16)

        if emit_l1 and upto >= 5:
            sb_done["wqkvT"] = emit_wbuild("wqkv", "wqkvT", 3 * D, D, F32, "oq",
                                           stg=big2, stoff=8192)
        phase += 1
        if phase <= upto:
            vxn = P.cnt("V")
            tw = ("T", P.cnt("T"))
            for k in (("fg", 0), ("fg", 1), ("fu", 0), ("fu", 1), ("fd", 0), ("fd", 1),
                      ("sg", 0), ("sg", 1), ("cfx", 0), ("cfx", 1), ("csl", 0),
                      ("csl", 1), ("cpr", 0), ("cpr", 1)):
                P.slot_mark_add(k, *tw)
            for sv_ in vspill:
                P.slot_mark_add("cfx", sv_[0], sv_[1])
            for grp in range(4):
                j0 = grp * 2
                for hc in range(HCN):
                    slot = hc % 2
                    gsm = "lb%d" % (2 * slot)
                    usm = "lb%d" % (2 * slot + 1)
                    lvg = P.load(stA[:, 1024 * slot:1024 * slot + 1024].rearrange(
                        "p (c q) -> p c q", q=128),
                        T["wgT"][hc].rearrange("(c p) q -> p c q", p=128), gsm,
                        waits=P.slot_waits(("fg", slot)) + list(sb_done.get("wgT", [])))
                    lvu = P.load(stA[:, 2048 + 1024 * slot:2048 + 1024 * slot + 1024].rearrange(
                        "p (c q) -> p c q", q=128),
                        T["wuT"][hc].rearrange("(c p) q -> p c q", p=128), usm,
                        waits=P.slot_waits(("fu", slot)) + list(sb_done.get("wuT", [])))
                    lastg = lastu = None
                    for j2 in range(2):
                        j = j0 + j2
                        gb = ("pA", pA) if j2 == 0 else ("pB", pB)
                        ub = ("pC", pC) if j2 == 0 else ("pD", pD)
                        gp = [(stA[:, 1024 * slot + 128 * kc:1024 * slot + 128 * kc + 128],
                               big0[:, 4096 * kc + 512 * j:4096 * kc + 512 * j + 512])
                              for kc in range(KC)]
                        upp = [(stA[:, 2048 + 1024 * slot + 128 * kc:2048 + 1024 * slot + 128 * kc + 128],
                                big0[:, 4096 * kc + 512 * j:4096 * kc + 512 * j + 512])
                               for kc in range(KC)]
                        vmg = P.mm_group(gb[1][:, 0:512], gp, waits=[(gsm, lvg), ("V", vxn)],
                                         bankname=gb[0])
                        vmu2 = P.mm_group(ub[1][:, 0:512], upp, waits=[(usm, lvu)],
                                          bankname=ub[0])
                        lastg, lastu = vmg, vmu2
                        sgo = [4096, 6656][j2]
                        sg = stA[:, sgo:sgo + 512]
                        vsg = P.do("scalar", lambda e, sg=sg, b=gb[1]: e.activation(
                            sg, b[:, 0:512], AF.Silu, scale=1.0 / 127.0),
                            waits=[("T", vmg)] + P.slot_waits(("sg", j2)) + vspill, inc="S")
                        P.bank_consumed(gb[0], "S", vsg)
                        hdst = big1[:, 1024 * hc + 512 * j2:1024 * hc + 512 * j2 + 512]
                        vh = P.do("vector", lambda e, hdst=hdst, sg=sg, b=ub[1]: e.tensor_mul(
                            hdst, sg, b[:, 0:512]), waits=[("S", vsg), ("T", vmu2)], inc="V")
                        P.bank_consumed(ub[0], "V", vh)
                        P.slot_mark(("sg", j2), "V", vh)
                    P.slot_mark(("fg", slot), "T", lastg)
                    P.slot_mark(("fu", slot), "T", lastu)
                vhall = P.cnt("V")
                for dd in range(KC):
                    dslot = dd % 2
                    do_ = 24576 + 3072 * dslot
                    dsm = "ld%d" % dslot
                    lvd = P.load(big1[:, do_:do_ + 3072].rearrange("p (c q) -> p c q", q=128),
                                 T["wdT"][dd].rearrange("(c p) q -> p c q", p=128), dsm,
                                 waits=P.slot_waits(("fd", dslot)) + list(sb_done.get("wdT", [])))
                    lastd = None
                    for j2 in range(2):
                        bank = ("pE", pE) if j2 == 0 else ("pF", pF)
                        pairs = [(big1[:, do_ + 128 * hcc:do_ + 128 * hcc + 128],
                                  big1[:, 1024 * hcc + 512 * j2:1024 * hcc + 512 * j2 + 512])
                                 for hcc in range(HCN)]
                        vm = P.mm_group(bank[1][:, 0:512], pairs,
                                        waits=[(dsm, lvd), ("V", vhall)], bankname=bank[0])
                        lastd = vm
                        ve = P.do("scalar", lambda e, dd=dd, j2=j2, b=bank[1]: e.activation(
                            big2[:, 1024 * dd + 512 * j2:1024 * dd + 512 * j2 + 512],
                            b[:, 0:512], AF.Silu, scale=1.0 / 16129.0),
                            waits=[("T", vm)] + P.slot_waits(("gch", dd * 2 + j2)), inc="S")
                        P.bank_consumed(bank[0], "S", ve)
                    P.slot_mark(("fd", dslot), "T", lastd)
                vgrp = P.cnt("S")
                if grp == 0 and "gg" in dbg:
                    P.do("gpsimd", lambda e: e.dma_start(out=T["dbg_gg"], in_=big2[:]),
                         waits=[("S", vgrp)], inc="gd", n=16)
                b2f = big2[:].bitcast(F32)
                for j2 in range(2):
                    j = j0 + j2
                    for c in range(KC):
                        sl = c % 2
                        fx = b2f[:, 8192 + 512 * sl:8192 + 512 * sl + 512]
                        fo = b2f[:, 9216 + 512 * sl:9216 + 512 * sl + 512]
                        ssl = big2[:, 20480 + 512 * sl:20480 + 512 * sl + 512]
                        prd = big2[:, 21504 + 512 * sl:21504 + 512 * sl + 512]
                        osm = "ob%d" % sl
                        lvx = P.load(fx, T["x"][c * 128:(c + 1) * 128, j * 512:(j + 1) * 512],
                                     "lc0", waits=P.slot_waits(("cfx", sl)))
                        lvs = P.load(ssl, T["s_s"][c * 128:(c + 1) * 128, j * 512:(j + 1) * 512],
                                     "lc1", waits=vspill + P.slot_waits(("csl", sl)))
                        vp2 = P.do("vector", lambda e, prd=prd, ssl=ssl, c=c, j2=j2: e.tensor_mul(
                            prd, ssl, big2[:, 1024 * c + 512 * j2:1024 * c + 512 * j2 + 512]),
                            waits=[("lc1", lvs), ("S", vgrp)] + P.slot_waits(("cpr", sl)), inc="V")
                        P.slot_mark(("csl", sl), "V", vp2)
                        P.slot_mark_add(("gch", c * 2 + j2), "V", vp2)
                        vo = P.do("vector", lambda e, fo=fo, fx=fx, prd=prd: e.tensor_add(
                            fo, fx, prd),
                            waits=[("lc0", lvx), (osm, P.cnt(osm))], inc="V")
                        P.slot_mark(("cfx", sl), "V", vo)
                        P.slot_mark(("cpr", sl), "V", vo)
                        P.store(T["x1"][c * 128:(c + 1) * 128, j * 512:(j + 1) * 512], fo,
                                osm, waits=[("V", vo)])
            vx1 = [("ob0", P.cnt("ob0")), ("ob1", P.cnt("ob1"))]

    if emit_l1:
        phase += 1
        if phase <= upto and "wqkvT" not in sb_done:
            sb_done["wqkvT"] = emit_wbuild("wqkv", "wqkvT", 3 * D, D, F32, "oq")
        phase += 1
        lnj1 = [0] * NB
        vneg = 0
        if phase <= upto:
            war_q = [(s, P.cnt(s)) for s in ("oq", "oq1") if P.cnt(s)] + [("T", P.cnt("T"))]
            P.slot_mark_add("wba", "V", P.cnt("V"))
            for idx, (an, bn) in enumerate((("aR", "bR"), ("aI", "bI"))):
                for ch in range(4):
                    for c in range(4):
                        fa = stAf[:, 0:512]
                        fb = stAf[:, 512:1024]
                        acc = stAf[:, 1024:1536]
                        la = P.load(fa, T[an][c][ch * 128:(ch + 1) * 128, :], "lb0",
                                    waits=P.slot_waits("wba") + war_q[-1:])
                        lb = P.load(fb, T[bn][c][ch * 128:(ch + 1) * 128, :], "lb1",
                                    waits=P.slot_waits("wba"))
                        if c == 0:
                            vmu0 = P.do("vector", lambda e, fa=fa, fb=fb, acc=acc: e.tensor_mul(
                                acc, fa, fb), waits=[("lb0", la), ("lb1", lb)], inc="V")
                        else:
                            vt0 = P.do("vector", lambda e, fa=fa, fb=fb: e.tensor_mul(
                                stAf[:, 1536:2048], fa, fb),
                                waits=[("lb0", la), ("lb1", lb)], inc="V")
                            vmu0 = P.do("vector", lambda e, acc=acc: e.tensor_add(
                                acc, acc, stAf[:, 1536:2048]), waits=[("V", vt0)], inc="V")
                        P.slot_mark("wba", "V", vmu0)
                    vcp = P.do("vector", lambda e, idx=idx, ch=ch, acc=acc: e.tensor_copy(
                        big1[:, 2048 * idx + 512 * ch:2048 * idx + 512 * ch + 512], acc),
                        waits=war_q + [("V", vmu0)], inc="V")
                    P.slot_mark("wba", "V", vcp)
            vwt = P.cnt("V")
            vneg = P.do("vector", lambda e: e.tensor_scalar_mul(
                big1[:, 4096:6144], big1[:, 2048:4096], -1.0), waits=[("V", vwt)] + war_q,
                inc="V")
            if "WT" in dbg:
                P.do("gpsimd", lambda e: e.dma_start(out=T["dbg_WT"], in_=big1[:, 0:6144]),
                     waits=[("V", vneg)], inc="gd", n=16)
            lnj1 = emit_ln("x1", wait_src=vx1)
            if "xn1" in dbg:
                P.do("gpsimd", lambda e: e.dma_start(out=T["dbg_xn1"], in_=big0[:]),
                     waits=[("V", P.cnt("V"))], inc="gd", n=16)
        phase += 1
        vgs = []
        if phase <= upto:
            vxnl = P.cnt("V")
            for j in range(NB):
                for half in range(2):
                    for oc in range(4):
                        bank = ("pA", pA) if oc % 2 == 0 else ("pB", pB)
                        pairs = []
                        for ic in range(4):
                            wofs = 0 if half == 0 else 2048
                            pairs.append((big1[:, wofs + 512 * ic + 128 * oc:wofs + 512 * ic + 128 * oc + 128],
                                          big0[:, 4096 * ic + 512 * j:4096 * ic + 512 * j + 512]))
                        for ic in range(4):
                            wofs = 4096 if half == 0 else 0
                            pairs.append((big1[:, wofs + 512 * ic + 128 * oc:wofs + 512 * ic + 128 * oc + 128],
                                          big0[:, 4096 * (4 + ic) + 512 * j:4096 * (4 + ic) + 512 * j + 512]))
                        vm = P.mm_group(bank[1][:, 0:512], pairs,
                                        waits=[("V", vneg), ("V", lnj1[j] or vxnl)],
                                        bankname=bank[0])
                        gsl = big2[:, 512 * (oc % 2):512 * (oc % 2) + 512]
                        osm = "oa%d" % (oc % 2)
                        bias = bias_r if half == 0 else bias_i
                        ve = P.do("scalar", lambda e, gsl=gsl, oc=oc, b=bank[1], bias=bias: e.activation(
                            gsl, b[:, 0:512], AF.Silu, bias=bias[:, oc:oc + 1]),
                            waits=[("T", vm), (osm, P.cnt(osm)), ("la0", vconsts)], inc="S")
                        P.bank_consumed(bank[0], "S", ve)
                        P.store(T["g_s"][512 * half + 128 * oc:512 * half + 128 * oc + 128,
                                         j * 512:(j + 1) * 512], gsl, osm,
                                waits=[("S", ve)])
            vgs = [("oa0", P.cnt("oa0")), ("oa1", P.cnt("oa1"))]
            if "g1" in dbg:
                P.do("gpsimd", lambda e: e.dma_start(out=T["dbg_g1"], in_=T["g_s"]),
                     waits=vgs, inc="gd", n=16)

        phase += 1
        if phase <= upto:
            vxnl = P.cnt("V")
            tw = ("T", P.cnt("T"))
            for k in (("l1k", 0), ("l1k", 1), "l1v"):
                P.slot_mark_add(k, *tw)
                P.slot_mark_add(k, "V", vxnl)
            for fc in range(KC):
                slot = fc % 2
                ksm = "lb%d" % slot
                lvk = P.load(stA[:, 1024 * slot:1024 * slot + 1024].rearrange(
                    "p (c q) -> p c q", q=128),
                    T["wqkvT"][8 + fc].rearrange("(c p) q -> p c q", p=128), ksm,
                    waits=P.slot_waits(("l1k", slot)) + list(sb_done.get("wqkvT", [])))
                lastm = None
                for j in range(NB):
                    bank = ("pA", pA) if (fc * NB + j) % 2 == 0 else ("pB", pB)
                    pairs = [(stA[:, 1024 * slot + 128 * kc:1024 * slot + 128 * kc + 128],
                              big0[:, 4096 * kc + 512 * j:4096 * kc + 512 * j + 512])
                             for kc in range(KC)]
                    vm = P.mm_group(bank[1][:, 0:512], pairs,
                                    waits=[(ksm, lvk), ("V", lnj1[j] or vxnl)], bankname=bank[0])
                    lastm = vm
                    ve = P.do("vector", lambda e, fc=fc, j=j, b=bank[1]: e.tensor_copy(
                        big1[:, 4096 * fc + 512 * j:4096 * fc + 512 * j + 512], b[:, 0:512]),
                        waits=[("T", vm)], inc="V")
                    P.bank_consumed(bank[0], "V", ve)
                P.slot_mark(("l1k", slot), "T", lastm)
            for fg in range(2):
                for kc in range(KC):
                    P.load(stA[:, 2048 + 512 * kc:2048 + 512 * kc + 512].rearrange(
                        "p (f q) -> p f q", q=128),
                        T["wqkvT"].rearrange("f (kc p) q -> p f kc q", p=128)[
                            :, 16 + 4 * fg:16 + 4 * fg + 4, kc, :],
                        "lb0", waits=P.slot_waits("l1v") + vgs +
                        list(sb_done.get("wqkvT", [])))
                lvv = P.cnt("lb0")
                lastm = None
                for i in range(NT):
                    bank = ("pC", pC) if i % 2 == 0 else ("pD", pD)
                    pairs = [(big0[:, 4096 * kc + 128 * i:4096 * kc + 128 * i + 128],
                              stA[:, 2048 + 512 * kc:2048 + 512 * kc + 512])
                             for kc in range(KC)]
                    vm = P.mm_group(bank[1][:, 0:512], pairs,
                                    waits=[("lb0", lvv), ("V", lnj1[i // 4] or vxnl)],
                                    bankname=bank[0])
                    lastm = vm
                    ve = P.do("vector", lambda e, i=i, fg=fg, b=bank[1]: e.tensor_copy(
                        big2[:, 1024 * i + 512 * fg:1024 * i + 512 * fg + 512], b[:, 0:512]),
                        waits=[("T", vm)], inc="V")
                    P.bank_consumed(bank[0], "V", ve)
                P.slot_mark("l1v", "T", lastm)
            BANKS = ["pA", "pB", "pC", "pD", "pE", "pF", "pR", "pT"]
            for j in range(NB):
                evs = []
                for fc in range(KC):
                    slot = fc % 2
                    lvq = P.load(stA[:, 1024 * slot:1024 * slot + 1024].rearrange(
                        "p (c q) -> p c q", q=128),
                        T["wqkvT"][fc].rearrange("(c p) q -> p c q", p=128),
                        "lb%d" % slot, waits=P.slot_waits(("l1k", slot)))
                    bank = BANKS[fc]
                    pairs = [(stA[:, 1024 * slot + 128 * kc:1024 * slot + 128 * kc + 128],
                              big0[:, 4096 * kc + 512 * j:4096 * kc + 512 * j + 512])
                             for kc in range(KC)]
                    vm = P.mm_group(PSUM[bank][:, 0:512], pairs,
                                    waits=[("lb%d" % slot, lvq), ("V", lnj1[j] or vxnl)],
                                    bankname=bank)
                    P.slot_mark(("l1k", slot), "T", vm)
                    evs.append((bank, vm))
                vall = P.cnt("T")
                for fc, (bank, vm) in enumerate(evs):
                    ve = P.do("vector", lambda e, fc=fc, j=j, bank=bank: e.tensor_copy(
                        big0[:, 4096 * fc + 512 * j:4096 * fc + 512 * j + 512],
                        PSUM[bank][:, 0:512]), waits=[("T", vall)], inc="V")
                    P.bank_consumed(bank, "V", ve)
            vqkv = P.cnt("V")
            if "kq1" in dbg:
                P.do("gpsimd", lambda e: e.dma_start(out=T["dbg_k1"], in_=big1[:]),
                     waits=[("V", vqkv)], inc="gd", n=16)
                P.do("gpsimd", lambda e: e.dma_start(out=T["dbg_q1"], in_=big0[:]),
                     waits=[("V", vqkv)], inc="gd", n=16)
            if "v1" in dbg:
                P.do("gpsimd", lambda e: e.dma_start(out=T["dbg_v1"], in_=big2[:]),
                     waits=[("V", vqkv)], inc="gd", n=16)

        phase += 1
        if phase <= upto:
            vqkv = P.cnt("V")

            def norm_max_l1(buf, vready, outcol):
                for j in range(NB):
                    for kc in range(KC):
                        wR = (P.bank_waits("pR") if kc == 0 else [])
                        vs = P.do("vector", lambda e, kc=kc, j=j, buf=buf: e.tensor_mul(
                            stA[:, 4096:4608],
                            buf[:, 4096 * kc + 512 * j:4096 * kc + 512 * j + 512],
                            buf[:, 4096 * kc + 512 * j:4096 * kc + 512 * j + 512]),
                            waits=wR + [("V", vready), ("T", P.cnt("T"))] + vgs, inc="V")
                        P.do("tensor", lambda e, kc=kc: e.matmul(
                            pR[0:1, 0:512], ones_c[:], stA[:, 4096:4608],
                            start=(kc == 0), stop=(kc == KC - 1)), waits=[("V", vs)], inc="T")
                    vt = P.cnt("T")
                    vv = P.do("vector", lambda e, j=j: e.tensor_reduce(
                        qmr[0:1, 8 + j:9 + j], pR[0:1, 0:512], AX.X, ALU.max),
                        waits=[("T", vt)], inc="V")
                    P.bank_consumed("pR", "V", vv)
                    P._wait("scalar", "V", vv)
                return P.do("vector", lambda e, outcol=outcol: e.tensor_reduce(
                    sc1[0:1, outcol:outcol + 1], qmr[0:1, 8:8 + NB], AX.X, ALU.max),
                    waits=[("V", P.cnt("V"))], inc="V")

            vqn = norm_max_l1(big0, vqkv, 4)
            vkn = norm_max_l1(big1, vqkv, 5)
            vpp = P.do("vector", lambda e: e.tensor_mul(sc1[0:1, 6:7], sc1[0:1, 4:5],
                                                        sc1[0:1, 5:6]),
                       waits=[("V", vqn), ("V", vkn)], inc="V")
            vsq = P.do("scalar", lambda e: e.activation(sc1[0:1, 7:8], sc1[0:1, 6:7],
                                                        AF.Sqrt), waits=[("V", vpp)], inc="S")
            wT2 = P.bank_waits("pT") + [("S", vsq)]
            vbc = P.do("tensor", lambda e: e.matmul(pT[:, 0:1], ones_rf[:], sc1[0:1, 7:8],
                                                    start=True, stop=True), waits=wT2, inc="T")
            vnc1 = P.do("scalar", lambda e: e.activation(negc[:, 1:2], pT[:, 0:1], AF.Identity,
                                                         scale=-(1.0 / 32.0) * 1.001),
                        waits=[("T", vbc)], inc="S")
            P.bank_consumed("pT", "S", vnc1)

            PVB = ["pC", "pD", "pE", "pF"]
            pv_war1 = [0, 0]
            qk_pend = {}

            def emit_qk(qb, kt):
                sb = ("pA", pA) if kt % 2 == 0 else ("pB", pB)
                pairs = [(big1[:, 4096 * kc + 128 * kt:4096 * kc + 128 * kt + 128],
                          big0[:, 4096 * kc + 256 * qb:4096 * kc + 256 * qb + 256])
                         for kc in range(KC)]
                vqk = P.mm_group(sb[1][:, 0:256], pairs, waits=[("S", vnc1), ("V", vqkv)],
                                 bankname=sb[0])
                qk_pend[(qb, kt)] = (sb, vqk)

            for qb in range(16):
                if qb == 0:
                    emit_qk(0, 0)
                pv_done = None
                for kt in range(NT):
                    sb, vqk = qk_pend.pop((qb, kt))
                    if kt + 1 < NT:
                        emit_qk(qb, kt + 1)
                    elif qb + 1 < 16:
                        emit_qk(qb + 1, 0)
                    ptile = stA[:, 4608 + 256 * (kt % 2):4608 + 256 * (kt % 2) + 256]
                    vex = P.do("scalar", lambda e, ptile=ptile, b=sb[1]: e.activation(
                        ptile, b[:, 0:256], AF.Exp, bias=negc[:, 1:2], scale=1.0 / 32.0),
                        waits=[("T", vqk), ("T", pv_war1[kt % 2])], inc="S")
                    P.bank_consumed(sb[0], "S", vex)
                    wz = [("S", vex)]
                    if kt == 0:
                        wz += P.bank_waits("pR")
                    P.do("tensor", lambda e, ptile=ptile, st=(kt == 0), sp=(kt == NT - 1): e.matmul(
                        pR[0:1, 0:256], ones_c[:], ptile, start=st, stop=sp), waits=wz, inc="T")
                    lastpv = None
                    for fc in range(KC):
                        bn = PVB[fc // 2]
                        col0 = 256 * (fc % 2)
                        wpv = []
                        if kt == 0 and fc % 2 == 0:
                            wpv += P.bank_waits(bn)
                        lvv = big2[:, 1024 * kt + 128 * fc:1024 * kt + 128 * fc + 128]
                        lastpv = P.do("tensor", lambda e, lvv=lvv, ptile=ptile, bn=bn, col0=col0,
                                      st=(kt == 0 and fc % 2 == 0),
                                      sp=(kt == NT - 1 and fc % 2 == 1): e.matmul(
                            PSUM[bn][:, col0:col0 + 256], lvv, ptile, start=st, stop=sp),
                            waits=wpv, inc="T" if fc == KC - 1 else None)
                    pv_war1[kt % 2] = lastpv
                    pv_done = lastpv
                rrow1 = stAf[0:1, 2816:3072]
                vln = P.do("scalar", lambda e: e.activation(
                    pR[0:1, 0:256], pR[0:1, 0:256], AF.Ln,
                    scale=float(2.0 ** 55)),
                    waits=[("T", P.cnt("T")), ("S", P.cnt("S")), ("V", vconstsV)], inc="S")
                vrc = P.do("scalar", lambda e: e.activation(
                    rrow1, pR[0:1, 0:256], AF.Exp, scale=-1.0,
                    bias=sc1[0:1, 14:15]),
                    waits=[("S", vln)], inc="S")
                P.bank_consumed("pR", "S", vrc)
                wbc = P.bank_waits("pT") + [("S", vrc)]
                vbc2 = P.do("tensor", lambda e: e.matmul(
                    pT[:, 0:256], ones_rf[:], rrow1, start=True, stop=True), waits=wbc, inc="T")
                for fc in range(KC):
                    bn = PVB[fc // 2]
                    col0 = 256 * (fc % 2)
                    ot = stA[:, 6144 + 256 * (fc % 2):6144 + 256 * (fc % 2) + 256]
                    voe = P.do("scalar", lambda e, ot=ot, bn=bn, col0=col0: e.copy(
                        ot, PSUM[bn][:, col0:col0 + 256]),
                        waits=[("T", pv_done), ("V", P.cnt("V"))], inc="S")
                    if fc % 2 == 1:
                        P.bank_consumed(bn, "S", voe)
                    dst = big0[:, 4096 * fc + 256 * qb:4096 * fc + 256 * qb + 256]
                    vml = P.do("vector", lambda e, dst=dst, ot=ot: e.tensor_mul(
                        dst, ot, pT[:, 0:256]), waits=[("S", voe), ("T", vbc2)], inc="V")
                P.bank_consumed("pT", "V", P.cnt("V"))

        phase += 1
        if phase <= upto:
            vs1 = P.cnt("V")
            b2f = big2[:].bitcast(F32)
            for j in range(NB):
                for c in range(KC):
                    sl = c % 2
                    fx = b2f[:, 8192 + 512 * sl:8192 + 512 * sl + 512]
                    fo = b2f[:, 9216 + 512 * sl:9216 + 512 * sl + 512]
                    gsl = big2[:, 20480 + 512 * sl:20480 + 512 * sl + 512]
                    prd = big2[:, 21504 + 512 * sl:21504 + 512 * sl + 512]
                    osm = "ob%d" % sl
                    lvx = P.load(fx, T["x1"][c * 128:(c + 1) * 128, j * 512:(j + 1) * 512],
                                 "lc0", waits=P.slot_waits(("cfx", sl)) + vx1 + [("V", vs1),
                                 ("T", P.cnt("T")), ("S", P.cnt("S"))])
                    lvg = P.load(gsl, T["g_s"][c * 128:(c + 1) * 128, j * 512:(j + 1) * 512],
                                 "lc1", waits=vgs + P.slot_waits(("csl", sl)) + [("V", vs1)])
                    vp2 = P.do("vector", lambda e, prd=prd, gsl=gsl, c=c, j=j: e.tensor_mul(
                        prd, gsl, big0[:, 4096 * c + 512 * j:4096 * c + 512 * j + 512]),
                        waits=[("lc1", lvg), ("V", vs1), ("S", P.cnt("S"))] + P.slot_waits(("cpr", sl)),
                        inc="V")
                    P.slot_mark(("csl", sl), "V", vp2)
                    vo = P.do("vector", lambda e, fo=fo, fx=fx, prd=prd: e.tensor_add(
                        fo, fx, prd),
                        waits=[("lc0", lvx), (osm, P.cnt(osm))], inc="V")
                    P.slot_mark(("cfx", sl), "V", vo)
                    P.slot_mark(("cpr", sl), "V", vo)
                    P.store(T["out"][c * 128:(c + 1) * 128, j * 512:(j + 1) * 512], fo,
                            osm, waits=[("V", vo)])

    if "x1d" in dbg and which == "full":
        P.do("gpsimd", lambda e: e.dma_start(out=T["dbg_x1d"], in_=T["x1"]),
             waits=vx1, inc="gd", n=16)
    final_waits = []
    for s in ("oa0", "oa1", "ob0", "ob1", "og", "ou", "od", "oq",
              "og1", "ou1", "od1", "oq1", "gd"):
        if P.cnt(s):
            final_waits.append((s, P.cnt(s)))
    for s, v in final_waits:
        P._wait("gpsimd", s, v)

    P.check()
    _maxsem = max(P.cnt(s) for s in P.sems)
    assert _maxsem < 60000, f"semaphore count overflow risk: {_maxsem}"

    nums = sorted(h.num for h, _ in P.sems.values())
    assert nums[-1] - nums[0] + 1 == len(nums), "sems not contiguous"
    nc.gpsimd.dma_reset(range(nums[0], nums[-1] + 1))
    nc.gpsimd.sem_clear(range(nums[0], nums[-1] + 1))
    nc.all_engine_barrier()

    with nc.Block() as block:
        @block.sync
        def _(e):
            for f in P.ops["sync"]:
                f(e)

        @block.tensor
        def _(e):
            for f in P.ops["tensor"]:
                f(e)

        @block.vector
        def _(e):
            for f in P.ops["vector"]:
                f(e)

        @block.scalar
        def _(e):
            for f in P.ops["scalar"]:
                f(e)

        @block.gpsimd
        def _(e):
            for f in P.ops["gpsimd"]:
                f(e)

    stack.close()
    return nc


def prep_inputs(inp, core):
    d = {}
    d["x"] = np.ascontiguousarray(inp["L0_attn_x"][core].T)
    d["wq"] = inp["L0_attn_wq"]
    d["wk"] = inp["L0_attn_wk"]
    d["wo"] = inp["L0_attn_wo"]
    wv = inp["L0_attn_wv"]
    wvp = np.zeros((D, H * (HD + 1)), np.float32)
    for h in range(H):
        wvp[:, h * 65:h * 65 + 64] = wv[:, h * 64:(h + 1) * 64]
    d["wv"] = wvp
    for nm, key in (("bq", "L0_attn_bq"), ("bk", "L0_attn_bk"), ("bo", "L0_attn_bo")):
        d[nm] = np.ascontiguousarray(inp[key].reshape(KC, 128).T)
    bv = inp["L0_attn_bv"]
    bvp = np.zeros((1, H * (HD + 1)), np.float32)
    for h in range(H):
        bvp[0, h * 65:h * 65 + 64] = bv[h * 64:(h + 1) * 64]
        bvp[0, h * 65 + 64] = 1.0
    d["bv"] = bvp
    d["wg"] = inp["L0_ffn_w_gate"]
    d["wu"] = inp["L0_ffn_w_up"]
    d["wd"] = inp["L0_ffn_w_down"]
    k = np.arange(128)[:, None]
    q = np.arange(512)[None, :]
    masks = np.zeros((4, 128, 512), np.float32)
    for dk in range(4):
        masks[dk] = (dk * 128 + k <= q).astype(np.float32)
    d["masks"] = masks
    d["ident"] = np.eye(128, dtype=np.float32)
    d["wqkv"] = inp["L1_attn_W_qkv"]

    def _arep(G1):
        A = np.transpose(G1[0], (1, 2, 0))
        A = np.broadcast_to(A[:, :, None, :, None], (4, 32, 16, 32, 16))
        return np.ascontiguousarray(A.reshape(4, 512, 512))

    def _brep(G2):
        B = np.transpose(G2[:, :, 0, :], (0, 2, 1))
        B = np.broadcast_to(B[:, None, :, None, :], (4, 32, 16, 32, 16))
        return np.ascontiguousarray(B.reshape(4, 512, 512))

    d["aR"] = _arep(inp["L1_ffn_G1_R"])
    d["aI"] = _arep(inp["L1_ffn_G1_I"])
    d["bR"] = _brep(inp["L1_ffn_G2_R"])
    d["bI"] = _brep(inp["L1_ffn_G2_I"])
    d["br"] = np.ascontiguousarray(inp["L1_ffn_bias_real"].reshape(4, 128).T)
    d["bi"] = np.ascontiguousarray(inp["L1_ffn_bias_imag"].reshape(4, 128).T)
    return d


def split_inputs(d, which="full"):
    l0 = {"x", "wq", "wk", "wv", "wo", "bq", "bk", "bv", "bo", "wg", "wu", "wd",
          "masks", "ident"}
    l1 = {"wqkv", "aR", "bR", "aI", "bI", "br", "bi", "ident"}
    if which == "l0":
        return {k: v for k, v in d.items() if k in l0}
    if which == "l1":
        return {k: v for k, v in d.items() if k in l1}
    return d


_NC_CACHE = [None]


def kernel(**inputs):
    import numpy as np
    from concourse.bass_utils import run_bass_kernel_spmd

    if _NC_CACHE[0] is None:
        _NC_CACHE[0] = build(which="full")
    nc = _NC_CACHE[0]
    in_maps = []
    for core in range(8):
        d = prep_inputs(inputs, core)
        in_maps.append(split_inputs(d, "full"))
    res = run_bass_kernel_spmd(nc, in_maps, core_ids=list(range(8)))
    outs = [np.asarray(res.results[c]["out"], dtype=np.float32).T for c in range(8)]
    return np.stack(outs, axis=0)
